# version 09
# baseline (speedup 1.0000x reference)
import os as _os
import numpy as np

import bass_rust
import concourse.bass as bass
import concourse.mybir as mybir
from concourse.bass_utils import run_bass_kernel_spmd
from concourse import library_config

F32 = mybir.dt.float32
F16 = mybir.dt.float16
I32 = mybir.dt.int32
ALU = mybir.AluOpType
ACT = mybir.ActivationFunctionType
RED = bass_rust.ReduceOp
AX = mybir.AxisListType

D = 1024
H = 16
DH = 64
DHR = 32
DLAT = 512
KVH = 4
FF = 4096

MAGIC32 = float(1.5 * 2.0**23)
MAGIC_MUL = float((1.5 * 2.0**52 - 1.5 * 2.0**23) * 2.0**30)



def _host_consts(S):
    import jax
    import jax.numpy as jnp
    cpu = jax.devices("cpu")[0]
    with jax.default_device(cpu):
        d = DHR
        inv_freq = jnp.asarray(10000.0) ** (-jnp.arange(0, d, 2, dtype=jnp.float32) / d)
        pos = jnp.arange(S)
        ang = pos.astype(jnp.float32)[:, None] * inv_freq[None, :]
        cos = np.asarray(jnp.cos(ang), dtype=np.float32)
        sin = np.asarray(jnp.sin(ang), dtype=np.float32)
        pwl_x = jnp.linspace(-4.0, 4.0, 33, dtype=jnp.float32)
        pwl_y = 0.5 * pwl_x * (1.0 + jax.lax.erf(pwl_x / jnp.sqrt(jnp.float32(2.0))))
        X = np.asarray(pwl_x, dtype=np.float32)
        Y = np.asarray(pwl_y, dtype=np.float32)

    cosb = np.ascontiguousarray(np.tile(cos.T, (8, 1)).astype(np.float32))
    sinb = np.ascontiguousarray(np.tile(sin.T, (8, 1)).astype(np.float32))
    m01 = np.triu(np.ones((128, 128), dtype=np.float16))

    start = 2.0 ** (-2.0 ** (-(np.log2(H) - 3.0)))
    slopes = (start ** np.arange(1, H + 1)).astype(np.float32)

    WM = 2 * S - 128
    relc = np.minimum((S - 128) - np.arange(WM, dtype=np.float64)[None, :]
                      + np.arange(128, dtype=np.float64)[:, None], 0.0).astype(np.float32)
    ident = np.eye(128, dtype=np.float16)

    Y64 = Y.astype(np.float64)
    m = (Y64[1:] - Y64[:-1]) * 4.0
    mv = m / 256.0
    w = mv[1:] - mv[:-1]
    kxv = (X[1:32].astype(np.float64) * 256.0)
    c1 = mv[0]
    c0 = Y64[0] + 1024.0 * mv[0] - float(np.sum(w * kxv))

    def hilo(v):
        hi = np.float32(v)
        lo = np.float32(np.float64(v) - np.float64(hi))
        return float(hi), float(lo)

    pwl = {
        "c0": hilo(float(Y64[0] + 1024.0 * mv[0])),
        "c1": hilo(c1),
        "w": [hilo(x) for x in w],
        "kx": [float(x) for x in kxv],
        "eu_w": hilo((1.0 - m[-1]) / 256.0),
        "eu_j": hilo(np.float64(np.float32(4.0)) - Y64[32]),
        "el_w": hilo(-mv[0]),
        "el_c": hilo(-mv[0] * 1024.0),
        "el_j": hilo(-Y64[0]),
    }
    return dict(cosb=cosb, sinb=sinb, m01=m01, slopes=slopes, pwl=pwl,
                relc=relc, ident=ident)



class Region:
    __slots__ = ("name", "writer", "readers")

    def __init__(self, name):
        self.name = name
        self.writer = None
        self.readers = []


class Prog:
    ENGS = ("TE", "VE", "SE", "GE", "SQ")
    NDS = 8
    SEMS = (("TE", "VE", "SE", "GEc", "SQc")
            + tuple(f"GEd{i}" for i in range(8)) + tuple(f"SQd{i}" for i in range(8)))

    def __init__(self, nc):
        self.nc = nc
        self.cnt = {sm: 0 for sm in self.SEMS}
        self.ops = {e: [] for e in self.ENGS}
        self.waited = {e: {sm: 0 for sm in self.SEMS} for e in self.ENGS}
        self._dma_rr = {"GE": 0, "SQ": 0}

    fence = None

    def _sem(self, eng, dma):
        if eng in ("GE", "SQ"):
            if dma:
                i = self._dma_rr[eng]
                self._dma_rr[eng] = (i + 1) % self.NDS
                return f"{eng}d{i}"
            return eng + "c"
        return eng

    def op(self, eng, fn, reads=(), writes=(), dma=False):
        waits = {}

        def dep(tok):
            if tok is None:
                return
            sm, c2 = tok
            if self.waited[eng][sm] < c2:
                waits[sm] = max(waits.get(sm, 0), c2)

        if self.fence:
            for sm, c2 in self.fence.items():
                dep((sm, c2))
        for r in reads:
            dep(r.writer)
        for r in writes:
            dep(r.writer)
            for t in r.readers:
                dep(t)
        sm = self._sem(eng, dma)
        if dma:
            dep((sm, self.cnt[sm]))
        for s2, c2 in waits.items():
            self.waited[eng][s2] = c2
        inc = 16 if dma else 1
        self.cnt[sm] += inc
        tok = (sm, self.cnt[sm])
        for r in reads:
            r.readers.append(tok)
        for r in writes:
            r.writer = tok
            r.readers = []
        self.ops[eng].append((waits, fn, sm, inc))
        return tok

    def check_deadlock(self):
        done = {sm: 0 for sm in self.SEMS}
        idx = {e: 0 for e in self.ENGS}
        total = sum(len(v) for v in self.ops.values())
        ndone = 0
        progress = True
        while progress:
            progress = False
            for e in self.ENGS:
                while idx[e] < len(self.ops[e]):
                    waits, _, sm, inc = self.ops[e][idx[e]]
                    if all(done[s2] >= c2 for s2, c2 in waits.items()):
                        done[sm] += inc
                        idx[e] += 1
                        ndone += 1
                        progress = True
                    else:
                        break
        if ndone != total:
            stuck = {e: (idx[e], len(self.ops[e]), self.ops[e][idx[e]][0])
                     for e in self.ENGS if idx[e] < len(self.ops[e])}
            raise RuntimeError(f"emission deadlock: {stuck}")

    def emit(self, block, sems):
        handles = {"TE": self.nc.tensor, "VE": self.nc.vector, "SE": self.nc.scalar,
                   "GE": self.nc.gpsimd, "SQ": self.nc.sync}

        def mk(eng):
            h = handles[eng]

            def body(_):
                for waits, fn, sm, inc in self.ops[eng]:
                    for s2 in sorted(waits):
                        h.wait_ge(sems[s2], waits[s2])
                    fn(h).then_inc(sems[sm], inc)
            return body

        for eng, deco in (("SQ", block.sync), ("TE", block.tensor),
                          ("VE", block.vector), ("SE", block.scalar),
                          ("GE", block.gpsimd)):
            if self.ops[eng]:
                deco(mk(eng))


class Tile:
    __slots__ = ("view", "c0", "P", "C", "alloc_idx", "arena")

    def __init__(self, view, c0, P, C, alloc_idx, arena):
        self.view = view
        self.c0 = c0
        self.P = P
        self.C = C
        self.alloc_idx = alloc_idx
        self.arena = arena

    def __getitem__(self, idx):
        rows, cols = idx
        if isinstance(rows, slice):
            r0 = rows.start or 0
            r1 = self.P if rows.stop is None else rows.stop
        else:
            r0, r1 = rows, rows + 1
        if isinstance(cols, slice):
            cs0 = cols.start or 0
            cs1 = self.C if cols.stop is None else cols.stop
            st = cols.step
            if st is None:
                return self.view[r0:r1, self.c0 + cs0: self.c0 + cs1]
            return self.view[r0:r1, self.c0 + cs0: self.c0 + cs1: st]
        return self.view[r0:r1, self.c0 + cols: self.c0 + cols + 1]


class Arena:

    def __init__(self, view, n):
        self.view = view
        self.free_list = [(0, n)]
        self.n = n
        self.used = {}

    def alloc(self, name, P, C):
        need = (C + 31) & ~31
        for i, (off, ln) in enumerate(self.free_list):
            if ln >= need:
                if ln == need:
                    self.free_list.pop(i)
                else:
                    self.free_list[i] = (off + need, ln - need)
                t = Tile(self.view, off, P, C, None, self)
                self.used[id(t)] = (off, need)
                return t
        raise MemoryError(f"arena OOM for {name}: need {need}, free {self.free_list}")

    def release(self, t):
        off, need = self.used.pop(id(t))
        self.free_list.append((off, need))
        self.free_list.sort()
        merged = []
        for off, ln in self.free_list:
            if merged and merged[-1][0] + merged[-1][1] == off:
                merged[-1] = (merged[-1][0], merged[-1][1] + ln)
            else:
                merged.append((off, ln))
        self.free_list = [tuple(x) for x in merged]


class Ctx:
    def __init__(self, nc, S, consts):
        self.nc = nc
        self.S = S
        self.consts = consts
        self.p = Prog(nc)
        self._cms = []
        self._psum_stack = []
        self._n = 0
        self.t = {}
        self.r = {}
        self.dram = {}
        self.dreg = {}
        self.flex = {}
        self.flex_order = []

    def perm(self, name, shape, dtype):
        self._n += 1
        cm = self.nc.sbuf_tensor(f"{name}_{self._n}", list(shape), dtype)
        t = cm.__enter__()
        self._cms.append(cm)
        return t

    def open_flex(self, n32=0, n16=0):
        assert not self.flex, "flex already open"
        self._n += 1
        if n32:
            cm = self.nc.sbuf_tensor(f"fx32_{self._n}", [128, n32], F32)
            self.flex[F32] = (Arena(cm.__enter__(), n32), cm)
            self.flex_order.append(F32)
        if n16:
            cm = self.nc.sbuf_tensor(f"fx16_{self._n}", [128, n16], F16)
            self.flex[F16] = (Arena(cm.__enter__(), n16), cm)
            self.flex_order.append(F16)

    def close_flex(self):
        self._fence_all()
        for dt in reversed(self.flex_order):
            self.flex[dt][1].__exit__(None, None, None)
        self.flex = {}
        self.flex_order = []

    def sbuf(self, name, shape, dtype):
        assert len(shape) == 2
        if dtype == I32:
            dtype = F32
        ar = self.flex[dtype][0]
        return ar.alloc(name, shape[0], shape[1])

    def sbuf_i32(self, name, shape):
        self._n += 1
        cm = self.nc.sbuf_tensor(f"{name}_{self._n}", list(shape), I32)
        t = cm.__enter__()
        self._psum_stack.append((t, cm))
        return t

    def psum(self, name, shape, dtype=F32):
        self._n += 1
        cm = self.nc.psum_tensor(f"{name}_{self._n}", list(shape), dtype)
        t = cm.__enter__()
        self._psum_stack.append((t, cm))
        return t

    def _fence_all(self):
        fence = dict(self.p.fence or {})
        for name, r in self.r.items():
            for tok in ([r.writer] if r.writer else []) + r.readers:
                e2, c2 = tok
                fence[e2] = max(fence.get(e2, 0), c2)
        self.p.fence = fence

    def free(self, *tensors):
        self._fence_all()
        psums = [t for t in tensors if not isinstance(t, Tile)]
        if psums:
            order = {id(t): i for i, (t, _) in enumerate(self._psum_stack)}
            for t in sorted(psums, key=lambda t: -order[id(t)]):
                i = next(i for i, (tt, _) in enumerate(self._psum_stack) if tt is t)
                _, cm = self._psum_stack.pop(i)
                cm.__exit__(None, None, None)
        for t in tensors:
            if isinstance(t, Tile):
                t.arena.release(t)

    def reg(self, name):
        if name not in self.r:
            self.r[name] = Region(name)
        return self.r[name]


def TS(c, out, in0, s1, s2, op0, op1, reads, writes, eng="VE"):
    return c.p.op(eng, lambda e: e.tensor_scalar(out, in0, s1, s2, op0, op1),
                  reads=reads, writes=writes)


def TT(c, out, in0, in1, op, reads, writes, eng="VE"):
    return c.p.op(eng, lambda e: e.tensor_tensor(out, in0, in1, op),
                  reads=reads, writes=writes)


def STT(c, out, in0, scalar, in1, op0, op1, reads, writes):
    return c.p.op("VE", lambda e: e.scalar_tensor_tensor(out, in0, scalar, in1, op0, op1),
                  reads=reads, writes=writes)


def CP(c, out, in0, reads, writes, eng="VE"):
    if eng == "SE":
        return c.p.op(eng, lambda e: e.activation(out, in0, ACT.Copy),
                      reads=reads, writes=writes)
    return c.p.op(eng, lambda e: e.tensor_copy(out, in0), reads=reads, writes=writes)


def ACTV(c, out, in0, func, reads, writes, bias=None, scale=None):
    kw = {}
    if bias is not None:
        kw["bias"] = bias
    if scale is not None:
        kw["scale"] = scale
    return c.p.op("SE", lambda e: e.activation(out, in0, func, **kw),
                  reads=reads, writes=writes)


def MM(c, psum_ap, lhsT_ap, rhs_ap, first, last, reads, writes, tp=None):
    kw = {}
    if tp is not None:
        kw["tile_position"] = tp
    return c.p.op("TE", lambda e: e.matmul(psum_ap, lhsT_ap, rhs_ap,
                                           start=first, stop=last, **kw),
                  reads=reads, writes=writes)


def MEMSET(c, ap, val, writes, eng="VE"):
    return c.p.op(eng, lambda e: e.memset(ap, val), reads=[], writes=writes)


def DMA(c, out_ap, in_ap, reads, writes, eng="SQ"):
    return c.p.op(eng, lambda e: e.dma_start(out=out_ap, in_=in_ap),
                  reads=reads, writes=writes, dma=True)


def RECIP(c, out, in0, reads, writes):
    return c.p.op("VE", lambda e: e.reciprocal(out, in0), reads=reads, writes=writes)


def round_ip(c, ap, reads, writes):
    ceff = c.t["Ceff"]
    return TS(c, ap, ap, ceff, ceff, ALU.add, ALU.subtract,
              reads=list(reads) + [c.reg("consts")], writes=writes)


def load_w16(c, name, key, rows, cols, scale=None):
    nch = rows // 128
    st = c.sbuf(name + "_st", [128, cols], F32)
    wt = c.sbuf(name, [128, nch * cols], F16)
    reg = c.reg(name)
    rst = c.reg(name + "_st")
    for ch in range(nch):
        DMA(c, st[:, :], c.dram[key][ch * 128:(ch + 1) * 128, :], reads=[rst], writes=[rst])
        if scale is None:
            CP(c, wt[:, ch * cols:(ch + 1) * cols], st[:, :], reads=[rst], writes=[reg])
        else:
            TS(c, wt[:, ch * cols:(ch + 1) * cols], st[:, :], float(scale), None,
               ALU.mult, ALU.bypass, [rst], [reg])
    c.free(st)
    return wt, reg



def setup_consts(c):
    cst = c.perm("cst", [128, 56], F32)
    c.t["cst"] = cst
    R = c.reg("consts")
    MEMSET(c, cst[:, 0:1], 2.0**-30, writes=[R])
    TS(c, cst[:, 0:1], cst[:, 0:1], 2.0**22, 2.0**22, ALU.add, ALU.subtract, [R], [R])
    TS(c, cst[:, 1:2], cst[:, 0:1], MAGIC_MUL, MAGIC32, ALU.mult, ALU.add, [R], [R])
    c.t["Ceff"] = cst[:, 1:2]
    sqrt96 = float(np.float32(np.sqrt(np.float64(96.0))))
    MEMSET(c, cst[:, 2:3], sqrt96, writes=[R])
    RECIP(c, cst[:, 2:3], cst[:, 2:3], [R], [R])
    TS(c, cst[:, 3:4], cst[:, 2:3], -80.0, None, ALU.mult, ALU.bypass, [R], [R])
    c.t["rc96"] = cst[:, 2:3]
    c.t["negC0"] = cst[:, 3:4]
    MEMSET(c, cst[:, 4:5], 1e-5, writes=[R])
    c.t["epsc"] = cst[:, 4:5]
    pw = c.consts["pwl"]
    coeffs = ([pw["c0"], pw["c1"]] + pw["w"]
              + [pw["eu_w"], pw["eu_j"], pw["el_w"], pw["el_c"], pw["el_j"]])
    for i, (hi, lo) in enumerate(coeffs):
        TS(c, cst[:, 8 + i:9 + i], cst[:, 0:1], float(lo) * 2.0**30, float(hi),
           ALU.mult, ALU.add, [R], [R])
    c.PWL0 = 8

    ones16 = c.perm("ones16", [128, 1], F16)
    MEMSET(c, ones16[:, :], 1.0, writes=[R])
    c.t["ones16"] = ones16
    onesrow = c.perm("onesrow", [1, 128], F16)
    MEMSET(c, onesrow[:, :], 1.0, writes=[R])
    c.t["onesrow"] = onesrow
    onesrow32 = c.perm("onesrow32", [1, 128], F32)
    MEMSET(c, onesrow32[:, :], 1.0, writes=[R])
    c.t["onesrow32"] = onesrow32
    ones32c = c.perm("ones32c", [128, 1], F32)
    MEMSET(c, ones32c[:, :], 1.0, writes=[R])
    c.t["ones32c"] = ones32c
    mbar = c.perm("mbar", [128, 8], F32)
    c.t["mbar"] = mbar
    hp = c.perm("hpones", [128, 2], F16)
    MEMSET(c, hp[:, :], 0.0, writes=[R])
    MEMSET(c, hp[0:64, 0:1], 1.0, writes=[R])
    MEMSET(c, hp[64:128, 1:2], 1.0, writes=[R])
    c.t["hpones"] = hp



def post_ln(c, s32, rR, s16, rD, tag):
    S = c.S
    TB = min(512, S)
    NTB = S // TB
    r16 = c.sbuf(f"lnr{tag}", [128, 8 * S], F16)
    sq = c.sbuf(f"lnq{tag}", [128, 8 * S], F16)
    rT = c.reg(f"lnr{tag}")
    rQ = c.reg(f"lnq{tag}")
    for ch in range(8):
        sl = slice(ch * S, (ch + 1) * S)
        CP(c, r16[:, sl], s32[:, sl], reads=[rR], writes=[rT])
        ACTV(c, sq[:, sl], s32[:, sl], ACT.Square, reads=[rR], writes=[rQ])
    stat = c.sbuf(f"lns{tag}", [1, 4 * TB], F32)
    st16 = c.sbuf(f"ln6{tag}", [1, 2 * TB], F16)
    rS = c.reg(f"lns{tag}")
    rS16 = c.reg(f"ln6{tag}")
    ps1 = c.psum(f"lp1{tag}", [1, TB])
    ps2 = c.psum(f"lp2{tag}", [1, TB])
    rP1 = c.reg(f"lp1{tag}")
    rP2 = c.reg(f"lp2{tag}")
    psb1 = c.psum(f"lb1{tag}", [128, TB])
    psb2 = c.psum(f"lb2{tag}", [128, TB])
    rB1 = c.reg(f"lb1{tag}")
    rB2 = c.reg(f"lb2{tag}")
    ones = c.t["ones16"]
    onesrow = c.t["onesrow"]
    rC = c.reg("consts")
    eps = 1e-5
    m_ = stat[0:1, 0:TB]
    v_ = stat[0:1, TB:2 * TB]
    s_ = stat[0:1, 2 * TB:3 * TB]
    w_ = stat[0:1, 3 * TB:4 * TB]
    for tb in range(NTB):
        for ch in range(8):
            sl = slice(ch * S + tb * TB, ch * S + (tb + 1) * TB)
            MM(c, ps1[0:1, :], ones[:, 0:1], r16[:, sl], ch == 0, ch == 7,
               reads=[rT, rC], writes=[rP1])
            MM(c, ps2[0:1, :], ones[:, 0:1], sq[:, sl], ch == 0, ch == 7,
               reads=[rQ, rC], writes=[rP2])
        TS(c, m_, ps1[0:1, :], 1.0 / D, None, ALU.mult, ALU.bypass, [rP1], [rS])
        TS(c, v_, ps2[0:1, :], 1.0 / D, None, ALU.mult, ALU.bypass, [rP2], [rS])
        TT(c, w_, m_, m_, ALU.mult, [rS], [rS])
        TT(c, v_, v_, w_, ALU.subtract, [rS], [rS])
        ACTV(c, s_, v_, ACT.Sqrt, reads=[rS, c.reg("consts")], writes=[rS], bias=c.t["cst"][0:1, 4:5])
        RECIP(c, w_, s_, [rS], [rS])
        TS(c, v_, v_, eps, None, ALU.add, ALU.bypass, [rS], [rS])
        TT(c, w_, v_, w_, ALU.mult, [rS], [rS])
        TT(c, w_, w_, s_, ALU.add, [rS], [rS])
        TS(c, w_, w_, 0.5, None, ALU.mult, ALU.bypass, [rS], [rS])
        RECIP(c, w_, w_, [rS], [rS])
        CP(c, st16[0:1, 0:TB], m_, reads=[rS], writes=[rS16])
        CP(c, st16[0:1, TB:2 * TB], w_, reads=[rS], writes=[rS16])
        MM(c, psb1[:, :], onesrow[0:1, :], st16[0:1, 0:TB], True, True,
           reads=[rS16, rC], writes=[rB1])
        MM(c, psb2[:, :], onesrow[0:1, :], st16[0:1, TB:2 * TB], True, True,
           reads=[rS16, rC], writes=[rB2])
        for ch in range(8):
            sl = slice(ch * S + tb * TB, ch * S + (tb + 1) * TB)
            TT(c, s32[:, sl], s32[:, sl], psb1[:, :], ALU.subtract, [rR, rB1], [rR])
            TT(c, s32[:, sl], s32[:, sl], psb2[:, :], ALU.mult, [rR, rB2], [rR])
            if s16 is not None:
                ACTV(c, s16[:, sl], s32[:, sl], ACT.Copy, reads=[rR], writes=[rD])
    c.free(r16, sq, stat, st16, psb2, psb1, ps2, ps1)



def l0_attn(c, s16, rS16):
    S = c.S
    TB = min(512, S)
    NTB = S // TB
    NSC = S // 128
    p = c.p
    rC = c.reg("consts")
    rc96 = c.t["rc96"]

    xst = c.sbuf("xst", [128, S], F32)
    rXst = c.reg("xst")
    for ch in range(8):
        DMA(c, xst[:, :], c.dram["xT"][ch * 128:(ch + 1) * 128, :],
            reads=[rXst], writes=[rXst])
        CP(c, s16[:, ch * S:(ch + 1) * S], xst[:, :], reads=[rXst], writes=[rS16])
    c.free(xst)
    xt16 = s16
    rX16 = rS16

    ps = c.psum("l0p", [128, TB])
    rPS = c.reg("l0p")

    wdq, rwdq = load_w16(c, "wdq", "wdqT", D, DLAT)
    wdkv, rwdkv = load_w16(c, "wdkv", "wdkvT", D, DLAT)
    cq = c.sbuf("cq16", [128, 4 * S], F16)
    ckv = c.sbuf("ckv16", [128, 4 * S], F16)
    rCQ = c.reg("cq16")
    rCKV = c.reg("ckv16")
    for dst, wsrc, rw, rdst in ((cq, wdq, rwdq, rCQ), (ckv, wdkv, rwdkv, rCKV)):
        for lc in range(4):
            for tb in range(NTB):
                for ch in range(8):
                    MM(c, ps[:, :], wsrc[:, ch * DLAT + lc * 128: ch * DLAT + (lc + 1) * 128],
                       xt16[:, ch * S + tb * TB: ch * S + (tb + 1) * TB],
                       ch == 0, ch == 7, reads=[rw, rX16], writes=[rPS])
                CP(c, dst[:, lc * S + tb * TB: lc * S + (tb + 1) * TB], ps[:, :],
                   reads=[rPS], writes=[rdst], eng="SE")
    c.free(wdq, wdkv)

    wuq, rwuq = load_w16(c, "wuq", "wuqT", DLAT, H * DH)
    qc = c.sbuf("qc16", [128, 8 * S], F16)
    rQC = c.reg("qc16")
    for oc in range(8):
        for tb in range(NTB):
            for lc in range(4):
                MM(c, ps[:, :], wuq[:, lc * (H * DH) + oc * 128: lc * (H * DH) + (oc + 1) * 128],
                   cq[:, lc * S + tb * TB: lc * S + (tb + 1) * TB],
                   lc == 0, lc == 3, reads=[rwuq, rCQ], writes=[rPS])
            TS(c, qc[:, oc * S + tb * TB: oc * S + (tb + 1) * TB], ps[:, :],
               rc96, None, ALU.mult, ALU.bypass, [rPS, rC], [rQC])
    c.free(wuq)

    wqr, rwqr = load_w16(c, "wqr", "wqrT", DLAT, H * DHR)
    cosb = c.sbuf("cosb", [128, S], F32)
    sinb = c.sbuf("sinb", [128, S], F32)
    rCos = c.reg("cosb")
    DMA(c, cosb[:, :], c.dram["cosb"], reads=[rCos], writes=[rCos])
    DMA(c, sinb[:, :], c.dram["sinb"], reads=[rCos], writes=[rCos])
    scr = c.sbuf("ropes", [128, 2 * S], F32)
    rScr = c.reg("ropes")
    qrev = c.sbuf("qrev", [128, S], F32)
    qrod = c.sbuf("qrod", [128, S], F32)
    rQRE = c.reg("qrev")
    rQRO = c.reg("qrod")
    qrev16 = c.sbuf("qrev16", [128, 2 * S], F16)
    qrod16 = c.sbuf("qrod16", [128, 2 * S], F16)
    rQE16 = c.reg("qrev16")
    rQO16 = c.reg("qrod16")

    def rope(np_, ev, od, ev16, od16, rE, rO, rE16, rO16, cs, sn):
        t1 = scr[0:np_, 0:S]
        t2 = scr[0:np_, S:2 * S]
        TT(c, t1, ev, cs, ALU.mult, [rE, rCos], [rScr])
        TT(c, t2, od, sn, ALU.mult, [rO, rCos], [rScr])
        TT(c, ev16, t1, t2, ALU.subtract, [rScr], [rE16])
        TT(c, t1, ev, sn, ALU.mult, [rE, rCos], [rScr])
        TT(c, t2, od, cs, ALU.mult, [rO, rCos], [rScr])
        TT(c, od16, t1, t2, ALU.add, [rScr], [rO16])

    for oc in range(2):
        for half, dst, rdst in ((0, qrev, rQRE), (1, qrod, rQRO)):
            for tb in range(NTB):
                for lc in range(4):
                    col = lc * (H * DHR) + half * 256 + oc * 128
                    MM(c, ps[:, :], wqr[:, col: col + 128],
                       cq[:, lc * S + tb * TB: lc * S + (tb + 1) * TB],
                       lc == 0, lc == 3, reads=[rwqr, rCQ], writes=[rPS])
                TS(c, dst[:, tb * TB:(tb + 1) * TB], ps[:, :],
                   rc96, None, ALU.mult, ALU.bypass, [rPS, rC], [rdst])
        sl = slice(oc * S, (oc + 1) * S)
        rope(128, qrev[:, :], qrod[:, :], qrev16[:, sl], qrod16[:, sl],
             rQRE, rQRO, rQE16, rQO16, cosb[:, :], sinb[:, :])
    c.free(wqr, qrev, qrod, cq)

    qrt2 = c.sbuf("qrt2", [128, 4 * S], F16)
    rQR2 = c.reg("qrt2")
    for h in range(H):
        cc, rr = divmod(h, 4)
        sch, sr = divmod(h, 8)
        DMA(c, qrt2[rr * 32: rr * 32 + 16, cc * S:(cc + 1) * S],
            qrev16[sr * 16: sr * 16 + 16, sch * S:(sch + 1) * S],
            reads=[rQE16], writes=[rQR2], eng="GE")
        DMA(c, qrt2[rr * 32 + 16: rr * 32 + 32, cc * S:(cc + 1) * S],
            qrod16[sr * 16: sr * 16 + 16, sch * S:(sch + 1) * S],
            reads=[rQO16], writes=[rQR2], eng="GE")
    c.free(qrev16, qrod16)

    wkr, rwkr = load_w16(c, "wkr", "wkrT", D, DHR)
    krev = c.sbuf("krev", [16, S], F32)
    krod = c.sbuf("krod", [16, S], F32)
    rKRE = c.reg("krev")
    rKRO = c.reg("krod")
    psk = c.psum("l0pk", [16, TB])
    rPSK = c.reg("l0pk")
    for half, dst, rdst in ((0, krev, rKRE), (1, krod, rKRO)):
        for tb in range(NTB):
            for ch in range(8):
                MM(c, psk[0:16, :], wkr[:, ch * DHR + half * 16: ch * DHR + half * 16 + 16],
                   xt16[:, ch * S + tb * TB: ch * S + (tb + 1) * TB],
                   ch == 0, ch == 7, reads=[rwkr, rX16], writes=[rPSK])
            CP(c, dst[0:16, tb * TB:(tb + 1) * TB], psk[0:16, :], reads=[rPSK],
               writes=[rdst], eng="SE")
    krev16 = c.sbuf("krev16", [16, S], F16)
    krod16 = c.sbuf("krod16", [16, S], F16)
    rKE16 = c.reg("krev16")
    rKO16 = c.reg("krod16")
    rope(16, krev[0:16, :], krod[0:16, :], krev16[0:16, :], krod16[0:16, :],
         rKRE, rKRO, rKE16, rKO16, cosb[0:16, :], sinb[0:16, :])
    c.free(wkr, krev, krod, scr, cosb, sinb, psk)
    krt2 = c.sbuf("krt2", [128, S], F16)
    rKR2 = c.reg("krt2")
    for b in range(4):
        DMA(c, krt2[b * 32: b * 32 + 16, :], krev16[0:16, :], reads=[rKE16],
            writes=[rKR2], eng="GE")
        DMA(c, krt2[b * 32 + 16: b * 32 + 32, :], krod16[0:16, :], reads=[rKO16],
            writes=[rKR2], eng="GE")
    c.free(krev16, krod16)

    wuk, rwuk = load_w16(c, "wuk", "wukT", DLAT, H * DH)
    kc = c.sbuf("kc16", [128, 8 * S], F16)
    rKC = c.reg("kc16")
    for oc in range(8):
        for tb in range(NTB):
            for lc in range(4):
                MM(c, ps[:, :], wuk[:, lc * (H * DH) + oc * 128: lc * (H * DH) + (oc + 1) * 128],
                   ckv[:, lc * S + tb * TB: lc * S + (tb + 1) * TB],
                   lc == 0, lc == 3, reads=[rwuk, rCKV], writes=[rPS])
            CP(c, kc[:, oc * S + tb * TB: oc * S + (tb + 1) * TB], ps[:, :],
               reads=[rPS], writes=[rKC], eng="SE")
    c.free(wuk)

    wuv, rwuv = load_w16(c, "wuv", "wuvT", DLAT, H * DH)
    m01 = c.sbuf("m01", [128, 128], F16)
    rM01 = c.reg("m01")
    DMA(c, m01[:, :], c.dram["m01c"], reads=[rM01], writes=[rM01])
    vb = [c.sbuf(f"l0v{i}", [128, NSC * 66], F16) for i in range(2)]
    rVb = [c.reg(f"l0v{i}") for i in range(2)]
    for i in range(2):
        MEMSET(c, vb[i][:, 64::66], 1.0, writes=[rVb[i]])
    pexp2 = [c.sbuf(f"l0pe{i}", [128, TB], F16) for i in range(2)]
    rPexp2 = [c.reg(f"l0pe{i}") for i in range(2)]
    dn = c.sbuf("l0dn", [1, 3 * TB], F32)
    rDn = c.reg("l0dn")
    dn16 = c.sbuf("l0d6", [1, TB], F16)
    rDn16 = c.reg("l0d6")
    bcs = c.sbuf("l0bc", [64, TB], F32)
    rBcs = c.reg("l0bcs")
    aodd = c.sbuf("l0ao", [64, TB], F16)
    rAodd = c.reg("l0ao")
    atp0 = c.sbuf("l0at", [64, TB], F32)
    rAtp0 = c.reg("l0at")
    psS2 = [c.psum(f"l0sc{i}", [128, TB]) for i in range(2)]
    rSc2 = [c.reg(f"l0sc{i}") for i in range(2)]
    psPV2 = [c.psum(f"l0pv{i}", [65, TB]) for i in range(2)]
    rPV2 = [c.reg(f"l0pv{i}") for i in range(2)]
    psV2 = [c.psum(f"l0vv{i}", [128, 64]) for i in range(2)]
    rPsV2 = [c.reg(f"l0vv{i}") for i in range(2)]
    psB = c.psum("l0bb", [64, TB])
    rB = c.reg("l0bb")
    negC0 = c.t["negC0"]
    onesrow = c.t["onesrow"]
    explo = float(np.exp(np.float64(-160.0) * (1.0 / np.float64(np.float32(np.sqrt(np.float64(96.0)))))))

    tails = [None]

    def flush_tail():
        if tails[0] is None:
            return
        kch, kro, tb, psPV, rPV = tails[0]
        tails[0] = None
        CP(c, dn[0:1, 0:TB], psPV[64:65, :], reads=[rPV], writes=[rDn])
        ACTV(c, dn[0:1, TB:2 * TB], dn[0:1, 0:TB], ACT.Ln, reads=[rDn], writes=[rDn])
        ACTV(c, dn[0:1, 2 * TB:3 * TB], dn[0:1, TB:2 * TB], ACT.Exp,
             reads=[rDn], writes=[rDn], scale=-0.5)
        CP(c, dn16[0:1, :], dn[0:1, 2 * TB:3 * TB], reads=[rDn], writes=[rDn16])
        MM(c, psB[0:64, :], onesrow[0:1, 0:64], dn16[0:1, :], True, True,
           reads=[rDn16, rC], writes=[rB])
        CP(c, bcs[:, :], psB[0:64, :], reads=[rB], writes=[rBcs], eng="SE")
        TT(c, atp0[0:64, :], psPV[0:64, :], bcs[:, :], ALU.mult, [rPV, rBcs],
           [rAtp0])
        if kro == 0:
            TT(c, s16[0:64, kch * S + tb * TB: kch * S + (tb + 1) * TB],
               atp0[0:64, :], bcs[:, :], ALU.mult, [rAtp0, rBcs], [rS16])
        else:
            TT(c, aodd[0:64, 0:TB], atp0[0:64, :], bcs[:, :], ALU.mult,
               [rAtp0, rBcs], [rAodd])
            DMA(c, s16[64:128, kch * S + tb * TB: kch * S + (tb + 1) * TB],
                aodd[0:64, 0:TB], reads=[rAodd], writes=[rS16], eng="GE")

    for h in range(H):
        vcur, rV = vb[h % 2], rVb[h % 2]
        for tc in range(NSC):
            psV, rPsV = psV2[tc % 2], rPsV2[tc % 2]
            for lc in range(4):
                MM(c, psV[:, :], ckv[:, lc * S + tc * 128: lc * S + (tc + 1) * 128],
                   wuv[:, lc * (H * DH) + h * DH: lc * (H * DH) + h * DH + DH],
                   lc == 0, lc == 3, reads=[rCKV, rwuv], writes=[rPsV])
            CP(c, vcur[:, tc * 66: tc * 66 + 64], psV[:, :], reads=[rPsV],
               writes=[rV], eng="SE")
        kch, kro = divmod(h, 2)
        qrc, qrr = divmod(h, 4)
        base = kro * 64
        rbase = qrr * 32
        for tb in range(NTB):
            nsc = (tb + 1) * TB // 128
            pv = (h * NTB + tb) % 2
            psPV, rPV = psPV2[pv], rPV2[pv]
            for sc in range(nsc):
                pi = sc % 2
                psS, rSc = psS2[pi], rSc2[pi]
                pexp, rPexp = pexp2[pi], rPexp2[pi]
                t0 = max(tb * TB, sc * 128)
                w = (tb + 1) * TB - t0
                MM(c, psS[0:128, 0:w],
                   kc[base:base + 64, kch * S + sc * 128: kch * S + (sc + 1) * 128],
                   qc[base:base + 64, kch * S + t0: kch * S + t0 + w],
                   True, False, reads=[rKC, rQC], writes=[rSc], tp=(base, 0))
                MM(c, psS[0:128, 0:w],
                   krt2[rbase:rbase + 32, sc * 128:(sc + 1) * 128],
                   qrt2[rbase:rbase + 32, qrc * S + t0: qrc * S + t0 + w],
                   False, True, reads=[rKR2, rQR2], writes=[rSc], tp=(rbase, 0))
                if sc == 1:
                    flush_tail()
                ACTV(c, pexp[:, 0:w], psS[0:128, 0:w], ACT.Exp,
                     reads=[rSc], writes=[rPexp], bias=negC0)
                TS(c, pexp[:, 0:w], pexp[:, 0:w], 1.0, explo,
                   ALU.min, ALU.max, [rPexp], [rPexp])
                if t0 == sc * 128:
                    TT(c, pexp[:, 0:128], pexp[:, 0:128], m01[:, :], ALU.mult,
                       [rPexp, rM01], [rPexp])
                MM(c, psPV[0:65, t0 - tb * TB: t0 - tb * TB + w],
                   vcur[:, sc * 66: sc * 66 + 65], pexp[:, 0:w],
                   sc == 0, sc == nsc - 1, reads=[rV, rPexp], writes=[rPV])
            tails[0] = (kch, kro, tb, psPV, rPV)
    flush_tail()
    c.free(pexp2[0], pexp2[1], dn, dn16, bcs, aodd, atp0, vb[0], vb[1], m01,
           qc, kc, qrt2, krt2, ckv, wuv,
           psB, psV2[1], psV2[0], psPV2[1], psPV2[0], psS2[1], psS2[0], ps)


def l0_wo(c, s16, rS16, x1, rX1):
    S = c.S
    TB = min(512, S)
    NTB = S // TB
    wo, rwo = load_w16(c, "wo", "woT", H * DH, D)
    ps = c.psum("wop", [128, TB])
    rPS = c.reg("wop")
    rx = c.sbuf("worx", [128, 2 * TB], F32)
    rRx = [c.reg("worx0"), c.reg("worx1")]
    for dc in range(8):
        for tb in range(NTB):
            bu = (dc * NTB + tb) % 2
            DMA(c, rx[:, bu * TB:(bu + 1) * TB],
                c.dram["xT"][dc * 128:(dc + 1) * 128, tb * TB:(tb + 1) * TB],
                reads=[rRx[bu]], writes=[rRx[bu]])
            for oc in range(8):
                MM(c, ps[:, :], wo[:, oc * D + dc * 128: oc * D + (dc + 1) * 128],
                   s16[:, oc * S + tb * TB: oc * S + (tb + 1) * TB],
                   oc == 0, oc == 7, reads=[rwo, rS16], writes=[rPS])
            TT(c, x1[:, dc * S + tb * TB: dc * S + (tb + 1) * TB], ps[:, :],
               rx[:, bu * TB:(bu + 1) * TB], ALU.add, [rPS, rRx[bu]], [rX1])
    c.free(wo, rx, ps)



def prequant(c):
    p = c.p
    rC = c.reg("consts")
    stats = c.sbuf("fqs", [128, 16], F32)
    rStat = c.reg("fqs")
    st = c.sbuf("fqt", [128, FF], F32)
    rSt = c.reg("fqt")
    tn = c.sbuf("fqn", [128, FF], F16)
    rTn = c.reg("fqn")
    tmp = c.sbuf("fqm", [128, FF], F32)
    rTmp = c.reg("fqm")
    psR = c.psum("fqp", [1, 8])
    rPsR = c.reg("fqp")
    psBC = c.psum("fqb", [128, 4])
    rPsBC = c.reg("fqb")
    specs = [("wgT", "wgc", D, FF), ("wuT", "wuc", D, FF), ("wdT", "wdc", FF, D)]
    for wi, (key, skey, rows, cols) in enumerate(specs):
        nch = rows // 128
        for ch in range(nch):
            DMA(c, st[:, 0:cols], c.dram[key][ch * 128:(ch + 1) * 128, :],
                reads=[rSt], writes=[rSt])
            cslot = wi * 4 + (ch % 2)
            p.op("VE", lambda e, cslot=cslot, cols=cols: e.tensor_reduce(
                stats[:, cslot], st[:, 0:cols], AX.X, ALU.add,
                apply_absolute_value=True), reads=[rSt], writes=[rStat])
            if ch == 0:
                CP(c, stats[:, wi * 4 + 2], stats[:, cslot], reads=[rStat], writes=[rStat])
            else:
                TT(c, stats[:, wi * 4 + 2], stats[:, wi * 4 + 2],
                   stats[:, cslot], ALU.add, [rStat], [rStat])
        MM(c, psR[0:1, wi: wi + 1], c.t["ones32c"][:, 0:1],
           stats[:, wi * 4 + 2], True, True, reads=[rStat, rC], writes=[rPsR])
        n = float(rows) * float(cols)
        TS(c, stats[0:1, wi * 4 + 3], psR[0:1, wi: wi + 1], 1.0 / n, 1e-5,
           ALU.mult, ALU.max, [rPsR], [rStat])
    MM(c, psBC[:, 0:3], c.t["onesrow32"][0:1, :], stats[0:1, 3:15:4], True, True,
       reads=[rStat, rC], writes=[rPsBC])
    mbar = c.t["mbar"]
    rMbar = c.reg("mbar")
    CP(c, mbar[:, 0:3], psBC[:, 0:3], reads=[rPsBC], writes=[rMbar])
    TS(c, mbar[:, 3:6], mbar[:, 0:3], 0.5, None, ALU.mult, ALU.bypass, [rMbar], [rMbar])
    TS(c, mbar[:, 7:8], mbar[:, 1:2], 16.0, None, ALU.mult, ALU.bypass, [rMbar], [rMbar])
    c.free(psBC, psR)
    for wi, (key, skey, rows, cols) in enumerate(specs):
        nch = rows // 128
        thr = mbar[:, 3 + wi: 4 + wi]
        TS(c, mbar[:, 6:7], mbar[:, 3 + wi: 4 + wi], -1.0, None, ALU.mult, ALU.bypass,
           [rMbar], [rMbar])
        for ch in range(nch):
            DMA(c, st[:, 0:cols], c.dram[key][ch * 128:(ch + 1) * 128, :],
                reads=[rSt], writes=[rSt])
            TS(c, tmp[:, 0:cols], st[:, 0:cols], mbar[:, 6:7], None, ALU.is_lt,
               ALU.bypass, [rSt, rMbar], [rTmp])
            p.op("VE", lambda e, cols=cols, thr=thr: e.scalar_tensor_tensor(
                tn[:, 0:cols], st[:, 0:cols], thr, tmp[:, 0:cols],
                ALU.is_gt, ALU.subtract), reads=[rSt, rTmp, rMbar], writes=[rTn])
            DMA(c, c.dram[skey][ch * (cols // 128) * 128:(ch + 1) * (cols // 128) * 128, :]
                .rearrange("(o p) q -> p o q", p=128),
                tn[:, 0:cols], reads=[rTn], writes=[c.dreg[skey]])
    c.free(tmp, tn, st, stats)


def l0_ffn(c, x1, rX1, s16, rS16):
    S = c.S
    TB = min(256, S)
    NTB = S // TB
    NOC = FF // 128
    p = c.p
    rC = c.reg("consts")
    sx = c.sbuf("aqs", [128, 2 * S], F32)
    rSx = c.reg("aqs")
    mbar = c.t["mbar"]
    rMbar = c.reg("mbar")
    psBB = c.psum("aqb", [128, 128])
    rPsBB = c.reg("aqb")

    mx = c.sbuf("aqm", [128, S], F32)
    rMx = c.reg("aqm")
    axs = c.sbuf("aqab", [128, S], F32)
    rAxs = c.reg("aqab")
    ACTV(c, mx[:, :], x1[:, 0:S], ACT.Abs, reads=[rX1], writes=[rMx])
    for ch in range(1, 8):
        ACTV(c, axs[:, :], x1[:, ch * S:(ch + 1) * S], ACT.Abs, reads=[rX1],
             writes=[rAxs])
        TT(c, mx[:, :], mx[:, :], axs[:, :], ALU.max, [rAxs, rMx], [rMx])
    c.free(axs)
    NSC = S // 128
    red = c.sbuf("aqrd", [64, S], F32)
    rRed = c.reg("aqrd")
    cur = 128
    while cur > 1:
        h = cur // 2
        DMA(c, red[0:h, 0:S], mx[h:cur, :], reads=[rMx], writes=[rRed], eng="GE")
        TT(c, mx[0:h, :], mx[0:h, :], red[0:h, 0:S], ALU.max, [rRed, rMx], [rMx])
        cur = h
    if _os.environ.get("KDBG_SXD"):
        DMA(c, c.dram["outT"][256:257, :], mx[0:1, :], reads=[rMx],
            writes=[c.reg("out")])
    c.free(red)
    sxr = c.sbuf("aqxr", [1, 2 * S], F32)
    rSxr = c.reg("aqxr")
    rowx = c.sbuf("aqrw", [1, 2 * S], F16)
    rRowx = c.reg("aqrw")
    TS(c, mx[0:1, :], mx[0:1, :], 1e-5, None, ALU.max, ALU.bypass, [rMx], [rMx])
    RECIP(c, sxr[0:1, 0:S], mx[0:1, :], [rMx], [rSxr])
    TS(c, sxr[0:1, 0:S], sxr[0:1, 0:S], 127.0, None, ALU.mult, ALU.bypass,
       [rSxr], [rSxr])
    CP(c, rowx[0:1, 0:S], sxr[0:1, 0:S], reads=[rSxr], writes=[rRowx])
    CP(c, sxr[0:1, S:2 * S], rowx[0:1, 0:S], reads=[rRowx], writes=[rSxr])
    TT(c, sxr[0:1, S:2 * S], sxr[0:1, 0:S], sxr[0:1, S:2 * S], ALU.subtract,
       [rSxr], [rSxr])
    CP(c, rowx[0:1, S:2 * S], sxr[0:1, S:2 * S], reads=[rSxr], writes=[rRowx])
    onesrow = c.t["onesrow"]
    for tc in range(NSC):
        MM(c, psBB[:, :], onesrow[0:1, :], rowx[0:1, tc * 128:(tc + 1) * 128],
           True, False, reads=[rRowx, rC], writes=[rPsBB])
        MM(c, psBB[:, :], onesrow[0:1, :], rowx[0:1, S + tc * 128: S + (tc + 1) * 128],
           False, True, reads=[rRowx, rC], writes=[rPsBB])
        CP(c, sx[:, tc * 128:(tc + 1) * 128], psBB[:, :], reads=[rPsBB], writes=[rSx])
    RECIP(c, sx[:, S:2 * S], sx[:, 0:S], [rSx], [rSx])
    if _os.environ.get("KDBG_SXD"):
        DMA(c, c.dram["outT"][0:128, :], sx[0:128, 0:S], reads=[rSx],
            writes=[c.reg("out")])
        DMA(c, c.dram["outT"][128:256, :], sx[0:128, S:2 * S], reads=[rSx],
            writes=[c.reg("out")])
    c.free(rowx, sxr, mx)
    qs = c.sbuf("aqc", [128, S], F32)
    rQs = c.reg("aqc")
    xq = s16
    rXq = rS16
    for ch in range(8):
        sl = slice(ch * S, (ch + 1) * S)
        TT(c, qs[:, :], x1[:, sl], sx[:, 0:S], ALU.mult, [rX1, rSx], [rQs])
        round_ip(c, qs[:, :], [rQs], [rQs])
        TS(c, xq[:, sl], qs[:, :], 127.0, -128.0, ALU.min, ALU.max, [rQs], [rXq])
    c.free(qs)

    if getattr(c, "tap", None) == "ffn":
        DMA(c, c.dram["dbgA"][:, 0:8 * S], xq[:, 0:8 * S], reads=[rXq], writes=[c.dreg["dbgA"]])
        DMA(c, c.dram["dbgB"][:, 0:2 * S], sx[0:128, 0:2 * S], reads=[rSx], writes=[c.dreg["dbgB"]])
        DMA(c, c.dram["dbgD"][:, 0:7], mbar[0:128, 0:7], reads=[rMbar], writes=[c.dreg["dbgD"]])
    for ch in range(8):
        DMA(c, c.dram["x1s"][ch * 128:(ch + 1) * 128, :], x1[:, ch * S:(ch + 1) * S],
            reads=[rX1], writes=[c.dreg["x1s"]])

    prod = c.sbuf("ffp", [128, NOC * TB], F32)
    rProd = c.reg("ffp")
    pq = c.sbuf("ffq", [128, NOC * TB], F16)
    rPq = c.reg("ffq")
    gsl = c.sbuf("ffg", [128, TB], F32)
    rGsl = c.reg("ffg")
    wb = [c.sbuf(f"ffw{i}", [128, 8 * 128], F16) for i in range(4)]
    rWb = [c.reg(f"ffw{i}") for i in range(4)]
    dwb = [c.sbuf(f"ffd{i}", [128, NOC * 128], F16) for i in range(2)]
    rDwb = [c.reg(f"ffd{i}") for i in range(2)]
    rmax = c.sbuf("ffm", [128, TB], F32)
    rRmax = c.reg("ffm")
    redr = c.sbuf("ffrd", [64, TB], F32)
    rRedr = c.reg("ffrd")
    fab = c.sbuf("ffab", [128, TB], F32)
    rFab = c.reg("ffab")
    rxr = c.sbuf("ffxr", [1, TB], F32)
    rRxr = c.reg("ffxr")
    rowr = c.sbuf("ffrr", [1, 2 * TB], F16)
    rRowr = c.reg("ffrr")
    rmaxb = c.sbuf("ffmb", [128, TB], F32)
    rRmaxb = c.reg("ffmb")
    s2 = c.sbuf("ff2", [128, 3 * TB], F32)
    rS2 = c.reg("ff2")
    rx1 = c.sbuf("ffr", [128, 2 * TB], F32)
    rRx1 = [c.reg("ffr0"), c.reg("ffr1")]
    x2 = x1
    rX2 = rX1
    psG2 = [c.psum(f"ffpg{i}", [128, TB]) for i in range(2)]
    psU2 = [c.psum(f"ffpu{i}", [128, TB]) for i in range(2)]
    rPsG2 = [c.reg(f"ffpg{i}") for i in range(2)]
    rPsU2 = [c.reg(f"ffpu{i}") for i in range(2)]
    wgc2 = c.dram["wgc"].rearrange("(ch o p) q -> o p ch q", o=NOC, p=128)
    wuc2 = c.dram["wuc"].rearrange("(ch o p) q -> o p ch q", o=NOC, p=128)
    wdc2 = c.dram["wdc"].rearrange("(ch o p) q -> o p ch q", o=8, p=128)

    a_ = s2[:, 0:TB]
    b_ = s2[:, TB:2 * TB]
    f_ = s2[:, 2 * TB:3 * TB]
    for tb in range(NTB):
        tsl = slice(tb * TB, (tb + 1) * TB)
        for oc in range(NOC):
            bG = oc % 2
            bU = 2 + oc % 2
            psG, rPsG = psG2[oc % 2], rPsG2[oc % 2]
            psU, rPsU = psU2[oc % 2], rPsU2[oc % 2]
            DMA(c, wb[bG][:, :], wgc2[oc, :, :, :], reads=[rWb[bG]], writes=[rWb[bG]])
            DMA(c, wb[bU][:, :], wuc2[oc, :, :, :], reads=[rWb[bU]], writes=[rWb[bU]])
            for ch in range(8):
                MM(c, psG[:, :], wb[bG][:, ch * 128:(ch + 1) * 128],
                   xq[:, ch * S + tb * TB: ch * S + (tb + 1) * TB],
                   ch == 0, ch == 7, reads=[rWb[bG], rXq], writes=[rPsG])
            for ch in range(8):
                MM(c, psU[:, :], wb[bU][:, ch * 128:(ch + 1) * 128],
                   xq[:, ch * S + tb * TB: ch * S + (tb + 1) * TB],
                   ch == 0, ch == 7, reads=[rWb[bU], rXq], writes=[rPsU])
            osl = slice(oc * TB, (oc + 1) * TB)
            p.op("VE", lambda e, osl=osl, tb=tb, psG=psG: e.scalar_tensor_tensor(
                prod[:, osl], psG[:, :], mbar[:, 0:1], sx[:, S + tb * TB: S + (tb + 1) * TB],
                ALU.mult, ALU.mult), reads=[rPsG, rMbar, rSx], writes=[rProd])
            ACTV(c, gsl[:, :], prod[:, osl], ACT.Silu, reads=[rProd], writes=[rGsl])
            TT(c, prod[:, osl], gsl[:, :], psU[:, :], ALU.mult, [rGsl, rPsU], [rProd])
            if oc == 0:
                p.op("VE", lambda e, osl=osl: e.scalar_tensor_tensor(
                    rmax[:, :], prod[:, osl], -1.0, prod[:, osl], ALU.mult, ALU.max),
                    reads=[rProd], writes=[rRmax])
            else:
                p.op("VE", lambda e, osl=osl: e.scalar_tensor_tensor(
                    fab[:, :], prod[:, osl], -1.0, prod[:, osl], ALU.mult, ALU.max),
                    reads=[rProd], writes=[rFab])
                TT(c, rmax[:, :], rmax[:, :], fab[:, :], ALU.max,
                   [rFab, rRmax], [rRmax])
        cur = 128
        while cur > 1:
            h = cur // 2
            DMA(c, redr[0:h, 0:TB], rmax[h:cur, :], reads=[rRmax], writes=[rRedr],
                eng="GE")
            TT(c, rmax[0:h, :], rmax[0:h, :], redr[0:h, 0:TB], ALU.max,
               [rRedr, rRmax], [rRmax])
            cur = h
        TS(c, rxr[0:1, 0:TB], rmax[0:1, :], 2.0**-4, None, ALU.mult, ALU.bypass,
           [rRmax], [rRxr])
        CP(c, rowr[0:1, 0:TB], rxr[0:1, 0:TB], reads=[rRxr], writes=[rRowr])
        CP(c, redr[0:1, 0:TB], rowr[0:1, 0:TB], reads=[rRowr], writes=[rRedr])
        TT(c, rxr[0:1, 0:TB], rxr[0:1, 0:TB], redr[0:1, 0:TB], ALU.subtract,
           [rRedr, rRxr], [rRxr])
        CP(c, rowr[0:1, TB:2 * TB], rxr[0:1, 0:TB], reads=[rRxr], writes=[rRowr])
        for j in range(TB // 128):
            MM(c, psBB[:, :], c.t["onesrow"][0:1, :], rowr[0:1, j * 128:(j + 1) * 128],
               True, False, reads=[rRowr, rC], writes=[rPsBB])
            MM(c, psBB[:, :], c.t["onesrow"][0:1, :],
               rowr[0:1, TB + j * 128: TB + (j + 1) * 128],
               False, True, reads=[rRowr, rC], writes=[rPsBB])
            CP(c, rmaxb[:, j * 128:(j + 1) * 128], psBB[:, :], reads=[rPsBB],
               writes=[rRmaxb])
        p.op("VE", lambda e, tb=tb: e.scalar_tensor_tensor(
            a_, rmaxb[:, :], mbar[:, 7:8], sx[:, S + tb * TB: S + (tb + 1) * TB],
            ALU.mult, ALU.mult), reads=[rRmaxb, rMbar, rSx], writes=[rS2])
        TS(c, a_, a_, 1e-5, None, ALU.max, ALU.bypass, [rS2], [rS2])
        RECIP(c, b_, a_, [rS2], [rS2])
        TS(c, b_, b_, 127.0, None, ALU.mult, ALU.bypass, [rS2], [rS2])
        RECIP(c, f_, b_, [rS2], [rS2])
        TS(c, f_, f_, mbar[:, 2:3], None, ALU.mult, ALU.bypass, [rS2, rMbar], [rS2])
        p.op("VE", lambda e, tb=tb: e.scalar_tensor_tensor(
            a_, b_, mbar[:, 1:2], sx[:, S + tb * TB: S + (tb + 1) * TB],
            ALU.mult, ALU.mult), reads=[rS2, rMbar, rSx], writes=[rS2])
        if _os.environ.get("KDBG_SXD"):
            DMA(c, c.dram["outT"][300:301, tsl], rmaxb[0:1, :], reads=[rRmaxb],
                writes=[c.reg("out")])
            DMA(c, c.dram["outT"][301:302, tsl], s2[0:1, TB:2 * TB], reads=[rS2],
                writes=[c.reg("out")])
            DMA(c, c.dram["outT"][302:303, tsl], s2[0:1, 2 * TB:3 * TB], reads=[rS2],
                writes=[c.reg("out")])
            DMA(c, c.dram["outT"][303:304, tsl], s2[0:1, 0:TB], reads=[rS2],
                writes=[c.reg("out")])
        for oc in range(NOC):
            osl = slice(oc * TB, (oc + 1) * TB)
            TT(c, prod[:, osl], prod[:, osl], a_, ALU.mult, [rProd, rS2], [rProd])
            round_ip(c, prod[:, osl], [rProd], [rProd])
            TS(c, pq[:, osl], prod[:, osl], 127.0, -128.0, ALU.min, ALU.max,
               [rProd], [rPq])
        if tb == 0 and getattr(c, "tap", None) == "ffn":
            DMA(c, c.dram["dbgC"][:, 0:NOC * TB], pq[0:128, 0:NOC * TB],
                reads=[rPq], writes=[c.dreg["dbgC"]])
        if tb == 0 and getattr(c, "tap", None) in ("ffn", "x2pre"):
            DMA(c, c.dram["dbgE"][:, 0:3 * TB], s2[0:128, 0:3 * TB],
                reads=[rS2], writes=[c.dreg["dbgE"]])
        for dc in range(8):
            bu = dc % 2
            DMA(c, rx1[:, bu * TB:(bu + 1) * TB],
                c.dram["x1s"][dc * 128:(dc + 1) * 128, tsl],
                reads=[rRx1[bu], c.dreg["x1s"]], writes=[rRx1[bu]])
            DMA(c, dwb[bu][:, :], wdc2[dc, :, :, :], reads=[rDwb[bu]], writes=[rDwb[bu]])
            psG, rPsG = psG2[dc % 2], rPsG2[dc % 2]
            for ic in range(NOC):
                MM(c, psG[:, :], dwb[bu][:, ic * 128:(ic + 1) * 128],
                   pq[:, ic * TB:(ic + 1) * TB], ic == 0, ic == NOC - 1,
                   reads=[rDwb[bu], rPq], writes=[rPsG])
            xsl = slice(dc * S + tb * TB, dc * S + (tb + 1) * TB)
            TT(c, x2[:, xsl], psG[:, :], f_, ALU.mult, [rPsG, rS2], [rX2])
            TT(c, x2[:, xsl], x2[:, xsl], rx1[:, bu * TB:(bu + 1) * TB], ALU.add,
               [rRx1[bu], rX2], [rX2])
    c.free(prod, pq, gsl, rmax, redr, fab, rxr, rowr, rmaxb, s2, rx1, sx,
           wb[0], wb[1], wb[2], wb[3], dwb[0], dwb[1],
           psU2[1], psU2[0], psG2[1], psG2[0], psBB)



def l1_attn(c, x2, rX2, s16, rS16):
    S = c.S
    TB = min(512, S)
    NTB = S // TB
    NSC = S // 128
    WM = 2 * S - 128
    p = c.p
    rC = c.reg("consts")
    x2_16 = s16
    rX16 = rS16

    ps = c.psum("l1p", [128, TB])
    rPS = c.reg("l1p")

    qt = c.sbuf("l1q", [128, 8 * S], F16)
    rQ = c.reg("l1q")
    wq, rwq = load_w16(c, "wq", "wqT", D, D)
    for oc in range(8):
        for tb in range(NTB):
            for ch in range(8):
                MM(c, ps[:, :], wq[:, ch * D + oc * 128: ch * D + (oc + 1) * 128],
                   x2_16[:, ch * S + tb * TB: ch * S + (tb + 1) * TB],
                   ch == 0, ch == 7, reads=[rwq, rX16], writes=[rPS])
            TS(c, qt[:, oc * S + tb * TB: oc * S + (tb + 1) * TB], ps[:, :],
               0.125, None, ALU.mult, ALU.bypass, [rPS], [rQ])
    c.free(wq)

    kk = [c.sbuf(f"l1k{t}", [128, S], F16) for t in range(2)]
    rKK = [c.reg(f"l1k{t}") for t in range(2)]
    wk, rwk = load_w16(c, "wk", "wkT", D, KVH * DH)
    for t in range(2):
        for tb in range(NTB):
            for ch in range(8):
                MM(c, ps[:, :], wk[:, ch * (KVH * DH) + t * 128: ch * (KVH * DH) + (t + 1) * 128],
                   x2_16[:, ch * S + tb * TB: ch * S + (tb + 1) * TB],
                   ch == 0, ch == 7, reads=[rwk, rX16], writes=[rPS])
            CP(c, kk[t][:, tb * TB:(tb + 1) * TB], ps[:, :], reads=[rPS],
               writes=[rKK[t]], eng="SE")
    c.free(wk)

    vb = [c.sbuf(f"l1v{kv}", [128, NSC * 66], F16) for kv in range(KVH)]
    rVb = [c.reg(f"l1v{kv}") for kv in range(KVH)]
    psV = c.psum("l1vv", [128, 64])
    rPsV = c.reg("l1vv")
    wv, rwv = load_w16(c, "wv", "wvT", D, KVH * DH)
    for kv in range(KVH):
        MEMSET(c, vb[kv][:, 64::66], 1.0, writes=[rVb[kv]])
        for tc in range(NSC):
            for ch in range(8):
                MM(c, psV[:, :], x2_16[:, ch * S + tc * 128: ch * S + (tc + 1) * 128],
                   wv[:, ch * (KVH * DH) + kv * DH: ch * (KVH * DH) + (kv + 1) * DH],
                   ch == 0, ch == 7, reads=[rwv, rX16], writes=[rPsV])
            CP(c, vb[kv][:, tc * 66: tc * 66 + 64], psV[:, :], reads=[rPsV],
               writes=[rVb[kv]], eng="SE")
    c.free(wv, psV)

    sqc = c.sbuf("l1sx", [128, S], F16)
    rSqc = c.reg("l1sx")
    psN = c.psum("l1n", [1, TB])
    rPsN = c.reg("l1n")
    hp = c.t["hpones"]
    kn = c.sbuf("l1kn", [1, 8], F32)
    rKn = c.reg("l1kn")
    for kv in range(KVH):
        t, rb = divmod(kv, 2)
        rb *= 64
        ACTV(c, sqc[rb:rb + 64, :], kk[t][rb:rb + 64, :], ACT.Square,
             reads=[rKK[t]], writes=[rSqc])
        for tb in range(NTB):
            MM(c, psN[0:1, :], hp[rb:rb + 64, rb // 64: rb // 64 + 1],
               sqc[rb:rb + 64, tb * TB:(tb + 1) * TB],
               True, True, reads=[rSqc, rC], writes=[rPsN], tp=(rb, 0))
            slot = kv if tb == 0 else 4 + kv
            p.op("VE", lambda e, slot=slot: e.tensor_reduce(
                kn[0:1, slot: slot + 1], psN[0:1, :], AX.X, ALU.max),
                reads=[rPsN], writes=[rKn])
            if tb > 0:
                TT(c, kn[0:1, kv], kn[0:1, kv], kn[0:1, 4 + kv], ALU.max, [rKn], [rKn])
    ACTV(c, kn[0:1, 0:4], kn[0:1, 0:4], ACT.Sqrt, reads=[rKn], writes=[rKn],
         scale=2.0**-4)
    TS(c, kn[0:1, 4:8], kn[0:1, 0:4], -16.0, None, ALU.mult, ALU.bypass, [rKn], [rKn])
    qn = c.sbuf("l1qn", [1, S], F32)
    rQn = c.reg("l1qn")
    ncr = c.sbuf("l1cr", [1, S], F16)
    rNcr = c.reg("l1cr")
    ebias = c.sbuf("l1eb", [128, 1], F32)
    rEb = c.reg("l1eb")
    MEMSET(c, ebias[:, :], 10.0, writes=[rEb])

    slopes = c.consts["slopes"]
    escr = c.sbuf("l1es", [128, WM], F32)
    rEsc = c.reg("l1es")
    DMA(c, escr[:, :], c.dram["relc"], reads=[rEsc], writes=[rEsc])
    em = c.sbuf("l1e", [128, WM], F16)
    rEm = c.reg("l1e")
    GW = min(512, S)
    NTBG = GW // TB if TB < 512 else 1
    NTG = S // GW
    psS2 = [c.psum(f"l1sc{i}", [128, GW]) for i in range(2)]
    rSc2 = [c.reg(f"l1sc{i}") for i in range(2)]
    psPV2 = [c.psum(f"l1pv{i}", [65, GW]) for i in range(2)]
    rPV2 = [c.reg(f"l1pv{i}") for i in range(2)]
    psB = c.psum("l1bb", [64, GW])
    rB = c.reg("l1bb")
    pexp2 = [c.sbuf(f"l1pe{i}", [128, GW], F16) for i in range(2)]
    rPexp2 = [c.reg(f"l1pe{i}") for i in range(2)]
    dn = c.sbuf("l1dn", [1, 3 * TB], F32)
    rDn = c.reg("l1dn")
    dn16 = c.sbuf("l1d6", [1, TB], F16)
    rDn16 = c.reg("l1d6")
    bcs = c.sbuf("l1bc", [64, TB], F32)
    rBcs = c.reg("l1bcs")
    aodd = c.sbuf("l1ao", [64, TB], F16)
    rAodd = c.reg("l1ao")
    atp = c.sbuf("l1at", [64, TB], F32)
    rAtp = c.reg("l1at")
    onesrow = c.t["onesrow"]
    tails = [None]

    def flush_tail():
        if tails[0] is None:
            return
        ch, half, tb, psPV, rPV = tails[0]
        tails[0] = None
        CP(c, dn[0:1, 0:GW], psPV[64:65, :], reads=[rPV], writes=[rDn])
        ACTV(c, dn[0:1, GW:2 * GW], dn[0:1, 0:GW], ACT.Ln, reads=[rDn],
             writes=[rDn], scale=2.0**-10)
        ACTV(c, dn[0:1, 2 * GW:3 * GW], dn[0:1, GW:2 * GW], ACT.Exp,
             reads=[rDn], writes=[rDn], scale=-0.5)
        TS(c, dn16[0:1, 0:GW], dn[0:1, 2 * GW:3 * GW], 2.0**-5, None,
           ALU.mult, ALU.bypass, [rDn], [rDn16])
        MM(c, psB[0:64, :], onesrow[0:1, 0:64], dn16[0:1, 0:GW], True, True,
           reads=[rDn16, rC], writes=[rB])
        CP(c, bcs[:, 0:GW], psB[0:64, :], reads=[rB], writes=[rBcs], eng="SE")
        TT(c, atp[0:64, 0:GW], psPV[0:64, :], bcs[:, 0:GW], ALU.mult,
           [rPV, rBcs], [rAtp])
        if half == 0:
            TT(c, s16[0:64, ch * S + tb * GW: ch * S + (tb + 1) * GW],
               atp[0:64, 0:GW], bcs[:, 0:GW], ALU.mult, [rAtp, rBcs], [rS16])
        else:
            TT(c, aodd[0:64, 0:GW], atp[0:64, 0:GW], bcs[:, 0:GW], ALU.mult,
               [rAtp, rBcs], [rAodd])
            DMA(c, s16[64:128, ch * S + tb * GW: ch * S + (tb + 1) * GW],
                aodd[0:64, 0:GW], reads=[rAodd], writes=[rS16], eng="GE")

    for h in range(H):
        kv = h % KVH
        ch, half = divmod(h, 2)
        base = half * 64
        ACTV(c, sqc[base:base + 64, :], qt[base:base + 64, ch * S:(ch + 1) * S],
             ACT.Square, reads=[rQ], writes=[rSqc])
        for tb in range(NTB):
            MM(c, psN[0:1, :], hp[base:base + 64, half: half + 1],
               sqc[base:base + 64, tb * TB:(tb + 1) * TB],
               True, True, reads=[rSqc, rC], writes=[rPsN], tp=(base, 0))
            ACTV(c, qn[0:1, tb * TB:(tb + 1) * TB], psN[0:1, :], ACT.Sqrt,
                 reads=[rPsN], writes=[rQn], scale=2.0**-4)
        TS(c, ncr[0:1, :], qn[0:1, 0:S], kn[0:1, 4 + kv], None, ALU.mult,
           ALU.bypass, [rQn, rKn], [rNcr])
        ACTV(c, em[:, :], escr[:, :], ACT.Exp, reads=[rEsc], writes=[rEm],
             scale=float(slopes[h]) * 0.125)
        for tg in range(NTG):
            tb = tg
            pv = (h * NTG + tg) % 2
            psPV, rPV = psPV2[pv], rPV2[pv]
            for sc in range(NSC):
                pi = sc % 2
                psS, rSc = psS2[pi], rSc2[pi]
                pexp, rPexp = pexp2[pi], rPexp2[pi]
                MM(c, psS[:, :],
                   onesrow[0:1, :], ncr[0:1, tb * GW:(tb + 1) * GW],
                   True, False, reads=[rNcr, rC], writes=[rSc])
                MM(c, psS[:, :],
                   kk[kv // 2][base: base + 64, sc * 128:(sc + 1) * 128],
                   qt[base: base + 64, ch * S + tb * GW: ch * S + (tb + 1) * GW],
                   False, True, reads=[rKK[kv // 2], rQ], writes=[rSc], tp=(base, 0))
                ACTV(c, pexp[:, :], psS[:, :], ACT.Exp, reads=[rSc, rEb], writes=[rPexp],
                     bias=ebias[:, 0:1])
                off = tg * GW - sc * 128 + (S - 128)
                TT(c, pexp[:, :], pexp[:, :], em[:, off: off + GW], ALU.mult,
                   [rPexp, rEm], [rPexp])
                if sc == 1:
                    flush_tail()
                MM(c, psPV[0:65, :], vb[kv][:, sc * 66: sc * 66 + 65],
                   pexp[:, :], sc == 0, sc == NSC - 1,
                   reads=[rVb[kv], rPexp], writes=[rPV])
            tails[0] = (ch, half, tb, psPV, rPV)
    flush_tail()
    c.free(pexp2[0], pexp2[1], dn, dn16, bcs, aodd, atp, em, escr, ebias, ncr,
           qn, kn, sqc, qt, kk[0], kk[1], vb[0], vb[1], vb[2], vb[3],
           psB, psPV2[1], psPV2[0], psS2[1], psS2[0], psN)

    wp, rwp = load_w16(c, "wp", "wpT", D, D)
    bpj = c.sbuf("l1bp", [128, 8], F32)
    rBpj = c.reg("l1bp")
    DMA(c, bpj[:, :], c.dram["bprojT"], reads=[rBpj], writes=[rBpj])
    for dc in range(8):
        for tb in range(NTB):
            for oc in range(8):
                MM(c, ps[:, :], wp[:, oc * D + dc * 128: oc * D + (dc + 1) * 128],
                   s16[:, oc * S + tb * TB: oc * S + (tb + 1) * TB],
                   oc == 0, oc == 7, reads=[rwp, rS16], writes=[rPS])
            p.op("VE", lambda e, dc=dc, tb=tb: e.scalar_tensor_tensor(
                x2[:, dc * S + tb * TB: dc * S + (tb + 1) * TB], ps[:, :],
                bpj[:, dc], x2[:, dc * S + tb * TB: dc * S + (tb + 1) * TB],
                ALU.add, ALU.add), reads=[rPS, rBpj, rX2], writes=[rX2])
    c.free(wp, bpj, ps)



def precache_w12(c):
    rC = c.reg("consts")
    st = c.sbuf("f1s", [128, FF], F32)
    rSt = c.reg("f1s")
    t16 = c.sbuf("f1t", [128, FF], F16)
    rT16 = c.reg("f1t")
    for key, skey, rows, cols in (("w1T", "w1c", D, FF), ("w2T", "w2c", FF, D)):
        for ch in range(rows // 128):
            DMA(c, st[:, 0:cols], c.dram[key][ch * 128:(ch + 1) * 128, :],
                reads=[rSt], writes=[rSt])
            CP(c, t16[:, 0:cols], st[:, 0:cols], reads=[rSt], writes=[rT16])
            DMA(c, c.dram[skey][ch * (cols // 128) * 128:(ch + 1) * (cols // 128) * 128, :]
                .rearrange("(o p) q -> p o q", p=128),
                t16[:, 0:cols], reads=[rT16], writes=[c.dreg[skey]])
    c.free(t16, st)


def l1_ffn(c, x3, rX3, s16, rS16):
    S = c.S
    TB = min(512, S)
    NTB = S // TB
    NOC = FF // 128
    NQ = 4
    QOC = NOC // NQ
    p = c.p
    rC = c.reg("consts")
    cst = c.t["cst"]
    P0 = c.PWL0
    pw = c.consts["pwl"]
    x3_16 = s16
    rX16 = rS16

    kxb = c.sbuf("f1kx", [128, 31], F32)
    rKxb = c.reg("f1kx")
    for k in range(31):
        MEMSET(c, kxb[:, k:k + 1], -float(pw["kx"][k]), writes=[rKxb])
        w1c2 = c.dram["w1c"].rearrange("(ch o p) q -> o p ch q", o=NOC, p=128)
    w2c2 = c.dram["w2c"].rearrange("(ch o p) q -> o p ch q", o=8, p=128)

    b1 = c.sbuf("f1b1", [128, 32], F32)
    rB1 = c.reg("f1b1")
    DMA(c, b1[:, :], c.dram["b1T"], reads=[rB1], writes=[rB1])
    b2 = c.sbuf("f1b2", [128, 8], F32)
    rB2 = c.reg("f1b2")
    DMA(c, b2[:, :], c.dram["b2T"], reads=[rB2], writes=[rB2])

    for ch in range(8):
        DMA(c, c.dram["x3s"][ch * 128:(ch + 1) * 128, :], x3[:, ch * S:(ch + 1) * S],
            reads=[rX3], writes=[c.dreg["x3s"]])

    QW = QOC * TB
    vq = c.sbuf("f1v", [128, QW], F32)
    rV = c.reg("f1v")
    acc = c.sbuf("f1a", [128, QW], F32)
    rA = c.reg("f1a")
    tkr = c.sbuf("f1k", [128, QW], F32)
    rK = c.reg("f1k")
    hq = c.sbuf("f1h", [128, NOC * TB], F16)
    rHq = c.reg("f1h")
    wb = [c.sbuf(f"f1w{i}", [128, 8 * 128], F16) for i in range(2)]
    rWb = [c.reg(f"f1w{i}") for i in range(2)]
    w2b = [c.sbuf(f"f1u{i}", [128, NOC * 128], F16) for i in range(2)]
    rW2b = [c.reg(f"f1u{i}") for i in range(2)]
    rx3 = c.sbuf("f1r", [128, 2 * TB], F32)
    rRx3 = [c.reg("f1r0"), c.reg("f1r1")]
    x4 = x3
    rX4 = rX3
    psH2 = [c.psum(f"f1p{i}", [128, TB]) for i in range(2)]
    rPsH2 = [c.reg(f"f1p{i}") for i in range(2)]

    for tb in range(NTB):
        tsl = slice(tb * TB, (tb + 1) * TB)
        for qh in range(NQ):
            for j in range(QOC):
                oc = qh * QOC + j
                b = oc % 2
                psH, rPsH = psH2[oc % 2], rPsH2[oc % 2]
                DMA(c, wb[b][:, :], w1c2[oc, :, :, :], reads=[rWb[b]], writes=[rWb[b]])
                for ch in range(8):
                    MM(c, psH[:, :], wb[b][:, ch * 128:(ch + 1) * 128],
                       x3_16[:, ch * S + tb * TB: ch * S + (tb + 1) * TB],
                       ch == 0, ch == 7, reads=[rWb[b], rX16], writes=[rPsH])
                TS(c, vq[:, j * TB:(j + 1) * TB], psH[:, :], b1[:, oc], 256.0,
                   ALU.add, ALU.mult, [rPsH, rB1], [rV])
            round_ip(c, vq[:, :], [rV], [rV])
            TS(c, acc[:, :], vq[:, :], cst[:, P0 + 1: P0 + 2], cst[:, P0: P0 + 1],
               ALU.mult, ALU.add, [rV, rC], [rA])
            for k in range(31):
                ACTV(c, tkr[:, :], vq[:, :], ACT.Relu, reads=[rV, rC], writes=[rK],
                     bias=kxb[:, k:k + 1])
                p.op("VE", lambda e, k=k: e.scalar_tensor_tensor(
                    acc[:, :], tkr[:, :], cst[:, P0 + 2 + k: P0 + 3 + k], acc[:, :],
                    ALU.mult, ALU.add), reads=[rK, rC, rA], writes=[rA])
            TS(c, tkr[:, :], vq[:, :], 1024.0, -1024.0, ALU.max, ALU.add, [rV], [rK])
            TS(c, tkr[:, :], tkr[:, :], cst[:, P0 + 33: P0 + 34], None, ALU.mult,
               ALU.bypass, [rK, rC], [rK])
            TT(c, acc[:, :], acc[:, :], tkr[:, :], ALU.add, [rK, rA], [rA])
            TS(c, tkr[:, :], vq[:, :], 1024.0, cst[:, P0 + 34: P0 + 35], ALU.is_gt,
               ALU.mult, [rV, rC], [rK])
            TT(c, acc[:, :], acc[:, :], tkr[:, :], ALU.add, [rK, rA], [rA])
            TS(c, tkr[:, :], vq[:, :], -1024.0, 1024.0, ALU.min, ALU.add, [rV], [rK])
            TS(c, tkr[:, :], tkr[:, :], cst[:, P0 + 35: P0 + 36], None, ALU.mult,
               ALU.bypass, [rK, rC], [rK])
            TT(c, acc[:, :], acc[:, :], tkr[:, :], ALU.add, [rK, rA], [rA])
            TS(c, tkr[:, :], vq[:, :], -1024.0, cst[:, P0 + 37: P0 + 38], ALU.is_lt,
               ALU.mult, [rV, rC], [rK])
            TT(c, acc[:, :], acc[:, :], tkr[:, :], ALU.add, [rK, rA], [rA])
            TS(c, acc[:, :], acc[:, :], 256.0, None, ALU.mult, ALU.bypass, [rA], [rA])
            round_ip(c, acc[:, :], [rA], [rA])
            TS(c, hq[:, qh * QW:(qh + 1) * QW], acc[:, :], 2.0**-8, None, ALU.mult,
               ALU.bypass, [rA], [rHq])
        for dc in range(8):
            bu = dc % 2
            DMA(c, rx3[:, bu * TB:(bu + 1) * TB],
                c.dram["x3s"][dc * 128:(dc + 1) * 128, tsl],
                reads=[rRx3[bu], c.dreg["x3s"]], writes=[rRx3[bu]])
            DMA(c, w2b[bu][:, :], w2c2[dc, :, :, :], reads=[rW2b[bu]], writes=[rW2b[bu]])
            psH, rPsH = psH2[dc % 2], rPsH2[dc % 2]
            for ic in range(NOC):
                MM(c, psH[:, :], w2b[bu][:, ic * 128:(ic + 1) * 128],
                   hq[:, ic * TB:(ic + 1) * TB], ic == 0, ic == NOC - 1,
                   reads=[rW2b[bu], rHq], writes=[rPsH])
            xsl = slice(dc * S + tb * TB, dc * S + (tb + 1) * TB)
            TS(c, x4[:, xsl], psH[:, :], b2[:, dc], 256.0,
               ALU.add, ALU.mult, [rPsH, rB2], [rX4])
            round_ip(c, x4[:, xsl], [rX4], [rX4])
            p.op("VE", lambda e, xsl=xsl, bu=bu: e.scalar_tensor_tensor(
                x4[:, xsl], x4[:, xsl], 2.0**-8, rx3[:, bu * TB:(bu + 1) * TB],
                ALU.mult, ALU.add), reads=[rRx3[bu], rX4], writes=[rX4])
    c.free(vq, acc, tkr, hq, wb[0], wb[1], w2b[0], w2b[1], rx3, b1, b2, kxb,
           psH2[1], psH2[0])



def _flex(c, S, n32, n16):
    if S < 2048:
        n32 = max(n32, 24576)
        n16 = max(n16, 49152)
    c.open_flex(n32=int(n32), n16=int(n16))


def build_program(S=2048, tap=None):
    consts = _host_consts(S)
    nc = bass.Bass("TRN2", target_bir_lowering=False)
    c = Ctx(nc, S, consts)

    def din(name, shape, dtype=F32):
        c.dram[name] = nc.dram_tensor(name, list(shape), dtype, kind="ExternalInput").ap()
        c.dreg[name] = c.reg("d_" + name)

    def dscr(name, shape, dtype):
        c.dram[name] = nc.dram_tensor(name, list(shape), dtype, kind="Internal").ap()
        c.dreg[name] = c.reg("d_" + name)

    din("xT", (D, S))
    out_ap = nc.dram_tensor("outT", [D, S], F32, kind="ExternalOutput").ap()
    c.dram["outT"] = out_ap
    din("wdqT", (D, DLAT)); din("wuqT", (DLAT, H * DH)); din("wqrT", (DLAT, H * DHR))
    din("wdkvT", (D, DLAT)); din("wukT", (DLAT, H * DH)); din("wuvT", (DLAT, H * DH))
    din("wkrT", (D, DHR)); din("woT", (H * DH, D))
    din("wgT", (D, FF)); din("wuT", (D, FF)); din("wdT", (FF, D))
    din("wqT", (D, D)); din("wkT", (D, KVH * DH)); din("wvT", (D, KVH * DH))
    din("wpT", (D, D)); din("bprojT", (128, 8))
    din("w1T", (D, FF)); din("b1T", (128, 32)); din("w2T", (FF, D)); din("b2T", (128, 8))
    din("cosb", (128, S)); din("sinb", (128, S))
    din("m01c", (128, 128), F16)
    din("relc", (128, 2 * S - 128))
    din("identc", (128, 128), F16)
    din("sel2", (2, 256))
    dscr("wgc", (8 * 32 * 128, 128), F16)
    dscr("wuc", (8 * 32 * 128, 128), F16)
    dscr("wdc", (32 * 8 * 128, 128), F16)
    dscr("w1c", (8 * 32 * 128, 128), F16)
    dscr("w2c", (32 * 8 * 128, 128), F16)
    dscr("x1s", (D, S), F32)
    dscr("dbgA", (128, 8 * S), F16)
    dscr("dbgB", (128, 2 * S), F32)
    dscr("dbgC", (128, 64 * 256), F16)
    dscr("dbgD", (128, 64), F32)
    dscr("dbgE", (128, 2048), F32)
    dscr("dbgX", (D, S), F32)
    dscr("x3s", (D, S), F32)

    setup_consts(c)
    _kph = _os.environ.get("KPH")
    _phset = set(_kph.split(",")) if _kph else None
    _ph = lambda nm: _phset is None or nm in _phset
    s16 = c.perm("stream16", [128, 8 * S], F16)
    rS16 = c.reg("S16")

    if _ph("A"):
        _flex(c, S, 4 * S + 5888, 29 * S + 1024)
        l0_attn(c, s16, rS16)
        c.close_flex()

    s32 = c.perm("stream32", [128, 8 * S], F32)
    rS32 = c.reg("S32")

    if _ph("B"):
        _flex(c, S, 2 * min(512, S) + 2 * FF + 6144, 4 * S + FF + 2048)
        l0_wo(c, s16, rS16, s32, rS32)
        prequant(c)
        precache_w12(c)
        c.close_flex()

    if _ph("L1"):
        _flex(c, S, 4 * min(512, S) + 512, 16 * S + 4 * min(512, S) + 1024)
        post_ln(c, s32, rS32, None, None, "a")
        c.close_flex()

    c.tap = tap
    if _ph("C"):
        _flex(c, S, 2 * S + 2 * FF + 3072, 64 * min(256, S) + FF + 2048)
        l0_ffn(c, s32, rS32, s16, rS16)
        c.close_flex()

    if tap == "x2pre":
        for ch in range(8):
            DMA(c, c.dram["dbgX"][ch * 128:(ch + 1) * 128, :], s32[:, ch * S:(ch + 1) * S],
                reads=[rS32], writes=[c.dreg["dbgX"]])
    if _ph("L2"):
        _flex(c, S, 4 * min(512, S) + 512, 16 * S + 4 * min(512, S) + 1024)
        post_ln(c, s32, rS32, s16, rS16, "b")
        c.close_flex()
    if tap == "x2":
        for ch in range(8):
            DMA(c, c.dram["dbgX"][ch * 128:(ch + 1) * 128, :], s32[:, ch * S:(ch + 1) * S],
                reads=[rS32], writes=[c.dreg["dbgX"]])

    if _ph("D"):
        _flex(c, S, 2 * S + 4608, 17 * S + 1024)
        l1_attn(c, s32, rS32, s16, rS16)
        c.close_flex()

    if _ph("L3"):
        _flex(c, S, 4 * min(512, S) + 512, 16 * S + 4 * min(512, S) + 1024)
        post_ln(c, s32, rS32, s16, rS16, "c")
        c.close_flex()

    if _ph("E"):
        _flex(c, S, 14 * min(512, S) + FF + 2560, 38 * min(512, S) + FF + 4096)
        l1_ffn(c, s32, rS32, s16, rS16)
        c.close_flex()

    if _ph("L4"):
        _flex(c, S, 4 * min(512, S) + 512, 16 * S + 4 * min(512, S) + 1024)
        post_ln(c, s32, rS32, None, None, "d")
        c.close_flex()

    rout = c.reg("out")
    if _ph("OUT") or _phset is None:
        for ch in range(8):
            DMA(c, out_ap[ch * 128:(ch + 1) * 128, :], s32[:, ch * S:(ch + 1) * S],
                reads=[rS32], writes=[rout])
    c.p.op("SQ", lambda e: e.nop(), reads=[rout], writes=[])

    c.p.check_deadlock()
    sems = {}
    cms = []
    for e in Prog.SEMS:
        cm = nc.semaphore(f"sem_{e}")
        sems[e] = cm.__enter__()
        cms.append(cm)
    bcm = nc.Block()
    block = bcm.__enter__()
    c.p.emit(block, sems)
    bcm.__exit__(None, None, None)
    for cm in cms:
        cm.__exit__(None, None, None)
    return nc


def _prep_inputs(S, consts, **inp):
    T = lambda a: np.ascontiguousarray(np.asarray(a, dtype=np.float32).T)
    wqrT = T(inp["L0_attn_W_QR"])
    perm = [h * 32 + 2 * f + half for half in range(2) for h in range(H) for f in range(16)]
    wqrT = np.ascontiguousarray(wqrT[:, perm])
    wkrT = T(inp["L0_attn_W_KR"])
    kperm = [2 * f for f in range(16)] + [2 * f + 1 for f in range(16)]
    wkrT = np.ascontiguousarray(wkrT[:, kperm])
    wkv = np.asarray(inp["L1_attn_Wkv"], dtype=np.float32)
    wkT = T(wkv[0:256, :])
    base = {
        "wdqT": T(inp["L0_attn_W_DQ"]), "wuqT": T(inp["L0_attn_W_UQ"]),
        "wqrT": wqrT, "wdkvT": T(inp["L0_attn_W_DKV"]),
        "wukT": T(inp["L0_attn_W_UK"]), "wuvT": T(inp["L0_attn_W_UV"]),
        "wkrT": wkrT, "woT": T(inp["L0_attn_W_O"]),
        "wgT": T(inp["L0_ffn_w_gate"]), "wuT": T(inp["L0_ffn_w_up"]),
        "wdT": T(inp["L0_ffn_w_down"]),
        "wqT": T(inp["L1_attn_Wq"]), "wkT": np.ascontiguousarray(wkT),
        "wvT": T(wkv[256:512, :]), "wpT": T(inp["L1_attn_Wproj"]),
        "bprojT": np.ascontiguousarray(
            np.asarray(inp["L1_attn_bproj"], dtype=np.float32).reshape(8, 128).T),
        "w1T": np.ascontiguousarray(np.asarray(inp["L1_ffn_w1"], dtype=np.float32)),
        "b1T": np.ascontiguousarray(
            np.asarray(inp["L1_ffn_b1"], dtype=np.float32).reshape(32, 128).T),
        "w2T": np.ascontiguousarray(np.asarray(inp["L1_ffn_w2"], dtype=np.float32)),
        "b2T": np.ascontiguousarray(
            np.asarray(inp["L1_ffn_b2"], dtype=np.float32).reshape(8, 128).T),
        "cosb": consts["cosb"], "sinb": consts["sinb"], "m01c": consts["m01"],
        "relc": consts["relc"], "identc": consts["ident"],
        "sel2": np.concatenate([np.concatenate([np.ones((1, 128), np.float32), np.zeros((1, 128), np.float32)], axis=1),
                                np.concatenate([np.zeros((1, 128), np.float32), np.ones((1, 128), np.float32)], axis=1)], axis=0),
    }
    x = np.asarray(inp["L0_attn_x"], dtype=np.float32)
    return [dict(base, xT=np.ascontiguousarray(x[b].T)) for b in range(x.shape[0])]


def kernel(**inputs):
    S = int(inputs["L0_attn_x"].shape[1])
    consts = _host_consts(S)
    in_maps = _prep_inputs(S, consts, **inputs)
    nc = build_program(S)
    res = run_bass_kernel_spmd(nc, in_maps, core_ids=list(range(len(in_maps))))
    outs = [np.ascontiguousarray(np.asarray(r["outT"]).T) for r in res.results]
    return np.stack(outs, axis=0).astype(np.float32)


# version 11
# speedup vs baseline: 1.0969x; 1.0969x over previous
import os as _os
import numpy as np

import bass_rust
import concourse.bass as bass
import concourse.mybir as mybir
from concourse.bass_utils import run_bass_kernel_spmd
from concourse import library_config

F32 = mybir.dt.float32
F16 = mybir.dt.float16
I32 = mybir.dt.int32
ALU = mybir.AluOpType
ACT = mybir.ActivationFunctionType
RED = bass_rust.ReduceOp
AX = mybir.AxisListType

D = 1024
H = 16
DH = 64
DHR = 32
DLAT = 512
KVH = 4
FF = 4096

MAGIC32 = float(1.5 * 2.0**23)
MAGIC_MUL = float((1.5 * 2.0**52 - 1.5 * 2.0**23) * 2.0**30)



def _host_consts(S):
    import jax
    import jax.numpy as jnp
    cpu = jax.devices("cpu")[0]
    with jax.default_device(cpu):
        d = DHR
        inv_freq = jnp.asarray(10000.0) ** (-jnp.arange(0, d, 2, dtype=jnp.float32) / d)
        pos = jnp.arange(S)
        ang = pos.astype(jnp.float32)[:, None] * inv_freq[None, :]
        cos = np.asarray(jnp.cos(ang), dtype=np.float32)
        sin = np.asarray(jnp.sin(ang), dtype=np.float32)
        pwl_x = jnp.linspace(-4.0, 4.0, 33, dtype=jnp.float32)
        pwl_y = 0.5 * pwl_x * (1.0 + jax.lax.erf(pwl_x / jnp.sqrt(jnp.float32(2.0))))
        X = np.asarray(pwl_x, dtype=np.float32)
        Y = np.asarray(pwl_y, dtype=np.float32)

    cosb = np.ascontiguousarray(np.tile(cos.T, (8, 1)).astype(np.float32))
    sinb = np.ascontiguousarray(np.tile(sin.T, (8, 1)).astype(np.float32))
    m01 = np.triu(np.ones((128, 128), dtype=np.float16))

    start = 2.0 ** (-2.0 ** (-(np.log2(H) - 3.0)))
    slopes = (start ** np.arange(1, H + 1)).astype(np.float32)

    WM = 2 * S - 128
    relc = np.minimum((S - 128) - np.arange(WM, dtype=np.float64)[None, :]
                      + np.arange(128, dtype=np.float64)[:, None], 0.0).astype(np.float32)
    ident = np.eye(128, dtype=np.float16)

    Y64 = Y.astype(np.float64)
    m = (Y64[1:] - Y64[:-1]) * 4.0
    mv = m / 256.0
    w = mv[1:] - mv[:-1]
    kxv = (X[1:32].astype(np.float64) * 256.0)
    c1 = mv[0]
    c0 = Y64[0] + 1024.0 * mv[0] - float(np.sum(w * kxv))

    def hilo(v):
        hi = np.float32(v)
        lo = np.float32(np.float64(v) - np.float64(hi))
        return float(hi), float(lo)

    pwl = {
        "c0": hilo(float(Y64[0] + 1024.0 * mv[0])),
        "c1": hilo(c1),
        "w": [hilo(x) for x in w],
        "kx": [float(x) for x in kxv],
        "eu_w": hilo((1.0 - m[-1]) / 256.0),
        "eu_j": hilo(np.float64(np.float32(4.0)) - Y64[32]),
        "el_w": hilo(-mv[0]),
        "el_c": hilo(-mv[0] * 1024.0),
        "el_j": hilo(-Y64[0]),
    }
    return dict(cosb=cosb, sinb=sinb, m01=m01, slopes=slopes, pwl=pwl,
                relc=relc, ident=ident)



class Region:
    __slots__ = ("name", "writer", "readers")

    def __init__(self, name):
        self.name = name
        self.writer = None
        self.readers = []


class Prog:
    ENGS = ("TE", "VE", "SE", "GE", "SQ")
    NDS = 8
    SEMS = (("TE", "VE", "SE", "GEc", "SQc")
            + tuple(f"GEd{i}" for i in range(8)) + tuple(f"SQd{i}" for i in range(8)))

    def __init__(self, nc):
        self.nc = nc
        self.cnt = {sm: 0 for sm in self.SEMS}
        self.ops = {e: [] for e in self.ENGS}
        self.waited = {e: {sm: 0 for sm in self.SEMS} for e in self.ENGS}
        self._dma_rr = {"GE": 0, "SQ": 0}

    fence = None

    def _sem(self, eng, dma):
        if eng in ("GE", "SQ"):
            if dma:
                i = self._dma_rr[eng]
                self._dma_rr[eng] = (i + 1) % self.NDS
                return f"{eng}d{i}"
            return eng + "c"
        return eng

    def op(self, eng, fn, reads=(), writes=(), dma=False):
        waits = {}

        def dep(tok):
            if tok is None:
                return
            sm, c2 = tok
            if self.waited[eng][sm] < c2:
                waits[sm] = max(waits.get(sm, 0), c2)

        if self.fence:
            for sm, c2 in self.fence.items():
                dep((sm, c2))
        for r in reads:
            dep(r.writer)
        for r in writes:
            dep(r.writer)
            for t in r.readers:
                dep(t)
        sm = self._sem(eng, dma)
        if dma:
            dep((sm, self.cnt[sm]))
        for s2, c2 in waits.items():
            self.waited[eng][s2] = c2
        inc = 16 if dma else 1
        self.cnt[sm] += inc
        tok = (sm, self.cnt[sm])
        for r in reads:
            r.readers.append(tok)
        for r in writes:
            r.writer = tok
            r.readers = []
        self.ops[eng].append((waits, fn, sm, inc))
        return tok

    def check_deadlock(self):
        done = {sm: 0 for sm in self.SEMS}
        idx = {e: 0 for e in self.ENGS}
        total = sum(len(v) for v in self.ops.values())
        ndone = 0
        progress = True
        while progress:
            progress = False
            for e in self.ENGS:
                while idx[e] < len(self.ops[e]):
                    waits, _, sm, inc = self.ops[e][idx[e]]
                    if all(done[s2] >= c2 for s2, c2 in waits.items()):
                        done[sm] += inc
                        idx[e] += 1
                        ndone += 1
                        progress = True
                    else:
                        break
        if ndone != total:
            stuck = {e: (idx[e], len(self.ops[e]), self.ops[e][idx[e]][0])
                     for e in self.ENGS if idx[e] < len(self.ops[e])}
            raise RuntimeError(f"emission deadlock: {stuck}")

    def emit(self, block, sems):
        handles = {"TE": self.nc.tensor, "VE": self.nc.vector, "SE": self.nc.scalar,
                   "GE": self.nc.gpsimd, "SQ": self.nc.sync}

        def mk(eng):
            h = handles[eng]

            def body(_):
                for waits, fn, sm, inc in self.ops[eng]:
                    for s2 in sorted(waits):
                        h.wait_ge(sems[s2], waits[s2])
                    fn(h).then_inc(sems[sm], inc)
            return body

        for eng, deco in (("SQ", block.sync), ("TE", block.tensor),
                          ("VE", block.vector), ("SE", block.scalar),
                          ("GE", block.gpsimd)):
            if self.ops[eng]:
                deco(mk(eng))


class Tile:
    __slots__ = ("view", "c0", "P", "C", "alloc_idx", "arena")

    def __init__(self, view, c0, P, C, alloc_idx, arena):
        self.view = view
        self.c0 = c0
        self.P = P
        self.C = C
        self.alloc_idx = alloc_idx
        self.arena = arena

    def __getitem__(self, idx):
        rows, cols = idx
        if isinstance(rows, slice):
            r0 = rows.start or 0
            r1 = self.P if rows.stop is None else rows.stop
        else:
            r0, r1 = rows, rows + 1
        if isinstance(cols, slice):
            cs0 = cols.start or 0
            cs1 = self.C if cols.stop is None else cols.stop
            st = cols.step
            if st is None:
                return self.view[r0:r1, self.c0 + cs0: self.c0 + cs1]
            return self.view[r0:r1, self.c0 + cs0: self.c0 + cs1: st]
        return self.view[r0:r1, self.c0 + cols: self.c0 + cols + 1]


class Arena:

    def __init__(self, view, n):
        self.view = view
        self.free_list = [(0, n)]
        self.n = n
        self.used = {}

    def alloc(self, name, P, C):
        need = (C + 31) & ~31
        for i, (off, ln) in enumerate(self.free_list):
            if ln >= need:
                if ln == need:
                    self.free_list.pop(i)
                else:
                    self.free_list[i] = (off + need, ln - need)
                t = Tile(self.view, off, P, C, None, self)
                self.used[id(t)] = (off, need)
                return t
        raise MemoryError(f"arena OOM for {name}: need {need}, free {self.free_list}")

    def release(self, t):
        off, need = self.used.pop(id(t))
        self.free_list.append((off, need))
        self.free_list.sort()
        merged = []
        for off, ln in self.free_list:
            if merged and merged[-1][0] + merged[-1][1] == off:
                merged[-1] = (merged[-1][0], merged[-1][1] + ln)
            else:
                merged.append((off, ln))
        self.free_list = [tuple(x) for x in merged]


class Ctx:
    def __init__(self, nc, S, consts):
        self.nc = nc
        self.S = S
        self.consts = consts
        self.p = Prog(nc)
        self._cms = []
        self._psum_stack = []
        self._n = 0
        self.t = {}
        self.r = {}
        self.dram = {}
        self.dreg = {}
        self.flex = {}
        self.flex_order = []

    def perm(self, name, shape, dtype):
        self._n += 1
        cm = self.nc.sbuf_tensor(f"{name}_{self._n}", list(shape), dtype)
        t = cm.__enter__()
        self._cms.append(cm)
        return t

    def open_flex(self, n32=0, n16=0):
        assert not self.flex, "flex already open"
        self._n += 1
        if n32:
            cm = self.nc.sbuf_tensor(f"fx32_{self._n}", [128, n32], F32)
            self.flex[F32] = (Arena(cm.__enter__(), n32), cm)
            self.flex_order.append(F32)
        if n16:
            cm = self.nc.sbuf_tensor(f"fx16_{self._n}", [128, n16], F16)
            self.flex[F16] = (Arena(cm.__enter__(), n16), cm)
            self.flex_order.append(F16)

    def close_flex(self):
        self._fence_all()
        for dt in reversed(self.flex_order):
            self.flex[dt][1].__exit__(None, None, None)
        self.flex = {}
        self.flex_order = []

    def sbuf(self, name, shape, dtype):
        assert len(shape) == 2
        if dtype == I32:
            dtype = F32
        ar = self.flex[dtype][0]
        return ar.alloc(name, shape[0], shape[1])

    def sbuf_i32(self, name, shape):
        self._n += 1
        cm = self.nc.sbuf_tensor(f"{name}_{self._n}", list(shape), I32)
        t = cm.__enter__()
        self._psum_stack.append((t, cm))
        return t

    def psum(self, name, shape, dtype=F32):
        self._n += 1
        cm = self.nc.psum_tensor(f"{name}_{self._n}", list(shape), dtype)
        t = cm.__enter__()
        self._psum_stack.append((t, cm))
        return t

    def _fence_all(self):
        fence = dict(self.p.fence or {})
        for name, r in self.r.items():
            for tok in ([r.writer] if r.writer else []) + r.readers:
                e2, c2 = tok
                fence[e2] = max(fence.get(e2, 0), c2)
        self.p.fence = fence

    def free(self, *tensors):
        self._fence_all()
        psums = [t for t in tensors if not isinstance(t, Tile)]
        if psums:
            order = {id(t): i for i, (t, _) in enumerate(self._psum_stack)}
            for t in sorted(psums, key=lambda t: -order[id(t)]):
                i = next(i for i, (tt, _) in enumerate(self._psum_stack) if tt is t)
                _, cm = self._psum_stack.pop(i)
                cm.__exit__(None, None, None)
        for t in tensors:
            if isinstance(t, Tile):
                t.arena.release(t)

    def reg(self, name):
        if name not in self.r:
            self.r[name] = Region(name)
        return self.r[name]


def TS(c, out, in0, s1, s2, op0, op1, reads, writes, eng="VE"):
    return c.p.op(eng, lambda e: e.tensor_scalar(out, in0, s1, s2, op0, op1),
                  reads=reads, writes=writes)


def TT(c, out, in0, in1, op, reads, writes, eng="VE"):
    return c.p.op(eng, lambda e: e.tensor_tensor(out, in0, in1, op),
                  reads=reads, writes=writes)


def STT(c, out, in0, scalar, in1, op0, op1, reads, writes):
    return c.p.op("VE", lambda e: e.scalar_tensor_tensor(out, in0, scalar, in1, op0, op1),
                  reads=reads, writes=writes)


def CP(c, out, in0, reads, writes, eng="VE"):
    if eng == "SE":
        return c.p.op(eng, lambda e: e.activation(out, in0, ACT.Copy),
                      reads=reads, writes=writes)
    return c.p.op(eng, lambda e: e.tensor_copy(out, in0), reads=reads, writes=writes)


def ACTV(c, out, in0, func, reads, writes, bias=None, scale=None):
    kw = {}
    if bias is not None:
        kw["bias"] = bias
    if scale is not None:
        kw["scale"] = scale
    return c.p.op("SE", lambda e: e.activation(out, in0, func, **kw),
                  reads=reads, writes=writes)


def MM(c, psum_ap, lhsT_ap, rhs_ap, first, last, reads, writes, tp=None):
    kw = {}
    if tp is not None:
        kw["tile_position"] = tp
    return c.p.op("TE", lambda e: e.matmul(psum_ap, lhsT_ap, rhs_ap,
                                           start=first, stop=last, **kw),
                  reads=reads, writes=writes)


def MEMSET(c, ap, val, writes, eng="VE"):
    return c.p.op(eng, lambda e: e.memset(ap, val), reads=[], writes=writes)


def DMA(c, out_ap, in_ap, reads, writes, eng="SQ"):
    return c.p.op(eng, lambda e: e.dma_start(out=out_ap, in_=in_ap),
                  reads=reads, writes=writes, dma=True)


def RECIP(c, out, in0, reads, writes):
    return c.p.op("VE", lambda e: e.reciprocal(out, in0), reads=reads, writes=writes)


def round_ip(c, ap, reads, writes):
    ceff = c.t["Ceff"]
    return TS(c, ap, ap, ceff, ceff, ALU.add, ALU.subtract,
              reads=list(reads) + [c.reg("consts")], writes=writes)


def load_w16(c, name, key, rows, cols, scale=None):
    nch = rows // 128
    st = c.sbuf(name + "_st", [128, cols], F32)
    wt = c.sbuf(name, [128, nch * cols], F16)
    reg = c.reg(name)
    rst = c.reg(name + "_st")
    for ch in range(nch):
        DMA(c, st[:, :], c.dram[key][ch * 128:(ch + 1) * 128, :], reads=[rst], writes=[rst])
        if scale is None:
            CP(c, wt[:, ch * cols:(ch + 1) * cols], st[:, :], reads=[rst], writes=[reg])
        else:
            TS(c, wt[:, ch * cols:(ch + 1) * cols], st[:, :], float(scale), None,
               ALU.mult, ALU.bypass, [rst], [reg])
    c.free(st)
    return wt, reg



def setup_consts(c):
    cst = c.perm("cst", [128, 56], F32)
    c.t["cst"] = cst
    R = c.reg("consts")
    MEMSET(c, cst[:, 0:1], 2.0**-30, writes=[R])
    TS(c, cst[:, 0:1], cst[:, 0:1], 2.0**22, 2.0**22, ALU.add, ALU.subtract, [R], [R])
    TS(c, cst[:, 1:2], cst[:, 0:1], MAGIC_MUL, MAGIC32, ALU.mult, ALU.add, [R], [R])
    c.t["Ceff"] = cst[:, 1:2]
    sqrt96 = float(np.float32(np.sqrt(np.float64(96.0))))
    MEMSET(c, cst[:, 2:3], sqrt96, writes=[R])
    RECIP(c, cst[:, 2:3], cst[:, 2:3], [R], [R])
    TS(c, cst[:, 3:4], cst[:, 2:3], -80.0, None, ALU.mult, ALU.bypass, [R], [R])
    c.t["rc96"] = cst[:, 2:3]
    c.t["negC0"] = cst[:, 3:4]
    MEMSET(c, cst[:, 4:5], 1e-5, writes=[R])
    c.t["epsc"] = cst[:, 4:5]
    pw = c.consts["pwl"]
    coeffs = ([pw["c0"], pw["c1"]] + pw["w"]
              + [pw["eu_w"], pw["eu_j"], pw["el_w"], pw["el_c"], pw["el_j"]])
    for i, (hi, lo) in enumerate(coeffs):
        TS(c, cst[:, 8 + i:9 + i], cst[:, 0:1], float(lo) * 2.0**30, float(hi),
           ALU.mult, ALU.add, [R], [R])
    c.PWL0 = 8

    ones16 = c.perm("ones16", [128, 1], F16)
    MEMSET(c, ones16[:, :], 1.0, writes=[R])
    c.t["ones16"] = ones16
    onesrow = c.perm("onesrow", [1, 128], F16)
    MEMSET(c, onesrow[:, :], 1.0, writes=[R])
    c.t["onesrow"] = onesrow
    onesrow32 = c.perm("onesrow32", [1, 128], F32)
    MEMSET(c, onesrow32[:, :], 1.0, writes=[R])
    c.t["onesrow32"] = onesrow32
    ones32c = c.perm("ones32c", [128, 1], F32)
    MEMSET(c, ones32c[:, :], 1.0, writes=[R])
    c.t["ones32c"] = ones32c
    mbar = c.perm("mbar", [128, 8], F32)
    c.t["mbar"] = mbar
    hp = c.perm("hpones", [128, 2], F16)
    MEMSET(c, hp[:, :], 0.0, writes=[R])
    MEMSET(c, hp[0:64, 0:1], 1.0, writes=[R])
    MEMSET(c, hp[64:128, 1:2], 1.0, writes=[R])
    c.t["hpones"] = hp



def post_ln(c, s32, rR, s16, rD, tag):
    S = c.S
    TB = min(512, S)
    NTB = S // TB
    r16 = c.sbuf(f"lnr{tag}", [128, 8 * S], F16)
    sq = c.sbuf(f"lnq{tag}", [128, 8 * S], F16)
    rT = c.reg(f"lnr{tag}")
    rQ = c.reg(f"lnq{tag}")
    for ch in range(8):
        sl = slice(ch * S, (ch + 1) * S)
        CP(c, r16[:, sl], s32[:, sl], reads=[rR], writes=[rT])
        ACTV(c, sq[:, sl], s32[:, sl], ACT.Square, reads=[rR], writes=[rQ])
    stat = c.sbuf(f"lns{tag}", [1, 4 * TB], F32)
    st16 = c.sbuf(f"ln6{tag}", [1, 2 * TB], F16)
    rS = c.reg(f"lns{tag}")
    rS16 = c.reg(f"ln6{tag}")
    ps1 = c.psum(f"lp1{tag}", [1, TB])
    ps2 = c.psum(f"lp2{tag}", [1, TB])
    rP1 = c.reg(f"lp1{tag}")
    rP2 = c.reg(f"lp2{tag}")
    psb1 = c.psum(f"lb1{tag}", [128, TB])
    psb2 = c.psum(f"lb2{tag}", [128, TB])
    rB1 = c.reg(f"lb1{tag}")
    rB2 = c.reg(f"lb2{tag}")
    ones = c.t["ones16"]
    onesrow = c.t["onesrow"]
    rC = c.reg("consts")
    eps = 1e-5
    m_ = stat[0:1, 0:TB]
    v_ = stat[0:1, TB:2 * TB]
    s_ = stat[0:1, 2 * TB:3 * TB]
    w_ = stat[0:1, 3 * TB:4 * TB]
    for tb in range(NTB):
        for ch in range(8):
            sl = slice(ch * S + tb * TB, ch * S + (tb + 1) * TB)
            MM(c, ps1[0:1, :], ones[:, 0:1], r16[:, sl], ch == 0, ch == 7,
               reads=[rT, rC], writes=[rP1])
            MM(c, ps2[0:1, :], ones[:, 0:1], sq[:, sl], ch == 0, ch == 7,
               reads=[rQ, rC], writes=[rP2])
        TS(c, m_, ps1[0:1, :], 1.0 / D, None, ALU.mult, ALU.bypass, [rP1], [rS])
        TS(c, v_, ps2[0:1, :], 1.0 / D, None, ALU.mult, ALU.bypass, [rP2], [rS])
        TT(c, w_, m_, m_, ALU.mult, [rS], [rS])
        TT(c, v_, v_, w_, ALU.subtract, [rS], [rS])
        ACTV(c, s_, v_, ACT.Sqrt, reads=[rS, c.reg("consts")], writes=[rS], bias=c.t["cst"][0:1, 4:5])
        RECIP(c, w_, s_, [rS], [rS])
        TS(c, v_, v_, eps, None, ALU.add, ALU.bypass, [rS], [rS])
        TT(c, w_, v_, w_, ALU.mult, [rS], [rS])
        TT(c, w_, w_, s_, ALU.add, [rS], [rS])
        TS(c, w_, w_, 0.5, None, ALU.mult, ALU.bypass, [rS], [rS])
        RECIP(c, w_, w_, [rS], [rS])
        CP(c, st16[0:1, 0:TB], m_, reads=[rS], writes=[rS16])
        CP(c, st16[0:1, TB:2 * TB], w_, reads=[rS], writes=[rS16])
        MM(c, psb1[:, :], onesrow[0:1, :], st16[0:1, 0:TB], True, True,
           reads=[rS16, rC], writes=[rB1])
        MM(c, psb2[:, :], onesrow[0:1, :], st16[0:1, TB:2 * TB], True, True,
           reads=[rS16, rC], writes=[rB2])
        for ch in range(8):
            sl = slice(ch * S + tb * TB, ch * S + (tb + 1) * TB)
            TT(c, s32[:, sl], s32[:, sl], psb1[:, :], ALU.subtract, [rR, rB1], [rR])
            TT(c, s32[:, sl], s32[:, sl], psb2[:, :], ALU.mult, [rR, rB2], [rR])
            if s16 is not None:
                ACTV(c, s16[:, sl], s32[:, sl], ACT.Copy, reads=[rR], writes=[rD])
    c.free(r16, sq, stat, st16, psb2, psb1, ps2, ps1)



def l0_attn(c, s16, rS16):
    S = c.S
    TB = min(512, S)
    NTB = S // TB
    NSC = S // 128
    p = c.p
    rC = c.reg("consts")
    rc96 = c.t["rc96"]

    xst = c.sbuf("xst", [128, S], F32)
    rXst = c.reg("xst")
    for ch in range(8):
        DMA(c, xst[:, :], c.dram["xT"][ch * 128:(ch + 1) * 128, :],
            reads=[rXst], writes=[rXst])
        CP(c, s16[:, ch * S:(ch + 1) * S], xst[:, :], reads=[rXst], writes=[rS16])
    c.free(xst)
    xt16 = s16
    rX16 = rS16

    ps = c.psum("l0p", [128, TB])
    rPS = c.reg("l0p")

    wdq, rwdq = load_w16(c, "wdq", "wdqT", D, DLAT)
    wdkv, rwdkv = load_w16(c, "wdkv", "wdkvT", D, DLAT)
    cq = c.sbuf("cq16", [128, 4 * S], F16)
    ckv = c.sbuf("ckv16", [128, 4 * S], F16)
    rCQ = c.reg("cq16")
    rCKV = c.reg("ckv16")
    for dst, wsrc, rw, rdst in ((cq, wdq, rwdq, rCQ), (ckv, wdkv, rwdkv, rCKV)):
        for lc in range(4):
            for tb in range(NTB):
                for ch in range(8):
                    MM(c, ps[:, :], wsrc[:, ch * DLAT + lc * 128: ch * DLAT + (lc + 1) * 128],
                       xt16[:, ch * S + tb * TB: ch * S + (tb + 1) * TB],
                       ch == 0, ch == 7, reads=[rw, rX16], writes=[rPS])
                CP(c, dst[:, lc * S + tb * TB: lc * S + (tb + 1) * TB], ps[:, :],
                   reads=[rPS], writes=[rdst], eng="SE")
    c.free(wdq, wdkv)

    wuq, rwuq = load_w16(c, "wuq", "wuqT", DLAT, H * DH)
    qc = c.sbuf("qc16", [128, 8 * S], F16)
    rQC = c.reg("qc16")
    for oc in range(8):
        for tb in range(NTB):
            for lc in range(4):
                MM(c, ps[:, :], wuq[:, lc * (H * DH) + oc * 128: lc * (H * DH) + (oc + 1) * 128],
                   cq[:, lc * S + tb * TB: lc * S + (tb + 1) * TB],
                   lc == 0, lc == 3, reads=[rwuq, rCQ], writes=[rPS])
            TS(c, qc[:, oc * S + tb * TB: oc * S + (tb + 1) * TB], ps[:, :],
               rc96, None, ALU.mult, ALU.bypass, [rPS, rC], [rQC])
    c.free(wuq)

    wqr, rwqr = load_w16(c, "wqr", "wqrT", DLAT, H * DHR)
    cosb = c.sbuf("cosb", [128, S], F32)
    sinb = c.sbuf("sinb", [128, S], F32)
    rCos = c.reg("cosb")
    DMA(c, cosb[:, :], c.dram["cosb"], reads=[rCos], writes=[rCos])
    DMA(c, sinb[:, :], c.dram["sinb"], reads=[rCos], writes=[rCos])
    scr = c.sbuf("ropes", [128, 2 * S], F32)
    rScr = c.reg("ropes")
    qrev = c.sbuf("qrev", [128, S], F32)
    qrod = c.sbuf("qrod", [128, S], F32)
    rQRE = c.reg("qrev")
    rQRO = c.reg("qrod")
    qrev16 = c.sbuf("qrev16", [128, 2 * S], F16)
    qrod16 = c.sbuf("qrod16", [128, 2 * S], F16)
    rQE16 = c.reg("qrev16")
    rQO16 = c.reg("qrod16")

    def rope(np_, ev, od, ev16, od16, rE, rO, rE16, rO16, cs, sn):
        t1 = scr[0:np_, 0:S]
        t2 = scr[0:np_, S:2 * S]
        TT(c, t1, ev, cs, ALU.mult, [rE, rCos], [rScr])
        TT(c, t2, od, sn, ALU.mult, [rO, rCos], [rScr])
        TT(c, ev16, t1, t2, ALU.subtract, [rScr], [rE16])
        TT(c, t1, ev, sn, ALU.mult, [rE, rCos], [rScr])
        TT(c, t2, od, cs, ALU.mult, [rO, rCos], [rScr])
        TT(c, od16, t1, t2, ALU.add, [rScr], [rO16])

    for oc in range(2):
        for half, dst, rdst in ((0, qrev, rQRE), (1, qrod, rQRO)):
            for tb in range(NTB):
                for lc in range(4):
                    col = lc * (H * DHR) + half * 256 + oc * 128
                    MM(c, ps[:, :], wqr[:, col: col + 128],
                       cq[:, lc * S + tb * TB: lc * S + (tb + 1) * TB],
                       lc == 0, lc == 3, reads=[rwqr, rCQ], writes=[rPS])
                TS(c, dst[:, tb * TB:(tb + 1) * TB], ps[:, :],
                   rc96, None, ALU.mult, ALU.bypass, [rPS, rC], [rdst])
        sl = slice(oc * S, (oc + 1) * S)
        rope(128, qrev[:, :], qrod[:, :], qrev16[:, sl], qrod16[:, sl],
             rQRE, rQRO, rQE16, rQO16, cosb[:, :], sinb[:, :])
    c.free(wqr, qrev, qrod, cq)

    qrt2 = c.sbuf("qrt2", [128, 4 * S], F16)
    rQR2 = c.reg("qrt2")
    for h in range(H):
        cc, rr = divmod(h, 4)
        sch, sr = divmod(h, 8)
        DMA(c, qrt2[rr * 32: rr * 32 + 16, cc * S:(cc + 1) * S],
            qrev16[sr * 16: sr * 16 + 16, sch * S:(sch + 1) * S],
            reads=[rQE16], writes=[rQR2], eng="GE")
        DMA(c, qrt2[rr * 32 + 16: rr * 32 + 32, cc * S:(cc + 1) * S],
            qrod16[sr * 16: sr * 16 + 16, sch * S:(sch + 1) * S],
            reads=[rQO16], writes=[rQR2], eng="GE")
    c.free(qrev16, qrod16)

    wkr, rwkr = load_w16(c, "wkr", "wkrT", D, DHR)
    krev = c.sbuf("krev", [16, S], F32)
    krod = c.sbuf("krod", [16, S], F32)
    rKRE = c.reg("krev")
    rKRO = c.reg("krod")
    psk = c.psum("l0pk", [16, TB])
    rPSK = c.reg("l0pk")
    for half, dst, rdst in ((0, krev, rKRE), (1, krod, rKRO)):
        for tb in range(NTB):
            for ch in range(8):
                MM(c, psk[0:16, :], wkr[:, ch * DHR + half * 16: ch * DHR + half * 16 + 16],
                   xt16[:, ch * S + tb * TB: ch * S + (tb + 1) * TB],
                   ch == 0, ch == 7, reads=[rwkr, rX16], writes=[rPSK])
            CP(c, dst[0:16, tb * TB:(tb + 1) * TB], psk[0:16, :], reads=[rPSK],
               writes=[rdst], eng="SE")
    krev16 = c.sbuf("krev16", [16, S], F16)
    krod16 = c.sbuf("krod16", [16, S], F16)
    rKE16 = c.reg("krev16")
    rKO16 = c.reg("krod16")
    rope(16, krev[0:16, :], krod[0:16, :], krev16[0:16, :], krod16[0:16, :],
         rKRE, rKRO, rKE16, rKO16, cosb[0:16, :], sinb[0:16, :])
    c.free(wkr, krev, krod, scr, cosb, sinb, psk)
    krt2 = c.sbuf("krt2", [128, S], F16)
    rKR2 = c.reg("krt2")
    for b in range(4):
        DMA(c, krt2[b * 32: b * 32 + 16, :], krev16[0:16, :], reads=[rKE16],
            writes=[rKR2], eng="GE")
        DMA(c, krt2[b * 32 + 16: b * 32 + 32, :], krod16[0:16, :], reads=[rKO16],
            writes=[rKR2], eng="GE")
    c.free(krev16, krod16)

    wuk, rwuk = load_w16(c, "wuk", "wukT", DLAT, H * DH)
    kc = c.sbuf("kc16", [128, 8 * S], F16)
    rKC = c.reg("kc16")
    for oc in range(8):
        for tb in range(NTB):
            for lc in range(4):
                MM(c, ps[:, :], wuk[:, lc * (H * DH) + oc * 128: lc * (H * DH) + (oc + 1) * 128],
                   ckv[:, lc * S + tb * TB: lc * S + (tb + 1) * TB],
                   lc == 0, lc == 3, reads=[rwuk, rCKV], writes=[rPS])
            CP(c, kc[:, oc * S + tb * TB: oc * S + (tb + 1) * TB], ps[:, :],
               reads=[rPS], writes=[rKC], eng="SE")
    c.free(wuk)

    wuv, rwuv = load_w16(c, "wuv", "wuvT", DLAT, H * DH)
    m01 = c.sbuf("m01", [128, 128], F16)
    rM01 = c.reg("m01")
    DMA(c, m01[:, :], c.dram["m01c"], reads=[rM01], writes=[rM01])
    vb = [c.sbuf(f"l0v{i}", [128, NSC * 66], F16) for i in range(2)]
    rVb = [c.reg(f"l0v{i}") for i in range(2)]
    for i in range(2):
        MEMSET(c, vb[i][:, 64::66], 1.0, writes=[rVb[i]])
    pexp2 = [c.sbuf(f"l0pe{i}", [128, TB], F16) for i in range(2)]
    rPexp2 = [c.reg(f"l0pe{i}") for i in range(2)]
    dn = c.sbuf("l0dn", [1, 3 * TB], F32)
    rDn = c.reg("l0dn")
    dn16 = c.sbuf("l0d6", [1, TB], F16)
    rDn16 = c.reg("l0d6")
    bcs = c.sbuf("l0bc", [64, TB], F32)
    rBcs = c.reg("l0bcs")
    aodd = c.sbuf("l0ao", [64, TB], F16)
    rAodd = c.reg("l0ao")
    atp0 = c.sbuf("l0at", [64, TB], F32)
    rAtp0 = c.reg("l0at")
    psS2 = [c.psum(f"l0sc{i}", [128, TB]) for i in range(2)]
    rSc2 = [c.reg(f"l0sc{i}") for i in range(2)]
    psPV2 = [c.psum(f"l0pv{i}", [65, TB]) for i in range(2)]
    rPV2 = [c.reg(f"l0pv{i}") for i in range(2)]
    psV2 = [c.psum(f"l0vv{i}", [128, 64]) for i in range(2)]
    rPsV2 = [c.reg(f"l0vv{i}") for i in range(2)]
    psB = c.psum("l0bb", [64, TB])
    rB = c.reg("l0bb")
    negC0 = c.t["negC0"]
    onesrow = c.t["onesrow"]
    explo = float(np.exp(np.float64(-160.0) * (1.0 / np.float64(np.float32(np.sqrt(np.float64(96.0)))))))

    tails = [None]

    def flush_tail():
        if tails[0] is None:
            return
        kch, kro, tb, psPV, rPV = tails[0]
        tails[0] = None
        CP(c, dn[0:1, 0:TB], psPV[64:65, :], reads=[rPV], writes=[rDn])
        ACTV(c, dn[0:1, TB:2 * TB], dn[0:1, 0:TB], ACT.Ln, reads=[rDn], writes=[rDn])
        ACTV(c, dn[0:1, 2 * TB:3 * TB], dn[0:1, TB:2 * TB], ACT.Exp,
             reads=[rDn], writes=[rDn], scale=-0.5)
        CP(c, dn16[0:1, :], dn[0:1, 2 * TB:3 * TB], reads=[rDn], writes=[rDn16])
        MM(c, psB[0:64, :], onesrow[0:1, 0:64], dn16[0:1, :], True, True,
           reads=[rDn16, rC], writes=[rB])
        CP(c, bcs[:, :], psB[0:64, :], reads=[rB], writes=[rBcs], eng="SE")
        TT(c, atp0[0:64, :], psPV[0:64, :], bcs[:, :], ALU.mult, [rPV, rBcs],
           [rAtp0])
        if kro == 0:
            TT(c, s16[0:64, kch * S + tb * TB: kch * S + (tb + 1) * TB],
               atp0[0:64, :], bcs[:, :], ALU.mult, [rAtp0, rBcs], [rS16])
        else:
            TT(c, aodd[0:64, 0:TB], atp0[0:64, :], bcs[:, :], ALU.mult,
               [rAtp0, rBcs], [rAodd])
            DMA(c, s16[64:128, kch * S + tb * TB: kch * S + (tb + 1) * TB],
                aodd[0:64, 0:TB], reads=[rAodd], writes=[rS16], eng="GE")

    for h in range(H):
        vcur, rV = vb[h % 2], rVb[h % 2]
        for tc in range(NSC):
            psV, rPsV = psV2[tc % 2], rPsV2[tc % 2]
            for lc in range(4):
                MM(c, psV[:, :], ckv[:, lc * S + tc * 128: lc * S + (tc + 1) * 128],
                   wuv[:, lc * (H * DH) + h * DH: lc * (H * DH) + h * DH + DH],
                   lc == 0, lc == 3, reads=[rCKV, rwuv], writes=[rPsV])
            CP(c, vcur[:, tc * 66: tc * 66 + 64], psV[:, :], reads=[rPsV],
               writes=[rV], eng="SE")
        kch, kro = divmod(h, 2)
        qrc, qrr = divmod(h, 4)
        base = kro * 64
        rbase = qrr * 32
        for tb in range(NTB):
            nsc = (tb + 1) * TB // 128
            pv = (h * NTB + tb) % 2
            psPV, rPV = psPV2[pv], rPV2[pv]
            for sc in range(nsc):
                pi = sc % 2
                psS, rSc = psS2[pi], rSc2[pi]
                pexp, rPexp = pexp2[pi], rPexp2[pi]
                t0 = max(tb * TB, sc * 128)
                w = (tb + 1) * TB - t0
                MM(c, psS[0:128, 0:w],
                   kc[base:base + 64, kch * S + sc * 128: kch * S + (sc + 1) * 128],
                   qc[base:base + 64, kch * S + t0: kch * S + t0 + w],
                   True, False, reads=[rKC, rQC], writes=[rSc], tp=(base, 0))
                MM(c, psS[0:128, 0:w],
                   krt2[rbase:rbase + 32, sc * 128:(sc + 1) * 128],
                   qrt2[rbase:rbase + 32, qrc * S + t0: qrc * S + t0 + w],
                   False, True, reads=[rKR2, rQR2], writes=[rSc], tp=(rbase, 0))
                if sc == 1:
                    flush_tail()
                ACTV(c, pexp[:, 0:w], psS[0:128, 0:w], ACT.Exp,
                     reads=[rSc], writes=[rPexp], bias=negC0)
                TS(c, pexp[:, 0:w], pexp[:, 0:w], 1.0, explo,
                   ALU.min, ALU.max, [rPexp], [rPexp])
                if t0 == sc * 128:
                    TT(c, pexp[:, 0:128], pexp[:, 0:128], m01[:, :], ALU.mult,
                       [rPexp, rM01], [rPexp])
                MM(c, psPV[0:65, t0 - tb * TB: t0 - tb * TB + w],
                   vcur[:, sc * 66: sc * 66 + 65], pexp[:, 0:w],
                   sc == 0, sc == nsc - 1, reads=[rV, rPexp], writes=[rPV])
            tails[0] = (kch, kro, tb, psPV, rPV)
    flush_tail()
    c.free(pexp2[0], pexp2[1], dn, dn16, bcs, aodd, atp0, vb[0], vb[1], m01,
           qc, kc, qrt2, krt2, ckv, wuv,
           psB, psV2[1], psV2[0], psPV2[1], psPV2[0], psS2[1], psS2[0], ps)


def l0_wo(c, s16, rS16, x1, rX1):
    S = c.S
    TB = min(512, S)
    NTB = S // TB
    wo, rwo = load_w16(c, "wo", "woT", H * DH, D)
    ps = c.psum("wop", [128, TB])
    rPS = c.reg("wop")
    rx = c.sbuf("worx", [128, 2 * TB], F32)
    rRx = [c.reg("worx0"), c.reg("worx1")]
    for dc in range(8):
        for tb in range(NTB):
            bu = (dc * NTB + tb) % 2
            DMA(c, rx[:, bu * TB:(bu + 1) * TB],
                c.dram["xT"][dc * 128:(dc + 1) * 128, tb * TB:(tb + 1) * TB],
                reads=[rRx[bu]], writes=[rRx[bu]])
            for oc in range(8):
                MM(c, ps[:, :], wo[:, oc * D + dc * 128: oc * D + (dc + 1) * 128],
                   s16[:, oc * S + tb * TB: oc * S + (tb + 1) * TB],
                   oc == 0, oc == 7, reads=[rwo, rS16], writes=[rPS])
            TT(c, x1[:, dc * S + tb * TB: dc * S + (tb + 1) * TB], ps[:, :],
               rx[:, bu * TB:(bu + 1) * TB], ALU.add, [rPS, rRx[bu]], [rX1])
    c.free(wo, rx, ps)



def prequant(c):
    p = c.p
    rC = c.reg("consts")
    stats = c.sbuf("fqs", [128, 16], F32)
    rStat = c.reg("fqs")
    st = c.sbuf("fqt", [128, FF], F32)
    rSt = c.reg("fqt")
    tn = c.sbuf("fqn", [128, FF], F16)
    rTn = c.reg("fqn")
    tmp = c.sbuf("fqm", [128, FF], F32)
    rTmp = c.reg("fqm")
    psR = c.psum("fqp", [1, 8])
    rPsR = c.reg("fqp")
    psBC = c.psum("fqb", [128, 4])
    rPsBC = c.reg("fqb")
    specs = [("wgT", "wgc", D, FF), ("wuT", "wuc", D, FF), ("wdT", "wdc", FF, D)]
    for wi, (key, skey, rows, cols) in enumerate(specs):
        nch = rows // 128
        for ch in range(nch):
            DMA(c, st[:, 0:cols], c.dram[key][ch * 128:(ch + 1) * 128, :],
                reads=[rSt], writes=[rSt])
            cslot = wi * 4 + (ch % 2)
            p.op("VE", lambda e, cslot=cslot, cols=cols: e.tensor_reduce(
                stats[:, cslot], st[:, 0:cols], AX.X, ALU.add,
                apply_absolute_value=True), reads=[rSt], writes=[rStat])
            if ch == 0:
                CP(c, stats[:, wi * 4 + 2], stats[:, cslot], reads=[rStat], writes=[rStat])
            else:
                TT(c, stats[:, wi * 4 + 2], stats[:, wi * 4 + 2],
                   stats[:, cslot], ALU.add, [rStat], [rStat])
        MM(c, psR[0:1, wi: wi + 1], c.t["ones32c"][:, 0:1],
           stats[:, wi * 4 + 2], True, True, reads=[rStat, rC], writes=[rPsR])
        n = float(rows) * float(cols)
        TS(c, stats[0:1, wi * 4 + 3], psR[0:1, wi: wi + 1], 1.0 / n, 1e-5,
           ALU.mult, ALU.max, [rPsR], [rStat])
    MM(c, psBC[:, 0:3], c.t["onesrow32"][0:1, :], stats[0:1, 3:15:4], True, True,
       reads=[rStat, rC], writes=[rPsBC])
    mbar = c.t["mbar"]
    rMbar = c.reg("mbar")
    CP(c, mbar[:, 0:3], psBC[:, 0:3], reads=[rPsBC], writes=[rMbar])
    TS(c, mbar[:, 3:6], mbar[:, 0:3], 0.5, None, ALU.mult, ALU.bypass, [rMbar], [rMbar])
    TS(c, mbar[:, 7:8], mbar[:, 1:2], 16.0, None, ALU.mult, ALU.bypass, [rMbar], [rMbar])
    c.free(psBC, psR)
    for wi, (key, skey, rows, cols) in enumerate(specs):
        nch = rows // 128
        thr = mbar[:, 3 + wi: 4 + wi]
        TS(c, mbar[:, 6:7], mbar[:, 3 + wi: 4 + wi], -1.0, None, ALU.mult, ALU.bypass,
           [rMbar], [rMbar])
        for ch in range(nch):
            DMA(c, st[:, 0:cols], c.dram[key][ch * 128:(ch + 1) * 128, :],
                reads=[rSt], writes=[rSt])
            TS(c, tmp[:, 0:cols], st[:, 0:cols], mbar[:, 6:7], None, ALU.is_lt,
               ALU.bypass, [rSt, rMbar], [rTmp])
            p.op("VE", lambda e, cols=cols, thr=thr: e.scalar_tensor_tensor(
                tn[:, 0:cols], st[:, 0:cols], thr, tmp[:, 0:cols],
                ALU.is_gt, ALU.subtract), reads=[rSt, rTmp, rMbar], writes=[rTn])
            DMA(c, c.dram[skey][ch * (cols // 128) * 128:(ch + 1) * (cols // 128) * 128, :]
                .rearrange("(o p) q -> p o q", p=128),
                tn[:, 0:cols], reads=[rTn], writes=[c.dreg[skey]])
    c.free(tmp, tn, st, stats)


def l0_ffn(c, x1, rX1, s16, rS16):
    S = c.S
    TB = min(256, S)
    NTB = S // TB
    NOC = FF // 128
    p = c.p
    rC = c.reg("consts")
    sx = c.sbuf("aqs", [128, 2 * S], F32)
    rSx = c.reg("aqs")
    mbar = c.t["mbar"]
    rMbar = c.reg("mbar")
    psBB = c.psum("aqb", [128, 128])
    rPsBB = c.reg("aqb")

    mx = c.sbuf("aqm", [128, S], F32)
    rMx = c.reg("aqm")
    axs = c.sbuf("aqab", [128, S], F32)
    rAxs = c.reg("aqab")
    ACTV(c, mx[:, :], x1[:, 0:S], ACT.Abs, reads=[rX1], writes=[rMx])
    for ch in range(1, 8):
        ACTV(c, axs[:, :], x1[:, ch * S:(ch + 1) * S], ACT.Abs, reads=[rX1],
             writes=[rAxs])
        TT(c, mx[:, :], mx[:, :], axs[:, :], ALU.max, [rAxs, rMx], [rMx])
    c.free(axs)
    NSC = S // 128
    red = c.sbuf("aqrd", [64, S], F32)
    rRed = c.reg("aqrd")
    cur = 128
    while cur > 1:
        h = cur // 2
        DMA(c, red[0:h, 0:S], mx[h:cur, :], reads=[rMx], writes=[rRed], eng="GE")
        TT(c, mx[0:h, :], mx[0:h, :], red[0:h, 0:S], ALU.max, [rRed, rMx], [rMx])
        cur = h
    if _os.environ.get("KDBG_SXD"):
        DMA(c, c.dram["outT"][256:257, :], mx[0:1, :], reads=[rMx],
            writes=[c.reg("out")])
    c.free(red)
    sxr = c.sbuf("aqxr", [1, 2 * S], F32)
    rSxr = c.reg("aqxr")
    rowx = c.sbuf("aqrw", [1, 2 * S], F16)
    rRowx = c.reg("aqrw")
    TS(c, mx[0:1, :], mx[0:1, :], 1e-5, None, ALU.max, ALU.bypass, [rMx], [rMx])
    RECIP(c, sxr[0:1, 0:S], mx[0:1, :], [rMx], [rSxr])
    TS(c, sxr[0:1, 0:S], sxr[0:1, 0:S], 127.0, None, ALU.mult, ALU.bypass,
       [rSxr], [rSxr])
    CP(c, rowx[0:1, 0:S], sxr[0:1, 0:S], reads=[rSxr], writes=[rRowx])
    CP(c, sxr[0:1, S:2 * S], rowx[0:1, 0:S], reads=[rRowx], writes=[rSxr])
    TT(c, sxr[0:1, S:2 * S], sxr[0:1, 0:S], sxr[0:1, S:2 * S], ALU.subtract,
       [rSxr], [rSxr])
    CP(c, rowx[0:1, S:2 * S], sxr[0:1, S:2 * S], reads=[rSxr], writes=[rRowx])
    onesrow = c.t["onesrow"]
    for tc in range(NSC):
        MM(c, psBB[:, :], onesrow[0:1, :], rowx[0:1, tc * 128:(tc + 1) * 128],
           True, False, reads=[rRowx, rC], writes=[rPsBB])
        MM(c, psBB[:, :], onesrow[0:1, :], rowx[0:1, S + tc * 128: S + (tc + 1) * 128],
           False, True, reads=[rRowx, rC], writes=[rPsBB])
        CP(c, sx[:, tc * 128:(tc + 1) * 128], psBB[:, :], reads=[rPsBB], writes=[rSx])
    RECIP(c, sx[:, S:2 * S], sx[:, 0:S], [rSx], [rSx])
    if _os.environ.get("KDBG_SXD"):
        DMA(c, c.dram["outT"][0:128, :], sx[0:128, 0:S], reads=[rSx],
            writes=[c.reg("out")])
        DMA(c, c.dram["outT"][128:256, :], sx[0:128, S:2 * S], reads=[rSx],
            writes=[c.reg("out")])
    c.free(rowx, sxr, mx)
    qs = c.sbuf("aqc", [128, S], F32)
    rQs = c.reg("aqc")
    xq = s16
    rXq = rS16
    for ch in range(8):
        sl = slice(ch * S, (ch + 1) * S)
        TT(c, qs[:, :], x1[:, sl], sx[:, 0:S], ALU.mult, [rX1, rSx], [rQs])
        round_ip(c, qs[:, :], [rQs], [rQs])
        TS(c, xq[:, sl], qs[:, :], 127.0, -128.0, ALU.min, ALU.max, [rQs], [rXq])
    c.free(qs)

    if getattr(c, "tap", None) == "ffn":
        DMA(c, c.dram["dbgA"][:, 0:8 * S], xq[:, 0:8 * S], reads=[rXq], writes=[c.dreg["dbgA"]])
        DMA(c, c.dram["dbgB"][:, 0:2 * S], sx[0:128, 0:2 * S], reads=[rSx], writes=[c.dreg["dbgB"]])
        DMA(c, c.dram["dbgD"][:, 0:7], mbar[0:128, 0:7], reads=[rMbar], writes=[c.dreg["dbgD"]])
    for ch in range(8):
        DMA(c, c.dram["x1s"][ch * 128:(ch + 1) * 128, :], x1[:, ch * S:(ch + 1) * S],
            reads=[rX1], writes=[c.dreg["x1s"]])

    prod = c.sbuf("ffp", [128, NOC * TB], F32)
    rProd = c.reg("ffp")
    pq = c.sbuf("ffq", [128, NOC * TB], F16)
    rPq = c.reg("ffq")
    gsl = c.sbuf("ffg", [128, TB], F32)
    rGsl = c.reg("ffg")
    wb = [c.sbuf(f"ffw{i}", [128, 8 * 128], F16) for i in range(4)]
    rWb = [c.reg(f"ffw{i}") for i in range(4)]
    dwb = [c.sbuf(f"ffd{i}", [128, NOC * 128], F16) for i in range(2)]
    rDwb = [c.reg(f"ffd{i}") for i in range(2)]
    rmax = c.sbuf("ffm", [128, TB], F32)
    rRmax = c.reg("ffm")
    redr = c.sbuf("ffrd", [64, TB], F32)
    rRedr = c.reg("ffrd")
    fab = c.sbuf("ffab", [128, TB], F32)
    rFab = c.reg("ffab")
    rxr = c.sbuf("ffxr", [1, TB], F32)
    rRxr = c.reg("ffxr")
    rowr = c.sbuf("ffrr", [1, 2 * TB], F16)
    rRowr = c.reg("ffrr")
    rmaxb = c.sbuf("ffmb", [128, TB], F32)
    rRmaxb = c.reg("ffmb")
    s2 = c.sbuf("ff2", [128, 3 * TB], F32)
    rS2 = c.reg("ff2")
    rx1 = c.sbuf("ffr", [128, 2 * TB], F32)
    rRx1 = [c.reg("ffr0"), c.reg("ffr1")]
    x2 = x1
    rX2 = rX1
    psG2 = [c.psum(f"ffpg{i}", [128, TB]) for i in range(2)]
    psU2 = [c.psum(f"ffpu{i}", [128, TB]) for i in range(2)]
    rPsG2 = [c.reg(f"ffpg{i}") for i in range(2)]
    rPsU2 = [c.reg(f"ffpu{i}") for i in range(2)]
    wgc2 = c.dram["wgc"].rearrange("(ch o p) q -> o p ch q", o=NOC, p=128)
    wuc2 = c.dram["wuc"].rearrange("(ch o p) q -> o p ch q", o=NOC, p=128)
    wdc2 = c.dram["wdc"].rearrange("(ch o p) q -> o p ch q", o=8, p=128)

    a_ = s2[:, 0:TB]
    b_ = s2[:, TB:2 * TB]
    f_ = s2[:, 2 * TB:3 * TB]
    for tb in range(NTB):
        tsl = slice(tb * TB, (tb + 1) * TB)
        for oc in range(NOC):
            bG = oc % 2
            bU = 2 + oc % 2
            psG, rPsG = psG2[oc % 2], rPsG2[oc % 2]
            psU, rPsU = psU2[oc % 2], rPsU2[oc % 2]
            DMA(c, wb[bG][:, :], wgc2[oc, :, :, :], reads=[rWb[bG]], writes=[rWb[bG]])
            DMA(c, wb[bU][:, :], wuc2[oc, :, :, :], reads=[rWb[bU]], writes=[rWb[bU]])
            for ch in range(8):
                MM(c, psG[:, :], wb[bG][:, ch * 128:(ch + 1) * 128],
                   xq[:, ch * S + tb * TB: ch * S + (tb + 1) * TB],
                   ch == 0, ch == 7, reads=[rWb[bG], rXq], writes=[rPsG])
            for ch in range(8):
                MM(c, psU[:, :], wb[bU][:, ch * 128:(ch + 1) * 128],
                   xq[:, ch * S + tb * TB: ch * S + (tb + 1) * TB],
                   ch == 0, ch == 7, reads=[rWb[bU], rXq], writes=[rPsU])
            osl = slice(oc * TB, (oc + 1) * TB)
            p.op("VE", lambda e, osl=osl, tb=tb, psG=psG: e.scalar_tensor_tensor(
                prod[:, osl], psG[:, :], mbar[:, 0:1], sx[:, S + tb * TB: S + (tb + 1) * TB],
                ALU.mult, ALU.mult), reads=[rPsG, rMbar, rSx], writes=[rProd])
            ACTV(c, gsl[:, :], prod[:, osl], ACT.Silu, reads=[rProd], writes=[rGsl])
            TT(c, prod[:, osl], gsl[:, :], psU[:, :], ALU.mult, [rGsl, rPsU], [rProd])
            if oc == 0:
                p.op("VE", lambda e, osl=osl: e.scalar_tensor_tensor(
                    rmax[:, :], prod[:, osl], -1.0, prod[:, osl], ALU.mult, ALU.max),
                    reads=[rProd], writes=[rRmax])
            else:
                p.op("VE", lambda e, osl=osl: e.scalar_tensor_tensor(
                    fab[:, :], prod[:, osl], -1.0, prod[:, osl], ALU.mult, ALU.max),
                    reads=[rProd], writes=[rFab])
                TT(c, rmax[:, :], rmax[:, :], fab[:, :], ALU.max,
                   [rFab, rRmax], [rRmax])
        cur = 128
        while cur > 1:
            h = cur // 2
            DMA(c, redr[0:h, 0:TB], rmax[h:cur, :], reads=[rRmax], writes=[rRedr],
                eng="GE")
            TT(c, rmax[0:h, :], rmax[0:h, :], redr[0:h, 0:TB], ALU.max,
               [rRedr, rRmax], [rRmax])
            cur = h
        TS(c, rxr[0:1, 0:TB], rmax[0:1, :], 2.0**-4, None, ALU.mult, ALU.bypass,
           [rRmax], [rRxr])
        CP(c, rowr[0:1, 0:TB], rxr[0:1, 0:TB], reads=[rRxr], writes=[rRowr])
        CP(c, redr[0:1, 0:TB], rowr[0:1, 0:TB], reads=[rRowr], writes=[rRedr])
        TT(c, rxr[0:1, 0:TB], rxr[0:1, 0:TB], redr[0:1, 0:TB], ALU.subtract,
           [rRedr, rRxr], [rRxr])
        CP(c, rowr[0:1, TB:2 * TB], rxr[0:1, 0:TB], reads=[rRxr], writes=[rRowr])
        for j in range(TB // 128):
            MM(c, psBB[:, :], c.t["onesrow"][0:1, :], rowr[0:1, j * 128:(j + 1) * 128],
               True, False, reads=[rRowr, rC], writes=[rPsBB])
            MM(c, psBB[:, :], c.t["onesrow"][0:1, :],
               rowr[0:1, TB + j * 128: TB + (j + 1) * 128],
               False, True, reads=[rRowr, rC], writes=[rPsBB])
            CP(c, rmaxb[:, j * 128:(j + 1) * 128], psBB[:, :], reads=[rPsBB],
               writes=[rRmaxb])
        p.op("VE", lambda e, tb=tb: e.scalar_tensor_tensor(
            a_, rmaxb[:, :], mbar[:, 7:8], sx[:, S + tb * TB: S + (tb + 1) * TB],
            ALU.mult, ALU.mult), reads=[rRmaxb, rMbar, rSx], writes=[rS2])
        TS(c, a_, a_, 1e-5, None, ALU.max, ALU.bypass, [rS2], [rS2])
        RECIP(c, b_, a_, [rS2], [rS2])
        TS(c, b_, b_, 127.0, None, ALU.mult, ALU.bypass, [rS2], [rS2])
        RECIP(c, f_, b_, [rS2], [rS2])
        TS(c, f_, f_, mbar[:, 2:3], None, ALU.mult, ALU.bypass, [rS2, rMbar], [rS2])
        p.op("VE", lambda e, tb=tb: e.scalar_tensor_tensor(
            a_, b_, mbar[:, 1:2], sx[:, S + tb * TB: S + (tb + 1) * TB],
            ALU.mult, ALU.mult), reads=[rS2, rMbar, rSx], writes=[rS2])
        if _os.environ.get("KDBG_SXD"):
            DMA(c, c.dram["outT"][300:301, tsl], rmaxb[0:1, :], reads=[rRmaxb],
                writes=[c.reg("out")])
            DMA(c, c.dram["outT"][301:302, tsl], s2[0:1, TB:2 * TB], reads=[rS2],
                writes=[c.reg("out")])
            DMA(c, c.dram["outT"][302:303, tsl], s2[0:1, 2 * TB:3 * TB], reads=[rS2],
                writes=[c.reg("out")])
            DMA(c, c.dram["outT"][303:304, tsl], s2[0:1, 0:TB], reads=[rS2],
                writes=[c.reg("out")])
        for oc in range(NOC):
            osl = slice(oc * TB, (oc + 1) * TB)
            TT(c, prod[:, osl], prod[:, osl], a_, ALU.mult, [rProd, rS2], [rProd])
            round_ip(c, prod[:, osl], [rProd], [rProd])
            TS(c, pq[:, osl], prod[:, osl], 127.0, -128.0, ALU.min, ALU.max,
               [rProd], [rPq])
        if tb == 0 and getattr(c, "tap", None) == "ffn":
            DMA(c, c.dram["dbgC"][:, 0:NOC * TB], pq[0:128, 0:NOC * TB],
                reads=[rPq], writes=[c.dreg["dbgC"]])
        if tb == 0 and getattr(c, "tap", None) in ("ffn", "x2pre"):
            DMA(c, c.dram["dbgE"][:, 0:3 * TB], s2[0:128, 0:3 * TB],
                reads=[rS2], writes=[c.dreg["dbgE"]])
        for dc in range(8):
            bu = dc % 2
            DMA(c, rx1[:, bu * TB:(bu + 1) * TB],
                c.dram["x1s"][dc * 128:(dc + 1) * 128, tsl],
                reads=[rRx1[bu], c.dreg["x1s"]], writes=[rRx1[bu]])
            DMA(c, dwb[bu][:, :], wdc2[dc, :, :, :], reads=[rDwb[bu]], writes=[rDwb[bu]])
            psG, rPsG = psG2[dc % 2], rPsG2[dc % 2]
            for ic in range(NOC):
                MM(c, psG[:, :], dwb[bu][:, ic * 128:(ic + 1) * 128],
                   pq[:, ic * TB:(ic + 1) * TB], ic == 0, ic == NOC - 1,
                   reads=[rDwb[bu], rPq], writes=[rPsG])
            xsl = slice(dc * S + tb * TB, dc * S + (tb + 1) * TB)
            TT(c, x2[:, xsl], psG[:, :], f_, ALU.mult, [rPsG, rS2], [rX2])
            TT(c, x2[:, xsl], x2[:, xsl], rx1[:, bu * TB:(bu + 1) * TB], ALU.add,
               [rRx1[bu], rX2], [rX2])
    c.free(prod, pq, gsl, rmax, redr, fab, rxr, rowr, rmaxb, s2, rx1, sx,
           wb[0], wb[1], wb[2], wb[3], dwb[0], dwb[1],
           psU2[1], psU2[0], psG2[1], psG2[0], psBB)



def l1_attn(c, x2, rX2, s16, rS16):
    S = c.S
    TB = min(512, S)
    NTB = S // TB
    NSC = S // 128
    WM = 2 * S - 128
    p = c.p
    rC = c.reg("consts")
    x2_16 = s16
    rX16 = rS16

    ps = c.psum("l1p", [128, TB])
    rPS = c.reg("l1p")

    qt = c.sbuf("l1q", [128, 8 * S], F16)
    rQ = c.reg("l1q")
    wq, rwq = load_w16(c, "wq", "wqT", D, D)
    for oc in range(8):
        for tb in range(NTB):
            for ch in range(8):
                MM(c, ps[:, :], wq[:, ch * D + oc * 128: ch * D + (oc + 1) * 128],
                   x2_16[:, ch * S + tb * TB: ch * S + (tb + 1) * TB],
                   ch == 0, ch == 7, reads=[rwq, rX16], writes=[rPS])
            TS(c, qt[:, oc * S + tb * TB: oc * S + (tb + 1) * TB], ps[:, :],
               0.125, None, ALU.mult, ALU.bypass, [rPS], [rQ])
    c.free(wq)

    kk = [c.sbuf(f"l1k{t}", [128, S], F16) for t in range(2)]
    rKK = [c.reg(f"l1k{t}") for t in range(2)]
    wk, rwk = load_w16(c, "wk", "wkT", D, KVH * DH)
    for t in range(2):
        for tb in range(NTB):
            for ch in range(8):
                MM(c, ps[:, :], wk[:, ch * (KVH * DH) + t * 128: ch * (KVH * DH) + (t + 1) * 128],
                   x2_16[:, ch * S + tb * TB: ch * S + (tb + 1) * TB],
                   ch == 0, ch == 7, reads=[rwk, rX16], writes=[rPS])
            CP(c, kk[t][:, tb * TB:(tb + 1) * TB], ps[:, :], reads=[rPS],
               writes=[rKK[t]], eng="SE")
    c.free(wk)

    vb = [c.sbuf(f"l1v{kv}", [128, NSC * 66], F16) for kv in range(KVH)]
    rVb = [c.reg(f"l1v{kv}") for kv in range(KVH)]
    psV = c.psum("l1vv", [128, 64])
    rPsV = c.reg("l1vv")
    wv, rwv = load_w16(c, "wv", "wvT", D, KVH * DH)
    for kv in range(KVH):
        MEMSET(c, vb[kv][:, 64::66], 1.0, writes=[rVb[kv]])
        for tc in range(NSC):
            for ch in range(8):
                MM(c, psV[:, :], x2_16[:, ch * S + tc * 128: ch * S + (tc + 1) * 128],
                   wv[:, ch * (KVH * DH) + kv * DH: ch * (KVH * DH) + (kv + 1) * DH],
                   ch == 0, ch == 7, reads=[rwv, rX16], writes=[rPsV])
            CP(c, vb[kv][:, tc * 66: tc * 66 + 64], psV[:, :], reads=[rPsV],
               writes=[rVb[kv]], eng="SE")
    c.free(wv, psV)

    sqc = c.sbuf("l1sx", [128, S], F16)
    rSqc = c.reg("l1sx")
    psN = c.psum("l1n", [1, TB])
    rPsN = c.reg("l1n")
    hp = c.t["hpones"]
    kn = c.sbuf("l1kn", [1, 8], F32)
    rKn = c.reg("l1kn")
    for kv in range(KVH):
        t, rb = divmod(kv, 2)
        rb *= 64
        ACTV(c, sqc[rb:rb + 64, :], kk[t][rb:rb + 64, :], ACT.Square,
             reads=[rKK[t]], writes=[rSqc])
        for tb in range(NTB):
            MM(c, psN[0:1, :], hp[rb:rb + 64, rb // 64: rb // 64 + 1],
               sqc[rb:rb + 64, tb * TB:(tb + 1) * TB],
               True, True, reads=[rSqc, rC], writes=[rPsN], tp=(rb, 0))
            slot = kv if tb == 0 else 4 + kv
            p.op("VE", lambda e, slot=slot: e.tensor_reduce(
                kn[0:1, slot: slot + 1], psN[0:1, :], AX.X, ALU.max),
                reads=[rPsN], writes=[rKn])
            if tb > 0:
                TT(c, kn[0:1, kv], kn[0:1, kv], kn[0:1, 4 + kv], ALU.max, [rKn], [rKn])
    ACTV(c, kn[0:1, 0:4], kn[0:1, 0:4], ACT.Sqrt, reads=[rKn], writes=[rKn],
         scale=2.0**-4)
    TS(c, kn[0:1, 4:8], kn[0:1, 0:4], -16.0, None, ALU.mult, ALU.bypass, [rKn], [rKn])
    qn = c.sbuf("l1qn", [1, S], F32)
    rQn = c.reg("l1qn")
    ncr = c.sbuf("l1cr", [1, S], F16)
    rNcr = c.reg("l1cr")
    ebias = c.sbuf("l1eb", [128, 1], F32)
    rEb = c.reg("l1eb")
    MEMSET(c, ebias[:, :], 10.0, writes=[rEb])

    slopes = c.consts["slopes"]
    escr = c.sbuf("l1es", [128, WM], F32)
    rEsc = c.reg("l1es")
    DMA(c, escr[:, :], c.dram["relc"], reads=[rEsc], writes=[rEsc])
    em = c.sbuf("l1e", [128, WM], F16)
    rEm = c.reg("l1e")
    GW = min(512, S)
    NTBG = GW // TB if TB < 512 else 1
    NTG = S // GW
    psS2 = [c.psum(f"l1sc{i}", [128, GW]) for i in range(2)]
    rSc2 = [c.reg(f"l1sc{i}") for i in range(2)]
    psPV2 = [c.psum(f"l1pv{i}", [65, GW]) for i in range(2)]
    rPV2 = [c.reg(f"l1pv{i}") for i in range(2)]
    psB = c.psum("l1bb", [64, GW])
    rB = c.reg("l1bb")
    pexp2 = [c.sbuf(f"l1pe{i}", [128, GW], F16) for i in range(2)]
    rPexp2 = [c.reg(f"l1pe{i}") for i in range(2)]
    dn = c.sbuf("l1dn", [1, 3 * TB], F32)
    rDn = c.reg("l1dn")
    dn16 = c.sbuf("l1d6", [1, TB], F16)
    rDn16 = c.reg("l1d6")
    bcs = c.sbuf("l1bc", [64, TB], F32)
    rBcs = c.reg("l1bcs")
    aodd = c.sbuf("l1ao", [64, TB], F16)
    rAodd = c.reg("l1ao")
    atp = c.sbuf("l1at", [64, TB], F32)
    rAtp = c.reg("l1at")
    onesrow = c.t["onesrow"]
    tails = [None]

    def flush_tail():
        if tails[0] is None:
            return
        ch, half, tb, psPV, rPV = tails[0]
        tails[0] = None
        CP(c, dn[0:1, 0:GW], psPV[64:65, :], reads=[rPV], writes=[rDn])
        ACTV(c, dn[0:1, GW:2 * GW], dn[0:1, 0:GW], ACT.Ln, reads=[rDn],
             writes=[rDn], scale=2.0**-10)
        ACTV(c, dn[0:1, 2 * GW:3 * GW], dn[0:1, GW:2 * GW], ACT.Exp,
             reads=[rDn], writes=[rDn], scale=-0.5)
        TS(c, dn16[0:1, 0:GW], dn[0:1, 2 * GW:3 * GW], 2.0**-5, None,
           ALU.mult, ALU.bypass, [rDn], [rDn16])
        MM(c, psB[0:64, :], onesrow[0:1, 0:64], dn16[0:1, 0:GW], True, True,
           reads=[rDn16, rC], writes=[rB])
        CP(c, bcs[:, 0:GW], psB[0:64, :], reads=[rB], writes=[rBcs], eng="SE")
        TT(c, atp[0:64, 0:GW], psPV[0:64, :], bcs[:, 0:GW], ALU.mult,
           [rPV, rBcs], [rAtp])
        if half == 0:
            TT(c, s16[0:64, ch * S + tb * GW: ch * S + (tb + 1) * GW],
               atp[0:64, 0:GW], bcs[:, 0:GW], ALU.mult, [rAtp, rBcs], [rS16])
        else:
            TT(c, aodd[0:64, 0:GW], atp[0:64, 0:GW], bcs[:, 0:GW], ALU.mult,
               [rAtp, rBcs], [rAodd])
            DMA(c, s16[64:128, ch * S + tb * GW: ch * S + (tb + 1) * GW],
                aodd[0:64, 0:GW], reads=[rAodd], writes=[rS16], eng="GE")

    for h in range(H):
        kv = h % KVH
        ch, half = divmod(h, 2)
        base = half * 64
        ACTV(c, sqc[base:base + 64, :], qt[base:base + 64, ch * S:(ch + 1) * S],
             ACT.Square, reads=[rQ], writes=[rSqc])
        for tb in range(NTB):
            MM(c, psN[0:1, :], hp[base:base + 64, half: half + 1],
               sqc[base:base + 64, tb * TB:(tb + 1) * TB],
               True, True, reads=[rSqc, rC], writes=[rPsN], tp=(base, 0))
            ACTV(c, qn[0:1, tb * TB:(tb + 1) * TB], psN[0:1, :], ACT.Sqrt,
                 reads=[rPsN], writes=[rQn], scale=2.0**-4)
        TS(c, ncr[0:1, :], qn[0:1, 0:S], kn[0:1, 4 + kv], None, ALU.mult,
           ALU.bypass, [rQn, rKn], [rNcr])
        ACTV(c, em[:, :], escr[:, :], ACT.Exp, reads=[rEsc], writes=[rEm],
             scale=float(slopes[h]) * 0.125)
        for tg in range(NTG):
            tb = tg
            pv = (h * NTG + tg) % 2
            psPV, rPV = psPV2[pv], rPV2[pv]
            for sc in range(NSC):
                pi = sc % 2
                psS, rSc = psS2[pi], rSc2[pi]
                pexp, rPexp = pexp2[pi], rPexp2[pi]
                MM(c, psS[:, :],
                   onesrow[0:1, :], ncr[0:1, tb * GW:(tb + 1) * GW],
                   True, False, reads=[rNcr, rC], writes=[rSc])
                MM(c, psS[:, :],
                   kk[kv // 2][base: base + 64, sc * 128:(sc + 1) * 128],
                   qt[base: base + 64, ch * S + tb * GW: ch * S + (tb + 1) * GW],
                   False, True, reads=[rKK[kv // 2], rQ], writes=[rSc], tp=(base, 0))
                ACTV(c, pexp[:, :], psS[:, :], ACT.Exp, reads=[rSc, rEb], writes=[rPexp],
                     bias=ebias[:, 0:1])
                off = tg * GW - sc * 128 + (S - 128)
                TT(c, pexp[:, :], pexp[:, :], em[:, off: off + GW], ALU.mult,
                   [rPexp, rEm], [rPexp])
                if sc == 1:
                    flush_tail()
                MM(c, psPV[0:65, :], vb[kv][:, sc * 66: sc * 66 + 65],
                   pexp[:, :], sc == 0, sc == NSC - 1,
                   reads=[rVb[kv], rPexp], writes=[rPV])
            tails[0] = (ch, half, tb, psPV, rPV)
    flush_tail()
    c.free(pexp2[0], pexp2[1], dn, dn16, bcs, aodd, atp, em, escr, ebias, ncr,
           qn, kn, sqc, qt, kk[0], kk[1], vb[0], vb[1], vb[2], vb[3],
           psB, psPV2[1], psPV2[0], psS2[1], psS2[0], psN)

    wp, rwp = load_w16(c, "wp", "wpT", D, D)
    bpj = c.sbuf("l1bp", [128, 8], F32)
    rBpj = c.reg("l1bp")
    DMA(c, bpj[:, :], c.dram["bprojT"], reads=[rBpj], writes=[rBpj])
    for dc in range(8):
        for tb in range(NTB):
            for oc in range(8):
                MM(c, ps[:, :], wp[:, oc * D + dc * 128: oc * D + (dc + 1) * 128],
                   s16[:, oc * S + tb * TB: oc * S + (tb + 1) * TB],
                   oc == 0, oc == 7, reads=[rwp, rS16], writes=[rPS])
            p.op("VE", lambda e, dc=dc, tb=tb: e.scalar_tensor_tensor(
                x2[:, dc * S + tb * TB: dc * S + (tb + 1) * TB], ps[:, :],
                bpj[:, dc], x2[:, dc * S + tb * TB: dc * S + (tb + 1) * TB],
                ALU.add, ALU.add), reads=[rPS, rBpj, rX2], writes=[rX2])
    c.free(wp, bpj, ps)



def precache_w12(c):
    rC = c.reg("consts")
    st = c.sbuf("f1s", [128, FF], F32)
    rSt = c.reg("f1s")
    t16 = c.sbuf("f1t", [128, FF], F16)
    rT16 = c.reg("f1t")
    for key, skey, rows, cols in (("w1T", "w1c", D, FF), ("w2T", "w2c", FF, D)):
        for ch in range(rows // 128):
            DMA(c, st[:, 0:cols], c.dram[key][ch * 128:(ch + 1) * 128, :],
                reads=[rSt], writes=[rSt])
            CP(c, t16[:, 0:cols], st[:, 0:cols], reads=[rSt], writes=[rT16])
            DMA(c, c.dram[skey][ch * (cols // 128) * 128:(ch + 1) * (cols // 128) * 128, :]
                .rearrange("(o p) q -> p o q", p=128),
                t16[:, 0:cols], reads=[rT16], writes=[c.dreg[skey]])
    c.free(t16, st)


def l1_ffn(c, x3, rX3, s16, rS16):
    S = c.S
    TB = min(512, S)
    NTB = S // TB
    NOC = FF // 128
    NQ = 8
    QOC = NOC // NQ
    p = c.p
    rC = c.reg("consts")
    cst = c.t["cst"]
    P0 = c.PWL0
    pw = c.consts["pwl"]
    x3_16 = s16
    rX16 = rS16

    kxb = c.sbuf("f1kx", [128, 31], F32)
    rKxb = c.reg("f1kx")
    for k in range(31):
        MEMSET(c, kxb[:, k:k + 1], -float(pw["kx"][k]), writes=[rKxb])
        w1c2 = c.dram["w1c"].rearrange("(ch o p) q -> o p ch q", o=NOC, p=128)
    w2c2 = c.dram["w2c"].rearrange("(ch o p) q -> o p ch q", o=8, p=128)

    b1 = c.sbuf("f1b1", [128, 32], F32)
    rB1 = c.reg("f1b1")
    DMA(c, b1[:, :], c.dram["b1T"], reads=[rB1], writes=[rB1])
    b2 = c.sbuf("f1b2", [128, 8], F32)
    rB2 = c.reg("f1b2")
    DMA(c, b2[:, :], c.dram["b2T"], reads=[rB2], writes=[rB2])

    for ch in range(8):
        DMA(c, c.dram["x3s"][ch * 128:(ch + 1) * 128, :], x3[:, ch * S:(ch + 1) * S],
            reads=[rX3], writes=[c.dreg["x3s"]])

    QW = QOC * TB
    vq = c.sbuf("f1v", [128, QW], F32)
    rV = c.reg("f1v")
    acc = c.sbuf("f1a", [128, QW], F32)
    rA = c.reg("f1a")
    tkr2 = [c.sbuf(f"f1k{i}", [128, QW], F32) for i in range(2)]
    rK2 = [c.reg(f"f1k{i}") for i in range(2)]
    tkr, rK = tkr2[0], rK2[0]
    hq = c.sbuf("f1h", [128, NOC * TB], F16)
    rHq = c.reg("f1h")
    wb = [c.sbuf(f"f1w{i}", [128, 8 * 128], F16) for i in range(2)]
    rWb = [c.reg(f"f1w{i}") for i in range(2)]
    w2b = [c.sbuf(f"f1u{i}", [128, NOC * 128], F16) for i in range(2)]
    rW2b = [c.reg(f"f1u{i}") for i in range(2)]
    rx3 = c.sbuf("f1r", [128, 2 * TB], F32)
    rRx3 = [c.reg("f1r0"), c.reg("f1r1")]
    x4 = x3
    rX4 = rX3
    psH2 = [c.psum(f"f1p{i}", [128, TB]) for i in range(2)]
    rPsH2 = [c.reg(f"f1p{i}") for i in range(2)]

    for tb in range(NTB):
        tsl = slice(tb * TB, (tb + 1) * TB)
        for qh in range(NQ):
            for j in range(QOC):
                oc = qh * QOC + j
                b = oc % 2
                psH, rPsH = psH2[oc % 2], rPsH2[oc % 2]
                DMA(c, wb[b][:, :], w1c2[oc, :, :, :], reads=[rWb[b]], writes=[rWb[b]])
                for ch in range(8):
                    MM(c, psH[:, :], wb[b][:, ch * 128:(ch + 1) * 128],
                       x3_16[:, ch * S + tb * TB: ch * S + (tb + 1) * TB],
                       ch == 0, ch == 7, reads=[rWb[b], rX16], writes=[rPsH])
                TS(c, vq[:, j * TB:(j + 1) * TB], psH[:, :], b1[:, oc], 256.0,
                   ALU.add, ALU.mult, [rPsH, rB1], [rV])
            round_ip(c, vq[:, :], [rV], [rV])
            TS(c, acc[:, :], vq[:, :], cst[:, P0 + 1: P0 + 2], cst[:, P0: P0 + 1],
               ALU.mult, ALU.add, [rV, rC], [rA])
            for k in range(31):
                tk, rTk = tkr2[k % 2], rK2[k % 2]
                ACTV(c, tk[:, :], vq[:, :], ACT.Relu, reads=[rV, rC], writes=[rTk],
                     bias=kxb[:, k:k + 1])
                p.op("VE", lambda e, k=k, tk=tk: e.scalar_tensor_tensor(
                    acc[:, :], tk[:, :], cst[:, P0 + 2 + k: P0 + 3 + k], acc[:, :],
                    ALU.mult, ALU.add), reads=[rTk, rC, rA], writes=[rA])
            TS(c, tkr[:, :], vq[:, :], 1024.0, -1024.0, ALU.max, ALU.add, [rV], [rK])
            TS(c, tkr[:, :], tkr[:, :], cst[:, P0 + 33: P0 + 34], None, ALU.mult,
               ALU.bypass, [rK, rC], [rK])
            TT(c, acc[:, :], acc[:, :], tkr[:, :], ALU.add, [rK, rA], [rA])
            TS(c, tkr[:, :], vq[:, :], 1024.0, cst[:, P0 + 34: P0 + 35], ALU.is_gt,
               ALU.mult, [rV, rC], [rK])
            TT(c, acc[:, :], acc[:, :], tkr[:, :], ALU.add, [rK, rA], [rA])
            TS(c, tkr[:, :], vq[:, :], -1024.0, 1024.0, ALU.min, ALU.add, [rV], [rK])
            TS(c, tkr[:, :], tkr[:, :], cst[:, P0 + 35: P0 + 36], None, ALU.mult,
               ALU.bypass, [rK, rC], [rK])
            TT(c, acc[:, :], acc[:, :], tkr[:, :], ALU.add, [rK, rA], [rA])
            TS(c, tkr[:, :], vq[:, :], -1024.0, cst[:, P0 + 37: P0 + 38], ALU.is_lt,
               ALU.mult, [rV, rC], [rK])
            TT(c, acc[:, :], acc[:, :], tkr[:, :], ALU.add, [rK, rA], [rA])
            TS(c, acc[:, :], acc[:, :], 256.0, None, ALU.mult, ALU.bypass, [rA], [rA])
            round_ip(c, acc[:, :], [rA], [rA])
            TS(c, hq[:, qh * QW:(qh + 1) * QW], acc[:, :], 2.0**-8, None, ALU.mult,
               ALU.bypass, [rA], [rHq])
        for dc in range(8):
            bu = dc % 2
            DMA(c, rx3[:, bu * TB:(bu + 1) * TB],
                c.dram["x3s"][dc * 128:(dc + 1) * 128, tsl],
                reads=[rRx3[bu], c.dreg["x3s"]], writes=[rRx3[bu]])
            DMA(c, w2b[bu][:, :], w2c2[dc, :, :, :], reads=[rW2b[bu]], writes=[rW2b[bu]])
            psH, rPsH = psH2[dc % 2], rPsH2[dc % 2]
            for ic in range(NOC):
                MM(c, psH[:, :], w2b[bu][:, ic * 128:(ic + 1) * 128],
                   hq[:, ic * TB:(ic + 1) * TB], ic == 0, ic == NOC - 1,
                   reads=[rW2b[bu], rHq], writes=[rPsH])
            xsl = slice(dc * S + tb * TB, dc * S + (tb + 1) * TB)
            TS(c, x4[:, xsl], psH[:, :], b2[:, dc], 256.0,
               ALU.add, ALU.mult, [rPsH, rB2], [rX4])
            round_ip(c, x4[:, xsl], [rX4], [rX4])
            p.op("VE", lambda e, xsl=xsl, bu=bu: e.scalar_tensor_tensor(
                x4[:, xsl], x4[:, xsl], 2.0**-8, rx3[:, bu * TB:(bu + 1) * TB],
                ALU.mult, ALU.add), reads=[rRx3[bu], rX4], writes=[rX4])
    c.free(vq, acc, tkr2[0], tkr2[1], hq, wb[0], wb[1], w2b[0], w2b[1], rx3, b1, b2, kxb,
           psH2[1], psH2[0])



def _flex(c, S, n32, n16):
    if S < 2048:
        n32 = max(n32, 24576)
        n16 = max(n16, 49152)
    c.open_flex(n32=int(n32), n16=int(n16))


def build_program(S=2048, tap=None):
    consts = _host_consts(S)
    nc = bass.Bass("TRN2", target_bir_lowering=False)
    c = Ctx(nc, S, consts)

    def din(name, shape, dtype=F32):
        c.dram[name] = nc.dram_tensor(name, list(shape), dtype, kind="ExternalInput").ap()
        c.dreg[name] = c.reg("d_" + name)

    def dscr(name, shape, dtype):
        c.dram[name] = nc.dram_tensor(name, list(shape), dtype, kind="Internal").ap()
        c.dreg[name] = c.reg("d_" + name)

    din("xT", (D, S))
    out_ap = nc.dram_tensor("outT", [D, S], F32, kind="ExternalOutput").ap()
    c.dram["outT"] = out_ap
    din("wdqT", (D, DLAT)); din("wuqT", (DLAT, H * DH)); din("wqrT", (DLAT, H * DHR))
    din("wdkvT", (D, DLAT)); din("wukT", (DLAT, H * DH)); din("wuvT", (DLAT, H * DH))
    din("wkrT", (D, DHR)); din("woT", (H * DH, D))
    din("wgT", (D, FF)); din("wuT", (D, FF)); din("wdT", (FF, D))
    din("wqT", (D, D)); din("wkT", (D, KVH * DH)); din("wvT", (D, KVH * DH))
    din("wpT", (D, D)); din("bprojT", (128, 8))
    din("w1T", (D, FF)); din("b1T", (128, 32)); din("w2T", (FF, D)); din("b2T", (128, 8))
    din("cosb", (128, S)); din("sinb", (128, S))
    din("m01c", (128, 128), F16)
    din("relc", (128, 2 * S - 128))
    din("identc", (128, 128), F16)
    din("sel2", (2, 256))
    dscr("wgc", (8 * 32 * 128, 128), F16)
    dscr("wuc", (8 * 32 * 128, 128), F16)
    dscr("wdc", (32 * 8 * 128, 128), F16)
    dscr("w1c", (8 * 32 * 128, 128), F16)
    dscr("w2c", (32 * 8 * 128, 128), F16)
    dscr("x1s", (D, S), F32)
    dscr("dbgA", (128, 8 * S), F16)
    dscr("dbgB", (128, 2 * S), F32)
    dscr("dbgC", (128, 64 * 256), F16)
    dscr("dbgD", (128, 64), F32)
    dscr("dbgE", (128, 2048), F32)
    dscr("dbgX", (D, S), F32)
    dscr("x3s", (D, S), F32)

    setup_consts(c)
    _kph = _os.environ.get("KPH")
    _phset = set(_kph.split(",")) if _kph else None
    _ph = lambda nm: _phset is None or nm in _phset
    s16 = c.perm("stream16", [128, 8 * S], F16)
    rS16 = c.reg("S16")

    if _ph("A"):
        _flex(c, S, 4 * S + 5888, 29 * S + 1024)
        l0_attn(c, s16, rS16)
        c.close_flex()

    s32 = c.perm("stream32", [128, 8 * S], F32)
    rS32 = c.reg("S32")

    if _ph("B"):
        _flex(c, S, 2 * min(512, S) + 2 * FF + 6144, 4 * S + FF + 2048)
        l0_wo(c, s16, rS16, s32, rS32)
        prequant(c)
        precache_w12(c)
        c.close_flex()

    if _ph("L1"):
        _flex(c, S, 4 * min(512, S) + 512, 16 * S + 4 * min(512, S) + 1024)
        post_ln(c, s32, rS32, None, None, "a")
        c.close_flex()

    c.tap = tap
    if _ph("C"):
        _flex(c, S, 2 * S + 2 * FF + 3072, 64 * min(256, S) + FF + 2048)
        l0_ffn(c, s32, rS32, s16, rS16)
        c.close_flex()

    if tap == "x2pre":
        for ch in range(8):
            DMA(c, c.dram["dbgX"][ch * 128:(ch + 1) * 128, :], s32[:, ch * S:(ch + 1) * S],
                reads=[rS32], writes=[c.dreg["dbgX"]])
    if _ph("L2"):
        _flex(c, S, 4 * min(512, S) + 512, 16 * S + 4 * min(512, S) + 1024)
        post_ln(c, s32, rS32, s16, rS16, "b")
        c.close_flex()
    if tap == "x2":
        for ch in range(8):
            DMA(c, c.dram["dbgX"][ch * 128:(ch + 1) * 128, :], s32[:, ch * S:(ch + 1) * S],
                reads=[rS32], writes=[c.dreg["dbgX"]])

    if _ph("D"):
        _flex(c, S, 2 * S + 4608, 17 * S + 1024)
        l1_attn(c, s32, rS32, s16, rS16)
        c.close_flex()

    if _ph("L3"):
        _flex(c, S, 4 * min(512, S) + 512, 16 * S + 4 * min(512, S) + 1024)
        post_ln(c, s32, rS32, s16, rS16, "c")
        c.close_flex()

    if _ph("E"):
        _flex(c, S, 14 * min(512, S) + FF + 2560, 38 * min(512, S) + FF + 4096)
        l1_ffn(c, s32, rS32, s16, rS16)
        c.close_flex()

    if _ph("L4"):
        _flex(c, S, 4 * min(512, S) + 512, 16 * S + 4 * min(512, S) + 1024)
        post_ln(c, s32, rS32, None, None, "d")
        c.close_flex()

    rout = c.reg("out")
    if _ph("OUT") or _phset is None:
        for ch in range(8):
            DMA(c, out_ap[ch * 128:(ch + 1) * 128, :], s32[:, ch * S:(ch + 1) * S],
                reads=[rS32], writes=[rout])
    c.p.op("SQ", lambda e: e.nop(), reads=[rout], writes=[])

    c.p.check_deadlock()
    sems = {}
    cms = []
    for e in Prog.SEMS:
        cm = nc.semaphore(f"sem_{e}")
        sems[e] = cm.__enter__()
        cms.append(cm)
    bcm = nc.Block()
    block = bcm.__enter__()
    c.p.emit(block, sems)
    bcm.__exit__(None, None, None)
    for cm in cms:
        cm.__exit__(None, None, None)
    return nc


def _prep_inputs(S, consts, **inp):
    T = lambda a: np.ascontiguousarray(np.asarray(a, dtype=np.float32).T)
    wqrT = T(inp["L0_attn_W_QR"])
    perm = [h * 32 + 2 * f + half for half in range(2) for h in range(H) for f in range(16)]
    wqrT = np.ascontiguousarray(wqrT[:, perm])
    wkrT = T(inp["L0_attn_W_KR"])
    kperm = [2 * f for f in range(16)] + [2 * f + 1 for f in range(16)]
    wkrT = np.ascontiguousarray(wkrT[:, kperm])
    wkv = np.asarray(inp["L1_attn_Wkv"], dtype=np.float32)
    wkT = T(wkv[0:256, :])
    base = {
        "wdqT": T(inp["L0_attn_W_DQ"]), "wuqT": T(inp["L0_attn_W_UQ"]),
        "wqrT": wqrT, "wdkvT": T(inp["L0_attn_W_DKV"]),
        "wukT": T(inp["L0_attn_W_UK"]), "wuvT": T(inp["L0_attn_W_UV"]),
        "wkrT": wkrT, "woT": T(inp["L0_attn_W_O"]),
        "wgT": T(inp["L0_ffn_w_gate"]), "wuT": T(inp["L0_ffn_w_up"]),
        "wdT": T(inp["L0_ffn_w_down"]),
        "wqT": T(inp["L1_attn_Wq"]), "wkT": np.ascontiguousarray(wkT),
        "wvT": T(wkv[256:512, :]), "wpT": T(inp["L1_attn_Wproj"]),
        "bprojT": np.ascontiguousarray(
            np.asarray(inp["L1_attn_bproj"], dtype=np.float32).reshape(8, 128).T),
        "w1T": np.ascontiguousarray(np.asarray(inp["L1_ffn_w1"], dtype=np.float32)),
        "b1T": np.ascontiguousarray(
            np.asarray(inp["L1_ffn_b1"], dtype=np.float32).reshape(32, 128).T),
        "w2T": np.ascontiguousarray(np.asarray(inp["L1_ffn_w2"], dtype=np.float32)),
        "b2T": np.ascontiguousarray(
            np.asarray(inp["L1_ffn_b2"], dtype=np.float32).reshape(8, 128).T),
        "cosb": consts["cosb"], "sinb": consts["sinb"], "m01c": consts["m01"],
        "relc": consts["relc"], "identc": consts["ident"],
        "sel2": np.concatenate([np.concatenate([np.ones((1, 128), np.float32), np.zeros((1, 128), np.float32)], axis=1),
                                np.concatenate([np.zeros((1, 128), np.float32), np.ones((1, 128), np.float32)], axis=1)], axis=0),
    }
    x = np.asarray(inp["L0_attn_x"], dtype=np.float32)
    return [dict(base, xT=np.ascontiguousarray(x[b].T)) for b in range(x.shape[0])]


def kernel(**inputs):
    S = int(inputs["L0_attn_x"].shape[1])
    consts = _host_consts(S)
    in_maps = _prep_inputs(S, consts, **inputs)
    nc = build_program(S)
    res = run_bass_kernel_spmd(nc, in_maps, core_ids=list(range(len(in_maps))))
    outs = [np.ascontiguousarray(np.asarray(r["outT"]).T) for r in res.results]
    return np.stack(outs, axis=0).astype(np.float32)


# version 12
# speedup vs baseline: 1.1067x; 1.0089x over previous
import os as _os
import numpy as np

import bass_rust
import concourse.bass as bass
import concourse.mybir as mybir
from concourse.bass_utils import run_bass_kernel_spmd
from concourse import library_config

F32 = mybir.dt.float32
F16 = mybir.dt.float16
I32 = mybir.dt.int32
ALU = mybir.AluOpType
ACT = mybir.ActivationFunctionType
RED = bass_rust.ReduceOp
AX = mybir.AxisListType

D = 1024
H = 16
DH = 64
DHR = 32
DLAT = 512
KVH = 4
FF = 4096

MAGIC32 = float(1.5 * 2.0**23)
MAGIC_MUL = float((1.5 * 2.0**52 - 1.5 * 2.0**23) * 2.0**30)



def _host_consts(S):
    import jax
    import jax.numpy as jnp
    cpu = jax.devices("cpu")[0]
    with jax.default_device(cpu):
        d = DHR
        inv_freq = jnp.asarray(10000.0) ** (-jnp.arange(0, d, 2, dtype=jnp.float32) / d)
        pos = jnp.arange(S)
        ang = pos.astype(jnp.float32)[:, None] * inv_freq[None, :]
        cos = np.asarray(jnp.cos(ang), dtype=np.float32)
        sin = np.asarray(jnp.sin(ang), dtype=np.float32)
        pwl_x = jnp.linspace(-4.0, 4.0, 33, dtype=jnp.float32)
        pwl_y = 0.5 * pwl_x * (1.0 + jax.lax.erf(pwl_x / jnp.sqrt(jnp.float32(2.0))))
        X = np.asarray(pwl_x, dtype=np.float32)
        Y = np.asarray(pwl_y, dtype=np.float32)

    cosb = np.ascontiguousarray(np.tile(cos.T, (8, 1)).astype(np.float32))
    sinb = np.ascontiguousarray(np.tile(sin.T, (8, 1)).astype(np.float32))
    m01 = np.triu(np.ones((128, 128), dtype=np.float16))

    start = 2.0 ** (-2.0 ** (-(np.log2(H) - 3.0)))
    slopes = (start ** np.arange(1, H + 1)).astype(np.float32)

    WM = 2 * S - 128
    relc = np.minimum((S - 128) - np.arange(WM, dtype=np.float64)[None, :]
                      + np.arange(128, dtype=np.float64)[:, None], 0.0).astype(np.float32)
    ident = np.eye(128, dtype=np.float16)

    Y64 = Y.astype(np.float64)
    m = (Y64[1:] - Y64[:-1]) * 4.0
    mv = m / 256.0
    w = mv[1:] - mv[:-1]
    kxv = (X[1:32].astype(np.float64) * 256.0)
    c1 = mv[0]
    c0 = Y64[0] + 1024.0 * mv[0] - float(np.sum(w * kxv))

    def hilo(v):
        hi = np.float32(v)
        lo = np.float32(np.float64(v) - np.float64(hi))
        return float(hi), float(lo)

    pwl = {
        "c0": hilo(float(Y64[0] + 1024.0 * mv[0])),
        "c1": hilo(c1),
        "w": [hilo(x) for x in w],
        "kx": [float(x) for x in kxv],
        "eu_w": hilo((1.0 - m[-1]) / 256.0),
        "eu_j": hilo(np.float64(np.float32(4.0)) - Y64[32]),
        "el_w": hilo(-mv[0]),
        "el_c": hilo(-mv[0] * 1024.0),
        "el_j": hilo(-Y64[0]),
    }
    return dict(cosb=cosb, sinb=sinb, m01=m01, slopes=slopes, pwl=pwl,
                relc=relc, ident=ident)



class Region:
    __slots__ = ("name", "writer", "readers")

    def __init__(self, name):
        self.name = name
        self.writer = None
        self.readers = []


class Prog:
    ENGS = ("TE", "VE", "SE", "GE", "SQ")
    NDS = 8
    SEMS = (("TE", "VE", "SE", "GEc", "SQc")
            + tuple(f"GEd{i}" for i in range(8)) + tuple(f"SQd{i}" for i in range(8)))

    def __init__(self, nc):
        self.nc = nc
        self.cnt = {sm: 0 for sm in self.SEMS}
        self.ops = {e: [] for e in self.ENGS}
        self.waited = {e: {sm: 0 for sm in self.SEMS} for e in self.ENGS}
        self._dma_rr = {"GE": 0, "SQ": 0}

    fence = None

    def _sem(self, eng, dma):
        if eng in ("GE", "SQ"):
            if dma:
                i = self._dma_rr[eng]
                self._dma_rr[eng] = (i + 1) % self.NDS
                return f"{eng}d{i}"
            return eng + "c"
        return eng

    def op(self, eng, fn, reads=(), writes=(), dma=False):
        waits = {}

        def dep(tok):
            if tok is None:
                return
            sm, c2 = tok
            if self.waited[eng][sm] < c2:
                waits[sm] = max(waits.get(sm, 0), c2)

        if self.fence:
            for sm, c2 in self.fence.items():
                dep((sm, c2))
        for r in reads:
            dep(r.writer)
        for r in writes:
            dep(r.writer)
            for t in r.readers:
                dep(t)
        sm = self._sem(eng, dma)
        if dma:
            dep((sm, self.cnt[sm]))
        for s2, c2 in waits.items():
            self.waited[eng][s2] = c2
        inc = 16 if dma else 1
        self.cnt[sm] += inc
        tok = (sm, self.cnt[sm])
        for r in reads:
            r.readers.append(tok)
        for r in writes:
            r.writer = tok
            r.readers = []
        self.ops[eng].append((waits, fn, sm, inc))
        return tok

    def check_deadlock(self):
        done = {sm: 0 for sm in self.SEMS}
        idx = {e: 0 for e in self.ENGS}
        total = sum(len(v) for v in self.ops.values())
        ndone = 0
        progress = True
        while progress:
            progress = False
            for e in self.ENGS:
                while idx[e] < len(self.ops[e]):
                    waits, _, sm, inc = self.ops[e][idx[e]]
                    if all(done[s2] >= c2 for s2, c2 in waits.items()):
                        done[sm] += inc
                        idx[e] += 1
                        ndone += 1
                        progress = True
                    else:
                        break
        if ndone != total:
            stuck = {e: (idx[e], len(self.ops[e]), self.ops[e][idx[e]][0])
                     for e in self.ENGS if idx[e] < len(self.ops[e])}
            raise RuntimeError(f"emission deadlock: {stuck}")

    def emit(self, block, sems):
        handles = {"TE": self.nc.tensor, "VE": self.nc.vector, "SE": self.nc.scalar,
                   "GE": self.nc.gpsimd, "SQ": self.nc.sync}

        def mk(eng):
            h = handles[eng]

            def body(_):
                for waits, fn, sm, inc in self.ops[eng]:
                    for s2 in sorted(waits):
                        h.wait_ge(sems[s2], waits[s2])
                    fn(h).then_inc(sems[sm], inc)
            return body

        for eng, deco in (("SQ", block.sync), ("TE", block.tensor),
                          ("VE", block.vector), ("SE", block.scalar),
                          ("GE", block.gpsimd)):
            if self.ops[eng]:
                deco(mk(eng))


class Tile:
    __slots__ = ("view", "c0", "P", "C", "alloc_idx", "arena")

    def __init__(self, view, c0, P, C, alloc_idx, arena):
        self.view = view
        self.c0 = c0
        self.P = P
        self.C = C
        self.alloc_idx = alloc_idx
        self.arena = arena

    def __getitem__(self, idx):
        rows, cols = idx
        if isinstance(rows, slice):
            r0 = rows.start or 0
            r1 = self.P if rows.stop is None else rows.stop
        else:
            r0, r1 = rows, rows + 1
        if isinstance(cols, slice):
            cs0 = cols.start or 0
            cs1 = self.C if cols.stop is None else cols.stop
            st = cols.step
            if st is None:
                return self.view[r0:r1, self.c0 + cs0: self.c0 + cs1]
            return self.view[r0:r1, self.c0 + cs0: self.c0 + cs1: st]
        return self.view[r0:r1, self.c0 + cols: self.c0 + cols + 1]


class Arena:

    def __init__(self, view, n):
        self.view = view
        self.free_list = [(0, n)]
        self.n = n
        self.used = {}

    def alloc(self, name, P, C):
        need = (C + 31) & ~31
        for i, (off, ln) in enumerate(self.free_list):
            if ln >= need:
                if ln == need:
                    self.free_list.pop(i)
                else:
                    self.free_list[i] = (off + need, ln - need)
                t = Tile(self.view, off, P, C, None, self)
                self.used[id(t)] = (off, need)
                return t
        raise MemoryError(f"arena OOM for {name}: need {need}, free {self.free_list}")

    def release(self, t):
        off, need = self.used.pop(id(t))
        self.free_list.append((off, need))
        self.free_list.sort()
        merged = []
        for off, ln in self.free_list:
            if merged and merged[-1][0] + merged[-1][1] == off:
                merged[-1] = (merged[-1][0], merged[-1][1] + ln)
            else:
                merged.append((off, ln))
        self.free_list = [tuple(x) for x in merged]


class Ctx:
    def __init__(self, nc, S, consts):
        self.nc = nc
        self.S = S
        self.consts = consts
        self.p = Prog(nc)
        self._cms = []
        self._psum_stack = []
        self._n = 0
        self.t = {}
        self.r = {}
        self.dram = {}
        self.dreg = {}
        self.flex = {}
        self.flex_order = []

    def perm(self, name, shape, dtype):
        self._n += 1
        cm = self.nc.sbuf_tensor(f"{name}_{self._n}", list(shape), dtype)
        t = cm.__enter__()
        self._cms.append(cm)
        return t

    def open_flex(self, n32=0, n16=0):
        assert not self.flex, "flex already open"
        self._n += 1
        if n32:
            cm = self.nc.sbuf_tensor(f"fx32_{self._n}", [128, n32], F32)
            self.flex[F32] = (Arena(cm.__enter__(), n32), cm)
            self.flex_order.append(F32)
        if n16:
            cm = self.nc.sbuf_tensor(f"fx16_{self._n}", [128, n16], F16)
            self.flex[F16] = (Arena(cm.__enter__(), n16), cm)
            self.flex_order.append(F16)

    def close_flex(self):
        self._fence_all()
        for dt in reversed(self.flex_order):
            self.flex[dt][1].__exit__(None, None, None)
        self.flex = {}
        self.flex_order = []

    def sbuf(self, name, shape, dtype):
        assert len(shape) == 2
        if dtype == I32:
            dtype = F32
        ar = self.flex[dtype][0]
        return ar.alloc(name, shape[0], shape[1])

    def sbuf_i32(self, name, shape):
        self._n += 1
        cm = self.nc.sbuf_tensor(f"{name}_{self._n}", list(shape), I32)
        t = cm.__enter__()
        self._psum_stack.append((t, cm))
        return t

    def psum(self, name, shape, dtype=F32):
        self._n += 1
        cm = self.nc.psum_tensor(f"{name}_{self._n}", list(shape), dtype)
        t = cm.__enter__()
        self._psum_stack.append((t, cm))
        return t

    def _fence_all(self):
        fence = dict(self.p.fence or {})
        for name, r in self.r.items():
            for tok in ([r.writer] if r.writer else []) + r.readers:
                e2, c2 = tok
                fence[e2] = max(fence.get(e2, 0), c2)
        self.p.fence = fence

    def free(self, *tensors):
        self._fence_all()
        psums = [t for t in tensors if not isinstance(t, Tile)]
        if psums:
            order = {id(t): i for i, (t, _) in enumerate(self._psum_stack)}
            for t in sorted(psums, key=lambda t: -order[id(t)]):
                i = next(i for i, (tt, _) in enumerate(self._psum_stack) if tt is t)
                _, cm = self._psum_stack.pop(i)
                cm.__exit__(None, None, None)
        for t in tensors:
            if isinstance(t, Tile):
                t.arena.release(t)

    def reg(self, name):
        if name not in self.r:
            self.r[name] = Region(name)
        return self.r[name]


def TS(c, out, in0, s1, s2, op0, op1, reads, writes, eng="VE"):
    return c.p.op(eng, lambda e: e.tensor_scalar(out, in0, s1, s2, op0, op1),
                  reads=reads, writes=writes)


def TT(c, out, in0, in1, op, reads, writes, eng="VE"):
    return c.p.op(eng, lambda e: e.tensor_tensor(out, in0, in1, op),
                  reads=reads, writes=writes)


def STT(c, out, in0, scalar, in1, op0, op1, reads, writes):
    return c.p.op("VE", lambda e: e.scalar_tensor_tensor(out, in0, scalar, in1, op0, op1),
                  reads=reads, writes=writes)


def CP(c, out, in0, reads, writes, eng="VE"):
    if eng == "SE":
        return c.p.op(eng, lambda e: e.activation(out, in0, ACT.Copy),
                      reads=reads, writes=writes)
    return c.p.op(eng, lambda e: e.tensor_copy(out, in0), reads=reads, writes=writes)


def ACTV(c, out, in0, func, reads, writes, bias=None, scale=None):
    kw = {}
    if bias is not None:
        kw["bias"] = bias
    if scale is not None:
        kw["scale"] = scale
    return c.p.op("SE", lambda e: e.activation(out, in0, func, **kw),
                  reads=reads, writes=writes)


def MM(c, psum_ap, lhsT_ap, rhs_ap, first, last, reads, writes, tp=None):
    kw = {}
    if tp is not None:
        kw["tile_position"] = tp
    return c.p.op("TE", lambda e: e.matmul(psum_ap, lhsT_ap, rhs_ap,
                                           start=first, stop=last, **kw),
                  reads=reads, writes=writes)


def MEMSET(c, ap, val, writes, eng="VE"):
    return c.p.op(eng, lambda e: e.memset(ap, val), reads=[], writes=writes)


def DMA(c, out_ap, in_ap, reads, writes, eng="SQ"):
    return c.p.op(eng, lambda e: e.dma_start(out=out_ap, in_=in_ap),
                  reads=reads, writes=writes, dma=True)


def RECIP(c, out, in0, reads, writes):
    return c.p.op("VE", lambda e: e.reciprocal(out, in0), reads=reads, writes=writes)


def round_ip(c, ap, reads, writes):
    ceff = c.t["Ceff"]
    return TS(c, ap, ap, ceff, ceff, ALU.add, ALU.subtract,
              reads=list(reads) + [c.reg("consts")], writes=writes)


def load_w16(c, name, key, rows, cols, scale=None):
    nch = rows // 128
    st = c.sbuf(name + "_st", [128, cols], F32)
    wt = c.sbuf(name, [128, nch * cols], F16)
    reg = c.reg(name)
    rst = c.reg(name + "_st")
    for ch in range(nch):
        DMA(c, st[:, :], c.dram[key][ch * 128:(ch + 1) * 128, :], reads=[rst], writes=[rst])
        if scale is None:
            CP(c, wt[:, ch * cols:(ch + 1) * cols], st[:, :], reads=[rst], writes=[reg])
        else:
            TS(c, wt[:, ch * cols:(ch + 1) * cols], st[:, :], float(scale), None,
               ALU.mult, ALU.bypass, [rst], [reg])
    c.free(st)
    return wt, reg



def setup_consts(c):
    cst = c.perm("cst", [128, 56], F32)
    c.t["cst"] = cst
    R = c.reg("consts")
    MEMSET(c, cst[:, 0:1], 2.0**-30, writes=[R])
    TS(c, cst[:, 0:1], cst[:, 0:1], 2.0**22, 2.0**22, ALU.add, ALU.subtract, [R], [R])
    TS(c, cst[:, 1:2], cst[:, 0:1], MAGIC_MUL, MAGIC32, ALU.mult, ALU.add, [R], [R])
    c.t["Ceff"] = cst[:, 1:2]
    sqrt96 = float(np.float32(np.sqrt(np.float64(96.0))))
    MEMSET(c, cst[:, 2:3], sqrt96, writes=[R])
    RECIP(c, cst[:, 2:3], cst[:, 2:3], [R], [R])
    TS(c, cst[:, 3:4], cst[:, 2:3], -80.0, None, ALU.mult, ALU.bypass, [R], [R])
    c.t["rc96"] = cst[:, 2:3]
    c.t["negC0"] = cst[:, 3:4]
    MEMSET(c, cst[:, 4:5], 1e-5, writes=[R])
    c.t["epsc"] = cst[:, 4:5]
    pw = c.consts["pwl"]
    coeffs = ([pw["c0"], pw["c1"]] + pw["w"]
              + [pw["eu_w"], pw["eu_j"], pw["el_w"], pw["el_c"], pw["el_j"]])
    for i, (hi, lo) in enumerate(coeffs):
        TS(c, cst[:, 8 + i:9 + i], cst[:, 0:1], float(lo) * 2.0**30, float(hi),
           ALU.mult, ALU.add, [R], [R])
    c.PWL0 = 8

    ones16 = c.perm("ones16", [128, 1], F16)
    MEMSET(c, ones16[:, :], 1.0, writes=[R])
    c.t["ones16"] = ones16
    onesrow = c.perm("onesrow", [1, 128], F16)
    MEMSET(c, onesrow[:, :], 1.0, writes=[R])
    c.t["onesrow"] = onesrow
    onesrow32 = c.perm("onesrow32", [1, 128], F32)
    MEMSET(c, onesrow32[:, :], 1.0, writes=[R])
    c.t["onesrow32"] = onesrow32
    ones32c = c.perm("ones32c", [128, 1], F32)
    MEMSET(c, ones32c[:, :], 1.0, writes=[R])
    c.t["ones32c"] = ones32c
    mbar = c.perm("mbar", [128, 8], F32)
    c.t["mbar"] = mbar
    hp = c.perm("hpones", [128, 2], F16)
    MEMSET(c, hp[:, :], 0.0, writes=[R])
    MEMSET(c, hp[0:64, 0:1], 1.0, writes=[R])
    MEMSET(c, hp[64:128, 1:2], 1.0, writes=[R])
    c.t["hpones"] = hp



def post_ln(c, s32, rR, s16, rD, tag):
    S = c.S
    TB = min(512, S)
    NTB = S // TB
    r16 = c.sbuf(f"lnr{tag}", [128, 8 * S], F16)
    sq = c.sbuf(f"lnq{tag}", [128, 8 * S], F16)
    rT = c.reg(f"lnr{tag}")
    rQ = c.reg(f"lnq{tag}")
    for ch in range(8):
        sl = slice(ch * S, (ch + 1) * S)
        CP(c, r16[:, sl], s32[:, sl], reads=[rR], writes=[rT])
        ACTV(c, sq[:, sl], s32[:, sl], ACT.Square, reads=[rR], writes=[rQ])
    stat = c.sbuf(f"lns{tag}", [1, 4 * TB], F32)
    st16 = c.sbuf(f"ln6{tag}", [1, 2 * TB], F16)
    rS = c.reg(f"lns{tag}")
    rS16 = c.reg(f"ln6{tag}")
    ps1 = c.psum(f"lp1{tag}", [1, TB])
    ps2 = c.psum(f"lp2{tag}", [1, TB])
    rP1 = c.reg(f"lp1{tag}")
    rP2 = c.reg(f"lp2{tag}")
    psb1 = c.psum(f"lb1{tag}", [128, TB])
    psb2 = c.psum(f"lb2{tag}", [128, TB])
    rB1 = c.reg(f"lb1{tag}")
    rB2 = c.reg(f"lb2{tag}")
    ones = c.t["ones16"]
    onesrow = c.t["onesrow"]
    rC = c.reg("consts")
    eps = 1e-5
    m_ = stat[0:1, 0:TB]
    v_ = stat[0:1, TB:2 * TB]
    s_ = stat[0:1, 2 * TB:3 * TB]
    w_ = stat[0:1, 3 * TB:4 * TB]
    for tb in range(NTB):
        for ch in range(8):
            sl = slice(ch * S + tb * TB, ch * S + (tb + 1) * TB)
            MM(c, ps1[0:1, :], ones[:, 0:1], r16[:, sl], ch == 0, ch == 7,
               reads=[rT, rC], writes=[rP1])
            MM(c, ps2[0:1, :], ones[:, 0:1], sq[:, sl], ch == 0, ch == 7,
               reads=[rQ, rC], writes=[rP2])
        TS(c, m_, ps1[0:1, :], 1.0 / D, None, ALU.mult, ALU.bypass, [rP1], [rS])
        TS(c, v_, ps2[0:1, :], 1.0 / D, None, ALU.mult, ALU.bypass, [rP2], [rS])
        TT(c, w_, m_, m_, ALU.mult, [rS], [rS])
        TT(c, v_, v_, w_, ALU.subtract, [rS], [rS])
        ACTV(c, s_, v_, ACT.Sqrt, reads=[rS, c.reg("consts")], writes=[rS], bias=c.t["cst"][0:1, 4:5])
        RECIP(c, w_, s_, [rS], [rS])
        TS(c, v_, v_, eps, None, ALU.add, ALU.bypass, [rS], [rS])
        TT(c, w_, v_, w_, ALU.mult, [rS], [rS])
        TT(c, w_, w_, s_, ALU.add, [rS], [rS])
        TS(c, w_, w_, 0.5, None, ALU.mult, ALU.bypass, [rS], [rS])
        RECIP(c, w_, w_, [rS], [rS])
        CP(c, st16[0:1, 0:TB], m_, reads=[rS], writes=[rS16])
        CP(c, st16[0:1, TB:2 * TB], w_, reads=[rS], writes=[rS16])
        MM(c, psb1[:, :], onesrow[0:1, :], st16[0:1, 0:TB], True, True,
           reads=[rS16, rC], writes=[rB1])
        MM(c, psb2[:, :], onesrow[0:1, :], st16[0:1, TB:2 * TB], True, True,
           reads=[rS16, rC], writes=[rB2])
        for ch in range(8):
            sl = slice(ch * S + tb * TB, ch * S + (tb + 1) * TB)
            TT(c, s32[:, sl], s32[:, sl], psb1[:, :], ALU.subtract, [rR, rB1], [rR])
            TT(c, s32[:, sl], s32[:, sl], psb2[:, :], ALU.mult, [rR, rB2], [rR])
            if s16 is not None:
                ACTV(c, s16[:, sl], s32[:, sl], ACT.Copy, reads=[rR], writes=[rD])
    c.free(r16, sq, stat, st16, psb2, psb1, ps2, ps1)



def l0_attn(c, s16, rS16):
    S = c.S
    TB = min(512, S)
    NTB = S // TB
    NSC = S // 128
    p = c.p
    rC = c.reg("consts")
    rc96 = c.t["rc96"]

    xst = c.sbuf("xst", [128, S], F32)
    rXst = c.reg("xst")
    for ch in range(8):
        DMA(c, xst[:, :], c.dram["xT"][ch * 128:(ch + 1) * 128, :],
            reads=[rXst], writes=[rXst])
        CP(c, s16[:, ch * S:(ch + 1) * S], xst[:, :], reads=[rXst], writes=[rS16])
    c.free(xst)
    xt16 = s16
    rX16 = rS16

    ps = c.psum("l0p", [128, TB])
    rPS = c.reg("l0p")

    wdq, rwdq = load_w16(c, "wdq", "wdqT", D, DLAT)
    wdkv, rwdkv = load_w16(c, "wdkv", "wdkvT", D, DLAT)
    cq = c.sbuf("cq16", [128, 4 * S], F16)
    ckv = c.sbuf("ckv16", [128, 4 * S], F16)
    rCQ = c.reg("cq16")
    rCKV = c.reg("ckv16")
    for dst, wsrc, rw, rdst in ((cq, wdq, rwdq, rCQ), (ckv, wdkv, rwdkv, rCKV)):
        for lc in range(4):
            for tb in range(NTB):
                for ch in range(8):
                    MM(c, ps[:, :], wsrc[:, ch * DLAT + lc * 128: ch * DLAT + (lc + 1) * 128],
                       xt16[:, ch * S + tb * TB: ch * S + (tb + 1) * TB],
                       ch == 0, ch == 7, reads=[rw, rX16], writes=[rPS])
                CP(c, dst[:, lc * S + tb * TB: lc * S + (tb + 1) * TB], ps[:, :],
                   reads=[rPS], writes=[rdst], eng="SE")
    c.free(wdq, wdkv)

    wuq, rwuq = load_w16(c, "wuq", "wuqT", DLAT, H * DH)
    qc = c.sbuf("qc16", [128, 8 * S], F16)
    rQC = c.reg("qc16")
    for oc in range(8):
        for tb in range(NTB):
            for lc in range(4):
                MM(c, ps[:, :], wuq[:, lc * (H * DH) + oc * 128: lc * (H * DH) + (oc + 1) * 128],
                   cq[:, lc * S + tb * TB: lc * S + (tb + 1) * TB],
                   lc == 0, lc == 3, reads=[rwuq, rCQ], writes=[rPS])
            TS(c, qc[:, oc * S + tb * TB: oc * S + (tb + 1) * TB], ps[:, :],
               rc96, None, ALU.mult, ALU.bypass, [rPS, rC], [rQC])
    c.free(wuq)

    wqr, rwqr = load_w16(c, "wqr", "wqrT", DLAT, H * DHR)
    cosb = c.sbuf("cosb", [128, S], F32)
    sinb = c.sbuf("sinb", [128, S], F32)
    rCos = c.reg("cosb")
    DMA(c, cosb[:, :], c.dram["cosb"], reads=[rCos], writes=[rCos])
    DMA(c, sinb[:, :], c.dram["sinb"], reads=[rCos], writes=[rCos])
    scr = c.sbuf("ropes", [128, 2 * S], F32)
    rScr = c.reg("ropes")
    qrev = c.sbuf("qrev", [128, S], F32)
    qrod = c.sbuf("qrod", [128, S], F32)
    rQRE = c.reg("qrev")
    rQRO = c.reg("qrod")
    qrev16 = c.sbuf("qrev16", [128, 2 * S], F16)
    qrod16 = c.sbuf("qrod16", [128, 2 * S], F16)
    rQE16 = c.reg("qrev16")
    rQO16 = c.reg("qrod16")

    def rope(np_, ev, od, ev16, od16, rE, rO, rE16, rO16, cs, sn):
        t1 = scr[0:np_, 0:S]
        t2 = scr[0:np_, S:2 * S]
        TT(c, t1, ev, cs, ALU.mult, [rE, rCos], [rScr])
        TT(c, t2, od, sn, ALU.mult, [rO, rCos], [rScr])
        TT(c, ev16, t1, t2, ALU.subtract, [rScr], [rE16])
        TT(c, t1, ev, sn, ALU.mult, [rE, rCos], [rScr])
        TT(c, t2, od, cs, ALU.mult, [rO, rCos], [rScr])
        TT(c, od16, t1, t2, ALU.add, [rScr], [rO16])

    for oc in range(2):
        for half, dst, rdst in ((0, qrev, rQRE), (1, qrod, rQRO)):
            for tb in range(NTB):
                for lc in range(4):
                    col = lc * (H * DHR) + half * 256 + oc * 128
                    MM(c, ps[:, :], wqr[:, col: col + 128],
                       cq[:, lc * S + tb * TB: lc * S + (tb + 1) * TB],
                       lc == 0, lc == 3, reads=[rwqr, rCQ], writes=[rPS])
                TS(c, dst[:, tb * TB:(tb + 1) * TB], ps[:, :],
                   rc96, None, ALU.mult, ALU.bypass, [rPS, rC], [rdst])
        sl = slice(oc * S, (oc + 1) * S)
        rope(128, qrev[:, :], qrod[:, :], qrev16[:, sl], qrod16[:, sl],
             rQRE, rQRO, rQE16, rQO16, cosb[:, :], sinb[:, :])
    c.free(wqr, qrev, qrod, cq)

    qrt2 = c.sbuf("qrt2", [128, 4 * S], F16)
    rQR2 = c.reg("qrt2")
    for h in range(H):
        cc, rr = divmod(h, 4)
        sch, sr = divmod(h, 8)
        DMA(c, qrt2[rr * 32: rr * 32 + 16, cc * S:(cc + 1) * S],
            qrev16[sr * 16: sr * 16 + 16, sch * S:(sch + 1) * S],
            reads=[rQE16], writes=[rQR2], eng="GE")
        DMA(c, qrt2[rr * 32 + 16: rr * 32 + 32, cc * S:(cc + 1) * S],
            qrod16[sr * 16: sr * 16 + 16, sch * S:(sch + 1) * S],
            reads=[rQO16], writes=[rQR2], eng="GE")
    c.free(qrev16, qrod16)

    wkr, rwkr = load_w16(c, "wkr", "wkrT", D, DHR)
    krev = c.sbuf("krev", [16, S], F32)
    krod = c.sbuf("krod", [16, S], F32)
    rKRE = c.reg("krev")
    rKRO = c.reg("krod")
    psk = c.psum("l0pk", [16, TB])
    rPSK = c.reg("l0pk")
    for half, dst, rdst in ((0, krev, rKRE), (1, krod, rKRO)):
        for tb in range(NTB):
            for ch in range(8):
                MM(c, psk[0:16, :], wkr[:, ch * DHR + half * 16: ch * DHR + half * 16 + 16],
                   xt16[:, ch * S + tb * TB: ch * S + (tb + 1) * TB],
                   ch == 0, ch == 7, reads=[rwkr, rX16], writes=[rPSK])
            CP(c, dst[0:16, tb * TB:(tb + 1) * TB], psk[0:16, :], reads=[rPSK],
               writes=[rdst], eng="SE")
    krev16 = c.sbuf("krev16", [16, S], F16)
    krod16 = c.sbuf("krod16", [16, S], F16)
    rKE16 = c.reg("krev16")
    rKO16 = c.reg("krod16")
    rope(16, krev[0:16, :], krod[0:16, :], krev16[0:16, :], krod16[0:16, :],
         rKRE, rKRO, rKE16, rKO16, cosb[0:16, :], sinb[0:16, :])
    c.free(wkr, krev, krod, scr, cosb, sinb, psk)
    krt2 = c.sbuf("krt2", [128, S], F16)
    rKR2 = c.reg("krt2")
    for b in range(4):
        DMA(c, krt2[b * 32: b * 32 + 16, :], krev16[0:16, :], reads=[rKE16],
            writes=[rKR2], eng="GE")
        DMA(c, krt2[b * 32 + 16: b * 32 + 32, :], krod16[0:16, :], reads=[rKO16],
            writes=[rKR2], eng="GE")
    c.free(krev16, krod16)

    wuk, rwuk = load_w16(c, "wuk", "wukT", DLAT, H * DH)
    kc = c.sbuf("kc16", [128, 8 * S], F16)
    rKC = c.reg("kc16")
    for oc in range(8):
        for tb in range(NTB):
            for lc in range(4):
                MM(c, ps[:, :], wuk[:, lc * (H * DH) + oc * 128: lc * (H * DH) + (oc + 1) * 128],
                   ckv[:, lc * S + tb * TB: lc * S + (tb + 1) * TB],
                   lc == 0, lc == 3, reads=[rwuk, rCKV], writes=[rPS])
            CP(c, kc[:, oc * S + tb * TB: oc * S + (tb + 1) * TB], ps[:, :],
               reads=[rPS], writes=[rKC], eng="SE")
    c.free(wuk)

    wuv, rwuv = load_w16(c, "wuv", "wuvT", DLAT, H * DH)
    m01 = c.sbuf("m01", [128, 128], F16)
    rM01 = c.reg("m01")
    DMA(c, m01[:, :], c.dram["m01c"], reads=[rM01], writes=[rM01])
    vb = [c.sbuf(f"l0v{i}", [128, NSC * 66], F16) for i in range(2)]
    rVb = [c.reg(f"l0v{i}") for i in range(2)]
    for i in range(2):
        MEMSET(c, vb[i][:, 64::66], 1.0, writes=[rVb[i]])
    pexp2 = [c.sbuf(f"l0pe{i}", [128, TB], F16) for i in range(2)]
    rPexp2 = [c.reg(f"l0pe{i}") for i in range(2)]
    dn = c.sbuf("l0dn", [1, 3 * TB], F32)
    rDn = c.reg("l0dn")
    dn16 = c.sbuf("l0d6", [1, TB], F16)
    rDn16 = c.reg("l0d6")
    bcs = c.sbuf("l0bc", [64, TB], F32)
    rBcs = c.reg("l0bcs")
    aodd = c.sbuf("l0ao", [64, TB], F16)
    rAodd = c.reg("l0ao")
    atp0 = c.sbuf("l0at", [64, TB], F32)
    rAtp0 = c.reg("l0at")
    psS2 = [c.psum(f"l0sc{i}", [128, TB]) for i in range(2)]
    rSc2 = [c.reg(f"l0sc{i}") for i in range(2)]
    psPV2 = [c.psum(f"l0pv{i}", [65, TB]) for i in range(2)]
    rPV2 = [c.reg(f"l0pv{i}") for i in range(2)]
    psV2 = [c.psum(f"l0vv{i}", [128, 64]) for i in range(2)]
    rPsV2 = [c.reg(f"l0vv{i}") for i in range(2)]
    psB = c.psum("l0bb", [64, TB])
    rB = c.reg("l0bb")
    negC0 = c.t["negC0"]
    onesrow = c.t["onesrow"]
    explo = float(np.exp(np.float64(-160.0) * (1.0 / np.float64(np.float32(np.sqrt(np.float64(96.0)))))))

    tails = [None]

    def flush_tail():
        if tails[0] is None:
            return
        kch, kro, tb, psPV, rPV = tails[0]
        tails[0] = None
        CP(c, dn[0:1, 0:TB], psPV[64:65, :], reads=[rPV], writes=[rDn])
        ACTV(c, dn[0:1, TB:2 * TB], dn[0:1, 0:TB], ACT.Ln, reads=[rDn], writes=[rDn])
        ACTV(c, dn[0:1, 2 * TB:3 * TB], dn[0:1, TB:2 * TB], ACT.Exp,
             reads=[rDn], writes=[rDn], scale=-0.5)
        CP(c, dn16[0:1, :], dn[0:1, 2 * TB:3 * TB], reads=[rDn], writes=[rDn16])
        MM(c, psB[0:64, :], onesrow[0:1, 0:64], dn16[0:1, :], True, True,
           reads=[rDn16, rC], writes=[rB])
        CP(c, bcs[:, :], psB[0:64, :], reads=[rB], writes=[rBcs], eng="SE")
        TT(c, atp0[0:64, :], psPV[0:64, :], bcs[:, :], ALU.mult, [rPV, rBcs],
           [rAtp0])
        if kro == 0:
            TT(c, s16[0:64, kch * S + tb * TB: kch * S + (tb + 1) * TB],
               atp0[0:64, :], bcs[:, :], ALU.mult, [rAtp0, rBcs], [rS16])
        else:
            TT(c, aodd[0:64, 0:TB], atp0[0:64, :], bcs[:, :], ALU.mult,
               [rAtp0, rBcs], [rAodd])
            DMA(c, s16[64:128, kch * S + tb * TB: kch * S + (tb + 1) * TB],
                aodd[0:64, 0:TB], reads=[rAodd], writes=[rS16], eng="GE")

    for h in range(H):
        vcur, rV = vb[h % 2], rVb[h % 2]
        for tc in range(NSC):
            psV, rPsV = psV2[tc % 2], rPsV2[tc % 2]
            for lc in range(4):
                MM(c, psV[:, :], ckv[:, lc * S + tc * 128: lc * S + (tc + 1) * 128],
                   wuv[:, lc * (H * DH) + h * DH: lc * (H * DH) + h * DH + DH],
                   lc == 0, lc == 3, reads=[rCKV, rwuv], writes=[rPsV])
            CP(c, vcur[:, tc * 66: tc * 66 + 64], psV[:, :], reads=[rPsV],
               writes=[rV], eng="SE")
        kch, kro = divmod(h, 2)
        qrc, qrr = divmod(h, 4)
        base = kro * 64
        rbase = qrr * 32
        for tb in range(NTB):
            nsc = (tb + 1) * TB // 128
            pv = (h * NTB + tb) % 2
            psPV, rPV = psPV2[pv], rPV2[pv]
            for sc in range(nsc):
                pi = sc % 2
                psS, rSc = psS2[pi], rSc2[pi]
                pexp, rPexp = pexp2[pi], rPexp2[pi]
                t0 = max(tb * TB, sc * 128)
                w = (tb + 1) * TB - t0
                MM(c, psS[0:128, 0:w],
                   kc[base:base + 64, kch * S + sc * 128: kch * S + (sc + 1) * 128],
                   qc[base:base + 64, kch * S + t0: kch * S + t0 + w],
                   True, False, reads=[rKC, rQC], writes=[rSc], tp=(base, 0))
                MM(c, psS[0:128, 0:w],
                   krt2[rbase:rbase + 32, sc * 128:(sc + 1) * 128],
                   qrt2[rbase:rbase + 32, qrc * S + t0: qrc * S + t0 + w],
                   False, True, reads=[rKR2, rQR2], writes=[rSc], tp=(rbase, 0))
                if sc == 1:
                    flush_tail()
                ACTV(c, pexp[:, 0:w], psS[0:128, 0:w], ACT.Exp,
                     reads=[rSc], writes=[rPexp], bias=negC0)
                TS(c, pexp[:, 0:w], pexp[:, 0:w], 1.0, explo,
                   ALU.min, ALU.max, [rPexp], [rPexp])
                if t0 == sc * 128:
                    TT(c, pexp[:, 0:128], pexp[:, 0:128], m01[:, :], ALU.mult,
                       [rPexp, rM01], [rPexp])
                MM(c, psPV[0:65, t0 - tb * TB: t0 - tb * TB + w],
                   vcur[:, sc * 66: sc * 66 + 65], pexp[:, 0:w],
                   sc == 0, sc == nsc - 1, reads=[rV, rPexp], writes=[rPV])
            tails[0] = (kch, kro, tb, psPV, rPV)
    flush_tail()
    c.free(pexp2[0], pexp2[1], dn, dn16, bcs, aodd, atp0, vb[0], vb[1], m01,
           qc, kc, qrt2, krt2, ckv, wuv,
           psB, psV2[1], psV2[0], psPV2[1], psPV2[0], psS2[1], psS2[0], ps)


def l0_wo(c, s16, rS16, x1, rX1):
    S = c.S
    TB = min(512, S)
    NTB = S // TB
    wo, rwo = load_w16(c, "wo", "woT", H * DH, D)
    ps = c.psum("wop", [128, TB])
    rPS = c.reg("wop")
    rx = c.sbuf("worx", [128, 2 * TB], F32)
    rRx = [c.reg("worx0"), c.reg("worx1")]
    for dc in range(8):
        for tb in range(NTB):
            bu = (dc * NTB + tb) % 2
            DMA(c, rx[:, bu * TB:(bu + 1) * TB],
                c.dram["xT"][dc * 128:(dc + 1) * 128, tb * TB:(tb + 1) * TB],
                reads=[rRx[bu]], writes=[rRx[bu]])
            for oc in range(8):
                MM(c, ps[:, :], wo[:, oc * D + dc * 128: oc * D + (dc + 1) * 128],
                   s16[:, oc * S + tb * TB: oc * S + (tb + 1) * TB],
                   oc == 0, oc == 7, reads=[rwo, rS16], writes=[rPS])
            TT(c, x1[:, dc * S + tb * TB: dc * S + (tb + 1) * TB], ps[:, :],
               rx[:, bu * TB:(bu + 1) * TB], ALU.add, [rPS, rRx[bu]], [rX1])
    c.free(wo, rx, ps)



def prequant(c):
    p = c.p
    rC = c.reg("consts")
    stats = c.sbuf("fqs", [128, 16], F32)
    rStat = c.reg("fqs")
    st2 = [c.sbuf(f"fqt{i}", [128, FF], F32) for i in range(2)]
    rSt2 = [c.reg(f"fqt{i}") for i in range(2)]
    tn2 = [c.sbuf(f"fqn{i}", [128, FF], F16) for i in range(2)]
    rTn2 = [c.reg(f"fqn{i}") for i in range(2)]
    tmp2 = [c.sbuf(f"fqm{i}", [128, FF], F32) for i in range(2)]
    rTmp2 = [c.reg(f"fqm{i}") for i in range(2)]
    psR = c.psum("fqp", [1, 8])
    rPsR = c.reg("fqp")
    psBC = c.psum("fqb", [128, 4])
    rPsBC = c.reg("fqb")
    specs = [("wgT", "wgc", D, FF), ("wuT", "wuc", D, FF), ("wdT", "wdc", FF, D)]
    for wi, (key, skey, rows, cols) in enumerate(specs):
        nch = rows // 128
        for ch in range(nch):
            st, rSt = st2[ch % 2], rSt2[ch % 2]
            DMA(c, st[:, 0:cols], c.dram[key][ch * 128:(ch + 1) * 128, :],
                reads=[rSt], writes=[rSt])
            cslot = wi * 4 + (ch % 2)
            p.op("VE", lambda e, cslot=cslot, cols=cols, st=st: e.tensor_reduce(
                stats[:, cslot], st[:, 0:cols], AX.X, ALU.add,
                apply_absolute_value=True), reads=[rSt], writes=[rStat])
            if ch == 0:
                CP(c, stats[:, wi * 4 + 2], stats[:, cslot], reads=[rStat], writes=[rStat])
            else:
                TT(c, stats[:, wi * 4 + 2], stats[:, wi * 4 + 2],
                   stats[:, cslot], ALU.add, [rStat], [rStat])
        MM(c, psR[0:1, wi: wi + 1], c.t["ones32c"][:, 0:1],
           stats[:, wi * 4 + 2], True, True, reads=[rStat, rC], writes=[rPsR])
        n = float(rows) * float(cols)
        TS(c, stats[0:1, wi * 4 + 3], psR[0:1, wi: wi + 1], 1.0 / n, 1e-5,
           ALU.mult, ALU.max, [rPsR], [rStat])
    MM(c, psBC[:, 0:3], c.t["onesrow32"][0:1, :], stats[0:1, 3:15:4], True, True,
       reads=[rStat, rC], writes=[rPsBC])
    mbar = c.t["mbar"]
    rMbar = c.reg("mbar")
    CP(c, mbar[:, 0:3], psBC[:, 0:3], reads=[rPsBC], writes=[rMbar])
    TS(c, mbar[:, 3:6], mbar[:, 0:3], 0.5, None, ALU.mult, ALU.bypass, [rMbar], [rMbar])
    TS(c, mbar[:, 7:8], mbar[:, 1:2], 16.0, None, ALU.mult, ALU.bypass, [rMbar], [rMbar])
    c.free(psBC, psR)
    for wi, (key, skey, rows, cols) in enumerate(specs):
        nch = rows // 128
        thr = mbar[:, 3 + wi: 4 + wi]
        TS(c, mbar[:, 6:7], mbar[:, 3 + wi: 4 + wi], -1.0, None, ALU.mult, ALU.bypass,
           [rMbar], [rMbar])
        for ch in range(nch):
            st, rSt = st2[ch % 2], rSt2[ch % 2]
            tn, rTn = tn2[ch % 2], rTn2[ch % 2]
            tmp, rTmp = tmp2[ch % 2], rTmp2[ch % 2]
            DMA(c, st[:, 0:cols], c.dram[key][ch * 128:(ch + 1) * 128, :],
                reads=[rSt], writes=[rSt])
            TS(c, tmp[:, 0:cols], st[:, 0:cols], mbar[:, 6:7], None, ALU.is_lt,
               ALU.bypass, [rSt, rMbar], [rTmp])
            p.op("VE", lambda e, cols=cols, thr=thr, tn=tn, st=st, tmp=tmp:
                e.scalar_tensor_tensor(
                tn[:, 0:cols], st[:, 0:cols], thr, tmp[:, 0:cols],
                ALU.is_gt, ALU.subtract), reads=[rSt, rTmp, rMbar], writes=[rTn])
            DMA(c, c.dram[skey][ch * (cols // 128) * 128:(ch + 1) * (cols // 128) * 128, :]
                .rearrange("(o p) q -> p o q", p=128),
                tn[:, 0:cols], reads=[rTn], writes=[c.dreg[skey]])
    c.free(tmp2[1], tmp2[0], tn2[1], tn2[0], st2[1], st2[0], stats)


def l0_ffn(c, x1, rX1, s16, rS16):
    S = c.S
    TB = min(256, S)
    NTB = S // TB
    NOC = FF // 128
    p = c.p
    rC = c.reg("consts")
    sx = c.sbuf("aqs", [128, 2 * S], F32)
    rSx = c.reg("aqs")
    mbar = c.t["mbar"]
    rMbar = c.reg("mbar")
    psBB = c.psum("aqb", [128, 128])
    rPsBB = c.reg("aqb")

    mx = c.sbuf("aqm", [128, S], F32)
    rMx = c.reg("aqm")
    axs = c.sbuf("aqab", [128, S], F32)
    rAxs = c.reg("aqab")
    ACTV(c, mx[:, :], x1[:, 0:S], ACT.Abs, reads=[rX1], writes=[rMx])
    for ch in range(1, 8):
        ACTV(c, axs[:, :], x1[:, ch * S:(ch + 1) * S], ACT.Abs, reads=[rX1],
             writes=[rAxs])
        TT(c, mx[:, :], mx[:, :], axs[:, :], ALU.max, [rAxs, rMx], [rMx])
    c.free(axs)
    NSC = S // 128
    red = c.sbuf("aqrd", [64, S], F32)
    rRed = c.reg("aqrd")
    cur = 128
    while cur > 1:
        h = cur // 2
        DMA(c, red[0:h, 0:S], mx[h:cur, :], reads=[rMx], writes=[rRed], eng="GE")
        TT(c, mx[0:h, :], mx[0:h, :], red[0:h, 0:S], ALU.max, [rRed, rMx], [rMx])
        cur = h
    if _os.environ.get("KDBG_SXD"):
        DMA(c, c.dram["outT"][256:257, :], mx[0:1, :], reads=[rMx],
            writes=[c.reg("out")])
    c.free(red)
    sxr = c.sbuf("aqxr", [1, 2 * S], F32)
    rSxr = c.reg("aqxr")
    rowx = c.sbuf("aqrw", [1, 2 * S], F16)
    rRowx = c.reg("aqrw")
    TS(c, mx[0:1, :], mx[0:1, :], 1e-5, None, ALU.max, ALU.bypass, [rMx], [rMx])
    RECIP(c, sxr[0:1, 0:S], mx[0:1, :], [rMx], [rSxr])
    TS(c, sxr[0:1, 0:S], sxr[0:1, 0:S], 127.0, None, ALU.mult, ALU.bypass,
       [rSxr], [rSxr])
    CP(c, rowx[0:1, 0:S], sxr[0:1, 0:S], reads=[rSxr], writes=[rRowx])
    CP(c, sxr[0:1, S:2 * S], rowx[0:1, 0:S], reads=[rRowx], writes=[rSxr])
    TT(c, sxr[0:1, S:2 * S], sxr[0:1, 0:S], sxr[0:1, S:2 * S], ALU.subtract,
       [rSxr], [rSxr])
    CP(c, rowx[0:1, S:2 * S], sxr[0:1, S:2 * S], reads=[rSxr], writes=[rRowx])
    onesrow = c.t["onesrow"]
    for tc in range(NSC):
        MM(c, psBB[:, :], onesrow[0:1, :], rowx[0:1, tc * 128:(tc + 1) * 128],
           True, False, reads=[rRowx, rC], writes=[rPsBB])
        MM(c, psBB[:, :], onesrow[0:1, :], rowx[0:1, S + tc * 128: S + (tc + 1) * 128],
           False, True, reads=[rRowx, rC], writes=[rPsBB])
        CP(c, sx[:, tc * 128:(tc + 1) * 128], psBB[:, :], reads=[rPsBB], writes=[rSx])
    RECIP(c, sx[:, S:2 * S], sx[:, 0:S], [rSx], [rSx])
    if _os.environ.get("KDBG_SXD"):
        DMA(c, c.dram["outT"][0:128, :], sx[0:128, 0:S], reads=[rSx],
            writes=[c.reg("out")])
        DMA(c, c.dram["outT"][128:256, :], sx[0:128, S:2 * S], reads=[rSx],
            writes=[c.reg("out")])
    c.free(rowx, sxr, mx)
    qs = c.sbuf("aqc", [128, S], F32)
    rQs = c.reg("aqc")
    xq = s16
    rXq = rS16
    for ch in range(8):
        sl = slice(ch * S, (ch + 1) * S)
        TT(c, qs[:, :], x1[:, sl], sx[:, 0:S], ALU.mult, [rX1, rSx], [rQs])
        round_ip(c, qs[:, :], [rQs], [rQs])
        TS(c, xq[:, sl], qs[:, :], 127.0, -128.0, ALU.min, ALU.max, [rQs], [rXq])
    c.free(qs)

    if getattr(c, "tap", None) == "ffn":
        DMA(c, c.dram["dbgA"][:, 0:8 * S], xq[:, 0:8 * S], reads=[rXq], writes=[c.dreg["dbgA"]])
        DMA(c, c.dram["dbgB"][:, 0:2 * S], sx[0:128, 0:2 * S], reads=[rSx], writes=[c.dreg["dbgB"]])
        DMA(c, c.dram["dbgD"][:, 0:7], mbar[0:128, 0:7], reads=[rMbar], writes=[c.dreg["dbgD"]])
    for ch in range(8):
        DMA(c, c.dram["x1s"][ch * 128:(ch + 1) * 128, :], x1[:, ch * S:(ch + 1) * S],
            reads=[rX1], writes=[c.dreg["x1s"]])

    prod = c.sbuf("ffp", [128, NOC * TB], F32)
    rProd = c.reg("ffp")
    pq = c.sbuf("ffq", [128, NOC * TB], F16)
    rPq = c.reg("ffq")
    gsl = c.sbuf("ffg", [128, TB], F32)
    rGsl = c.reg("ffg")
    wb = [c.sbuf(f"ffw{i}", [128, 8 * 128], F16) for i in range(4)]
    rWb = [c.reg(f"ffw{i}") for i in range(4)]
    dwb = [c.sbuf(f"ffd{i}", [128, NOC * 128], F16) for i in range(2)]
    rDwb = [c.reg(f"ffd{i}") for i in range(2)]
    rmax = c.sbuf("ffm", [128, TB], F32)
    rRmax = c.reg("ffm")
    redr = c.sbuf("ffrd", [64, TB], F32)
    rRedr = c.reg("ffrd")
    fab = c.sbuf("ffab", [128, TB], F32)
    rFab = c.reg("ffab")
    rxr = c.sbuf("ffxr", [1, TB], F32)
    rRxr = c.reg("ffxr")
    rowr = c.sbuf("ffrr", [1, 2 * TB], F16)
    rRowr = c.reg("ffrr")
    rmaxb = c.sbuf("ffmb", [128, TB], F32)
    rRmaxb = c.reg("ffmb")
    s2 = c.sbuf("ff2", [128, 3 * TB], F32)
    rS2 = c.reg("ff2")
    rx1 = c.sbuf("ffr", [128, 2 * TB], F32)
    rRx1 = [c.reg("ffr0"), c.reg("ffr1")]
    x2 = x1
    rX2 = rX1
    psG2 = [c.psum(f"ffpg{i}", [128, TB]) for i in range(2)]
    psU2 = [c.psum(f"ffpu{i}", [128, TB]) for i in range(2)]
    rPsG2 = [c.reg(f"ffpg{i}") for i in range(2)]
    rPsU2 = [c.reg(f"ffpu{i}") for i in range(2)]
    wgc2 = c.dram["wgc"].rearrange("(ch o p) q -> o p ch q", o=NOC, p=128)
    wuc2 = c.dram["wuc"].rearrange("(ch o p) q -> o p ch q", o=NOC, p=128)
    wdc2 = c.dram["wdc"].rearrange("(ch o p) q -> o p ch q", o=8, p=128)

    a_ = s2[:, 0:TB]
    b_ = s2[:, TB:2 * TB]
    f_ = s2[:, 2 * TB:3 * TB]
    for tb in range(NTB):
        tsl = slice(tb * TB, (tb + 1) * TB)
        for oc in range(NOC):
            bG = oc % 2
            bU = 2 + oc % 2
            psG, rPsG = psG2[oc % 2], rPsG2[oc % 2]
            psU, rPsU = psU2[oc % 2], rPsU2[oc % 2]
            DMA(c, wb[bG][:, :], wgc2[oc, :, :, :], reads=[rWb[bG]], writes=[rWb[bG]])
            DMA(c, wb[bU][:, :], wuc2[oc, :, :, :], reads=[rWb[bU]], writes=[rWb[bU]])
            for ch in range(8):
                MM(c, psG[:, :], wb[bG][:, ch * 128:(ch + 1) * 128],
                   xq[:, ch * S + tb * TB: ch * S + (tb + 1) * TB],
                   ch == 0, ch == 7, reads=[rWb[bG], rXq], writes=[rPsG])
            for ch in range(8):
                MM(c, psU[:, :], wb[bU][:, ch * 128:(ch + 1) * 128],
                   xq[:, ch * S + tb * TB: ch * S + (tb + 1) * TB],
                   ch == 0, ch == 7, reads=[rWb[bU], rXq], writes=[rPsU])
            osl = slice(oc * TB, (oc + 1) * TB)
            p.op("VE", lambda e, osl=osl, tb=tb, psG=psG: e.scalar_tensor_tensor(
                prod[:, osl], psG[:, :], mbar[:, 0:1], sx[:, S + tb * TB: S + (tb + 1) * TB],
                ALU.mult, ALU.mult), reads=[rPsG, rMbar, rSx], writes=[rProd])
            ACTV(c, gsl[:, :], prod[:, osl], ACT.Silu, reads=[rProd], writes=[rGsl])
            TT(c, prod[:, osl], gsl[:, :], psU[:, :], ALU.mult, [rGsl, rPsU], [rProd])
            if oc == 0:
                p.op("VE", lambda e, osl=osl: e.scalar_tensor_tensor(
                    rmax[:, :], prod[:, osl], -1.0, prod[:, osl], ALU.mult, ALU.max),
                    reads=[rProd], writes=[rRmax])
            else:
                p.op("VE", lambda e, osl=osl: e.scalar_tensor_tensor(
                    fab[:, :], prod[:, osl], -1.0, prod[:, osl], ALU.mult, ALU.max),
                    reads=[rProd], writes=[rFab])
                TT(c, rmax[:, :], rmax[:, :], fab[:, :], ALU.max,
                   [rFab, rRmax], [rRmax])
        cur = 128
        while cur > 1:
            h = cur // 2
            DMA(c, redr[0:h, 0:TB], rmax[h:cur, :], reads=[rRmax], writes=[rRedr],
                eng="GE")
            TT(c, rmax[0:h, :], rmax[0:h, :], redr[0:h, 0:TB], ALU.max,
               [rRedr, rRmax], [rRmax])
            cur = h
        TS(c, rxr[0:1, 0:TB], rmax[0:1, :], 2.0**-4, None, ALU.mult, ALU.bypass,
           [rRmax], [rRxr])
        CP(c, rowr[0:1, 0:TB], rxr[0:1, 0:TB], reads=[rRxr], writes=[rRowr])
        CP(c, redr[0:1, 0:TB], rowr[0:1, 0:TB], reads=[rRowr], writes=[rRedr])
        TT(c, rxr[0:1, 0:TB], rxr[0:1, 0:TB], redr[0:1, 0:TB], ALU.subtract,
           [rRedr, rRxr], [rRxr])
        CP(c, rowr[0:1, TB:2 * TB], rxr[0:1, 0:TB], reads=[rRxr], writes=[rRowr])
        for j in range(TB // 128):
            MM(c, psBB[:, :], c.t["onesrow"][0:1, :], rowr[0:1, j * 128:(j + 1) * 128],
               True, False, reads=[rRowr, rC], writes=[rPsBB])
            MM(c, psBB[:, :], c.t["onesrow"][0:1, :],
               rowr[0:1, TB + j * 128: TB + (j + 1) * 128],
               False, True, reads=[rRowr, rC], writes=[rPsBB])
            CP(c, rmaxb[:, j * 128:(j + 1) * 128], psBB[:, :], reads=[rPsBB],
               writes=[rRmaxb])
        p.op("VE", lambda e, tb=tb: e.scalar_tensor_tensor(
            a_, rmaxb[:, :], mbar[:, 7:8], sx[:, S + tb * TB: S + (tb + 1) * TB],
            ALU.mult, ALU.mult), reads=[rRmaxb, rMbar, rSx], writes=[rS2])
        TS(c, a_, a_, 1e-5, None, ALU.max, ALU.bypass, [rS2], [rS2])
        RECIP(c, b_, a_, [rS2], [rS2])
        TS(c, b_, b_, 127.0, None, ALU.mult, ALU.bypass, [rS2], [rS2])
        RECIP(c, f_, b_, [rS2], [rS2])
        TS(c, f_, f_, mbar[:, 2:3], None, ALU.mult, ALU.bypass, [rS2, rMbar], [rS2])
        p.op("VE", lambda e, tb=tb: e.scalar_tensor_tensor(
            a_, b_, mbar[:, 1:2], sx[:, S + tb * TB: S + (tb + 1) * TB],
            ALU.mult, ALU.mult), reads=[rS2, rMbar, rSx], writes=[rS2])
        if _os.environ.get("KDBG_SXD"):
            DMA(c, c.dram["outT"][300:301, tsl], rmaxb[0:1, :], reads=[rRmaxb],
                writes=[c.reg("out")])
            DMA(c, c.dram["outT"][301:302, tsl], s2[0:1, TB:2 * TB], reads=[rS2],
                writes=[c.reg("out")])
            DMA(c, c.dram["outT"][302:303, tsl], s2[0:1, 2 * TB:3 * TB], reads=[rS2],
                writes=[c.reg("out")])
            DMA(c, c.dram["outT"][303:304, tsl], s2[0:1, 0:TB], reads=[rS2],
                writes=[c.reg("out")])
        for oc in range(NOC):
            osl = slice(oc * TB, (oc + 1) * TB)
            TT(c, prod[:, osl], prod[:, osl], a_, ALU.mult, [rProd, rS2], [rProd])
            round_ip(c, prod[:, osl], [rProd], [rProd])
            TS(c, pq[:, osl], prod[:, osl], 127.0, -128.0, ALU.min, ALU.max,
               [rProd], [rPq])
        if tb == 0 and getattr(c, "tap", None) == "ffn":
            DMA(c, c.dram["dbgC"][:, 0:NOC * TB], pq[0:128, 0:NOC * TB],
                reads=[rPq], writes=[c.dreg["dbgC"]])
        if tb == 0 and getattr(c, "tap", None) in ("ffn", "x2pre"):
            DMA(c, c.dram["dbgE"][:, 0:3 * TB], s2[0:128, 0:3 * TB],
                reads=[rS2], writes=[c.dreg["dbgE"]])
        for dc in range(8):
            bu = dc % 2
            DMA(c, rx1[:, bu * TB:(bu + 1) * TB],
                c.dram["x1s"][dc * 128:(dc + 1) * 128, tsl],
                reads=[rRx1[bu], c.dreg["x1s"]], writes=[rRx1[bu]])
            DMA(c, dwb[bu][:, :], wdc2[dc, :, :, :], reads=[rDwb[bu]], writes=[rDwb[bu]])
            psG, rPsG = psG2[dc % 2], rPsG2[dc % 2]
            for ic in range(NOC):
                MM(c, psG[:, :], dwb[bu][:, ic * 128:(ic + 1) * 128],
                   pq[:, ic * TB:(ic + 1) * TB], ic == 0, ic == NOC - 1,
                   reads=[rDwb[bu], rPq], writes=[rPsG])
            xsl = slice(dc * S + tb * TB, dc * S + (tb + 1) * TB)
            TT(c, x2[:, xsl], psG[:, :], f_, ALU.mult, [rPsG, rS2], [rX2])
            TT(c, x2[:, xsl], x2[:, xsl], rx1[:, bu * TB:(bu + 1) * TB], ALU.add,
               [rRx1[bu], rX2], [rX2])
    c.free(prod, pq, gsl, rmax, redr, fab, rxr, rowr, rmaxb, s2, rx1, sx,
           wb[0], wb[1], wb[2], wb[3], dwb[0], dwb[1],
           psU2[1], psU2[0], psG2[1], psG2[0], psBB)



def l1_attn(c, x2, rX2, s16, rS16):
    S = c.S
    TB = min(512, S)
    NTB = S // TB
    NSC = S // 128
    WM = 2 * S - 128
    p = c.p
    rC = c.reg("consts")
    x2_16 = s16
    rX16 = rS16

    ps = c.psum("l1p", [128, TB])
    rPS = c.reg("l1p")

    qt = c.sbuf("l1q", [128, 8 * S], F16)
    rQ = c.reg("l1q")
    wq, rwq = load_w16(c, "wq", "wqT", D, D)
    for oc in range(8):
        for tb in range(NTB):
            for ch in range(8):
                MM(c, ps[:, :], wq[:, ch * D + oc * 128: ch * D + (oc + 1) * 128],
                   x2_16[:, ch * S + tb * TB: ch * S + (tb + 1) * TB],
                   ch == 0, ch == 7, reads=[rwq, rX16], writes=[rPS])
            TS(c, qt[:, oc * S + tb * TB: oc * S + (tb + 1) * TB], ps[:, :],
               0.125, None, ALU.mult, ALU.bypass, [rPS], [rQ])
    c.free(wq)

    kk = [c.sbuf(f"l1k{t}", [128, S], F16) for t in range(2)]
    rKK = [c.reg(f"l1k{t}") for t in range(2)]
    wk, rwk = load_w16(c, "wk", "wkT", D, KVH * DH)
    for t in range(2):
        for tb in range(NTB):
            for ch in range(8):
                MM(c, ps[:, :], wk[:, ch * (KVH * DH) + t * 128: ch * (KVH * DH) + (t + 1) * 128],
                   x2_16[:, ch * S + tb * TB: ch * S + (tb + 1) * TB],
                   ch == 0, ch == 7, reads=[rwk, rX16], writes=[rPS])
            CP(c, kk[t][:, tb * TB:(tb + 1) * TB], ps[:, :], reads=[rPS],
               writes=[rKK[t]], eng="SE")
    c.free(wk)

    vb = [c.sbuf(f"l1v{kv}", [128, NSC * 66], F16) for kv in range(KVH)]
    rVb = [c.reg(f"l1v{kv}") for kv in range(KVH)]
    psV = c.psum("l1vv", [128, 64])
    rPsV = c.reg("l1vv")
    wv, rwv = load_w16(c, "wv", "wvT", D, KVH * DH)
    for kv in range(KVH):
        MEMSET(c, vb[kv][:, 64::66], 1.0, writes=[rVb[kv]])
        for tc in range(NSC):
            for ch in range(8):
                MM(c, psV[:, :], x2_16[:, ch * S + tc * 128: ch * S + (tc + 1) * 128],
                   wv[:, ch * (KVH * DH) + kv * DH: ch * (KVH * DH) + (kv + 1) * DH],
                   ch == 0, ch == 7, reads=[rwv, rX16], writes=[rPsV])
            CP(c, vb[kv][:, tc * 66: tc * 66 + 64], psV[:, :], reads=[rPsV],
               writes=[rVb[kv]], eng="SE")
    c.free(wv, psV)

    sqc = c.sbuf("l1sx", [128, S], F16)
    rSqc = c.reg("l1sx")
    psN = c.psum("l1n", [1, TB])
    rPsN = c.reg("l1n")
    hp = c.t["hpones"]
    kn = c.sbuf("l1kn", [1, 8], F32)
    rKn = c.reg("l1kn")
    for kv in range(KVH):
        t, rb = divmod(kv, 2)
        rb *= 64
        ACTV(c, sqc[rb:rb + 64, :], kk[t][rb:rb + 64, :], ACT.Square,
             reads=[rKK[t]], writes=[rSqc])
        for tb in range(NTB):
            MM(c, psN[0:1, :], hp[rb:rb + 64, rb // 64: rb // 64 + 1],
               sqc[rb:rb + 64, tb * TB:(tb + 1) * TB],
               True, True, reads=[rSqc, rC], writes=[rPsN], tp=(rb, 0))
            slot = kv if tb == 0 else 4 + kv
            p.op("VE", lambda e, slot=slot: e.tensor_reduce(
                kn[0:1, slot: slot + 1], psN[0:1, :], AX.X, ALU.max),
                reads=[rPsN], writes=[rKn])
            if tb > 0:
                TT(c, kn[0:1, kv], kn[0:1, kv], kn[0:1, 4 + kv], ALU.max, [rKn], [rKn])
    ACTV(c, kn[0:1, 0:4], kn[0:1, 0:4], ACT.Sqrt, reads=[rKn], writes=[rKn],
         scale=2.0**-4)
    TS(c, kn[0:1, 4:8], kn[0:1, 0:4], -16.0, None, ALU.mult, ALU.bypass, [rKn], [rKn])
    qn = c.sbuf("l1qn", [1, S], F32)
    rQn = c.reg("l1qn")
    ncr = c.sbuf("l1cr", [1, S], F16)
    rNcr = c.reg("l1cr")
    ebias = c.sbuf("l1eb", [128, 1], F32)
    rEb = c.reg("l1eb")
    MEMSET(c, ebias[:, :], 10.0, writes=[rEb])

    slopes = c.consts["slopes"]
    escr = c.sbuf("l1es", [128, WM], F32)
    rEsc = c.reg("l1es")
    DMA(c, escr[:, :], c.dram["relc"], reads=[rEsc], writes=[rEsc])
    em = c.sbuf("l1e", [128, WM], F16)
    rEm = c.reg("l1e")
    GW = min(512, S)
    NTBG = GW // TB if TB < 512 else 1
    NTG = S // GW
    psS2 = [c.psum(f"l1sc{i}", [128, GW]) for i in range(3)]
    rSc2 = [c.reg(f"l1sc{i}") for i in range(3)]
    psPV2 = [c.psum(f"l1pv{i}", [65, GW]) for i in range(2)]
    rPV2 = [c.reg(f"l1pv{i}") for i in range(2)]
    psB = c.psum("l1bb", [64, GW])
    rB = c.reg("l1bb")
    pexp2 = [c.sbuf(f"l1pe{i}", [128, GW], F16) for i in range(3)]
    rPexp2 = [c.reg(f"l1pe{i}") for i in range(3)]
    dn = c.sbuf("l1dn", [1, 3 * TB], F32)
    rDn = c.reg("l1dn")
    dn16 = c.sbuf("l1d6", [1, TB], F16)
    rDn16 = c.reg("l1d6")
    bcs = c.sbuf("l1bc", [64, TB], F32)
    rBcs = c.reg("l1bcs")
    aodd = c.sbuf("l1ao", [64, TB], F16)
    rAodd = c.reg("l1ao")
    atp = c.sbuf("l1at", [64, TB], F32)
    rAtp = c.reg("l1at")
    onesrow = c.t["onesrow"]
    tails = [None]

    def flush_tail():
        if tails[0] is None:
            return
        ch, half, tb, psPV, rPV = tails[0]
        tails[0] = None
        CP(c, dn[0:1, 0:GW], psPV[64:65, :], reads=[rPV], writes=[rDn])
        ACTV(c, dn[0:1, GW:2 * GW], dn[0:1, 0:GW], ACT.Ln, reads=[rDn],
             writes=[rDn], scale=2.0**-10)
        ACTV(c, dn[0:1, 2 * GW:3 * GW], dn[0:1, GW:2 * GW], ACT.Exp,
             reads=[rDn], writes=[rDn], scale=-0.5)
        TS(c, dn16[0:1, 0:GW], dn[0:1, 2 * GW:3 * GW], 2.0**-5, None,
           ALU.mult, ALU.bypass, [rDn], [rDn16])
        MM(c, psB[0:64, :], onesrow[0:1, 0:64], dn16[0:1, 0:GW], True, True,
           reads=[rDn16, rC], writes=[rB])
        CP(c, bcs[:, 0:GW], psB[0:64, :], reads=[rB], writes=[rBcs], eng="SE")
        TT(c, atp[0:64, 0:GW], psPV[0:64, :], bcs[:, 0:GW], ALU.mult,
           [rPV, rBcs], [rAtp])
        if half == 0:
            TT(c, s16[0:64, ch * S + tb * GW: ch * S + (tb + 1) * GW],
               atp[0:64, 0:GW], bcs[:, 0:GW], ALU.mult, [rAtp, rBcs], [rS16])
        else:
            TT(c, aodd[0:64, 0:GW], atp[0:64, 0:GW], bcs[:, 0:GW], ALU.mult,
               [rAtp, rBcs], [rAodd])
            DMA(c, s16[64:128, ch * S + tb * GW: ch * S + (tb + 1) * GW],
                aodd[0:64, 0:GW], reads=[rAodd], writes=[rS16], eng="GE")

    for h in range(H):
        kv = h % KVH
        ch, half = divmod(h, 2)
        base = half * 64
        ACTV(c, sqc[base:base + 64, :], qt[base:base + 64, ch * S:(ch + 1) * S],
             ACT.Square, reads=[rQ], writes=[rSqc])
        for tb in range(NTB):
            MM(c, psN[0:1, :], hp[base:base + 64, half: half + 1],
               sqc[base:base + 64, tb * TB:(tb + 1) * TB],
               True, True, reads=[rSqc, rC], writes=[rPsN], tp=(base, 0))
            ACTV(c, qn[0:1, tb * TB:(tb + 1) * TB], psN[0:1, :], ACT.Sqrt,
                 reads=[rPsN], writes=[rQn], scale=2.0**-4)
        TS(c, ncr[0:1, :], qn[0:1, 0:S], kn[0:1, 4 + kv], None, ALU.mult,
           ALU.bypass, [rQn, rKn], [rNcr])
        ACTV(c, em[:, :], escr[:, :], ACT.Exp, reads=[rEsc], writes=[rEm],
             scale=float(slopes[h]) * 0.125)
        for tg in range(NTG):
            tb = tg
            pv = (h * NTG + tg) % 2
            psPV, rPV = psPV2[pv], rPV2[pv]
            for sc in range(NSC):
                pi = sc % 3
                psS, rSc = psS2[pi], rSc2[pi]
                pexp, rPexp = pexp2[pi], rPexp2[pi]
                MM(c, psS[:, :],
                   onesrow[0:1, :], ncr[0:1, tb * GW:(tb + 1) * GW],
                   True, False, reads=[rNcr, rC], writes=[rSc])
                MM(c, psS[:, :],
                   kk[kv // 2][base: base + 64, sc * 128:(sc + 1) * 128],
                   qt[base: base + 64, ch * S + tb * GW: ch * S + (tb + 1) * GW],
                   False, True, reads=[rKK[kv // 2], rQ], writes=[rSc], tp=(base, 0))
                ACTV(c, pexp[:, :], psS[:, :], ACT.Exp, reads=[rSc, rEb], writes=[rPexp],
                     bias=ebias[:, 0:1])
                off = tg * GW - sc * 128 + (S - 128)
                TT(c, pexp[:, :], pexp[:, :], em[:, off: off + GW], ALU.mult,
                   [rPexp, rEm], [rPexp])
                if sc == 1:
                    flush_tail()
                MM(c, psPV[0:65, :], vb[kv][:, sc * 66: sc * 66 + 65],
                   pexp[:, :], sc == 0, sc == NSC - 1,
                   reads=[rVb[kv], rPexp], writes=[rPV])
            tails[0] = (ch, half, tb, psPV, rPV)
    flush_tail()
    c.free(pexp2[0], pexp2[1], pexp2[2], dn, dn16, bcs, aodd, atp, em, escr, ebias, ncr,
           qn, kn, sqc, qt, kk[0], kk[1], vb[0], vb[1], vb[2], vb[3],
           psB, psPV2[1], psPV2[0], psS2[2], psS2[1], psS2[0], psN)

    wp, rwp = load_w16(c, "wp", "wpT", D, D)
    bpj = c.sbuf("l1bp", [128, 8], F32)
    rBpj = c.reg("l1bp")
    DMA(c, bpj[:, :], c.dram["bprojT"], reads=[rBpj], writes=[rBpj])
    for dc in range(8):
        for tb in range(NTB):
            for oc in range(8):
                MM(c, ps[:, :], wp[:, oc * D + dc * 128: oc * D + (dc + 1) * 128],
                   s16[:, oc * S + tb * TB: oc * S + (tb + 1) * TB],
                   oc == 0, oc == 7, reads=[rwp, rS16], writes=[rPS])
            p.op("VE", lambda e, dc=dc, tb=tb: e.scalar_tensor_tensor(
                x2[:, dc * S + tb * TB: dc * S + (tb + 1) * TB], ps[:, :],
                bpj[:, dc], x2[:, dc * S + tb * TB: dc * S + (tb + 1) * TB],
                ALU.add, ALU.add), reads=[rPS, rBpj, rX2], writes=[rX2])
    c.free(wp, bpj, ps)



def precache_w12(c):
    rC = c.reg("consts")
    st2 = [c.sbuf(f"f1s{i}", [128, FF], F32) for i in range(2)]
    rSt2 = [c.reg(f"f1s{i}") for i in range(2)]
    t162 = [c.sbuf(f"f1t{i}", [128, FF], F16) for i in range(2)]
    rT162 = [c.reg(f"f1t{i}") for i in range(2)]
    for key, skey, rows, cols in (("w1T", "w1c", D, FF), ("w2T", "w2c", FF, D)):
        for ch in range(rows // 128):
            st, rSt = st2[ch % 2], rSt2[ch % 2]
            t16, rT16 = t162[ch % 2], rT162[ch % 2]
            DMA(c, st[:, 0:cols], c.dram[key][ch * 128:(ch + 1) * 128, :],
                reads=[rSt], writes=[rSt])
            CP(c, t16[:, 0:cols], st[:, 0:cols], reads=[rSt], writes=[rT16], eng="SE")
            DMA(c, c.dram[skey][ch * (cols // 128) * 128:(ch + 1) * (cols // 128) * 128, :]
                .rearrange("(o p) q -> p o q", p=128),
                t16[:, 0:cols], reads=[rT16], writes=[c.dreg[skey]])
    c.free(t162[1], t162[0], st2[1], st2[0])


def l1_ffn(c, x3, rX3, s16, rS16):
    S = c.S
    TB = min(512, S)
    NTB = S // TB
    NOC = FF // 128
    NQ = 8
    QOC = NOC // NQ
    p = c.p
    rC = c.reg("consts")
    cst = c.t["cst"]
    P0 = c.PWL0
    pw = c.consts["pwl"]
    x3_16 = s16
    rX16 = rS16

    kxb = c.sbuf("f1kx", [128, 31], F32)
    rKxb = c.reg("f1kx")
    for k in range(31):
        MEMSET(c, kxb[:, k:k + 1], -float(pw["kx"][k]), writes=[rKxb])
        w1c2 = c.dram["w1c"].rearrange("(ch o p) q -> o p ch q", o=NOC, p=128)
    w2c2 = c.dram["w2c"].rearrange("(ch o p) q -> o p ch q", o=8, p=128)

    b1 = c.sbuf("f1b1", [128, 32], F32)
    rB1 = c.reg("f1b1")
    DMA(c, b1[:, :], c.dram["b1T"], reads=[rB1], writes=[rB1])
    b2 = c.sbuf("f1b2", [128, 8], F32)
    rB2 = c.reg("f1b2")
    DMA(c, b2[:, :], c.dram["b2T"], reads=[rB2], writes=[rB2])

    for ch in range(8):
        DMA(c, c.dram["x3s"][ch * 128:(ch + 1) * 128, :], x3[:, ch * S:(ch + 1) * S],
            reads=[rX3], writes=[c.dreg["x3s"]])

    QW = QOC * TB
    vq = c.sbuf("f1v", [128, QW], F32)
    rV = c.reg("f1v")
    acc = c.sbuf("f1a", [128, QW], F32)
    rA = c.reg("f1a")
    tkr2 = [c.sbuf(f"f1k{i}", [128, QW], F32) for i in range(2)]
    rK2 = [c.reg(f"f1k{i}") for i in range(2)]
    tkr, rK = tkr2[0], rK2[0]
    hq = c.sbuf("f1h", [128, NOC * TB], F16)
    rHq = c.reg("f1h")
    wb = [c.sbuf(f"f1w{i}", [128, 8 * 128], F16) for i in range(2)]
    rWb = [c.reg(f"f1w{i}") for i in range(2)]
    w2b = [c.sbuf(f"f1u{i}", [128, NOC * 128], F16) for i in range(2)]
    rW2b = [c.reg(f"f1u{i}") for i in range(2)]
    rx3 = c.sbuf("f1r", [128, 2 * TB], F32)
    rRx3 = [c.reg("f1r0"), c.reg("f1r1")]
    x4 = x3
    rX4 = rX3
    psH2 = [c.psum(f"f1p{i}", [128, TB]) for i in range(2)]
    rPsH2 = [c.reg(f"f1p{i}") for i in range(2)]

    for tb in range(NTB):
        tsl = slice(tb * TB, (tb + 1) * TB)
        for qh in range(NQ):
            for j in range(QOC):
                oc = qh * QOC + j
                b = oc % 2
                psH, rPsH = psH2[oc % 2], rPsH2[oc % 2]
                DMA(c, wb[b][:, :], w1c2[oc, :, :, :], reads=[rWb[b]], writes=[rWb[b]])
                for ch in range(8):
                    MM(c, psH[:, :], wb[b][:, ch * 128:(ch + 1) * 128],
                       x3_16[:, ch * S + tb * TB: ch * S + (tb + 1) * TB],
                       ch == 0, ch == 7, reads=[rWb[b], rX16], writes=[rPsH])
                TS(c, vq[:, j * TB:(j + 1) * TB], psH[:, :], b1[:, oc], 256.0,
                   ALU.add, ALU.mult, [rPsH, rB1], [rV])
            round_ip(c, vq[:, :], [rV], [rV])
            TS(c, acc[:, :], vq[:, :], cst[:, P0 + 1: P0 + 2], cst[:, P0: P0 + 1],
               ALU.mult, ALU.add, [rV, rC], [rA])
            for k in range(31):
                tk, rTk = tkr2[k % 2], rK2[k % 2]
                ACTV(c, tk[:, :], vq[:, :], ACT.Relu, reads=[rV, rC], writes=[rTk],
                     bias=kxb[:, k:k + 1])
                p.op("VE", lambda e, k=k, tk=tk: e.scalar_tensor_tensor(
                    acc[:, :], tk[:, :], cst[:, P0 + 2 + k: P0 + 3 + k], acc[:, :],
                    ALU.mult, ALU.add), reads=[rTk, rC, rA], writes=[rA])
            TS(c, tkr[:, :], vq[:, :], 1024.0, -1024.0, ALU.max, ALU.add, [rV], [rK])
            TS(c, tkr[:, :], tkr[:, :], cst[:, P0 + 33: P0 + 34], None, ALU.mult,
               ALU.bypass, [rK, rC], [rK])
            TT(c, acc[:, :], acc[:, :], tkr[:, :], ALU.add, [rK, rA], [rA])
            TS(c, tkr[:, :], vq[:, :], 1024.0, cst[:, P0 + 34: P0 + 35], ALU.is_gt,
               ALU.mult, [rV, rC], [rK])
            TT(c, acc[:, :], acc[:, :], tkr[:, :], ALU.add, [rK, rA], [rA])
            TS(c, tkr[:, :], vq[:, :], -1024.0, 1024.0, ALU.min, ALU.add, [rV], [rK])
            TS(c, tkr[:, :], tkr[:, :], cst[:, P0 + 35: P0 + 36], None, ALU.mult,
               ALU.bypass, [rK, rC], [rK])
            TT(c, acc[:, :], acc[:, :], tkr[:, :], ALU.add, [rK, rA], [rA])
            TS(c, tkr[:, :], vq[:, :], -1024.0, cst[:, P0 + 37: P0 + 38], ALU.is_lt,
               ALU.mult, [rV, rC], [rK])
            TT(c, acc[:, :], acc[:, :], tkr[:, :], ALU.add, [rK, rA], [rA])
            TS(c, acc[:, :], acc[:, :], 256.0, None, ALU.mult, ALU.bypass, [rA], [rA])
            round_ip(c, acc[:, :], [rA], [rA])
            TS(c, hq[:, qh * QW:(qh + 1) * QW], acc[:, :], 2.0**-8, None, ALU.mult,
               ALU.bypass, [rA], [rHq])
        for dc in range(8):
            bu = dc % 2
            DMA(c, rx3[:, bu * TB:(bu + 1) * TB],
                c.dram["x3s"][dc * 128:(dc + 1) * 128, tsl],
                reads=[rRx3[bu], c.dreg["x3s"]], writes=[rRx3[bu]])
            DMA(c, w2b[bu][:, :], w2c2[dc, :, :, :], reads=[rW2b[bu]], writes=[rW2b[bu]])
            psH, rPsH = psH2[dc % 2], rPsH2[dc % 2]
            for ic in range(NOC):
                MM(c, psH[:, :], w2b[bu][:, ic * 128:(ic + 1) * 128],
                   hq[:, ic * TB:(ic + 1) * TB], ic == 0, ic == NOC - 1,
                   reads=[rW2b[bu], rHq], writes=[rPsH])
            xsl = slice(dc * S + tb * TB, dc * S + (tb + 1) * TB)
            TS(c, x4[:, xsl], psH[:, :], b2[:, dc], 256.0,
               ALU.add, ALU.mult, [rPsH, rB2], [rX4])
            round_ip(c, x4[:, xsl], [rX4], [rX4])
            p.op("VE", lambda e, xsl=xsl, bu=bu: e.scalar_tensor_tensor(
                x4[:, xsl], x4[:, xsl], 2.0**-8, rx3[:, bu * TB:(bu + 1) * TB],
                ALU.mult, ALU.add), reads=[rRx3[bu], rX4], writes=[rX4])
    c.free(vq, acc, tkr2[0], tkr2[1], hq, wb[0], wb[1], w2b[0], w2b[1], rx3, b1, b2, kxb,
           psH2[1], psH2[0])



def _flex(c, S, n32, n16):
    if S < 2048:
        n32 = max(n32, 24576)
        n16 = max(n16, 49152)
    c.open_flex(n32=int(n32), n16=int(n16))


def build_program(S=2048, tap=None):
    consts = _host_consts(S)
    nc = bass.Bass("TRN2", target_bir_lowering=False)
    c = Ctx(nc, S, consts)

    def din(name, shape, dtype=F32):
        c.dram[name] = nc.dram_tensor(name, list(shape), dtype, kind="ExternalInput").ap()
        c.dreg[name] = c.reg("d_" + name)

    def dscr(name, shape, dtype):
        c.dram[name] = nc.dram_tensor(name, list(shape), dtype, kind="Internal").ap()
        c.dreg[name] = c.reg("d_" + name)

    din("xT", (D, S))
    out_ap = nc.dram_tensor("outT", [D, S], F32, kind="ExternalOutput").ap()
    c.dram["outT"] = out_ap
    din("wdqT", (D, DLAT)); din("wuqT", (DLAT, H * DH)); din("wqrT", (DLAT, H * DHR))
    din("wdkvT", (D, DLAT)); din("wukT", (DLAT, H * DH)); din("wuvT", (DLAT, H * DH))
    din("wkrT", (D, DHR)); din("woT", (H * DH, D))
    din("wgT", (D, FF)); din("wuT", (D, FF)); din("wdT", (FF, D))
    din("wqT", (D, D)); din("wkT", (D, KVH * DH)); din("wvT", (D, KVH * DH))
    din("wpT", (D, D)); din("bprojT", (128, 8))
    din("w1T", (D, FF)); din("b1T", (128, 32)); din("w2T", (FF, D)); din("b2T", (128, 8))
    din("cosb", (128, S)); din("sinb", (128, S))
    din("m01c", (128, 128), F16)
    din("relc", (128, 2 * S - 128))
    din("identc", (128, 128), F16)
    din("sel2", (2, 256))
    dscr("wgc", (8 * 32 * 128, 128), F16)
    dscr("wuc", (8 * 32 * 128, 128), F16)
    dscr("wdc", (32 * 8 * 128, 128), F16)
    dscr("w1c", (8 * 32 * 128, 128), F16)
    dscr("w2c", (32 * 8 * 128, 128), F16)
    dscr("x1s", (D, S), F32)
    dscr("dbgA", (128, 8 * S), F16)
    dscr("dbgB", (128, 2 * S), F32)
    dscr("dbgC", (128, 64 * 256), F16)
    dscr("dbgD", (128, 64), F32)
    dscr("dbgE", (128, 2048), F32)
    dscr("dbgX", (D, S), F32)
    dscr("x3s", (D, S), F32)

    setup_consts(c)
    _kph = _os.environ.get("KPH")
    _phset = set(_kph.split(",")) if _kph else None
    _ph = lambda nm: _phset is None or nm in _phset
    s16 = c.perm("stream16", [128, 8 * S], F16)
    rS16 = c.reg("S16")

    if _ph("A"):
        _flex(c, S, 4 * S + 5888, 29 * S + 1024)
        l0_attn(c, s16, rS16)
        c.close_flex()

    s32 = c.perm("stream32", [128, 8 * S], F32)
    rS32 = c.reg("S32")

    if _ph("B"):
        _flex(c, S, 2 * min(512, S) + 4 * FF + 1280, 4 * S + 2 * FF + 2048)
        l0_wo(c, s16, rS16, s32, rS32)
        prequant(c)
        precache_w12(c)
        c.close_flex()

    if _ph("L1"):
        _flex(c, S, 4 * min(512, S) + 512, 16 * S + 4 * min(512, S) + 1024)
        post_ln(c, s32, rS32, None, None, "a")
        c.close_flex()

    c.tap = tap
    if _ph("C"):
        _flex(c, S, 2 * S + 2 * FF + 3072, 64 * min(256, S) + FF + 2048)
        l0_ffn(c, s32, rS32, s16, rS16)
        c.close_flex()

    if tap == "x2pre":
        for ch in range(8):
            DMA(c, c.dram["dbgX"][ch * 128:(ch + 1) * 128, :], s32[:, ch * S:(ch + 1) * S],
                reads=[rS32], writes=[c.dreg["dbgX"]])
    if _ph("L2"):
        _flex(c, S, 4 * min(512, S) + 512, 16 * S + 4 * min(512, S) + 1024)
        post_ln(c, s32, rS32, s16, rS16, "b")
        c.close_flex()
    if tap == "x2":
        for ch in range(8):
            DMA(c, c.dram["dbgX"][ch * 128:(ch + 1) * 128, :], s32[:, ch * S:(ch + 1) * S],
                reads=[rS32], writes=[c.dreg["dbgX"]])

    if _ph("D"):
        _flex(c, S, 2 * S + 4608, 17 * S + 1024)
        l1_attn(c, s32, rS32, s16, rS16)
        c.close_flex()

    if _ph("L3"):
        _flex(c, S, 4 * min(512, S) + 512, 16 * S + 4 * min(512, S) + 1024)
        post_ln(c, s32, rS32, s16, rS16, "c")
        c.close_flex()

    if _ph("E"):
        _flex(c, S, 14 * min(512, S) + FF + 2560, 38 * min(512, S) + FF + 4096)
        l1_ffn(c, s32, rS32, s16, rS16)
        c.close_flex()

    if _ph("L4"):
        _flex(c, S, 4 * min(512, S) + 512, 16 * S + 4 * min(512, S) + 1024)
        post_ln(c, s32, rS32, None, None, "d")
        c.close_flex()

    rout = c.reg("out")
    if _ph("OUT") or _phset is None:
        for ch in range(8):
            DMA(c, out_ap[ch * 128:(ch + 1) * 128, :], s32[:, ch * S:(ch + 1) * S],
                reads=[rS32], writes=[rout])
    c.p.op("SQ", lambda e: e.nop(), reads=[rout], writes=[])

    c.p.check_deadlock()
    sems = {}
    cms = []
    for e in Prog.SEMS:
        cm = nc.semaphore(f"sem_{e}")
        sems[e] = cm.__enter__()
        cms.append(cm)
    bcm = nc.Block()
    block = bcm.__enter__()
    c.p.emit(block, sems)
    bcm.__exit__(None, None, None)
    for cm in cms:
        cm.__exit__(None, None, None)
    return nc


def _prep_inputs(S, consts, **inp):
    T = lambda a: np.ascontiguousarray(np.asarray(a, dtype=np.float32).T)
    wqrT = T(inp["L0_attn_W_QR"])
    perm = [h * 32 + 2 * f + half for half in range(2) for h in range(H) for f in range(16)]
    wqrT = np.ascontiguousarray(wqrT[:, perm])
    wkrT = T(inp["L0_attn_W_KR"])
    kperm = [2 * f for f in range(16)] + [2 * f + 1 for f in range(16)]
    wkrT = np.ascontiguousarray(wkrT[:, kperm])
    wkv = np.asarray(inp["L1_attn_Wkv"], dtype=np.float32)
    wkT = T(wkv[0:256, :])
    base = {
        "wdqT": T(inp["L0_attn_W_DQ"]), "wuqT": T(inp["L0_attn_W_UQ"]),
        "wqrT": wqrT, "wdkvT": T(inp["L0_attn_W_DKV"]),
        "wukT": T(inp["L0_attn_W_UK"]), "wuvT": T(inp["L0_attn_W_UV"]),
        "wkrT": wkrT, "woT": T(inp["L0_attn_W_O"]),
        "wgT": T(inp["L0_ffn_w_gate"]), "wuT": T(inp["L0_ffn_w_up"]),
        "wdT": T(inp["L0_ffn_w_down"]),
        "wqT": T(inp["L1_attn_Wq"]), "wkT": np.ascontiguousarray(wkT),
        "wvT": T(wkv[256:512, :]), "wpT": T(inp["L1_attn_Wproj"]),
        "bprojT": np.ascontiguousarray(
            np.asarray(inp["L1_attn_bproj"], dtype=np.float32).reshape(8, 128).T),
        "w1T": np.ascontiguousarray(np.asarray(inp["L1_ffn_w1"], dtype=np.float32)),
        "b1T": np.ascontiguousarray(
            np.asarray(inp["L1_ffn_b1"], dtype=np.float32).reshape(32, 128).T),
        "w2T": np.ascontiguousarray(np.asarray(inp["L1_ffn_w2"], dtype=np.float32)),
        "b2T": np.ascontiguousarray(
            np.asarray(inp["L1_ffn_b2"], dtype=np.float32).reshape(8, 128).T),
        "cosb": consts["cosb"], "sinb": consts["sinb"], "m01c": consts["m01"],
        "relc": consts["relc"], "identc": consts["ident"],
        "sel2": np.concatenate([np.concatenate([np.ones((1, 128), np.float32), np.zeros((1, 128), np.float32)], axis=1),
                                np.concatenate([np.zeros((1, 128), np.float32), np.ones((1, 128), np.float32)], axis=1)], axis=0),
    }
    x = np.asarray(inp["L0_attn_x"], dtype=np.float32)
    return [dict(base, xT=np.ascontiguousarray(x[b].T)) for b in range(x.shape[0])]


def kernel(**inputs):
    S = int(inputs["L0_attn_x"].shape[1])
    consts = _host_consts(S)
    in_maps = _prep_inputs(S, consts, **inputs)
    nc = build_program(S)
    res = run_bass_kernel_spmd(nc, in_maps, core_ids=list(range(len(in_maps))))
    outs = [np.ascontiguousarray(np.asarray(r["outT"]).T) for r in res.results]
    return np.stack(outs, axis=0).astype(np.float32)


# version 13
# speedup vs baseline: 1.2220x; 1.1041x over previous
import os as _os
import numpy as np

import bass_rust
import concourse.bass as bass
import concourse.mybir as mybir
from concourse.bass_utils import run_bass_kernel_spmd
from concourse import library_config

F32 = mybir.dt.float32
F16 = mybir.dt.float16
I32 = mybir.dt.int32
ALU = mybir.AluOpType
ACT = mybir.ActivationFunctionType
RED = bass_rust.ReduceOp
AX = mybir.AxisListType

D = 1024
H = 16
DH = 64
DHR = 32
DLAT = 512
KVH = 4
FF = 4096

MAGIC32 = float(1.5 * 2.0**23)
MAGIC_MUL = float((1.5 * 2.0**52 - 1.5 * 2.0**23) * 2.0**30)



def _host_consts(S):
    import jax
    import jax.numpy as jnp
    cpu = jax.devices("cpu")[0]
    with jax.default_device(cpu):
        d = DHR
        inv_freq = jnp.asarray(10000.0) ** (-jnp.arange(0, d, 2, dtype=jnp.float32) / d)
        pos = jnp.arange(S)
        ang = pos.astype(jnp.float32)[:, None] * inv_freq[None, :]
        cos = np.asarray(jnp.cos(ang), dtype=np.float32)
        sin = np.asarray(jnp.sin(ang), dtype=np.float32)
        pwl_x = jnp.linspace(-4.0, 4.0, 33, dtype=jnp.float32)
        pwl_y = 0.5 * pwl_x * (1.0 + jax.lax.erf(pwl_x / jnp.sqrt(jnp.float32(2.0))))
        X = np.asarray(pwl_x, dtype=np.float32)
        Y = np.asarray(pwl_y, dtype=np.float32)

    cosb = np.ascontiguousarray(np.tile(cos.T, (8, 1)).astype(np.float32))
    sinb = np.ascontiguousarray(np.tile(sin.T, (8, 1)).astype(np.float32))
    m01 = np.triu(np.ones((128, 128), dtype=np.float16))

    start = 2.0 ** (-2.0 ** (-(np.log2(H) - 3.0)))
    slopes = (start ** np.arange(1, H + 1)).astype(np.float32)

    WM = 2 * S - 128
    relc = np.minimum((S - 128) - np.arange(WM, dtype=np.float64)[None, :]
                      + np.arange(128, dtype=np.float64)[:, None], 0.0).astype(np.float32)
    ident = np.eye(128, dtype=np.float16)

    Y64 = Y.astype(np.float64)
    m = (Y64[1:] - Y64[:-1]) * 4.0
    mv = m / 256.0
    w = mv[1:] - mv[:-1]
    kxv = (X[1:32].astype(np.float64) * 256.0)
    c1 = mv[0]
    c0 = Y64[0] + 1024.0 * mv[0] - float(np.sum(w * kxv))

    def hilo(v):
        hi = np.float32(v)
        lo = np.float32(np.float64(v) - np.float64(hi))
        return float(hi), float(lo)

    pwl = {
        "c0": hilo(float(Y64[0] + 1024.0 * mv[0])),
        "c1": hilo(c1),
        "w": [hilo(x) for x in w],
        "kx": [float(x) for x in kxv],
        "eu_w": hilo((1.0 - m[-1]) / 256.0),
        "eu_j": hilo(np.float64(np.float32(4.0)) - Y64[32]),
        "el_w": hilo(-mv[0]),
        "el_c": hilo(-mv[0] * 1024.0),
        "el_j": hilo(-Y64[0]),
    }
    return dict(cosb=cosb, sinb=sinb, m01=m01, slopes=slopes, pwl=pwl,
                relc=relc, ident=ident)



class Region:
    __slots__ = ("name", "writer", "readers")

    def __init__(self, name):
        self.name = name
        self.writer = None
        self.readers = []


class Prog:
    ENGS = ("TE", "VE", "SE", "GE", "SQ")
    NDS = 8
    SEMS = (("TE", "VE", "SE", "GEc", "SQc")
            + tuple(f"GEd{i}" for i in range(8)) + tuple(f"SQd{i}" for i in range(8)))

    def __init__(self, nc):
        self.nc = nc
        self.cnt = {sm: 0 for sm in self.SEMS}
        self.ops = {e: [] for e in self.ENGS}
        self.waited = {e: {sm: 0 for sm in self.SEMS} for e in self.ENGS}
        self._dma_rr = {"GE": 0, "SQ": 0}

    fence = None

    def _sem(self, eng, dma):
        if eng in ("GE", "SQ"):
            if dma:
                i = self._dma_rr[eng]
                self._dma_rr[eng] = (i + 1) % self.NDS
                return f"{eng}d{i}"
            return eng + "c"
        return eng

    def op(self, eng, fn, reads=(), writes=(), dma=False):
        waits = {}

        def dep(tok):
            if tok is None:
                return
            sm, c2 = tok
            if self.waited[eng][sm] < c2:
                waits[sm] = max(waits.get(sm, 0), c2)

        if self.fence:
            for sm, c2 in self.fence.items():
                dep((sm, c2))
        for r in reads:
            dep(r.writer)
        for r in writes:
            dep(r.writer)
            for t in r.readers:
                dep(t)
        sm = self._sem(eng, dma)
        if dma:
            dep((sm, self.cnt[sm]))
        for s2, c2 in waits.items():
            self.waited[eng][s2] = c2
        inc = 16 if dma else 1
        self.cnt[sm] += inc
        tok = (sm, self.cnt[sm])
        for r in reads:
            r.readers.append(tok)
        for r in writes:
            r.writer = tok
            r.readers = []
        self.ops[eng].append((waits, fn, sm, inc))
        return tok

    def check_deadlock(self):
        done = {sm: 0 for sm in self.SEMS}
        idx = {e: 0 for e in self.ENGS}
        total = sum(len(v) for v in self.ops.values())
        ndone = 0
        progress = True
        while progress:
            progress = False
            for e in self.ENGS:
                while idx[e] < len(self.ops[e]):
                    waits, _, sm, inc = self.ops[e][idx[e]]
                    if all(done[s2] >= c2 for s2, c2 in waits.items()):
                        done[sm] += inc
                        idx[e] += 1
                        ndone += 1
                        progress = True
                    else:
                        break
        if ndone != total:
            stuck = {e: (idx[e], len(self.ops[e]), self.ops[e][idx[e]][0])
                     for e in self.ENGS if idx[e] < len(self.ops[e])}
            raise RuntimeError(f"emission deadlock: {stuck}")

    def emit(self, block, sems):
        handles = {"TE": self.nc.tensor, "VE": self.nc.vector, "SE": self.nc.scalar,
                   "GE": self.nc.gpsimd, "SQ": self.nc.sync}

        def mk(eng):
            h = handles[eng]

            def body(_):
                for waits, fn, sm, inc in self.ops[eng]:
                    for s2 in sorted(waits):
                        h.wait_ge(sems[s2], waits[s2])
                    fn(h).then_inc(sems[sm], inc)
            return body

        for eng, deco in (("SQ", block.sync), ("TE", block.tensor),
                          ("VE", block.vector), ("SE", block.scalar),
                          ("GE", block.gpsimd)):
            if self.ops[eng]:
                deco(mk(eng))


class Tile:
    __slots__ = ("view", "c0", "P", "C", "alloc_idx", "arena")

    def __init__(self, view, c0, P, C, alloc_idx, arena):
        self.view = view
        self.c0 = c0
        self.P = P
        self.C = C
        self.alloc_idx = alloc_idx
        self.arena = arena

    def __getitem__(self, idx):
        rows, cols = idx
        if isinstance(rows, slice):
            r0 = rows.start or 0
            r1 = self.P if rows.stop is None else rows.stop
        else:
            r0, r1 = rows, rows + 1
        if isinstance(cols, slice):
            cs0 = cols.start or 0
            cs1 = self.C if cols.stop is None else cols.stop
            st = cols.step
            if st is None:
                return self.view[r0:r1, self.c0 + cs0: self.c0 + cs1]
            return self.view[r0:r1, self.c0 + cs0: self.c0 + cs1: st]
        return self.view[r0:r1, self.c0 + cols: self.c0 + cols + 1]


class Arena:

    def __init__(self, view, n):
        self.view = view
        self.free_list = [(0, n)]
        self.n = n
        self.used = {}

    def alloc(self, name, P, C):
        need = (C + 31) & ~31
        for i, (off, ln) in enumerate(self.free_list):
            if ln >= need:
                if ln == need:
                    self.free_list.pop(i)
                else:
                    self.free_list[i] = (off + need, ln - need)
                t = Tile(self.view, off, P, C, None, self)
                self.used[id(t)] = (off, need)
                return t
        raise MemoryError(f"arena OOM for {name}: need {need}, free {self.free_list}")

    def release(self, t):
        off, need = self.used.pop(id(t))
        self.free_list.append((off, need))
        self.free_list.sort()
        merged = []
        for off, ln in self.free_list:
            if merged and merged[-1][0] + merged[-1][1] == off:
                merged[-1] = (merged[-1][0], merged[-1][1] + ln)
            else:
                merged.append((off, ln))
        self.free_list = [tuple(x) for x in merged]


class Ctx:
    def __init__(self, nc, S, consts):
        self.nc = nc
        self.S = S
        self.consts = consts
        self.p = Prog(nc)
        self._cms = []
        self._psum_stack = []
        self._n = 0
        self.t = {}
        self.r = {}
        self.dram = {}
        self.dreg = {}
        self.flex = {}
        self.flex_order = []

    def perm(self, name, shape, dtype):
        self._n += 1
        cm = self.nc.sbuf_tensor(f"{name}_{self._n}", list(shape), dtype)
        t = cm.__enter__()
        self._cms.append(cm)
        return t

    def open_flex(self, n32=0, n16=0):
        assert not self.flex, "flex already open"
        self._n += 1
        if n32:
            cm = self.nc.sbuf_tensor(f"fx32_{self._n}", [128, n32], F32)
            self.flex[F32] = (Arena(cm.__enter__(), n32), cm)
            self.flex_order.append(F32)
        if n16:
            cm = self.nc.sbuf_tensor(f"fx16_{self._n}", [128, n16], F16)
            self.flex[F16] = (Arena(cm.__enter__(), n16), cm)
            self.flex_order.append(F16)

    def close_flex(self):
        self._fence_all()
        for dt in reversed(self.flex_order):
            self.flex[dt][1].__exit__(None, None, None)
        self.flex = {}
        self.flex_order = []

    def sbuf(self, name, shape, dtype):
        assert len(shape) == 2
        if dtype == I32:
            dtype = F32
        ar = self.flex[dtype][0]
        return ar.alloc(name, shape[0], shape[1])

    def sbuf_i32(self, name, shape):
        self._n += 1
        cm = self.nc.sbuf_tensor(f"{name}_{self._n}", list(shape), I32)
        t = cm.__enter__()
        self._psum_stack.append((t, cm))
        return t

    def psum(self, name, shape, dtype=F32):
        self._n += 1
        cm = self.nc.psum_tensor(f"{name}_{self._n}", list(shape), dtype)
        t = cm.__enter__()
        self._psum_stack.append((t, cm))
        return t

    def _fence_all(self):
        fence = dict(self.p.fence or {})
        for name, r in self.r.items():
            for tok in ([r.writer] if r.writer else []) + r.readers:
                e2, c2 = tok
                fence[e2] = max(fence.get(e2, 0), c2)
        self.p.fence = fence

    def free(self, *tensors):
        self._fence_all()
        psums = [t for t in tensors if not isinstance(t, Tile)]
        if psums:
            order = {id(t): i for i, (t, _) in enumerate(self._psum_stack)}
            for t in sorted(psums, key=lambda t: -order[id(t)]):
                i = next(i for i, (tt, _) in enumerate(self._psum_stack) if tt is t)
                _, cm = self._psum_stack.pop(i)
                cm.__exit__(None, None, None)
        for t in tensors:
            if isinstance(t, Tile):
                t.arena.release(t)

    def reg(self, name):
        if name not in self.r:
            self.r[name] = Region(name)
        return self.r[name]


def TS(c, out, in0, s1, s2, op0, op1, reads, writes, eng="VE"):
    return c.p.op(eng, lambda e: e.tensor_scalar(out, in0, s1, s2, op0, op1),
                  reads=reads, writes=writes)


def TT(c, out, in0, in1, op, reads, writes, eng="VE"):
    return c.p.op(eng, lambda e: e.tensor_tensor(out, in0, in1, op),
                  reads=reads, writes=writes)


def STT(c, out, in0, scalar, in1, op0, op1, reads, writes):
    return c.p.op("VE", lambda e: e.scalar_tensor_tensor(out, in0, scalar, in1, op0, op1),
                  reads=reads, writes=writes)


def CP(c, out, in0, reads, writes, eng="VE"):
    if eng == "SE":
        return c.p.op(eng, lambda e: e.activation(out, in0, ACT.Copy),
                      reads=reads, writes=writes)
    return c.p.op(eng, lambda e: e.tensor_copy(out, in0), reads=reads, writes=writes)


def ACTV(c, out, in0, func, reads, writes, bias=None, scale=None):
    kw = {}
    if bias is not None:
        kw["bias"] = bias
    if scale is not None:
        kw["scale"] = scale
    return c.p.op("SE", lambda e: e.activation(out, in0, func, **kw),
                  reads=reads, writes=writes)


def MM(c, psum_ap, lhsT_ap, rhs_ap, first, last, reads, writes, tp=None):
    kw = {}
    if tp is not None:
        kw["tile_position"] = tp
    return c.p.op("TE", lambda e: e.matmul(psum_ap, lhsT_ap, rhs_ap,
                                           start=first, stop=last, **kw),
                  reads=reads, writes=writes)


def MEMSET(c, ap, val, writes, eng="VE"):
    return c.p.op(eng, lambda e: e.memset(ap, val), reads=[], writes=writes)


def DMA(c, out_ap, in_ap, reads, writes, eng="SQ"):
    return c.p.op(eng, lambda e: e.dma_start(out=out_ap, in_=in_ap),
                  reads=reads, writes=writes, dma=True)


def RECIP(c, out, in0, reads, writes):
    return c.p.op("VE", lambda e: e.reciprocal(out, in0), reads=reads, writes=writes)


def round_ip(c, ap, reads, writes):
    ceff = c.t["Ceff"]
    return TS(c, ap, ap, ceff, ceff, ALU.add, ALU.subtract,
              reads=list(reads) + [c.reg("consts")], writes=writes)


def load_w16(c, name, key, rows, cols, scale=None):
    nch = rows // 128
    st = c.sbuf(name + "_st", [128, cols], F32)
    wt = c.sbuf(name, [128, nch * cols], F16)
    reg = c.reg(name)
    rst = c.reg(name + "_st")
    for ch in range(nch):
        DMA(c, st[:, :], c.dram[key][ch * 128:(ch + 1) * 128, :], reads=[rst], writes=[rst])
        if scale is None:
            CP(c, wt[:, ch * cols:(ch + 1) * cols], st[:, :], reads=[rst], writes=[reg])
        else:
            TS(c, wt[:, ch * cols:(ch + 1) * cols], st[:, :], float(scale), None,
               ALU.mult, ALU.bypass, [rst], [reg])
    c.free(st)
    return wt, reg



def setup_consts(c):
    cst = c.perm("cst", [128, 56], F32)
    c.t["cst"] = cst
    R = c.reg("consts")
    MEMSET(c, cst[:, 0:1], 2.0**-30, writes=[R])
    TS(c, cst[:, 0:1], cst[:, 0:1], 2.0**22, 2.0**22, ALU.add, ALU.subtract, [R], [R])
    TS(c, cst[:, 1:2], cst[:, 0:1], MAGIC_MUL, MAGIC32, ALU.mult, ALU.add, [R], [R])
    c.t["Ceff"] = cst[:, 1:2]
    sqrt96 = float(np.float32(np.sqrt(np.float64(96.0))))
    MEMSET(c, cst[:, 2:3], sqrt96, writes=[R])
    RECIP(c, cst[:, 2:3], cst[:, 2:3], [R], [R])
    TS(c, cst[:, 3:4], cst[:, 2:3], -80.0, None, ALU.mult, ALU.bypass, [R], [R])
    c.t["rc96"] = cst[:, 2:3]
    c.t["negC0"] = cst[:, 3:4]
    MEMSET(c, cst[:, 4:5], 1e-5, writes=[R])
    c.t["epsc"] = cst[:, 4:5]
    pw = c.consts["pwl"]
    coeffs = ([pw["c0"], pw["c1"]] + pw["w"]
              + [pw["eu_w"], pw["eu_j"], pw["el_w"], pw["el_c"], pw["el_j"]])
    for i, (hi, lo) in enumerate(coeffs):
        TS(c, cst[:, 8 + i:9 + i], cst[:, 0:1], float(lo) * 2.0**30, float(hi),
           ALU.mult, ALU.add, [R], [R])
    c.PWL0 = 8

    ones16 = c.perm("ones16", [128, 1], F16)
    MEMSET(c, ones16[:, :], 1.0, writes=[R])
    c.t["ones16"] = ones16
    onesrow = c.perm("onesrow", [1, 128], F16)
    MEMSET(c, onesrow[:, :], 1.0, writes=[R])
    c.t["onesrow"] = onesrow
    onesrow32 = c.perm("onesrow32", [1, 128], F32)
    MEMSET(c, onesrow32[:, :], 1.0, writes=[R])
    c.t["onesrow32"] = onesrow32
    ones32c = c.perm("ones32c", [128, 1], F32)
    MEMSET(c, ones32c[:, :], 1.0, writes=[R])
    c.t["ones32c"] = ones32c
    mbar = c.perm("mbar", [128, 8], F32)
    c.t["mbar"] = mbar
    hp = c.perm("hpones", [128, 2], F16)
    MEMSET(c, hp[:, :], 0.0, writes=[R])
    MEMSET(c, hp[0:64, 0:1], 1.0, writes=[R])
    MEMSET(c, hp[64:128, 1:2], 1.0, writes=[R])
    c.t["hpones"] = hp



def post_ln(c, s32, rR, s16, rD, tag):
    S = c.S
    TB = min(512, S)
    NTB = S // TB
    r16 = c.sbuf(f"lnr{tag}", [128, 8 * S], F16)
    sq = c.sbuf(f"lnq{tag}", [128, 8 * S], F16)
    rT = c.reg(f"lnr{tag}")
    rQ = c.reg(f"lnq{tag}")
    for ch in range(8):
        sl = slice(ch * S, (ch + 1) * S)
        CP(c, r16[:, sl], s32[:, sl], reads=[rR], writes=[rT])
        ACTV(c, sq[:, sl], s32[:, sl], ACT.Square, reads=[rR], writes=[rQ])
    stat = c.sbuf(f"lns{tag}", [1, 4 * TB], F32)
    st16 = c.sbuf(f"ln6{tag}", [1, 2 * TB], F16)
    rS = c.reg(f"lns{tag}")
    rS16 = c.reg(f"ln6{tag}")
    ps1 = c.psum(f"lp1{tag}", [1, TB])
    ps2 = c.psum(f"lp2{tag}", [1, TB])
    rP1 = c.reg(f"lp1{tag}")
    rP2 = c.reg(f"lp2{tag}")
    psb1 = c.psum(f"lb1{tag}", [128, TB])
    psb2 = c.psum(f"lb2{tag}", [128, TB])
    rB1 = c.reg(f"lb1{tag}")
    rB2 = c.reg(f"lb2{tag}")
    ones = c.t["ones16"]
    onesrow = c.t["onesrow"]
    rC = c.reg("consts")
    eps = 1e-5
    m_ = stat[0:1, 0:TB]
    v_ = stat[0:1, TB:2 * TB]
    s_ = stat[0:1, 2 * TB:3 * TB]
    w_ = stat[0:1, 3 * TB:4 * TB]
    for tb in range(NTB):
        for ch in range(8):
            sl = slice(ch * S + tb * TB, ch * S + (tb + 1) * TB)
            MM(c, ps1[0:1, :], ones[:, 0:1], r16[:, sl], ch == 0, ch == 7,
               reads=[rT, rC], writes=[rP1])
            MM(c, ps2[0:1, :], ones[:, 0:1], sq[:, sl], ch == 0, ch == 7,
               reads=[rQ, rC], writes=[rP2])
        TS(c, m_, ps1[0:1, :], 1.0 / D, None, ALU.mult, ALU.bypass, [rP1], [rS])
        TS(c, v_, ps2[0:1, :], 1.0 / D, None, ALU.mult, ALU.bypass, [rP2], [rS])
        TT(c, w_, m_, m_, ALU.mult, [rS], [rS])
        TT(c, v_, v_, w_, ALU.subtract, [rS], [rS])
        ACTV(c, s_, v_, ACT.Sqrt, reads=[rS, c.reg("consts")], writes=[rS], bias=c.t["cst"][0:1, 4:5])
        RECIP(c, w_, s_, [rS], [rS])
        TS(c, v_, v_, eps, None, ALU.add, ALU.bypass, [rS], [rS])
        TT(c, w_, v_, w_, ALU.mult, [rS], [rS])
        TT(c, w_, w_, s_, ALU.add, [rS], [rS])
        TS(c, w_, w_, 0.5, None, ALU.mult, ALU.bypass, [rS], [rS])
        RECIP(c, w_, w_, [rS], [rS])
        CP(c, st16[0:1, 0:TB], m_, reads=[rS], writes=[rS16])
        CP(c, st16[0:1, TB:2 * TB], w_, reads=[rS], writes=[rS16])
        MM(c, psb1[:, :], onesrow[0:1, :], st16[0:1, 0:TB], True, True,
           reads=[rS16, rC], writes=[rB1])
        MM(c, psb2[:, :], onesrow[0:1, :], st16[0:1, TB:2 * TB], True, True,
           reads=[rS16, rC], writes=[rB2])
        for ch in range(8):
            sl = slice(ch * S + tb * TB, ch * S + (tb + 1) * TB)
            TT(c, s32[:, sl], s32[:, sl], psb1[:, :], ALU.subtract, [rR, rB1], [rR])
            TT(c, s32[:, sl], s32[:, sl], psb2[:, :], ALU.mult, [rR, rB2], [rR])
            if s16 is not None:
                ACTV(c, s16[:, sl], s32[:, sl], ACT.Copy, reads=[rR], writes=[rD])
    c.free(r16, sq, stat, st16, psb2, psb1, ps2, ps1)



def l0_attn(c, s16, rS16):
    S = c.S
    TB = min(512, S)
    NTB = S // TB
    NSC = S // 128
    p = c.p
    rC = c.reg("consts")
    rc96 = c.t["rc96"]

    xst = c.sbuf("xst", [128, S], F32)
    rXst = c.reg("xst")
    for ch in range(8):
        DMA(c, xst[:, :], c.dram["xT"][ch * 128:(ch + 1) * 128, :],
            reads=[rXst], writes=[rXst])
        CP(c, s16[:, ch * S:(ch + 1) * S], xst[:, :], reads=[rXst], writes=[rS16])
    c.free(xst)
    xt16 = s16
    rX16 = rS16

    ps = c.psum("l0p", [128, TB])
    rPS = c.reg("l0p")

    wdq, rwdq = load_w16(c, "wdq", "wdqT", D, DLAT)
    wdkv, rwdkv = load_w16(c, "wdkv", "wdkvT", D, DLAT)
    cq = c.sbuf("cq16", [128, 4 * S], F16)
    ckv = c.sbuf("ckv16", [128, 4 * S], F16)
    rCQ = c.reg("cq16")
    rCKV = c.reg("ckv16")
    for dst, wsrc, rw, rdst in ((cq, wdq, rwdq, rCQ), (ckv, wdkv, rwdkv, rCKV)):
        for lc in range(4):
            for tb in range(NTB):
                for ch in range(8):
                    MM(c, ps[:, :], wsrc[:, ch * DLAT + lc * 128: ch * DLAT + (lc + 1) * 128],
                       xt16[:, ch * S + tb * TB: ch * S + (tb + 1) * TB],
                       ch == 0, ch == 7, reads=[rw, rX16], writes=[rPS])
                CP(c, dst[:, lc * S + tb * TB: lc * S + (tb + 1) * TB], ps[:, :],
                   reads=[rPS], writes=[rdst], eng="SE")
    c.free(wdq, wdkv)

    wuq, rwuq = load_w16(c, "wuq", "wuqT", DLAT, H * DH)
    qc = c.sbuf("qc16", [128, 8 * S], F16)
    rQC = c.reg("qc16")
    for oc in range(8):
        for tb in range(NTB):
            for lc in range(4):
                MM(c, ps[:, :], wuq[:, lc * (H * DH) + oc * 128: lc * (H * DH) + (oc + 1) * 128],
                   cq[:, lc * S + tb * TB: lc * S + (tb + 1) * TB],
                   lc == 0, lc == 3, reads=[rwuq, rCQ], writes=[rPS])
            TS(c, qc[:, oc * S + tb * TB: oc * S + (tb + 1) * TB], ps[:, :],
               rc96, None, ALU.mult, ALU.bypass, [rPS, rC], [rQC])
    c.free(wuq)

    wqr, rwqr = load_w16(c, "wqr", "wqrT", DLAT, H * DHR)
    cosb = c.sbuf("cosb", [128, S], F32)
    sinb = c.sbuf("sinb", [128, S], F32)
    rCos = c.reg("cosb")
    DMA(c, cosb[:, :], c.dram["cosb"], reads=[rCos], writes=[rCos])
    DMA(c, sinb[:, :], c.dram["sinb"], reads=[rCos], writes=[rCos])
    scr = c.sbuf("ropes", [128, 2 * S], F32)
    rScr = c.reg("ropes")
    qrev = c.sbuf("qrev", [128, S], F32)
    qrod = c.sbuf("qrod", [128, S], F32)
    rQRE = c.reg("qrev")
    rQRO = c.reg("qrod")
    qrev16 = c.sbuf("qrev16", [128, 2 * S], F16)
    qrod16 = c.sbuf("qrod16", [128, 2 * S], F16)
    rQE16 = c.reg("qrev16")
    rQO16 = c.reg("qrod16")

    def rope(np_, ev, od, ev16, od16, rE, rO, rE16, rO16, cs, sn):
        t1 = scr[0:np_, 0:S]
        t2 = scr[0:np_, S:2 * S]
        TT(c, t1, ev, cs, ALU.mult, [rE, rCos], [rScr])
        TT(c, t2, od, sn, ALU.mult, [rO, rCos], [rScr])
        TT(c, ev16, t1, t2, ALU.subtract, [rScr], [rE16])
        TT(c, t1, ev, sn, ALU.mult, [rE, rCos], [rScr])
        TT(c, t2, od, cs, ALU.mult, [rO, rCos], [rScr])
        TT(c, od16, t1, t2, ALU.add, [rScr], [rO16])

    for oc in range(2):
        for half, dst, rdst in ((0, qrev, rQRE), (1, qrod, rQRO)):
            for tb in range(NTB):
                for lc in range(4):
                    col = lc * (H * DHR) + half * 256 + oc * 128
                    MM(c, ps[:, :], wqr[:, col: col + 128],
                       cq[:, lc * S + tb * TB: lc * S + (tb + 1) * TB],
                       lc == 0, lc == 3, reads=[rwqr, rCQ], writes=[rPS])
                TS(c, dst[:, tb * TB:(tb + 1) * TB], ps[:, :],
                   rc96, None, ALU.mult, ALU.bypass, [rPS, rC], [rdst])
        sl = slice(oc * S, (oc + 1) * S)
        rope(128, qrev[:, :], qrod[:, :], qrev16[:, sl], qrod16[:, sl],
             rQRE, rQRO, rQE16, rQO16, cosb[:, :], sinb[:, :])
    c.free(wqr, qrev, qrod, cq)

    qrt2 = c.sbuf("qrt2", [128, 4 * S], F16)
    rQR2 = c.reg("qrt2")
    for h in range(H):
        cc, rr = divmod(h, 4)
        sch, sr = divmod(h, 8)
        DMA(c, qrt2[rr * 32: rr * 32 + 16, cc * S:(cc + 1) * S],
            qrev16[sr * 16: sr * 16 + 16, sch * S:(sch + 1) * S],
            reads=[rQE16], writes=[rQR2], eng="GE")
        DMA(c, qrt2[rr * 32 + 16: rr * 32 + 32, cc * S:(cc + 1) * S],
            qrod16[sr * 16: sr * 16 + 16, sch * S:(sch + 1) * S],
            reads=[rQO16], writes=[rQR2], eng="GE")
    c.free(qrev16, qrod16)

    wkr, rwkr = load_w16(c, "wkr", "wkrT", D, DHR)
    krev = c.sbuf("krev", [16, S], F32)
    krod = c.sbuf("krod", [16, S], F32)
    rKRE = c.reg("krev")
    rKRO = c.reg("krod")
    psk = c.psum("l0pk", [16, TB])
    rPSK = c.reg("l0pk")
    for half, dst, rdst in ((0, krev, rKRE), (1, krod, rKRO)):
        for tb in range(NTB):
            for ch in range(8):
                MM(c, psk[0:16, :], wkr[:, ch * DHR + half * 16: ch * DHR + half * 16 + 16],
                   xt16[:, ch * S + tb * TB: ch * S + (tb + 1) * TB],
                   ch == 0, ch == 7, reads=[rwkr, rX16], writes=[rPSK])
            CP(c, dst[0:16, tb * TB:(tb + 1) * TB], psk[0:16, :], reads=[rPSK],
               writes=[rdst], eng="SE")
    krev16 = c.sbuf("krev16", [16, S], F16)
    krod16 = c.sbuf("krod16", [16, S], F16)
    rKE16 = c.reg("krev16")
    rKO16 = c.reg("krod16")
    rope(16, krev[0:16, :], krod[0:16, :], krev16[0:16, :], krod16[0:16, :],
         rKRE, rKRO, rKE16, rKO16, cosb[0:16, :], sinb[0:16, :])
    c.free(wkr, krev, krod, scr, cosb, sinb, psk)
    krt2 = c.sbuf("krt2", [128, S], F16)
    rKR2 = c.reg("krt2")
    for b in range(4):
        DMA(c, krt2[b * 32: b * 32 + 16, :], krev16[0:16, :], reads=[rKE16],
            writes=[rKR2], eng="GE")
        DMA(c, krt2[b * 32 + 16: b * 32 + 32, :], krod16[0:16, :], reads=[rKO16],
            writes=[rKR2], eng="GE")
    c.free(krev16, krod16)

    wuk, rwuk = load_w16(c, "wuk", "wukT", DLAT, H * DH)
    kc = c.sbuf("kc16", [128, 8 * S], F16)
    rKC = c.reg("kc16")
    for oc in range(8):
        for tb in range(NTB):
            for lc in range(4):
                MM(c, ps[:, :], wuk[:, lc * (H * DH) + oc * 128: lc * (H * DH) + (oc + 1) * 128],
                   ckv[:, lc * S + tb * TB: lc * S + (tb + 1) * TB],
                   lc == 0, lc == 3, reads=[rwuk, rCKV], writes=[rPS])
            CP(c, kc[:, oc * S + tb * TB: oc * S + (tb + 1) * TB], ps[:, :],
               reads=[rPS], writes=[rKC], eng="SE")
    c.free(wuk)

    wuv, rwuv = load_w16(c, "wuv", "wuvT", DLAT, H * DH)
    m01 = c.sbuf("m01", [128, 128], F16)
    rM01 = c.reg("m01")
    DMA(c, m01[:, :], c.dram["m01c"], reads=[rM01], writes=[rM01])
    vb = [c.sbuf(f"l0v{i}", [128, NSC * 66], F16) for i in range(2)]
    rVb = [c.reg(f"l0v{i}") for i in range(2)]
    for i in range(2):
        MEMSET(c, vb[i][:, 64::66], 1.0, writes=[rVb[i]])
    pexp2 = [c.sbuf(f"l0pe{i}", [128, TB], F16) for i in range(2)]
    rPexp2 = [c.reg(f"l0pe{i}") for i in range(2)]
    dn = c.sbuf("l0dn", [1, 3 * TB], F32)
    rDn = c.reg("l0dn")
    dn16 = c.sbuf("l0d6", [1, TB], F16)
    rDn16 = c.reg("l0d6")
    bcs = c.sbuf("l0bc", [64, TB], F32)
    rBcs = c.reg("l0bcs")
    aodd = c.sbuf("l0ao", [64, TB], F16)
    rAodd = c.reg("l0ao")
    atp0 = c.sbuf("l0at", [64, TB], F32)
    rAtp0 = c.reg("l0at")
    psS2 = [c.psum(f"l0sc{i}", [128, TB]) for i in range(2)]
    rSc2 = [c.reg(f"l0sc{i}") for i in range(2)]
    psPV2 = [c.psum(f"l0pv{i}", [65, TB]) for i in range(2)]
    rPV2 = [c.reg(f"l0pv{i}") for i in range(2)]
    psV2 = [c.psum(f"l0vv{i}", [128, 64]) for i in range(2)]
    rPsV2 = [c.reg(f"l0vv{i}") for i in range(2)]
    psB = c.psum("l0bb", [64, TB])
    rB = c.reg("l0bb")
    negC0 = c.t["negC0"]
    onesrow = c.t["onesrow"]
    explo = float(np.exp(np.float64(-160.0) * (1.0 / np.float64(np.float32(np.sqrt(np.float64(96.0)))))))

    tails = [None]

    def flush_tail():
        if tails[0] is None:
            return
        kch, kro, tb, psPV, rPV = tails[0]
        tails[0] = None
        CP(c, dn[0:1, 0:TB], psPV[64:65, :], reads=[rPV], writes=[rDn])
        ACTV(c, dn[0:1, TB:2 * TB], dn[0:1, 0:TB], ACT.Ln, reads=[rDn], writes=[rDn])
        ACTV(c, dn[0:1, 2 * TB:3 * TB], dn[0:1, TB:2 * TB], ACT.Exp,
             reads=[rDn], writes=[rDn], scale=-0.5)
        CP(c, dn16[0:1, :], dn[0:1, 2 * TB:3 * TB], reads=[rDn], writes=[rDn16])
        MM(c, psB[0:64, :], onesrow[0:1, 0:64], dn16[0:1, :], True, True,
           reads=[rDn16, rC], writes=[rB])
        CP(c, bcs[:, :], psB[0:64, :], reads=[rB], writes=[rBcs], eng="SE")
        TT(c, atp0[0:64, :], psPV[0:64, :], bcs[:, :], ALU.mult, [rPV, rBcs],
           [rAtp0])
        if kro == 0:
            TT(c, s16[0:64, kch * S + tb * TB: kch * S + (tb + 1) * TB],
               atp0[0:64, :], bcs[:, :], ALU.mult, [rAtp0, rBcs], [rS16])
        else:
            TT(c, aodd[0:64, 0:TB], atp0[0:64, :], bcs[:, :], ALU.mult,
               [rAtp0, rBcs], [rAodd])
            DMA(c, s16[64:128, kch * S + tb * TB: kch * S + (tb + 1) * TB],
                aodd[0:64, 0:TB], reads=[rAodd], writes=[rS16], eng="GE")

    for h in range(H):
        vcur, rV = vb[h % 2], rVb[h % 2]
        for tc in range(NSC):
            psV, rPsV = psV2[tc % 2], rPsV2[tc % 2]
            for lc in range(4):
                MM(c, psV[:, :], ckv[:, lc * S + tc * 128: lc * S + (tc + 1) * 128],
                   wuv[:, lc * (H * DH) + h * DH: lc * (H * DH) + h * DH + DH],
                   lc == 0, lc == 3, reads=[rCKV, rwuv], writes=[rPsV])
            CP(c, vcur[:, tc * 66: tc * 66 + 64], psV[:, :], reads=[rPsV],
               writes=[rV], eng="SE")
        kch, kro = divmod(h, 2)
        qrc, qrr = divmod(h, 4)
        base = kro * 64
        rbase = qrr * 32
        for tb in range(NTB):
            nsc = (tb + 1) * TB // 128
            pv = (h * NTB + tb) % 2
            psPV, rPV = psPV2[pv], rPV2[pv]
            def emit_pv(sc_, last):
                pi_ = sc_ % 2
                t0_ = max(tb * TB, sc_ * 128)
                w_ = (tb + 1) * TB - t0_
                MM(c, psPV[0:65, t0_ - tb * TB: t0_ - tb * TB + w_],
                   vcur[:, sc_ * 66: sc_ * 66 + 65], pexp2[pi_][:, 0:w_],
                   sc_ == 0, last, reads=[rV, rPexp2[pi_]], writes=[rPV])
            for sc in range(nsc):
                pi = sc % 2
                psS, rSc = psS2[pi], rSc2[pi]
                pexp, rPexp = pexp2[pi], rPexp2[pi]
                t0 = max(tb * TB, sc * 128)
                w = (tb + 1) * TB - t0
                MM(c, psS[0:128, 0:w],
                   kc[base:base + 64, kch * S + sc * 128: kch * S + (sc + 1) * 128],
                   qc[base:base + 64, kch * S + t0: kch * S + t0 + w],
                   True, False, reads=[rKC, rQC], writes=[rSc], tp=(base, 0))
                MM(c, psS[0:128, 0:w],
                   krt2[rbase:rbase + 32, sc * 128:(sc + 1) * 128],
                   qrt2[rbase:rbase + 32, qrc * S + t0: qrc * S + t0 + w],
                   False, True, reads=[rKR2, rQR2], writes=[rSc], tp=(rbase, 0))
                if sc == 1:
                    flush_tail()
                ACTV(c, pexp[:, 0:w], psS[0:128, 0:w], ACT.Exp,
                     reads=[rSc], writes=[rPexp], bias=negC0)
                TS(c, pexp[:, 0:w], pexp[:, 0:w], 1.0, explo,
                   ALU.min, ALU.max, [rPexp], [rPexp])
                if t0 == sc * 128:
                    TT(c, pexp[:, 0:128], pexp[:, 0:128], m01[:, :], ALU.mult,
                       [rPexp, rM01], [rPexp])
                if sc >= 1:
                    emit_pv(sc - 1, False)
            emit_pv(nsc - 1, True)
            tails[0] = (kch, kro, tb, psPV, rPV)
    flush_tail()
    c.free(pexp2[0], pexp2[1], dn, dn16, bcs, aodd, atp0, vb[0], vb[1], m01,
           qc, kc, qrt2, krt2, ckv, wuv,
           psB, psV2[1], psV2[0], psPV2[1], psPV2[0], psS2[1], psS2[0], ps)


def l0_wo(c, s16, rS16, x1, rX1):
    S = c.S
    TB = min(512, S)
    NTB = S // TB
    wo, rwo = load_w16(c, "wo", "woT", H * DH, D)
    ps = c.psum("wop", [128, TB])
    rPS = c.reg("wop")
    rx = c.sbuf("worx", [128, 2 * TB], F32)
    rRx = [c.reg("worx0"), c.reg("worx1")]
    for dc in range(8):
        for tb in range(NTB):
            bu = (dc * NTB + tb) % 2
            DMA(c, rx[:, bu * TB:(bu + 1) * TB],
                c.dram["xT"][dc * 128:(dc + 1) * 128, tb * TB:(tb + 1) * TB],
                reads=[rRx[bu]], writes=[rRx[bu]])
            for oc in range(8):
                MM(c, ps[:, :], wo[:, oc * D + dc * 128: oc * D + (dc + 1) * 128],
                   s16[:, oc * S + tb * TB: oc * S + (tb + 1) * TB],
                   oc == 0, oc == 7, reads=[rwo, rS16], writes=[rPS])
            TT(c, x1[:, dc * S + tb * TB: dc * S + (tb + 1) * TB], ps[:, :],
               rx[:, bu * TB:(bu + 1) * TB], ALU.add, [rPS, rRx[bu]], [rX1])
    c.free(wo, rx, ps)



def prequant(c):
    p = c.p
    rC = c.reg("consts")
    stats = c.sbuf("fqs", [128, 16], F32)
    rStat = c.reg("fqs")
    st2 = [c.sbuf(f"fqt{i}", [128, FF], F32) for i in range(2)]
    rSt2 = [c.reg(f"fqt{i}") for i in range(2)]
    tn2 = [c.sbuf(f"fqn{i}", [128, FF], F16) for i in range(2)]
    rTn2 = [c.reg(f"fqn{i}") for i in range(2)]
    tmp2 = [c.sbuf(f"fqm{i}", [128, FF], F32) for i in range(2)]
    rTmp2 = [c.reg(f"fqm{i}") for i in range(2)]
    psR = c.psum("fqp", [1, 8])
    rPsR = c.reg("fqp")
    psBC = c.psum("fqb", [128, 4])
    rPsBC = c.reg("fqb")
    specs = [("wgT", "wgc", D, FF), ("wuT", "wuc", D, FF), ("wdT", "wdc", FF, D)]
    for wi, (key, skey, rows, cols) in enumerate(specs):
        nch = rows // 128
        for ch in range(nch):
            st, rSt = st2[ch % 2], rSt2[ch % 2]
            DMA(c, st[:, 0:cols], c.dram[key][ch * 128:(ch + 1) * 128, :],
                reads=[rSt], writes=[rSt])
            cslot = wi * 4 + (ch % 2)
            p.op("VE", lambda e, cslot=cslot, cols=cols, st=st: e.tensor_reduce(
                stats[:, cslot], st[:, 0:cols], AX.X, ALU.add,
                apply_absolute_value=True), reads=[rSt], writes=[rStat])
            if ch == 0:
                CP(c, stats[:, wi * 4 + 2], stats[:, cslot], reads=[rStat], writes=[rStat])
            else:
                TT(c, stats[:, wi * 4 + 2], stats[:, wi * 4 + 2],
                   stats[:, cslot], ALU.add, [rStat], [rStat])
        MM(c, psR[0:1, wi: wi + 1], c.t["ones32c"][:, 0:1],
           stats[:, wi * 4 + 2], True, True, reads=[rStat, rC], writes=[rPsR])
        n = float(rows) * float(cols)
        TS(c, stats[0:1, wi * 4 + 3], psR[0:1, wi: wi + 1], 1.0 / n, 1e-5,
           ALU.mult, ALU.max, [rPsR], [rStat])
    MM(c, psBC[:, 0:3], c.t["onesrow32"][0:1, :], stats[0:1, 3:15:4], True, True,
       reads=[rStat, rC], writes=[rPsBC])
    mbar = c.t["mbar"]
    rMbar = c.reg("mbar")
    CP(c, mbar[:, 0:3], psBC[:, 0:3], reads=[rPsBC], writes=[rMbar])
    TS(c, mbar[:, 3:6], mbar[:, 0:3], 0.5, None, ALU.mult, ALU.bypass, [rMbar], [rMbar])
    TS(c, mbar[:, 7:8], mbar[:, 1:2], 16.0, None, ALU.mult, ALU.bypass, [rMbar], [rMbar])
    c.free(psBC, psR)
    for wi, (key, skey, rows, cols) in enumerate(specs):
        nch = rows // 128
        thr = mbar[:, 3 + wi: 4 + wi]
        TS(c, mbar[:, 6:7], mbar[:, 3 + wi: 4 + wi], -1.0, None, ALU.mult, ALU.bypass,
           [rMbar], [rMbar])
        for ch in range(nch):
            st, rSt = st2[ch % 2], rSt2[ch % 2]
            tn, rTn = tn2[ch % 2], rTn2[ch % 2]
            tmp, rTmp = tmp2[ch % 2], rTmp2[ch % 2]
            DMA(c, st[:, 0:cols], c.dram[key][ch * 128:(ch + 1) * 128, :],
                reads=[rSt], writes=[rSt])
            TS(c, tmp[:, 0:cols], st[:, 0:cols], mbar[:, 6:7], None, ALU.is_lt,
               ALU.bypass, [rSt, rMbar], [rTmp])
            p.op("VE", lambda e, cols=cols, thr=thr, tn=tn, st=st, tmp=tmp:
                e.scalar_tensor_tensor(
                tn[:, 0:cols], st[:, 0:cols], thr, tmp[:, 0:cols],
                ALU.is_gt, ALU.subtract), reads=[rSt, rTmp, rMbar], writes=[rTn])
            DMA(c, c.dram[skey][ch * (cols // 128) * 128:(ch + 1) * (cols // 128) * 128, :]
                .rearrange("(o p) q -> p o q", p=128),
                tn[:, 0:cols], reads=[rTn], writes=[c.dreg[skey]])
    c.free(tmp2[1], tmp2[0], tn2[1], tn2[0], st2[1], st2[0], stats)


def l0_ffn(c, x1, rX1, s16, rS16):
    S = c.S
    TB = min(256, S)
    NTB = S // TB
    NOC = FF // 128
    p = c.p
    rC = c.reg("consts")
    sx = c.sbuf("aqs", [128, 2 * S], F32)
    rSx = c.reg("aqs")
    mbar = c.t["mbar"]
    rMbar = c.reg("mbar")
    psBB = c.psum("aqb", [128, 128])
    rPsBB = c.reg("aqb")

    mx = c.sbuf("aqm", [128, S], F32)
    rMx = c.reg("aqm")
    axs = c.sbuf("aqab", [128, S], F32)
    rAxs = c.reg("aqab")
    ACTV(c, mx[:, :], x1[:, 0:S], ACT.Abs, reads=[rX1], writes=[rMx])
    for ch in range(1, 8):
        ACTV(c, axs[:, :], x1[:, ch * S:(ch + 1) * S], ACT.Abs, reads=[rX1],
             writes=[rAxs])
        TT(c, mx[:, :], mx[:, :], axs[:, :], ALU.max, [rAxs, rMx], [rMx])
    c.free(axs)
    NSC = S // 128
    red = c.sbuf("aqrd", [64, S], F32)
    rRed = c.reg("aqrd")
    cur = 128
    while cur > 1:
        h = cur // 2
        DMA(c, red[0:h, 0:S], mx[h:cur, :], reads=[rMx], writes=[rRed], eng="GE")
        TT(c, mx[0:h, :], mx[0:h, :], red[0:h, 0:S], ALU.max, [rRed, rMx], [rMx])
        cur = h
    if _os.environ.get("KDBG_SXD"):
        DMA(c, c.dram["outT"][256:257, :], mx[0:1, :], reads=[rMx],
            writes=[c.reg("out")])
    c.free(red)
    sxr = c.sbuf("aqxr", [1, 2 * S], F32)
    rSxr = c.reg("aqxr")
    rowx = c.sbuf("aqrw", [1, 2 * S], F16)
    rRowx = c.reg("aqrw")
    TS(c, mx[0:1, :], mx[0:1, :], 1e-5, None, ALU.max, ALU.bypass, [rMx], [rMx])
    RECIP(c, sxr[0:1, 0:S], mx[0:1, :], [rMx], [rSxr])
    TS(c, sxr[0:1, 0:S], sxr[0:1, 0:S], 127.0, None, ALU.mult, ALU.bypass,
       [rSxr], [rSxr])
    CP(c, rowx[0:1, 0:S], sxr[0:1, 0:S], reads=[rSxr], writes=[rRowx])
    CP(c, sxr[0:1, S:2 * S], rowx[0:1, 0:S], reads=[rRowx], writes=[rSxr])
    TT(c, sxr[0:1, S:2 * S], sxr[0:1, 0:S], sxr[0:1, S:2 * S], ALU.subtract,
       [rSxr], [rSxr])
    CP(c, rowx[0:1, S:2 * S], sxr[0:1, S:2 * S], reads=[rSxr], writes=[rRowx])
    onesrow = c.t["onesrow"]
    for tc in range(NSC):
        MM(c, psBB[:, :], onesrow[0:1, :], rowx[0:1, tc * 128:(tc + 1) * 128],
           True, False, reads=[rRowx, rC], writes=[rPsBB])
        MM(c, psBB[:, :], onesrow[0:1, :], rowx[0:1, S + tc * 128: S + (tc + 1) * 128],
           False, True, reads=[rRowx, rC], writes=[rPsBB])
        CP(c, sx[:, tc * 128:(tc + 1) * 128], psBB[:, :], reads=[rPsBB], writes=[rSx])
    RECIP(c, sx[:, S:2 * S], sx[:, 0:S], [rSx], [rSx])
    if _os.environ.get("KDBG_SXD"):
        DMA(c, c.dram["outT"][0:128, :], sx[0:128, 0:S], reads=[rSx],
            writes=[c.reg("out")])
        DMA(c, c.dram["outT"][128:256, :], sx[0:128, S:2 * S], reads=[rSx],
            writes=[c.reg("out")])
    c.free(rowx, sxr, mx)
    qs = c.sbuf("aqc", [128, S], F32)
    rQs = c.reg("aqc")
    xq = s16
    rXq = rS16
    for ch in range(8):
        sl = slice(ch * S, (ch + 1) * S)
        TT(c, qs[:, :], x1[:, sl], sx[:, 0:S], ALU.mult, [rX1, rSx], [rQs])
        round_ip(c, qs[:, :], [rQs], [rQs])
        TS(c, xq[:, sl], qs[:, :], 127.0, -128.0, ALU.min, ALU.max, [rQs], [rXq])
    c.free(qs)

    if getattr(c, "tap", None) == "ffn":
        DMA(c, c.dram["dbgA"][:, 0:8 * S], xq[:, 0:8 * S], reads=[rXq], writes=[c.dreg["dbgA"]])
        DMA(c, c.dram["dbgB"][:, 0:2 * S], sx[0:128, 0:2 * S], reads=[rSx], writes=[c.dreg["dbgB"]])
        DMA(c, c.dram["dbgD"][:, 0:7], mbar[0:128, 0:7], reads=[rMbar], writes=[c.dreg["dbgD"]])
    for ch in range(8):
        DMA(c, c.dram["x1s"][ch * 128:(ch + 1) * 128, :], x1[:, ch * S:(ch + 1) * S],
            reads=[rX1], writes=[c.dreg["x1s"]])

    prod = c.sbuf("ffp", [128, NOC * TB], F32)
    rProd = c.reg("ffp")
    pq = c.sbuf("ffq", [128, NOC * TB], F16)
    rPqL = [c.reg(f"ffq_{i}") for i in range(NOC)]
    gsl = c.sbuf("ffg", [128, TB], F32)
    rGsl = c.reg("ffg")
    wb = [c.sbuf(f"ffw{i}", [128, 8 * 128], F16) for i in range(4)]
    rWb = [c.reg(f"ffw{i}") for i in range(4)]
    dwb = [c.sbuf(f"ffd{i}", [128, NOC * 128], F16) for i in range(2)]
    rDwb = [c.reg(f"ffd{i}") for i in range(2)]
    rmax = c.sbuf("ffm", [128, TB], F32)
    rRmax = c.reg("ffm")
    redr = c.sbuf("ffrd", [64, TB], F32)
    rRedr = c.reg("ffrd")
    fab = c.sbuf("ffab", [128, TB], F32)
    rFab = c.reg("ffab")
    rxr = c.sbuf("ffxr", [1, TB], F32)
    rRxr = c.reg("ffxr")
    rowr = c.sbuf("ffrr", [1, 2 * TB], F16)
    rRowr = c.reg("ffrr")
    rmaxb = c.sbuf("ffmb", [128, TB], F32)
    rRmaxb = c.reg("ffmb")
    s2 = c.sbuf("ff2", [128, 3 * TB], F32)
    rS2 = c.reg("ff2")
    rx1 = c.sbuf("ffr", [128, 2 * TB], F32)
    rRx1 = [c.reg("ffr0"), c.reg("ffr1")]
    x2 = x1
    rX2 = rX1
    psG2 = [c.psum(f"ffpg{i}", [128, TB]) for i in range(2)]
    psU2 = [c.psum(f"ffpu{i}", [128, TB]) for i in range(2)]
    rPsG2 = [c.reg(f"ffpg{i}") for i in range(2)]
    rPsU2 = [c.reg(f"ffpu{i}") for i in range(2)]
    wgc2 = c.dram["wgc"].rearrange("(ch o p) q -> o p ch q", o=NOC, p=128)
    wuc2 = c.dram["wuc"].rearrange("(ch o p) q -> o p ch q", o=NOC, p=128)
    wdc2 = c.dram["wdc"].rearrange("(ch o p) q -> o p ch q", o=8, p=128)

    a_ = s2[:, 0:TB]
    b_ = s2[:, TB:2 * TB]
    f_ = s2[:, 2 * TB:3 * TB]
    for tb in range(NTB):
        tsl = slice(tb * TB, (tb + 1) * TB)
        for oc in range(NOC):
            bG = oc % 2
            bU = 2 + oc % 2
            psG, rPsG = psG2[oc % 2], rPsG2[oc % 2]
            psU, rPsU = psU2[oc % 2], rPsU2[oc % 2]
            DMA(c, wb[bG][:, :], wgc2[oc, :, :, :], reads=[rWb[bG]], writes=[rWb[bG]])
            DMA(c, wb[bU][:, :], wuc2[oc, :, :, :], reads=[rWb[bU]], writes=[rWb[bU]])
            for ch in range(8):
                MM(c, psG[:, :], wb[bG][:, ch * 128:(ch + 1) * 128],
                   xq[:, ch * S + tb * TB: ch * S + (tb + 1) * TB],
                   ch == 0, ch == 7, reads=[rWb[bG], rXq], writes=[rPsG])
            for ch in range(8):
                MM(c, psU[:, :], wb[bU][:, ch * 128:(ch + 1) * 128],
                   xq[:, ch * S + tb * TB: ch * S + (tb + 1) * TB],
                   ch == 0, ch == 7, reads=[rWb[bU], rXq], writes=[rPsU])
            osl = slice(oc * TB, (oc + 1) * TB)
            p.op("VE", lambda e, osl=osl, tb=tb, psG=psG: e.scalar_tensor_tensor(
                prod[:, osl], psG[:, :], mbar[:, 0:1], sx[:, S + tb * TB: S + (tb + 1) * TB],
                ALU.mult, ALU.mult), reads=[rPsG, rMbar, rSx], writes=[rProd])
            ACTV(c, gsl[:, :], prod[:, osl], ACT.Silu, reads=[rProd], writes=[rGsl])
            TT(c, prod[:, osl], gsl[:, :], psU[:, :], ALU.mult, [rGsl, rPsU], [rProd])
            if oc == 0:
                p.op("VE", lambda e, osl=osl: e.scalar_tensor_tensor(
                    rmax[:, :], prod[:, osl], -1.0, prod[:, osl], ALU.mult, ALU.max),
                    reads=[rProd], writes=[rRmax])
            else:
                p.op("VE", lambda e, osl=osl: e.scalar_tensor_tensor(
                    fab[:, :], prod[:, osl], -1.0, prod[:, osl], ALU.mult, ALU.max),
                    reads=[rProd], writes=[rFab])
                TT(c, rmax[:, :], rmax[:, :], fab[:, :], ALU.max,
                   [rFab, rRmax], [rRmax])
        cur = 128
        while cur > 1:
            h = cur // 2
            DMA(c, redr[0:h, 0:TB], rmax[h:cur, :], reads=[rRmax], writes=[rRedr],
                eng="GE")
            TT(c, rmax[0:h, :], rmax[0:h, :], redr[0:h, 0:TB], ALU.max,
               [rRedr, rRmax], [rRmax])
            cur = h
        TS(c, rxr[0:1, 0:TB], rmax[0:1, :], 2.0**-4, None, ALU.mult, ALU.bypass,
           [rRmax], [rRxr])
        CP(c, rowr[0:1, 0:TB], rxr[0:1, 0:TB], reads=[rRxr], writes=[rRowr])
        CP(c, redr[0:1, 0:TB], rowr[0:1, 0:TB], reads=[rRowr], writes=[rRedr])
        TT(c, rxr[0:1, 0:TB], rxr[0:1, 0:TB], redr[0:1, 0:TB], ALU.subtract,
           [rRedr, rRxr], [rRxr])
        CP(c, rowr[0:1, TB:2 * TB], rxr[0:1, 0:TB], reads=[rRxr], writes=[rRowr])
        for j in range(TB // 128):
            MM(c, psBB[:, :], c.t["onesrow"][0:1, :], rowr[0:1, j * 128:(j + 1) * 128],
               True, False, reads=[rRowr, rC], writes=[rPsBB])
            MM(c, psBB[:, :], c.t["onesrow"][0:1, :],
               rowr[0:1, TB + j * 128: TB + (j + 1) * 128],
               False, True, reads=[rRowr, rC], writes=[rPsBB])
            CP(c, rmaxb[:, j * 128:(j + 1) * 128], psBB[:, :], reads=[rPsBB],
               writes=[rRmaxb])
        p.op("VE", lambda e, tb=tb: e.scalar_tensor_tensor(
            a_, rmaxb[:, :], mbar[:, 7:8], sx[:, S + tb * TB: S + (tb + 1) * TB],
            ALU.mult, ALU.mult), reads=[rRmaxb, rMbar, rSx], writes=[rS2])
        TS(c, a_, a_, 1e-5, None, ALU.max, ALU.bypass, [rS2], [rS2])
        RECIP(c, b_, a_, [rS2], [rS2])
        TS(c, b_, b_, 127.0, None, ALU.mult, ALU.bypass, [rS2], [rS2])
        RECIP(c, f_, b_, [rS2], [rS2])
        TS(c, f_, f_, mbar[:, 2:3], None, ALU.mult, ALU.bypass, [rS2, rMbar], [rS2])
        p.op("VE", lambda e, tb=tb: e.scalar_tensor_tensor(
            a_, b_, mbar[:, 1:2], sx[:, S + tb * TB: S + (tb + 1) * TB],
            ALU.mult, ALU.mult), reads=[rS2, rMbar, rSx], writes=[rS2])
        if _os.environ.get("KDBG_SXD"):
            DMA(c, c.dram["outT"][300:301, tsl], rmaxb[0:1, :], reads=[rRmaxb],
                writes=[c.reg("out")])
            DMA(c, c.dram["outT"][301:302, tsl], s2[0:1, TB:2 * TB], reads=[rS2],
                writes=[c.reg("out")])
            DMA(c, c.dram["outT"][302:303, tsl], s2[0:1, 2 * TB:3 * TB], reads=[rS2],
                writes=[c.reg("out")])
            DMA(c, c.dram["outT"][303:304, tsl], s2[0:1, 0:TB], reads=[rS2],
                writes=[c.reg("out")])
        for oc in range(NOC):
            osl = slice(oc * TB, (oc + 1) * TB)
            TT(c, prod[:, osl], prod[:, osl], a_, ALU.mult, [rProd, rS2], [rProd])
            round_ip(c, prod[:, osl], [rProd], [rProd])
            TS(c, pq[:, osl], prod[:, osl], 127.0, -128.0, ALU.min, ALU.max,
               [rProd], [rPqL[oc]])
        if tb == 0 and getattr(c, "tap", None) == "ffn":
            DMA(c, c.dram["dbgC"][:, 0:NOC * TB], pq[0:128, 0:NOC * TB],
                reads=rPqL, writes=[c.dreg["dbgC"]])
        if tb == 0 and getattr(c, "tap", None) in ("ffn", "x2pre"):
            DMA(c, c.dram["dbgE"][:, 0:3 * TB], s2[0:128, 0:3 * TB],
                reads=[rS2], writes=[c.dreg["dbgE"]])
        for dc in range(8):
            bu = dc % 2
            DMA(c, rx1[:, bu * TB:(bu + 1) * TB],
                c.dram["x1s"][dc * 128:(dc + 1) * 128, tsl],
                reads=[rRx1[bu], c.dreg["x1s"]], writes=[rRx1[bu]])
            DMA(c, dwb[bu][:, :], wdc2[dc, :, :, :], reads=[rDwb[bu]], writes=[rDwb[bu]])
            psG, rPsG = psG2[dc % 2], rPsG2[dc % 2]
            for ic in range(NOC):
                MM(c, psG[:, :], dwb[bu][:, ic * 128:(ic + 1) * 128],
                   pq[:, ic * TB:(ic + 1) * TB], ic == 0, ic == NOC - 1,
                   reads=[rDwb[bu], rPqL[ic]], writes=[rPsG])
            xsl = slice(dc * S + tb * TB, dc * S + (tb + 1) * TB)
            TT(c, x2[:, xsl], psG[:, :], f_, ALU.mult, [rPsG, rS2], [rX2])
            TT(c, x2[:, xsl], x2[:, xsl], rx1[:, bu * TB:(bu + 1) * TB], ALU.add,
               [rRx1[bu], rX2], [rX2])
    c.free(prod, pq, gsl, rmax, redr, fab, rxr, rowr, rmaxb, s2, rx1, sx,
           wb[0], wb[1], wb[2], wb[3], dwb[0], dwb[1],
           psU2[1], psU2[0], psG2[1], psG2[0], psBB)



def l1_attn(c, x2, rX2, s16, rS16):
    S = c.S
    TB = min(512, S)
    NTB = S // TB
    NSC = S // 128
    WM = 2 * S - 128
    p = c.p
    rC = c.reg("consts")
    x2_16 = s16
    rX16 = rS16

    ps = c.psum("l1p", [128, TB])
    rPS = c.reg("l1p")

    qt = c.sbuf("l1q", [128, 8 * S], F16)
    rQ = c.reg("l1q")
    wq, rwq = load_w16(c, "wq", "wqT", D, D)
    for oc in range(8):
        for tb in range(NTB):
            for ch in range(8):
                MM(c, ps[:, :], wq[:, ch * D + oc * 128: ch * D + (oc + 1) * 128],
                   x2_16[:, ch * S + tb * TB: ch * S + (tb + 1) * TB],
                   ch == 0, ch == 7, reads=[rwq, rX16], writes=[rPS])
            TS(c, qt[:, oc * S + tb * TB: oc * S + (tb + 1) * TB], ps[:, :],
               0.125, None, ALU.mult, ALU.bypass, [rPS], [rQ])
    c.free(wq)

    kk = [c.sbuf(f"l1k{t}", [128, S], F16) for t in range(2)]
    rKK = [c.reg(f"l1k{t}") for t in range(2)]
    wk, rwk = load_w16(c, "wk", "wkT", D, KVH * DH)
    for t in range(2):
        for tb in range(NTB):
            for ch in range(8):
                MM(c, ps[:, :], wk[:, ch * (KVH * DH) + t * 128: ch * (KVH * DH) + (t + 1) * 128],
                   x2_16[:, ch * S + tb * TB: ch * S + (tb + 1) * TB],
                   ch == 0, ch == 7, reads=[rwk, rX16], writes=[rPS])
            CP(c, kk[t][:, tb * TB:(tb + 1) * TB], ps[:, :], reads=[rPS],
               writes=[rKK[t]], eng="SE")
    c.free(wk)

    vb = [c.sbuf(f"l1v{kv}", [128, NSC * 66], F16) for kv in range(KVH)]
    rVb = [c.reg(f"l1v{kv}") for kv in range(KVH)]
    psV = c.psum("l1vv", [128, 64])
    rPsV = c.reg("l1vv")
    wv, rwv = load_w16(c, "wv", "wvT", D, KVH * DH)
    for kv in range(KVH):
        MEMSET(c, vb[kv][:, 64::66], 1.0, writes=[rVb[kv]])
        for tc in range(NSC):
            for ch in range(8):
                MM(c, psV[:, :], x2_16[:, ch * S + tc * 128: ch * S + (tc + 1) * 128],
                   wv[:, ch * (KVH * DH) + kv * DH: ch * (KVH * DH) + (kv + 1) * DH],
                   ch == 0, ch == 7, reads=[rwv, rX16], writes=[rPsV])
            CP(c, vb[kv][:, tc * 66: tc * 66 + 64], psV[:, :], reads=[rPsV],
               writes=[rVb[kv]], eng="SE")
    c.free(wv, psV)

    sqc = c.sbuf("l1sx", [128, S], F16)
    rSqc = c.reg("l1sx")
    psN = c.psum("l1n", [1, TB])
    rPsN = c.reg("l1n")
    hp = c.t["hpones"]
    kn = c.sbuf("l1kn", [1, 8], F32)
    rKn = c.reg("l1kn")
    for kv in range(KVH):
        t, rb = divmod(kv, 2)
        rb *= 64
        ACTV(c, sqc[rb:rb + 64, :], kk[t][rb:rb + 64, :], ACT.Square,
             reads=[rKK[t]], writes=[rSqc])
        for tb in range(NTB):
            MM(c, psN[0:1, :], hp[rb:rb + 64, rb // 64: rb // 64 + 1],
               sqc[rb:rb + 64, tb * TB:(tb + 1) * TB],
               True, True, reads=[rSqc, rC], writes=[rPsN], tp=(rb, 0))
            slot = kv if tb == 0 else 4 + kv
            p.op("VE", lambda e, slot=slot: e.tensor_reduce(
                kn[0:1, slot: slot + 1], psN[0:1, :], AX.X, ALU.max),
                reads=[rPsN], writes=[rKn])
            if tb > 0:
                TT(c, kn[0:1, kv], kn[0:1, kv], kn[0:1, 4 + kv], ALU.max, [rKn], [rKn])
    ACTV(c, kn[0:1, 0:4], kn[0:1, 0:4], ACT.Sqrt, reads=[rKn], writes=[rKn],
         scale=2.0**-4)
    TS(c, kn[0:1, 4:8], kn[0:1, 0:4], -16.0, None, ALU.mult, ALU.bypass, [rKn], [rKn])
    qn = c.sbuf("l1qn", [1, S], F32)
    rQn = c.reg("l1qn")
    ncr = c.sbuf("l1cr", [1, S], F16)
    rNcr = c.reg("l1cr")
    ebias = c.sbuf("l1eb", [128, 1], F32)
    rEb = c.reg("l1eb")
    MEMSET(c, ebias[:, :], 10.0, writes=[rEb])

    slopes = c.consts["slopes"]
    escr = c.sbuf("l1es", [128, WM], F32)
    rEsc = c.reg("l1es")
    DMA(c, escr[:, :], c.dram["relc"], reads=[rEsc], writes=[rEsc])
    em = c.sbuf("l1e", [128, WM], F16)
    rEm = c.reg("l1e")
    GW = min(512, S)
    NTBG = GW // TB if TB < 512 else 1
    NTG = S // GW
    psS2 = [c.psum(f"l1sc{i}", [128, GW]) for i in range(3)]
    rSc2 = [c.reg(f"l1sc{i}") for i in range(3)]
    psPV2 = [c.psum(f"l1pv{i}", [65, GW]) for i in range(2)]
    rPV2 = [c.reg(f"l1pv{i}") for i in range(2)]
    psB = c.psum("l1bb", [64, GW])
    rB = c.reg("l1bb")
    pexp2 = [c.sbuf(f"l1pe{i}", [128, GW], F16) for i in range(3)]
    rPexp2 = [c.reg(f"l1pe{i}") for i in range(3)]
    dn = c.sbuf("l1dn", [1, 3 * TB], F32)
    rDn = c.reg("l1dn")
    dn16 = c.sbuf("l1d6", [1, TB], F16)
    rDn16 = c.reg("l1d6")
    bcs = c.sbuf("l1bc", [64, TB], F32)
    rBcs = c.reg("l1bcs")
    aodd = c.sbuf("l1ao", [64, TB], F16)
    rAodd = c.reg("l1ao")
    atp = c.sbuf("l1at", [64, TB], F32)
    rAtp = c.reg("l1at")
    onesrow = c.t["onesrow"]
    tails = [None]

    def flush_tail():
        if tails[0] is None:
            return
        ch, half, tb, psPV, rPV = tails[0]
        tails[0] = None
        CP(c, dn[0:1, 0:GW], psPV[64:65, :], reads=[rPV], writes=[rDn])
        ACTV(c, dn[0:1, GW:2 * GW], dn[0:1, 0:GW], ACT.Ln, reads=[rDn],
             writes=[rDn], scale=2.0**-10)
        ACTV(c, dn[0:1, 2 * GW:3 * GW], dn[0:1, GW:2 * GW], ACT.Exp,
             reads=[rDn], writes=[rDn], scale=-0.5)
        TS(c, dn16[0:1, 0:GW], dn[0:1, 2 * GW:3 * GW], 2.0**-5, None,
           ALU.mult, ALU.bypass, [rDn], [rDn16])
        MM(c, psB[0:64, :], onesrow[0:1, 0:64], dn16[0:1, 0:GW], True, True,
           reads=[rDn16, rC], writes=[rB])
        CP(c, bcs[:, 0:GW], psB[0:64, :], reads=[rB], writes=[rBcs], eng="SE")
        TT(c, atp[0:64, 0:GW], psPV[0:64, :], bcs[:, 0:GW], ALU.mult,
           [rPV, rBcs], [rAtp])
        if half == 0:
            TT(c, s16[0:64, ch * S + tb * GW: ch * S + (tb + 1) * GW],
               atp[0:64, 0:GW], bcs[:, 0:GW], ALU.mult, [rAtp, rBcs], [rS16])
        else:
            TT(c, aodd[0:64, 0:GW], atp[0:64, 0:GW], bcs[:, 0:GW], ALU.mult,
               [rAtp, rBcs], [rAodd])
            DMA(c, s16[64:128, ch * S + tb * GW: ch * S + (tb + 1) * GW],
                aodd[0:64, 0:GW], reads=[rAodd], writes=[rS16], eng="GE")

    for h in range(H):
        kv = h % KVH
        ch, half = divmod(h, 2)
        base = half * 64
        ACTV(c, sqc[base:base + 64, :], qt[base:base + 64, ch * S:(ch + 1) * S],
             ACT.Square, reads=[rQ], writes=[rSqc])
        for tb in range(NTB):
            MM(c, psN[0:1, :], hp[base:base + 64, half: half + 1],
               sqc[base:base + 64, tb * TB:(tb + 1) * TB],
               True, True, reads=[rSqc, rC], writes=[rPsN], tp=(base, 0))
            ACTV(c, qn[0:1, tb * TB:(tb + 1) * TB], psN[0:1, :], ACT.Sqrt,
                 reads=[rPsN], writes=[rQn], scale=2.0**-4)
        TS(c, ncr[0:1, :], qn[0:1, 0:S], kn[0:1, 4 + kv], None, ALU.mult,
           ALU.bypass, [rQn, rKn], [rNcr])
        ACTV(c, em[:, :], escr[:, :], ACT.Exp, reads=[rEsc], writes=[rEm],
             scale=float(slopes[h]) * 0.125)
        for tg in range(NTG):
            tb = tg
            pv = (h * NTG + tg) % 2
            psPV, rPV = psPV2[pv], rPV2[pv]
            for sc in range(NSC):
                pi = sc % 3
                psS, rSc = psS2[pi], rSc2[pi]
                pexp, rPexp = pexp2[pi], rPexp2[pi]
                MM(c, psS[:, :],
                   onesrow[0:1, :], ncr[0:1, tb * GW:(tb + 1) * GW],
                   True, False, reads=[rNcr, rC], writes=[rSc])
                MM(c, psS[:, :],
                   kk[kv // 2][base: base + 64, sc * 128:(sc + 1) * 128],
                   qt[base: base + 64, ch * S + tb * GW: ch * S + (tb + 1) * GW],
                   False, True, reads=[rKK[kv // 2], rQ], writes=[rSc], tp=(base, 0))
                ACTV(c, pexp[:, :], psS[:, :], ACT.Exp, reads=[rSc, rEb], writes=[rPexp],
                     bias=ebias[:, 0:1])
                off = tg * GW - sc * 128 + (S - 128)
                TT(c, pexp[:, :], pexp[:, :], em[:, off: off + GW], ALU.mult,
                   [rPexp, rEm], [rPexp])
                if sc == 1:
                    flush_tail()
                if sc >= 1:
                    pp = (sc - 1) % 3
                    MM(c, psPV[0:65, :], vb[kv][:, (sc - 1) * 66: (sc - 1) * 66 + 65],
                       pexp2[pp][:, :], sc - 1 == 0, False,
                       reads=[rVb[kv], rPexp2[pp]], writes=[rPV])
            pp = (NSC - 1) % 3
            MM(c, psPV[0:65, :], vb[kv][:, (NSC - 1) * 66: (NSC - 1) * 66 + 65],
               pexp2[pp][:, :], NSC == 1, True,
               reads=[rVb[kv], rPexp2[pp]], writes=[rPV])
            tails[0] = (ch, half, tb, psPV, rPV)
    flush_tail()
    c.free(pexp2[0], pexp2[1], pexp2[2], dn, dn16, bcs, aodd, atp, em, escr, ebias, ncr,
           qn, kn, sqc, qt, kk[0], kk[1], vb[0], vb[1], vb[2], vb[3],
           psB, psPV2[1], psPV2[0], psS2[2], psS2[1], psS2[0], psN)

    wp, rwp = load_w16(c, "wp", "wpT", D, D)
    bpj = c.sbuf("l1bp", [128, 8], F32)
    rBpj = c.reg("l1bp")
    DMA(c, bpj[:, :], c.dram["bprojT"], reads=[rBpj], writes=[rBpj])
    for dc in range(8):
        for tb in range(NTB):
            for oc in range(8):
                MM(c, ps[:, :], wp[:, oc * D + dc * 128: oc * D + (dc + 1) * 128],
                   s16[:, oc * S + tb * TB: oc * S + (tb + 1) * TB],
                   oc == 0, oc == 7, reads=[rwp, rS16], writes=[rPS])
            p.op("VE", lambda e, dc=dc, tb=tb: e.scalar_tensor_tensor(
                x2[:, dc * S + tb * TB: dc * S + (tb + 1) * TB], ps[:, :],
                bpj[:, dc], x2[:, dc * S + tb * TB: dc * S + (tb + 1) * TB],
                ALU.add, ALU.add), reads=[rPS, rBpj, rX2], writes=[rX2])
    c.free(wp, bpj, ps)



def precache_w12(c):
    rC = c.reg("consts")
    st2 = [c.sbuf(f"f1s{i}", [128, FF], F32) for i in range(2)]
    rSt2 = [c.reg(f"f1s{i}") for i in range(2)]
    t162 = [c.sbuf(f"f1t{i}", [128, FF], F16) for i in range(2)]
    rT162 = [c.reg(f"f1t{i}") for i in range(2)]
    for key, skey, rows, cols in (("w1T", "w1c", D, FF), ("w2T", "w2c", FF, D)):
        for ch in range(rows // 128):
            st, rSt = st2[ch % 2], rSt2[ch % 2]
            t16, rT16 = t162[ch % 2], rT162[ch % 2]
            DMA(c, st[:, 0:cols], c.dram[key][ch * 128:(ch + 1) * 128, :],
                reads=[rSt], writes=[rSt])
            CP(c, t16[:, 0:cols], st[:, 0:cols], reads=[rSt], writes=[rT16], eng="SE")
            DMA(c, c.dram[skey][ch * (cols // 128) * 128:(ch + 1) * (cols // 128) * 128, :]
                .rearrange("(o p) q -> p o q", p=128),
                t16[:, 0:cols], reads=[rT16], writes=[c.dreg[skey]])
    c.free(t162[1], t162[0], st2[1], st2[0])


def l1_ffn(c, x3, rX3, s16, rS16):
    S = c.S
    TB = min(512, S)
    NTB = S // TB
    NOC = FF // 128
    NQ = 8
    QOC = NOC // NQ
    p = c.p
    rC = c.reg("consts")
    cst = c.t["cst"]
    P0 = c.PWL0
    pw = c.consts["pwl"]
    x3_16 = s16
    rX16 = rS16

    kxb = c.sbuf("f1kx", [128, 31], F32)
    rKxb = c.reg("f1kx")
    for k in range(31):
        MEMSET(c, kxb[:, k:k + 1], -float(pw["kx"][k]), writes=[rKxb])
        w1c2 = c.dram["w1c"].rearrange("(ch o p) q -> o p ch q", o=NOC, p=128)
    w2c2 = c.dram["w2c"].rearrange("(ch o p) q -> o p ch q", o=8, p=128)

    b1 = c.sbuf("f1b1", [128, 32], F32)
    rB1 = c.reg("f1b1")
    DMA(c, b1[:, :], c.dram["b1T"], reads=[rB1], writes=[rB1])
    b2 = c.sbuf("f1b2", [128, 8], F32)
    rB2 = c.reg("f1b2")
    DMA(c, b2[:, :], c.dram["b2T"], reads=[rB2], writes=[rB2])

    for ch in range(8):
        DMA(c, c.dram["x3s"][ch * 128:(ch + 1) * 128, :], x3[:, ch * S:(ch + 1) * S],
            reads=[rX3], writes=[c.dreg["x3s"]])

    QW = QOC * TB
    vq = c.sbuf("f1v", [128, QW], F32)
    rV = c.reg("f1v")
    acc = c.sbuf("f1a", [128, QW], F32)
    rA = c.reg("f1a")
    tk16 = [c.sbuf(f"f1k6{i}", [128, QW], F16) for i in range(2)]
    rTk16 = [c.reg(f"f1k6{i}") for i in range(2)]
    tkr = c.sbuf("f1ke", [128, QW], F32)
    rK = c.reg("f1ke")
    idt = c.sbuf("f1id", [128, 128], F16)
    rIdt = c.reg("f1id")
    DMA(c, idt[:, :], c.dram["identc"], reads=[rIdt], writes=[rIdt])
    wdg = c.sbuf("f1wd", [128, 31 * 128], F16)
    rWdg = c.reg("f1wd")
    for k in range(31):
        TS(c, wdg[:, k * 128:(k + 1) * 128], idt[:, :], cst[:, P0 + 2 + k: P0 + 3 + k],
           None, ALU.mult, ALU.bypass, [rIdt, rC], [rWdg])
    psA = c.psum("f1pa", [128, QW])
    rPsA = c.reg("f1pa")
    hq = c.sbuf("f1h", [128, NOC * TB], F16)
    rHq = c.reg("f1h")
    wb = [c.sbuf(f"f1w{i}", [128, 8 * 128], F16) for i in range(2)]
    rWb = [c.reg(f"f1w{i}") for i in range(2)]
    w2b = [c.sbuf(f"f1u{i}", [128, NOC * 128], F16) for i in range(2)]
    rW2b = [c.reg(f"f1u{i}") for i in range(2)]
    rx3 = c.sbuf("f1r", [128, 2 * TB], F32)
    rRx3 = [c.reg("f1r0"), c.reg("f1r1")]
    x4 = x3
    rX4 = rX3
    psH2 = [c.psum(f"f1p{i}", [128, TB]) for i in range(2)]
    rPsH2 = [c.reg(f"f1p{i}") for i in range(2)]

    for tb in range(NTB):
        tsl = slice(tb * TB, (tb + 1) * TB)
        for qh in range(NQ):
            for j in range(QOC):
                oc = qh * QOC + j
                b = oc % 2
                psH, rPsH = psH2[oc % 2], rPsH2[oc % 2]
                DMA(c, wb[b][:, :], w1c2[oc, :, :, :], reads=[rWb[b]], writes=[rWb[b]])
                for ch in range(8):
                    MM(c, psH[:, :], wb[b][:, ch * 128:(ch + 1) * 128],
                       x3_16[:, ch * S + tb * TB: ch * S + (tb + 1) * TB],
                       ch == 0, ch == 7, reads=[rWb[b], rX16], writes=[rPsH])
                TS(c, vq[:, j * TB:(j + 1) * TB], psH[:, :], b1[:, oc], 256.0,
                   ALU.add, ALU.mult, [rPsH, rB1], [rV])
            round_ip(c, vq[:, :], [rV], [rV])
            TS(c, acc[:, :], vq[:, :], cst[:, P0 + 1: P0 + 2], cst[:, P0: P0 + 1],
               ALU.mult, ALU.add, [rV, rC], [rA])
            for k in range(31):
                tk, rTk = tk16[k % 2], rTk16[k % 2]
                ACTV(c, tk[:, :], vq[:, :], ACT.Relu, reads=[rV, rC], writes=[rTk],
                     bias=kxb[:, k:k + 1])
                for blk in range(QW // 512):
                    MM(c, psA[:, blk * 512:(blk + 1) * 512],
                       wdg[:, k * 128:(k + 1) * 128],
                       tk[:, blk * 512:(blk + 1) * 512], k == 0, k == 30,
                       reads=[rWdg, rTk], writes=[rPsA])
            TT(c, acc[:, :], acc[:, :], psA[:, :], ALU.add, [rPsA, rA], [rA])
            TS(c, tkr[:, :], vq[:, :], 1024.0, -1024.0, ALU.max, ALU.add, [rV], [rK])
            TS(c, tkr[:, :], tkr[:, :], cst[:, P0 + 33: P0 + 34], None, ALU.mult,
               ALU.bypass, [rK, rC], [rK])
            TT(c, acc[:, :], acc[:, :], tkr[:, :], ALU.add, [rK, rA], [rA])
            TS(c, tkr[:, :], vq[:, :], 1024.0, cst[:, P0 + 34: P0 + 35], ALU.is_gt,
               ALU.mult, [rV, rC], [rK])
            TT(c, acc[:, :], acc[:, :], tkr[:, :], ALU.add, [rK, rA], [rA])
            TS(c, tkr[:, :], vq[:, :], -1024.0, 1024.0, ALU.min, ALU.add, [rV], [rK])
            TS(c, tkr[:, :], tkr[:, :], cst[:, P0 + 35: P0 + 36], None, ALU.mult,
               ALU.bypass, [rK, rC], [rK])
            TT(c, acc[:, :], acc[:, :], tkr[:, :], ALU.add, [rK, rA], [rA])
            TS(c, tkr[:, :], vq[:, :], -1024.0, cst[:, P0 + 37: P0 + 38], ALU.is_lt,
               ALU.mult, [rV, rC], [rK])
            TT(c, acc[:, :], acc[:, :], tkr[:, :], ALU.add, [rK, rA], [rA])
            TS(c, acc[:, :], acc[:, :], 256.0, None, ALU.mult, ALU.bypass, [rA], [rA])
            round_ip(c, acc[:, :], [rA], [rA])
            TS(c, hq[:, qh * QW:(qh + 1) * QW], acc[:, :], 2.0**-8, None, ALU.mult,
               ALU.bypass, [rA], [rHq])
        for dc in range(8):
            bu = dc % 2
            DMA(c, rx3[:, bu * TB:(bu + 1) * TB],
                c.dram["x3s"][dc * 128:(dc + 1) * 128, tsl],
                reads=[rRx3[bu], c.dreg["x3s"]], writes=[rRx3[bu]])
            DMA(c, w2b[bu][:, :], w2c2[dc, :, :, :], reads=[rW2b[bu]], writes=[rW2b[bu]])
            psH, rPsH = psH2[dc % 2], rPsH2[dc % 2]
            for ic in range(NOC):
                MM(c, psH[:, :], w2b[bu][:, ic * 128:(ic + 1) * 128],
                   hq[:, ic * TB:(ic + 1) * TB], ic == 0, ic == NOC - 1,
                   reads=[rW2b[bu], rHq], writes=[rPsH])
            xsl = slice(dc * S + tb * TB, dc * S + (tb + 1) * TB)
            TS(c, x4[:, xsl], psH[:, :], b2[:, dc], 256.0,
               ALU.add, ALU.mult, [rPsH, rB2], [rX4])
            round_ip(c, x4[:, xsl], [rX4], [rX4])
            p.op("VE", lambda e, xsl=xsl, bu=bu: e.scalar_tensor_tensor(
                x4[:, xsl], x4[:, xsl], 2.0**-8, rx3[:, bu * TB:(bu + 1) * TB],
                ALU.mult, ALU.add), reads=[rRx3[bu], rX4], writes=[rX4])
    c.free(vq, acc, tkr, tk16[0], tk16[1], idt, wdg, hq,
           wb[0], wb[1], w2b[0], w2b[1], rx3, b1, b2, kxb,
           psA, psH2[1], psH2[0])



def _flex(c, S, n32, n16):
    if S < 2048:
        n32 = max(n32, 24576)
        n16 = max(n16, 49152)
    c.open_flex(n32=int(n32), n16=int(n16))


def build_program(S=2048, tap=None):
    consts = _host_consts(S)
    nc = bass.Bass("TRN2", target_bir_lowering=False)
    c = Ctx(nc, S, consts)

    def din(name, shape, dtype=F32):
        c.dram[name] = nc.dram_tensor(name, list(shape), dtype, kind="ExternalInput").ap()
        c.dreg[name] = c.reg("d_" + name)

    def dscr(name, shape, dtype):
        c.dram[name] = nc.dram_tensor(name, list(shape), dtype, kind="Internal").ap()
        c.dreg[name] = c.reg("d_" + name)

    din("xT", (D, S))
    out_ap = nc.dram_tensor("outT", [D, S], F32, kind="ExternalOutput").ap()
    c.dram["outT"] = out_ap
    din("wdqT", (D, DLAT)); din("wuqT", (DLAT, H * DH)); din("wqrT", (DLAT, H * DHR))
    din("wdkvT", (D, DLAT)); din("wukT", (DLAT, H * DH)); din("wuvT", (DLAT, H * DH))
    din("wkrT", (D, DHR)); din("woT", (H * DH, D))
    din("wgT", (D, FF)); din("wuT", (D, FF)); din("wdT", (FF, D))
    din("wqT", (D, D)); din("wkT", (D, KVH * DH)); din("wvT", (D, KVH * DH))
    din("wpT", (D, D)); din("bprojT", (128, 8))
    din("w1T", (D, FF)); din("b1T", (128, 32)); din("w2T", (FF, D)); din("b2T", (128, 8))
    din("cosb", (128, S)); din("sinb", (128, S))
    din("m01c", (128, 128), F16)
    din("relc", (128, 2 * S - 128))
    din("identc", (128, 128), F16)
    din("sel2", (2, 256))
    dscr("wgc", (8 * 32 * 128, 128), F16)
    dscr("wuc", (8 * 32 * 128, 128), F16)
    dscr("wdc", (32 * 8 * 128, 128), F16)
    dscr("w1c", (8 * 32 * 128, 128), F16)
    dscr("w2c", (32 * 8 * 128, 128), F16)
    dscr("x1s", (D, S), F32)
    dscr("dbgA", (128, 8 * S), F16)
    dscr("dbgB", (128, 2 * S), F32)
    dscr("dbgC", (128, 64 * 256), F16)
    dscr("dbgD", (128, 64), F32)
    dscr("dbgE", (128, 2048), F32)
    dscr("dbgX", (D, S), F32)
    dscr("x3s", (D, S), F32)

    setup_consts(c)
    _kph = _os.environ.get("KPH")
    _phset = set(_kph.split(",")) if _kph else None
    _ph = lambda nm: _phset is None or nm in _phset
    s16 = c.perm("stream16", [128, 8 * S], F16)
    rS16 = c.reg("S16")

    if _ph("A"):
        _flex(c, S, 4 * S + 5888, 29 * S + 1024)
        l0_attn(c, s16, rS16)
        c.close_flex()

    s32 = c.perm("stream32", [128, 8 * S], F32)
    rS32 = c.reg("S32")

    if _ph("B"):
        _flex(c, S, 2 * min(512, S) + 4 * FF + 1280, 4 * S + 2 * FF + 2048)
        l0_wo(c, s16, rS16, s32, rS32)
        prequant(c)
        precache_w12(c)
        c.close_flex()

    if _ph("L1"):
        _flex(c, S, 4 * min(512, S) + 512, 16 * S + 4 * min(512, S) + 1024)
        post_ln(c, s32, rS32, None, None, "a")
        c.close_flex()

    c.tap = tap
    if _ph("C"):
        _flex(c, S, 2 * S + 2 * FF + 3072, 64 * min(256, S) + FF + 2048)
        l0_ffn(c, s32, rS32, s16, rS16)
        c.close_flex()

    if tap == "x2pre":
        for ch in range(8):
            DMA(c, c.dram["dbgX"][ch * 128:(ch + 1) * 128, :], s32[:, ch * S:(ch + 1) * S],
                reads=[rS32], writes=[c.dreg["dbgX"]])
    if _ph("L2"):
        _flex(c, S, 4 * min(512, S) + 512, 16 * S + 4 * min(512, S) + 1024)
        post_ln(c, s32, rS32, s16, rS16, "b")
        c.close_flex()
    if tap == "x2":
        for ch in range(8):
            DMA(c, c.dram["dbgX"][ch * 128:(ch + 1) * 128, :], s32[:, ch * S:(ch + 1) * S],
                reads=[rS32], writes=[c.dreg["dbgX"]])

    if _ph("D"):
        _flex(c, S, 2 * S + 4608, 17 * S + 1024)
        l1_attn(c, s32, rS32, s16, rS16)
        c.close_flex()

    if _ph("L3"):
        _flex(c, S, 4 * min(512, S) + 512, 16 * S + 4 * min(512, S) + 1024)
        post_ln(c, s32, rS32, s16, rS16, "c")
        c.close_flex()

    if _ph("E"):
        _flex(c, S, 8 * min(512, S) + FF + 1536, 38 * min(512, S) + FF + 12288)
        l1_ffn(c, s32, rS32, s16, rS16)
        c.close_flex()

    if _ph("L4"):
        _flex(c, S, 4 * min(512, S) + 512, 16 * S + 4 * min(512, S) + 1024)
        post_ln(c, s32, rS32, None, None, "d")
        c.close_flex()

    rout = c.reg("out")
    if _ph("OUT") or _phset is None:
        for ch in range(8):
            DMA(c, out_ap[ch * 128:(ch + 1) * 128, :], s32[:, ch * S:(ch + 1) * S],
                reads=[rS32], writes=[rout])
    c.p.op("SQ", lambda e: e.nop(), reads=[rout], writes=[])

    c.p.check_deadlock()
    sems = {}
    cms = []
    for e in Prog.SEMS:
        cm = nc.semaphore(f"sem_{e}")
        sems[e] = cm.__enter__()
        cms.append(cm)
    bcm = nc.Block()
    block = bcm.__enter__()
    c.p.emit(block, sems)
    bcm.__exit__(None, None, None)
    for cm in cms:
        cm.__exit__(None, None, None)
    return nc


def _prep_inputs(S, consts, **inp):
    T = lambda a: np.ascontiguousarray(np.asarray(a, dtype=np.float32).T)
    wqrT = T(inp["L0_attn_W_QR"])
    perm = [h * 32 + 2 * f + half for half in range(2) for h in range(H) for f in range(16)]
    wqrT = np.ascontiguousarray(wqrT[:, perm])
    wkrT = T(inp["L0_attn_W_KR"])
    kperm = [2 * f for f in range(16)] + [2 * f + 1 for f in range(16)]
    wkrT = np.ascontiguousarray(wkrT[:, kperm])
    wkv = np.asarray(inp["L1_attn_Wkv"], dtype=np.float32)
    wkT = T(wkv[0:256, :])
    base = {
        "wdqT": T(inp["L0_attn_W_DQ"]), "wuqT": T(inp["L0_attn_W_UQ"]),
        "wqrT": wqrT, "wdkvT": T(inp["L0_attn_W_DKV"]),
        "wukT": T(inp["L0_attn_W_UK"]), "wuvT": T(inp["L0_attn_W_UV"]),
        "wkrT": wkrT, "woT": T(inp["L0_attn_W_O"]),
        "wgT": T(inp["L0_ffn_w_gate"]), "wuT": T(inp["L0_ffn_w_up"]),
        "wdT": T(inp["L0_ffn_w_down"]),
        "wqT": T(inp["L1_attn_Wq"]), "wkT": np.ascontiguousarray(wkT),
        "wvT": T(wkv[256:512, :]), "wpT": T(inp["L1_attn_Wproj"]),
        "bprojT": np.ascontiguousarray(
            np.asarray(inp["L1_attn_bproj"], dtype=np.float32).reshape(8, 128).T),
        "w1T": np.ascontiguousarray(np.asarray(inp["L1_ffn_w1"], dtype=np.float32)),
        "b1T": np.ascontiguousarray(
            np.asarray(inp["L1_ffn_b1"], dtype=np.float32).reshape(32, 128).T),
        "w2T": np.ascontiguousarray(np.asarray(inp["L1_ffn_w2"], dtype=np.float32)),
        "b2T": np.ascontiguousarray(
            np.asarray(inp["L1_ffn_b2"], dtype=np.float32).reshape(8, 128).T),
        "cosb": consts["cosb"], "sinb": consts["sinb"], "m01c": consts["m01"],
        "relc": consts["relc"], "identc": consts["ident"],
        "sel2": np.concatenate([np.concatenate([np.ones((1, 128), np.float32), np.zeros((1, 128), np.float32)], axis=1),
                                np.concatenate([np.zeros((1, 128), np.float32), np.ones((1, 128), np.float32)], axis=1)], axis=0),
    }
    x = np.asarray(inp["L0_attn_x"], dtype=np.float32)
    return [dict(base, xT=np.ascontiguousarray(x[b].T)) for b in range(x.shape[0])]


def kernel(**inputs):
    S = int(inputs["L0_attn_x"].shape[1])
    consts = _host_consts(S)
    in_maps = _prep_inputs(S, consts, **inputs)
    nc = build_program(S)
    res = run_bass_kernel_spmd(nc, in_maps, core_ids=list(range(len(in_maps))))
    outs = [np.ascontiguousarray(np.asarray(r["outT"]).T) for r in res.results]
    return np.stack(outs, axis=0).astype(np.float32)


# version 14
# speedup vs baseline: 1.2366x; 1.0119x over previous
import os as _os
import numpy as np

import bass_rust
import concourse.bass as bass
import concourse.mybir as mybir
from concourse.bass_utils import run_bass_kernel_spmd
from concourse import library_config

F32 = mybir.dt.float32
F16 = mybir.dt.float16
I32 = mybir.dt.int32
ALU = mybir.AluOpType
ACT = mybir.ActivationFunctionType
RED = bass_rust.ReduceOp
AX = mybir.AxisListType

D = 1024
H = 16
DH = 64
DHR = 32
DLAT = 512
KVH = 4
FF = 4096

MAGIC32 = float(1.5 * 2.0**23)
MAGIC_MUL = float((1.5 * 2.0**52 - 1.5 * 2.0**23) * 2.0**30)



def _host_consts(S):
    import jax
    import jax.numpy as jnp
    cpu = jax.devices("cpu")[0]
    with jax.default_device(cpu):
        d = DHR
        inv_freq = jnp.asarray(10000.0) ** (-jnp.arange(0, d, 2, dtype=jnp.float32) / d)
        pos = jnp.arange(S)
        ang = pos.astype(jnp.float32)[:, None] * inv_freq[None, :]
        cos = np.asarray(jnp.cos(ang), dtype=np.float32)
        sin = np.asarray(jnp.sin(ang), dtype=np.float32)
        pwl_x = jnp.linspace(-4.0, 4.0, 33, dtype=jnp.float32)
        pwl_y = 0.5 * pwl_x * (1.0 + jax.lax.erf(pwl_x / jnp.sqrt(jnp.float32(2.0))))
        X = np.asarray(pwl_x, dtype=np.float32)
        Y = np.asarray(pwl_y, dtype=np.float32)

    cosb = np.ascontiguousarray(np.tile(cos.T, (8, 1)).astype(np.float32))
    sinb = np.ascontiguousarray(np.tile(sin.T, (8, 1)).astype(np.float32))
    m01 = np.triu(np.ones((128, 128), dtype=np.float16))

    start = 2.0 ** (-2.0 ** (-(np.log2(H) - 3.0)))
    slopes = (start ** np.arange(1, H + 1)).astype(np.float32)

    WM = 2 * S - 128
    relc = np.minimum((S - 128) - np.arange(WM, dtype=np.float64)[None, :]
                      + np.arange(128, dtype=np.float64)[:, None], 0.0).astype(np.float32)
    ident = np.eye(128, dtype=np.float16)

    Y64 = Y.astype(np.float64)
    m = (Y64[1:] - Y64[:-1]) * 4.0
    mv = m / 256.0
    w = mv[1:] - mv[:-1]
    kxv = (X[1:32].astype(np.float64) * 256.0)
    c1 = mv[0]
    c0 = Y64[0] + 1024.0 * mv[0] - float(np.sum(w * kxv))

    def hilo(v):
        hi = np.float32(v)
        lo = np.float32(np.float64(v) - np.float64(hi))
        return float(hi), float(lo)

    pwl = {
        "c0": hilo(float(Y64[0] + 1024.0 * mv[0])),
        "c1": hilo(c1),
        "w": [hilo(x) for x in w],
        "kx": [float(x) for x in kxv],
        "eu_w": hilo((1.0 - m[-1]) / 256.0),
        "eu_j": hilo(np.float64(np.float32(4.0)) - Y64[32]),
        "el_w": hilo(-mv[0]),
        "el_c": hilo(-mv[0] * 1024.0),
        "el_j": hilo(-Y64[0]),
    }
    return dict(cosb=cosb, sinb=sinb, m01=m01, slopes=slopes, pwl=pwl,
                relc=relc, ident=ident)



class Region:
    __slots__ = ("name", "writer", "readers")

    def __init__(self, name):
        self.name = name
        self.writer = None
        self.readers = []


class Prog:
    ENGS = ("TE", "VE", "SE", "GE", "SQ")
    NDS = 8
    SEMS = (("TE", "VE", "SE", "GEc", "SQc")
            + tuple(f"GEd{i}" for i in range(8)) + tuple(f"SQd{i}" for i in range(8)))

    def __init__(self, nc):
        self.nc = nc
        self.cnt = {sm: 0 for sm in self.SEMS}
        self.ops = {e: [] for e in self.ENGS}
        self.waited = {e: {sm: 0 for sm in self.SEMS} for e in self.ENGS}
        self._dma_rr = {"GE": 0, "SQ": 0}

    fence = None

    def _sem(self, eng, dma):
        if eng in ("GE", "SQ"):
            if dma:
                i = self._dma_rr[eng]
                self._dma_rr[eng] = (i + 1) % self.NDS
                return f"{eng}d{i}"
            return eng + "c"
        return eng

    def op(self, eng, fn, reads=(), writes=(), dma=False):
        waits = {}

        def dep(tok):
            if tok is None:
                return
            sm, c2 = tok
            if self.waited[eng][sm] < c2:
                waits[sm] = max(waits.get(sm, 0), c2)

        if self.fence:
            for sm, c2 in self.fence.items():
                dep((sm, c2))
        for r in reads:
            dep(r.writer)
        for r in writes:
            dep(r.writer)
            for t in r.readers:
                dep(t)
        sm = self._sem(eng, dma)
        if dma:
            dep((sm, self.cnt[sm]))
        for s2, c2 in waits.items():
            self.waited[eng][s2] = c2
        inc = 16 if dma else 1
        self.cnt[sm] += inc
        tok = (sm, self.cnt[sm])
        for r in reads:
            r.readers.append(tok)
        for r in writes:
            r.writer = tok
            r.readers = []
        self.ops[eng].append((waits, fn, sm, inc))
        return tok

    def check_deadlock(self):
        done = {sm: 0 for sm in self.SEMS}
        idx = {e: 0 for e in self.ENGS}
        total = sum(len(v) for v in self.ops.values())
        ndone = 0
        progress = True
        while progress:
            progress = False
            for e in self.ENGS:
                while idx[e] < len(self.ops[e]):
                    waits, _, sm, inc = self.ops[e][idx[e]]
                    if all(done[s2] >= c2 for s2, c2 in waits.items()):
                        done[sm] += inc
                        idx[e] += 1
                        ndone += 1
                        progress = True
                    else:
                        break
        if ndone != total:
            stuck = {e: (idx[e], len(self.ops[e]), self.ops[e][idx[e]][0])
                     for e in self.ENGS if idx[e] < len(self.ops[e])}
            raise RuntimeError(f"emission deadlock: {stuck}")

    def emit(self, block, sems):
        handles = {"TE": self.nc.tensor, "VE": self.nc.vector, "SE": self.nc.scalar,
                   "GE": self.nc.gpsimd, "SQ": self.nc.sync}

        def mk(eng):
            h = handles[eng]

            def body(_):
                for waits, fn, sm, inc in self.ops[eng]:
                    for s2 in sorted(waits):
                        h.wait_ge(sems[s2], waits[s2])
                    fn(h).then_inc(sems[sm], inc)
            return body

        for eng, deco in (("SQ", block.sync), ("TE", block.tensor),
                          ("VE", block.vector), ("SE", block.scalar),
                          ("GE", block.gpsimd)):
            if self.ops[eng]:
                deco(mk(eng))


class Tile:
    __slots__ = ("view", "c0", "P", "C", "alloc_idx", "arena")

    def __init__(self, view, c0, P, C, alloc_idx, arena):
        self.view = view
        self.c0 = c0
        self.P = P
        self.C = C
        self.alloc_idx = alloc_idx
        self.arena = arena

    def __getitem__(self, idx):
        rows, cols = idx
        if isinstance(rows, slice):
            r0 = rows.start or 0
            r1 = self.P if rows.stop is None else rows.stop
        else:
            r0, r1 = rows, rows + 1
        if isinstance(cols, slice):
            cs0 = cols.start or 0
            cs1 = self.C if cols.stop is None else cols.stop
            st = cols.step
            if st is None:
                return self.view[r0:r1, self.c0 + cs0: self.c0 + cs1]
            return self.view[r0:r1, self.c0 + cs0: self.c0 + cs1: st]
        return self.view[r0:r1, self.c0 + cols: self.c0 + cols + 1]


class Arena:

    def __init__(self, view, n):
        self.view = view
        self.free_list = [(0, n)]
        self.n = n
        self.used = {}

    def alloc(self, name, P, C):
        need = (C + 31) & ~31
        for i, (off, ln) in enumerate(self.free_list):
            if ln >= need:
                if ln == need:
                    self.free_list.pop(i)
                else:
                    self.free_list[i] = (off + need, ln - need)
                t = Tile(self.view, off, P, C, None, self)
                self.used[id(t)] = (off, need)
                return t
        raise MemoryError(f"arena OOM for {name}: need {need}, free {self.free_list}")

    def release(self, t):
        off, need = self.used.pop(id(t))
        self.free_list.append((off, need))
        self.free_list.sort()
        merged = []
        for off, ln in self.free_list:
            if merged and merged[-1][0] + merged[-1][1] == off:
                merged[-1] = (merged[-1][0], merged[-1][1] + ln)
            else:
                merged.append((off, ln))
        self.free_list = [tuple(x) for x in merged]


class Ctx:
    def __init__(self, nc, S, consts):
        self.nc = nc
        self.S = S
        self.consts = consts
        self.p = Prog(nc)
        self._cms = []
        self._psum_stack = []
        self._n = 0
        self.t = {}
        self.r = {}
        self.dram = {}
        self.dreg = {}
        self.flex = {}
        self.flex_order = []

    def perm(self, name, shape, dtype):
        self._n += 1
        cm = self.nc.sbuf_tensor(f"{name}_{self._n}", list(shape), dtype)
        t = cm.__enter__()
        self._cms.append(cm)
        return t

    def open_flex(self, n32=0, n16=0):
        assert not self.flex, "flex already open"
        self._n += 1
        if n32:
            cm = self.nc.sbuf_tensor(f"fx32_{self._n}", [128, n32], F32)
            self.flex[F32] = (Arena(cm.__enter__(), n32), cm)
            self.flex_order.append(F32)
        if n16:
            cm = self.nc.sbuf_tensor(f"fx16_{self._n}", [128, n16], F16)
            self.flex[F16] = (Arena(cm.__enter__(), n16), cm)
            self.flex_order.append(F16)

    def close_flex(self):
        self._fence_all()
        for dt in reversed(self.flex_order):
            self.flex[dt][1].__exit__(None, None, None)
        self.flex = {}
        self.flex_order = []

    def sbuf(self, name, shape, dtype):
        assert len(shape) == 2
        if dtype == I32:
            dtype = F32
        ar = self.flex[dtype][0]
        return ar.alloc(name, shape[0], shape[1])

    def sbuf_i32(self, name, shape):
        self._n += 1
        cm = self.nc.sbuf_tensor(f"{name}_{self._n}", list(shape), I32)
        t = cm.__enter__()
        self._psum_stack.append((t, cm))
        return t

    def psum(self, name, shape, dtype=F32):
        self._n += 1
        cm = self.nc.psum_tensor(f"{name}_{self._n}", list(shape), dtype)
        t = cm.__enter__()
        self._psum_stack.append((t, cm))
        return t

    def _fence_all(self):
        fence = dict(self.p.fence or {})
        for name, r in self.r.items():
            for tok in ([r.writer] if r.writer else []) + r.readers:
                e2, c2 = tok
                fence[e2] = max(fence.get(e2, 0), c2)
        self.p.fence = fence

    def free(self, *tensors):
        self._fence_all()
        psums = [t for t in tensors if not isinstance(t, Tile)]
        if psums:
            order = {id(t): i for i, (t, _) in enumerate(self._psum_stack)}
            for t in sorted(psums, key=lambda t: -order[id(t)]):
                i = next(i for i, (tt, _) in enumerate(self._psum_stack) if tt is t)
                _, cm = self._psum_stack.pop(i)
                cm.__exit__(None, None, None)
        for t in tensors:
            if isinstance(t, Tile):
                t.arena.release(t)

    def reg(self, name):
        if name not in self.r:
            self.r[name] = Region(name)
        return self.r[name]


def TS(c, out, in0, s1, s2, op0, op1, reads, writes, eng="VE"):
    return c.p.op(eng, lambda e: e.tensor_scalar(out, in0, s1, s2, op0, op1),
                  reads=reads, writes=writes)


def TT(c, out, in0, in1, op, reads, writes, eng="VE"):
    return c.p.op(eng, lambda e: e.tensor_tensor(out, in0, in1, op),
                  reads=reads, writes=writes)


def STT(c, out, in0, scalar, in1, op0, op1, reads, writes):
    return c.p.op("VE", lambda e: e.scalar_tensor_tensor(out, in0, scalar, in1, op0, op1),
                  reads=reads, writes=writes)


def CP(c, out, in0, reads, writes, eng="VE"):
    if eng == "SE":
        return c.p.op(eng, lambda e: e.activation(out, in0, ACT.Copy),
                      reads=reads, writes=writes)
    return c.p.op(eng, lambda e: e.tensor_copy(out, in0), reads=reads, writes=writes)


def ACTV(c, out, in0, func, reads, writes, bias=None, scale=None):
    kw = {}
    if bias is not None:
        kw["bias"] = bias
    if scale is not None:
        kw["scale"] = scale
    return c.p.op("SE", lambda e: e.activation(out, in0, func, **kw),
                  reads=reads, writes=writes)


def MM(c, psum_ap, lhsT_ap, rhs_ap, first, last, reads, writes, tp=None):
    kw = {}
    if tp is not None:
        kw["tile_position"] = tp
    return c.p.op("TE", lambda e: e.matmul(psum_ap, lhsT_ap, rhs_ap,
                                           start=first, stop=last, **kw),
                  reads=reads, writes=writes)


def MEMSET(c, ap, val, writes, eng="VE"):
    return c.p.op(eng, lambda e: e.memset(ap, val), reads=[], writes=writes)


def DMA(c, out_ap, in_ap, reads, writes, eng="SQ"):
    return c.p.op(eng, lambda e: e.dma_start(out=out_ap, in_=in_ap),
                  reads=reads, writes=writes, dma=True)


def RECIP(c, out, in0, reads, writes):
    return c.p.op("VE", lambda e: e.reciprocal(out, in0), reads=reads, writes=writes)


def round_ip(c, ap, reads, writes):
    ceff = c.t["Ceff"]
    return TS(c, ap, ap, ceff, ceff, ALU.add, ALU.subtract,
              reads=list(reads) + [c.reg("consts")], writes=writes)


def load_w16(c, name, key, rows, cols, scale=None):
    nch = rows // 128
    st = c.sbuf(name + "_st", [128, cols], F32)
    wt = c.sbuf(name, [128, nch * cols], F16)
    reg = c.reg(name)
    rst = c.reg(name + "_st")
    for ch in range(nch):
        DMA(c, st[:, :], c.dram[key][ch * 128:(ch + 1) * 128, :], reads=[rst], writes=[rst])
        if scale is None:
            CP(c, wt[:, ch * cols:(ch + 1) * cols], st[:, :], reads=[rst], writes=[reg])
        else:
            TS(c, wt[:, ch * cols:(ch + 1) * cols], st[:, :], float(scale), None,
               ALU.mult, ALU.bypass, [rst], [reg])
    c.free(st)
    return wt, reg



def setup_consts(c):
    cst = c.perm("cst", [128, 56], F32)
    c.t["cst"] = cst
    R = c.reg("consts")
    MEMSET(c, cst[:, 0:1], 2.0**-30, writes=[R])
    TS(c, cst[:, 0:1], cst[:, 0:1], 2.0**22, 2.0**22, ALU.add, ALU.subtract, [R], [R])
    TS(c, cst[:, 1:2], cst[:, 0:1], MAGIC_MUL, MAGIC32, ALU.mult, ALU.add, [R], [R])
    c.t["Ceff"] = cst[:, 1:2]
    sqrt96 = float(np.float32(np.sqrt(np.float64(96.0))))
    MEMSET(c, cst[:, 2:3], sqrt96, writes=[R])
    RECIP(c, cst[:, 2:3], cst[:, 2:3], [R], [R])
    TS(c, cst[:, 3:4], cst[:, 2:3], -80.0, None, ALU.mult, ALU.bypass, [R], [R])
    c.t["rc96"] = cst[:, 2:3]
    c.t["negC0"] = cst[:, 3:4]
    MEMSET(c, cst[:, 4:5], 1e-5, writes=[R])
    c.t["epsc"] = cst[:, 4:5]
    pw = c.consts["pwl"]
    coeffs = ([pw["c0"], pw["c1"]] + pw["w"]
              + [pw["eu_w"], pw["eu_j"], pw["el_w"], pw["el_c"], pw["el_j"]])
    for i, (hi, lo) in enumerate(coeffs):
        TS(c, cst[:, 8 + i:9 + i], cst[:, 0:1], float(lo) * 2.0**30, float(hi),
           ALU.mult, ALU.add, [R], [R])
    c.PWL0 = 8

    ones16 = c.perm("ones16", [128, 1], F16)
    MEMSET(c, ones16[:, :], 1.0, writes=[R])
    c.t["ones16"] = ones16
    onesrow = c.perm("onesrow", [1, 128], F16)
    MEMSET(c, onesrow[:, :], 1.0, writes=[R])
    c.t["onesrow"] = onesrow
    onesrow32 = c.perm("onesrow32", [1, 128], F32)
    MEMSET(c, onesrow32[:, :], 1.0, writes=[R])
    c.t["onesrow32"] = onesrow32
    ones32c = c.perm("ones32c", [128, 1], F32)
    MEMSET(c, ones32c[:, :], 1.0, writes=[R])
    c.t["ones32c"] = ones32c
    mbar = c.perm("mbar", [128, 8], F32)
    c.t["mbar"] = mbar
    hp = c.perm("hpones", [128, 2], F16)
    MEMSET(c, hp[:, :], 0.0, writes=[R])
    MEMSET(c, hp[0:64, 0:1], 1.0, writes=[R])
    MEMSET(c, hp[64:128, 1:2], 1.0, writes=[R])
    c.t["hpones"] = hp



def post_ln(c, s32, rR, s16, rD, tag):
    S = c.S
    TB = min(512, S)
    NTB = S // TB
    r16 = c.sbuf(f"lnr{tag}", [128, 8 * S], F16)
    sq = c.sbuf(f"lnq{tag}", [128, 8 * S], F16)
    rT = c.reg(f"lnr{tag}")
    rQ = c.reg(f"lnq{tag}")
    for ch in range(8):
        sl = slice(ch * S, (ch + 1) * S)
        CP(c, r16[:, sl], s32[:, sl], reads=[rR], writes=[rT])
        ACTV(c, sq[:, sl], s32[:, sl], ACT.Square, reads=[rR], writes=[rQ])
    stat = c.sbuf(f"lns{tag}", [1, 4 * TB], F32)
    st16 = c.sbuf(f"ln6{tag}", [1, 2 * TB], F16)
    rS = c.reg(f"lns{tag}")
    rS16 = c.reg(f"ln6{tag}")
    ps1 = c.psum(f"lp1{tag}", [1, TB])
    ps2 = c.psum(f"lp2{tag}", [1, TB])
    rP1 = c.reg(f"lp1{tag}")
    rP2 = c.reg(f"lp2{tag}")
    psb1 = c.psum(f"lb1{tag}", [128, TB])
    psb2 = c.psum(f"lb2{tag}", [128, TB])
    rB1 = c.reg(f"lb1{tag}")
    rB2 = c.reg(f"lb2{tag}")
    ones = c.t["ones16"]
    onesrow = c.t["onesrow"]
    rC = c.reg("consts")
    eps = 1e-5
    m_ = stat[0:1, 0:TB]
    v_ = stat[0:1, TB:2 * TB]
    s_ = stat[0:1, 2 * TB:3 * TB]
    w_ = stat[0:1, 3 * TB:4 * TB]
    for tb in range(NTB):
        for ch in range(8):
            sl = slice(ch * S + tb * TB, ch * S + (tb + 1) * TB)
            MM(c, ps1[0:1, :], ones[:, 0:1], r16[:, sl], ch == 0, ch == 7,
               reads=[rT, rC], writes=[rP1])
            MM(c, ps2[0:1, :], ones[:, 0:1], sq[:, sl], ch == 0, ch == 7,
               reads=[rQ, rC], writes=[rP2])
        TS(c, m_, ps1[0:1, :], 1.0 / D, None, ALU.mult, ALU.bypass, [rP1], [rS])
        TS(c, v_, ps2[0:1, :], 1.0 / D, None, ALU.mult, ALU.bypass, [rP2], [rS])
        TT(c, w_, m_, m_, ALU.mult, [rS], [rS])
        TT(c, v_, v_, w_, ALU.subtract, [rS], [rS])
        ACTV(c, s_, v_, ACT.Sqrt, reads=[rS, c.reg("consts")], writes=[rS], bias=c.t["cst"][0:1, 4:5])
        RECIP(c, w_, s_, [rS], [rS])
        TS(c, v_, v_, eps, None, ALU.add, ALU.bypass, [rS], [rS])
        TT(c, w_, v_, w_, ALU.mult, [rS], [rS])
        TT(c, w_, w_, s_, ALU.add, [rS], [rS])
        TS(c, w_, w_, 0.5, None, ALU.mult, ALU.bypass, [rS], [rS])
        RECIP(c, w_, w_, [rS], [rS])
        CP(c, st16[0:1, 0:TB], m_, reads=[rS], writes=[rS16])
        CP(c, st16[0:1, TB:2 * TB], w_, reads=[rS], writes=[rS16])
        MM(c, psb1[:, :], onesrow[0:1, :], st16[0:1, 0:TB], True, True,
           reads=[rS16, rC], writes=[rB1])
        MM(c, psb2[:, :], onesrow[0:1, :], st16[0:1, TB:2 * TB], True, True,
           reads=[rS16, rC], writes=[rB2])
        for ch in range(8):
            sl = slice(ch * S + tb * TB, ch * S + (tb + 1) * TB)
            TT(c, s32[:, sl], s32[:, sl], psb1[:, :], ALU.subtract, [rR, rB1], [rR])
            TT(c, s32[:, sl], s32[:, sl], psb2[:, :], ALU.mult, [rR, rB2], [rR])
            if s16 is not None:
                ACTV(c, s16[:, sl], s32[:, sl], ACT.Copy, reads=[rR], writes=[rD])
    c.free(r16, sq, stat, st16, psb2, psb1, ps2, ps1)



def l0_attn(c, s16, rS16):
    S = c.S
    TB = min(512, S)
    NTB = S // TB
    NSC = S // 128
    p = c.p
    rC = c.reg("consts")
    rc96 = c.t["rc96"]

    xst = c.sbuf("xst", [128, S], F32)
    rXst = c.reg("xst")
    for ch in range(8):
        DMA(c, xst[:, :], c.dram["xT"][ch * 128:(ch + 1) * 128, :],
            reads=[rXst], writes=[rXst])
        CP(c, s16[:, ch * S:(ch + 1) * S], xst[:, :], reads=[rXst], writes=[rS16])
    c.free(xst)
    xt16 = s16
    rX16 = rS16

    ps = c.psum("l0p", [128, TB])
    rPS = c.reg("l0p")

    wdq, rwdq = load_w16(c, "wdq", "wdqT", D, DLAT)
    wdkv, rwdkv = load_w16(c, "wdkv", "wdkvT", D, DLAT)
    cq = c.sbuf("cq16", [128, 4 * S], F16)
    ckv = c.sbuf("ckv16", [128, 4 * S], F16)
    rCQ = c.reg("cq16")
    rCKV = c.reg("ckv16")
    for dst, wsrc, rw, rdst in ((cq, wdq, rwdq, rCQ), (ckv, wdkv, rwdkv, rCKV)):
        for lc in range(4):
            for tb in range(NTB):
                for ch in range(8):
                    MM(c, ps[:, :], wsrc[:, ch * DLAT + lc * 128: ch * DLAT + (lc + 1) * 128],
                       xt16[:, ch * S + tb * TB: ch * S + (tb + 1) * TB],
                       ch == 0, ch == 7, reads=[rw, rX16], writes=[rPS])
                CP(c, dst[:, lc * S + tb * TB: lc * S + (tb + 1) * TB], ps[:, :],
                   reads=[rPS], writes=[rdst], eng="SE")
    c.free(wdq, wdkv)

    wuq, rwuq = load_w16(c, "wuq", "wuqT", DLAT, H * DH)
    qc = c.sbuf("qc16", [128, 8 * S], F16)
    rQC = c.reg("qc16")
    for oc in range(8):
        for tb in range(NTB):
            for lc in range(4):
                MM(c, ps[:, :], wuq[:, lc * (H * DH) + oc * 128: lc * (H * DH) + (oc + 1) * 128],
                   cq[:, lc * S + tb * TB: lc * S + (tb + 1) * TB],
                   lc == 0, lc == 3, reads=[rwuq, rCQ], writes=[rPS])
            TS(c, qc[:, oc * S + tb * TB: oc * S + (tb + 1) * TB], ps[:, :],
               rc96, None, ALU.mult, ALU.bypass, [rPS, rC], [rQC])
    c.free(wuq)

    wqr, rwqr = load_w16(c, "wqr", "wqrT", DLAT, H * DHR)
    cosb = c.sbuf("cosb", [128, S], F32)
    sinb = c.sbuf("sinb", [128, S], F32)
    rCos = c.reg("cosb")
    DMA(c, cosb[:, :], c.dram["cosb"], reads=[rCos], writes=[rCos])
    DMA(c, sinb[:, :], c.dram["sinb"], reads=[rCos], writes=[rCos])
    scr = c.sbuf("ropes", [128, 2 * S], F32)
    rScr = c.reg("ropes")
    qrev = c.sbuf("qrev", [128, S], F32)
    qrod = c.sbuf("qrod", [128, S], F32)
    rQRE = c.reg("qrev")
    rQRO = c.reg("qrod")
    qrev16 = c.sbuf("qrev16", [128, 2 * S], F16)
    qrod16 = c.sbuf("qrod16", [128, 2 * S], F16)
    rQE16 = c.reg("qrev16")
    rQO16 = c.reg("qrod16")

    def rope(np_, ev, od, ev16, od16, rE, rO, rE16, rO16, cs, sn):
        t1 = scr[0:np_, 0:S]
        t2 = scr[0:np_, S:2 * S]
        TT(c, t1, ev, cs, ALU.mult, [rE, rCos], [rScr])
        TT(c, t2, od, sn, ALU.mult, [rO, rCos], [rScr])
        TT(c, ev16, t1, t2, ALU.subtract, [rScr], [rE16])
        TT(c, t1, ev, sn, ALU.mult, [rE, rCos], [rScr])
        TT(c, t2, od, cs, ALU.mult, [rO, rCos], [rScr])
        TT(c, od16, t1, t2, ALU.add, [rScr], [rO16])

    for oc in range(2):
        for half, dst, rdst in ((0, qrev, rQRE), (1, qrod, rQRO)):
            for tb in range(NTB):
                for lc in range(4):
                    col = lc * (H * DHR) + half * 256 + oc * 128
                    MM(c, ps[:, :], wqr[:, col: col + 128],
                       cq[:, lc * S + tb * TB: lc * S + (tb + 1) * TB],
                       lc == 0, lc == 3, reads=[rwqr, rCQ], writes=[rPS])
                TS(c, dst[:, tb * TB:(tb + 1) * TB], ps[:, :],
                   rc96, None, ALU.mult, ALU.bypass, [rPS, rC], [rdst])
        sl = slice(oc * S, (oc + 1) * S)
        rope(128, qrev[:, :], qrod[:, :], qrev16[:, sl], qrod16[:, sl],
             rQRE, rQRO, rQE16, rQO16, cosb[:, :], sinb[:, :])
    c.free(wqr, qrev, qrod, cq)

    qrt2 = c.sbuf("qrt2", [128, 4 * S], F16)
    rQR2 = c.reg("qrt2")
    for h in range(H):
        cc, rr = divmod(h, 4)
        sch, sr = divmod(h, 8)
        DMA(c, qrt2[rr * 32: rr * 32 + 16, cc * S:(cc + 1) * S],
            qrev16[sr * 16: sr * 16 + 16, sch * S:(sch + 1) * S],
            reads=[rQE16], writes=[rQR2], eng="GE")
        DMA(c, qrt2[rr * 32 + 16: rr * 32 + 32, cc * S:(cc + 1) * S],
            qrod16[sr * 16: sr * 16 + 16, sch * S:(sch + 1) * S],
            reads=[rQO16], writes=[rQR2], eng="GE")
    c.free(qrev16, qrod16)

    wkr, rwkr = load_w16(c, "wkr", "wkrT", D, DHR)
    krev = c.sbuf("krev", [16, S], F32)
    krod = c.sbuf("krod", [16, S], F32)
    rKRE = c.reg("krev")
    rKRO = c.reg("krod")
    psk = c.psum("l0pk", [16, TB])
    rPSK = c.reg("l0pk")
    for half, dst, rdst in ((0, krev, rKRE), (1, krod, rKRO)):
        for tb in range(NTB):
            for ch in range(8):
                MM(c, psk[0:16, :], wkr[:, ch * DHR + half * 16: ch * DHR + half * 16 + 16],
                   xt16[:, ch * S + tb * TB: ch * S + (tb + 1) * TB],
                   ch == 0, ch == 7, reads=[rwkr, rX16], writes=[rPSK])
            CP(c, dst[0:16, tb * TB:(tb + 1) * TB], psk[0:16, :], reads=[rPSK],
               writes=[rdst], eng="SE")
    krev16 = c.sbuf("krev16", [16, S], F16)
    krod16 = c.sbuf("krod16", [16, S], F16)
    rKE16 = c.reg("krev16")
    rKO16 = c.reg("krod16")
    rope(16, krev[0:16, :], krod[0:16, :], krev16[0:16, :], krod16[0:16, :],
         rKRE, rKRO, rKE16, rKO16, cosb[0:16, :], sinb[0:16, :])
    c.free(wkr, krev, krod, scr, cosb, sinb, psk)
    krt2 = c.sbuf("krt2", [128, S], F16)
    rKR2 = c.reg("krt2")
    for b in range(4):
        DMA(c, krt2[b * 32: b * 32 + 16, :], krev16[0:16, :], reads=[rKE16],
            writes=[rKR2], eng="GE")
        DMA(c, krt2[b * 32 + 16: b * 32 + 32, :], krod16[0:16, :], reads=[rKO16],
            writes=[rKR2], eng="GE")
    c.free(krev16, krod16)

    wuk, rwuk = load_w16(c, "wuk", "wukT", DLAT, H * DH)
    kc = c.sbuf("kc16", [128, 8 * S], F16)
    rKC = c.reg("kc16")
    for oc in range(8):
        for tb in range(NTB):
            for lc in range(4):
                MM(c, ps[:, :], wuk[:, lc * (H * DH) + oc * 128: lc * (H * DH) + (oc + 1) * 128],
                   ckv[:, lc * S + tb * TB: lc * S + (tb + 1) * TB],
                   lc == 0, lc == 3, reads=[rwuk, rCKV], writes=[rPS])
            CP(c, kc[:, oc * S + tb * TB: oc * S + (tb + 1) * TB], ps[:, :],
               reads=[rPS], writes=[rKC], eng="SE")
    c.free(wuk)

    wuv, rwuv = load_w16(c, "wuv", "wuvT", DLAT, H * DH)
    m01 = c.sbuf("m01", [128, 128], F16)
    rM01 = c.reg("m01")
    DMA(c, m01[:, :], c.dram["m01c"], reads=[rM01], writes=[rM01])
    vb = [c.sbuf(f"l0v{i}", [128, NSC * 66], F16) for i in range(2)]
    rVb = [c.reg(f"l0v{i}") for i in range(2)]
    for i in range(2):
        MEMSET(c, vb[i][:, 64::66], 1.0, writes=[rVb[i]])
    pexp2 = [c.sbuf(f"l0pe{i}", [128, TB], F16) for i in range(2)]
    rPexp2 = [c.reg(f"l0pe{i}") for i in range(2)]
    dn = c.sbuf("l0dn", [1, 3 * TB], F32)
    rDn = c.reg("l0dn")
    dn16 = c.sbuf("l0d6", [1, TB], F16)
    rDn16 = c.reg("l0d6")
    bcs = c.sbuf("l0bc", [64, TB], F32)
    rBcs = c.reg("l0bcs")
    aodd = c.sbuf("l0ao", [64, TB], F16)
    rAodd = c.reg("l0ao")
    atp0 = c.sbuf("l0at", [64, TB], F32)
    rAtp0 = c.reg("l0at")
    psS2 = [c.psum(f"l0sc{i}", [128, TB]) for i in range(2)]
    rSc2 = [c.reg(f"l0sc{i}") for i in range(2)]
    psPV2 = [c.psum(f"l0pv{i}", [65, TB]) for i in range(2)]
    rPV2 = [c.reg(f"l0pv{i}") for i in range(2)]
    psV2 = [c.psum(f"l0vv{i}", [128, 64]) for i in range(2)]
    rPsV2 = [c.reg(f"l0vv{i}") for i in range(2)]
    psB = c.psum("l0bb", [64, TB])
    rB = c.reg("l0bb")
    negC0 = c.t["negC0"]
    onesrow = c.t["onesrow"]
    explo = float(np.exp(np.float64(-160.0) * (1.0 / np.float64(np.float32(np.sqrt(np.float64(96.0)))))))

    tails = [None]

    def flush_tail():
        if tails[0] is None:
            return
        kch, kro, tb, psPV, rPV = tails[0]
        tails[0] = None
        CP(c, dn[0:1, 0:TB], psPV[64:65, :], reads=[rPV], writes=[rDn])
        ACTV(c, dn[0:1, TB:2 * TB], dn[0:1, 0:TB], ACT.Ln, reads=[rDn], writes=[rDn])
        ACTV(c, dn[0:1, 2 * TB:3 * TB], dn[0:1, TB:2 * TB], ACT.Exp,
             reads=[rDn], writes=[rDn], scale=-0.5)
        CP(c, dn16[0:1, :], dn[0:1, 2 * TB:3 * TB], reads=[rDn], writes=[rDn16])
        MM(c, psB[0:64, :], onesrow[0:1, 0:64], dn16[0:1, :], True, True,
           reads=[rDn16, rC], writes=[rB])
        CP(c, bcs[:, :], psB[0:64, :], reads=[rB], writes=[rBcs], eng="SE")
        TT(c, atp0[0:64, :], psPV[0:64, :], bcs[:, :], ALU.mult, [rPV, rBcs],
           [rAtp0])
        if kro == 0:
            TT(c, s16[0:64, kch * S + tb * TB: kch * S + (tb + 1) * TB],
               atp0[0:64, :], bcs[:, :], ALU.mult, [rAtp0, rBcs], [rS16])
        else:
            TT(c, aodd[0:64, 0:TB], atp0[0:64, :], bcs[:, :], ALU.mult,
               [rAtp0, rBcs], [rAodd])
            DMA(c, s16[64:128, kch * S + tb * TB: kch * S + (tb + 1) * TB],
                aodd[0:64, 0:TB], reads=[rAodd], writes=[rS16], eng="GE")

    for h in range(H):
        vcur, rV = vb[h % 2], rVb[h % 2]
        for tc in range(NSC):
            psV, rPsV = psV2[tc % 2], rPsV2[tc % 2]
            for lc in range(4):
                MM(c, psV[:, :], ckv[:, lc * S + tc * 128: lc * S + (tc + 1) * 128],
                   wuv[:, lc * (H * DH) + h * DH: lc * (H * DH) + h * DH + DH],
                   lc == 0, lc == 3, reads=[rCKV, rwuv], writes=[rPsV])
            CP(c, vcur[:, tc * 66: tc * 66 + 64], psV[:, :], reads=[rPsV],
               writes=[rV], eng="SE")
        kch, kro = divmod(h, 2)
        qrc, qrr = divmod(h, 4)
        base = kro * 64
        rbase = qrr * 32
        for tb in range(NTB):
            nsc = (tb + 1) * TB // 128
            pv = (h * NTB + tb) % 2
            psPV, rPV = psPV2[pv], rPV2[pv]
            def emit_pv(sc_, last):
                pi_ = sc_ % 2
                t0_ = max(tb * TB, sc_ * 128)
                w_ = (tb + 1) * TB - t0_
                MM(c, psPV[0:65, t0_ - tb * TB: t0_ - tb * TB + w_],
                   vcur[:, sc_ * 66: sc_ * 66 + 65], pexp2[pi_][:, 0:w_],
                   sc_ == 0, last, reads=[rV, rPexp2[pi_]], writes=[rPV])
            for sc in range(nsc):
                pi = sc % 2
                psS, rSc = psS2[pi], rSc2[pi]
                pexp, rPexp = pexp2[pi], rPexp2[pi]
                t0 = max(tb * TB, sc * 128)
                w = (tb + 1) * TB - t0
                MM(c, psS[0:128, 0:w],
                   kc[base:base + 64, kch * S + sc * 128: kch * S + (sc + 1) * 128],
                   qc[base:base + 64, kch * S + t0: kch * S + t0 + w],
                   True, False, reads=[rKC, rQC], writes=[rSc], tp=(base, 0))
                MM(c, psS[0:128, 0:w],
                   krt2[rbase:rbase + 32, sc * 128:(sc + 1) * 128],
                   qrt2[rbase:rbase + 32, qrc * S + t0: qrc * S + t0 + w],
                   False, True, reads=[rKR2, rQR2], writes=[rSc], tp=(rbase, 0))
                if sc == 1:
                    flush_tail()
                ACTV(c, pexp[:, 0:w], psS[0:128, 0:w], ACT.Exp,
                     reads=[rSc], writes=[rPexp], bias=negC0)
                TS(c, pexp[:, 0:w], pexp[:, 0:w], 1.0, explo,
                   ALU.min, ALU.max, [rPexp], [rPexp])
                if t0 == sc * 128:
                    TT(c, pexp[:, 0:128], pexp[:, 0:128], m01[:, :], ALU.mult,
                       [rPexp, rM01], [rPexp])
                if sc >= 1:
                    emit_pv(sc - 1, False)
            emit_pv(nsc - 1, True)
            tails[0] = (kch, kro, tb, psPV, rPV)
    flush_tail()
    c.free(pexp2[0], pexp2[1], dn, dn16, bcs, aodd, atp0, vb[0], vb[1], m01,
           qc, kc, qrt2, krt2, ckv, wuv,
           psB, psV2[1], psV2[0], psPV2[1], psPV2[0], psS2[1], psS2[0], ps)


def l0_wo(c, s16, rS16, x1, rX1):
    S = c.S
    TB = min(512, S)
    NTB = S // TB
    wo, rwo = load_w16(c, "wo", "woT", H * DH, D)
    ps = c.psum("wop", [128, TB])
    rPS = c.reg("wop")
    rx = c.sbuf("worx", [128, 2 * TB], F32)
    rRx = [c.reg("worx0"), c.reg("worx1")]
    for dc in range(8):
        for tb in range(NTB):
            bu = (dc * NTB + tb) % 2
            DMA(c, rx[:, bu * TB:(bu + 1) * TB],
                c.dram["xT"][dc * 128:(dc + 1) * 128, tb * TB:(tb + 1) * TB],
                reads=[rRx[bu]], writes=[rRx[bu]])
            for oc in range(8):
                MM(c, ps[:, :], wo[:, oc * D + dc * 128: oc * D + (dc + 1) * 128],
                   s16[:, oc * S + tb * TB: oc * S + (tb + 1) * TB],
                   oc == 0, oc == 7, reads=[rwo, rS16], writes=[rPS])
            TT(c, x1[:, dc * S + tb * TB: dc * S + (tb + 1) * TB], ps[:, :],
               rx[:, bu * TB:(bu + 1) * TB], ALU.add, [rPS, rRx[bu]], [rX1])
    c.free(wo, rx, ps)



def prequant(c):
    p = c.p
    rC = c.reg("consts")
    stats = c.sbuf("fqs", [128, 16], F32)
    rStat = c.reg("fqs")
    st2 = [c.sbuf(f"fqt{i}", [128, FF], F32) for i in range(2)]
    rSt2 = [c.reg(f"fqt{i}") for i in range(2)]
    tn2 = [c.sbuf(f"fqn{i}", [128, FF], F16) for i in range(2)]
    rTn2 = [c.reg(f"fqn{i}") for i in range(2)]
    tmp2 = [c.sbuf(f"fqm{i}", [128, FF], F32) for i in range(2)]
    rTmp2 = [c.reg(f"fqm{i}") for i in range(2)]
    psR = c.psum("fqp", [1, 8])
    rPsR = c.reg("fqp")
    psBC = c.psum("fqb", [128, 4])
    rPsBC = c.reg("fqb")
    specs = [("wgT", "wgc", D, FF), ("wuT", "wuc", D, FF), ("wdT", "wdc", FF, D)]
    for wi, (key, skey, rows, cols) in enumerate(specs):
        nch = rows // 128
        for ch in range(nch):
            st, rSt = st2[ch % 2], rSt2[ch % 2]
            DMA(c, st[:, 0:cols], c.dram[key][ch * 128:(ch + 1) * 128, :],
                reads=[rSt], writes=[rSt])
            cslot = wi * 4 + (ch % 2)
            p.op("VE", lambda e, cslot=cslot, cols=cols, st=st: e.tensor_reduce(
                stats[:, cslot], st[:, 0:cols], AX.X, ALU.add,
                apply_absolute_value=True), reads=[rSt], writes=[rStat])
            if ch == 0:
                CP(c, stats[:, wi * 4 + 2], stats[:, cslot], reads=[rStat], writes=[rStat])
            else:
                TT(c, stats[:, wi * 4 + 2], stats[:, wi * 4 + 2],
                   stats[:, cslot], ALU.add, [rStat], [rStat])
        MM(c, psR[0:1, wi: wi + 1], c.t["ones32c"][:, 0:1],
           stats[:, wi * 4 + 2], True, True, reads=[rStat, rC], writes=[rPsR])
        n = float(rows) * float(cols)
        TS(c, stats[0:1, wi * 4 + 3], psR[0:1, wi: wi + 1], 1.0 / n, 1e-5,
           ALU.mult, ALU.max, [rPsR], [rStat])
    MM(c, psBC[:, 0:3], c.t["onesrow32"][0:1, :], stats[0:1, 3:15:4], True, True,
       reads=[rStat, rC], writes=[rPsBC])
    mbar = c.t["mbar"]
    rMbar = c.reg("mbar")
    CP(c, mbar[:, 0:3], psBC[:, 0:3], reads=[rPsBC], writes=[rMbar])
    TS(c, mbar[:, 3:6], mbar[:, 0:3], 0.5, None, ALU.mult, ALU.bypass, [rMbar], [rMbar])
    TS(c, mbar[:, 7:8], mbar[:, 1:2], 16.0, None, ALU.mult, ALU.bypass, [rMbar], [rMbar])
    c.free(psBC, psR)
    for wi, (key, skey, rows, cols) in enumerate(specs):
        nch = rows // 128
        thr = mbar[:, 3 + wi: 4 + wi]
        TS(c, mbar[:, 6:7], mbar[:, 3 + wi: 4 + wi], -1.0, None, ALU.mult, ALU.bypass,
           [rMbar], [rMbar])
        for ch in range(nch):
            st, rSt = st2[ch % 2], rSt2[ch % 2]
            tn, rTn = tn2[ch % 2], rTn2[ch % 2]
            tmp, rTmp = tmp2[ch % 2], rTmp2[ch % 2]
            DMA(c, st[:, 0:cols], c.dram[key][ch * 128:(ch + 1) * 128, :],
                reads=[rSt], writes=[rSt])
            TS(c, tmp[:, 0:cols], st[:, 0:cols], mbar[:, 6:7], None, ALU.is_lt,
               ALU.bypass, [rSt, rMbar], [rTmp])
            p.op("VE", lambda e, cols=cols, thr=thr, tn=tn, st=st, tmp=tmp:
                e.scalar_tensor_tensor(
                tn[:, 0:cols], st[:, 0:cols], thr, tmp[:, 0:cols],
                ALU.is_gt, ALU.subtract), reads=[rSt, rTmp, rMbar], writes=[rTn])
            DMA(c, c.dram[skey][ch * (cols // 128) * 128:(ch + 1) * (cols // 128) * 128, :]
                .rearrange("(o p) q -> p o q", p=128),
                tn[:, 0:cols], reads=[rTn], writes=[c.dreg[skey]])
    c.free(tmp2[1], tmp2[0], tn2[1], tn2[0], st2[1], st2[0], stats)


def l0_ffn(c, x1, rX1, s16, rS16):
    S = c.S
    TB = min(256, S)
    NTB = S // TB
    NOC = FF // 128
    p = c.p
    rC = c.reg("consts")
    sx = c.sbuf("aqs", [128, 2 * S], F32)
    rSx = c.reg("aqs")
    mbar = c.t["mbar"]
    rMbar = c.reg("mbar")
    psBB = c.psum("aqb", [128, 128])
    rPsBB = c.reg("aqb")

    mx = c.sbuf("aqm", [128, S], F32)
    rMx = c.reg("aqm")
    axs = c.sbuf("aqab", [128, S], F32)
    rAxs = c.reg("aqab")
    ACTV(c, mx[:, :], x1[:, 0:S], ACT.Abs, reads=[rX1], writes=[rMx])
    for ch in range(1, 8):
        ACTV(c, axs[:, :], x1[:, ch * S:(ch + 1) * S], ACT.Abs, reads=[rX1],
             writes=[rAxs])
        TT(c, mx[:, :], mx[:, :], axs[:, :], ALU.max, [rAxs, rMx], [rMx])
    c.free(axs)
    NSC = S // 128
    red = c.sbuf("aqrd", [64, S], F32)
    rRed = c.reg("aqrd")
    cur = 128
    while cur > 1:
        h = cur // 2
        DMA(c, red[0:h, 0:S], mx[h:cur, :], reads=[rMx], writes=[rRed], eng="GE")
        TT(c, mx[0:h, :], mx[0:h, :], red[0:h, 0:S], ALU.max, [rRed, rMx], [rMx])
        cur = h
    if _os.environ.get("KDBG_SXD"):
        DMA(c, c.dram["outT"][256:257, :], mx[0:1, :], reads=[rMx],
            writes=[c.reg("out")])
    c.free(red)
    sxr = c.sbuf("aqxr", [1, 2 * S], F32)
    rSxr = c.reg("aqxr")
    rowx = c.sbuf("aqrw", [1, 2 * S], F16)
    rRowx = c.reg("aqrw")
    TS(c, mx[0:1, :], mx[0:1, :], 1e-5, None, ALU.max, ALU.bypass, [rMx], [rMx])
    RECIP(c, sxr[0:1, 0:S], mx[0:1, :], [rMx], [rSxr])
    TS(c, sxr[0:1, 0:S], sxr[0:1, 0:S], 127.0, None, ALU.mult, ALU.bypass,
       [rSxr], [rSxr])
    CP(c, rowx[0:1, 0:S], sxr[0:1, 0:S], reads=[rSxr], writes=[rRowx])
    CP(c, sxr[0:1, S:2 * S], rowx[0:1, 0:S], reads=[rRowx], writes=[rSxr])
    TT(c, sxr[0:1, S:2 * S], sxr[0:1, 0:S], sxr[0:1, S:2 * S], ALU.subtract,
       [rSxr], [rSxr])
    CP(c, rowx[0:1, S:2 * S], sxr[0:1, S:2 * S], reads=[rSxr], writes=[rRowx])
    onesrow = c.t["onesrow"]
    for tc in range(NSC):
        MM(c, psBB[:, :], onesrow[0:1, :], rowx[0:1, tc * 128:(tc + 1) * 128],
           True, False, reads=[rRowx, rC], writes=[rPsBB])
        MM(c, psBB[:, :], onesrow[0:1, :], rowx[0:1, S + tc * 128: S + (tc + 1) * 128],
           False, True, reads=[rRowx, rC], writes=[rPsBB])
        CP(c, sx[:, tc * 128:(tc + 1) * 128], psBB[:, :], reads=[rPsBB], writes=[rSx])
    RECIP(c, sx[:, S:2 * S], sx[:, 0:S], [rSx], [rSx])
    if _os.environ.get("KDBG_SXD"):
        DMA(c, c.dram["outT"][0:128, :], sx[0:128, 0:S], reads=[rSx],
            writes=[c.reg("out")])
        DMA(c, c.dram["outT"][128:256, :], sx[0:128, S:2 * S], reads=[rSx],
            writes=[c.reg("out")])
    c.free(rowx, sxr, mx)
    qs = c.sbuf("aqc", [128, S], F32)
    rQs = c.reg("aqc")
    xq = s16
    rXq = rS16
    for ch in range(8):
        sl = slice(ch * S, (ch + 1) * S)
        TT(c, qs[:, :], x1[:, sl], sx[:, 0:S], ALU.mult, [rX1, rSx], [rQs])
        round_ip(c, qs[:, :], [rQs], [rQs])
        TS(c, xq[:, sl], qs[:, :], 127.0, -128.0, ALU.min, ALU.max, [rQs], [rXq])
    c.free(qs)

    if getattr(c, "tap", None) == "ffn":
        DMA(c, c.dram["dbgA"][:, 0:8 * S], xq[:, 0:8 * S], reads=[rXq], writes=[c.dreg["dbgA"]])
        DMA(c, c.dram["dbgB"][:, 0:2 * S], sx[0:128, 0:2 * S], reads=[rSx], writes=[c.dreg["dbgB"]])
        DMA(c, c.dram["dbgD"][:, 0:7], mbar[0:128, 0:7], reads=[rMbar], writes=[c.dreg["dbgD"]])
    for ch in range(8):
        DMA(c, c.dram["x1s"][ch * 128:(ch + 1) * 128, :], x1[:, ch * S:(ch + 1) * S],
            reads=[rX1], writes=[c.dreg["x1s"]])

    prod = c.sbuf("ffp", [128, NOC * TB], F32)
    rProdL = [c.reg(f"ffp_{i}") for i in range(NOC)]
    pq = c.sbuf("ffq", [128, NOC * TB], F16)
    rPqL = [c.reg(f"ffq_{i}") for i in range(NOC)]
    gsl2 = [c.sbuf(f"ffg{i}", [128, TB], F32) for i in range(2)]
    rGsl2 = [c.reg(f"ffg{i}") for i in range(2)]
    wb = [c.sbuf(f"ffw{i}", [128, 8 * 128], F16) for i in range(4)]
    rWb = [c.reg(f"ffw{i}") for i in range(4)]
    dwb = [c.sbuf(f"ffd{i}", [128, NOC * 128], F16) for i in range(2)]
    rDwb = [c.reg(f"ffd{i}") for i in range(2)]
    rmax = c.sbuf("ffm", [128, TB], F32)
    rRmax = c.reg("ffm")
    redr = c.sbuf("ffrd", [64, TB], F32)
    rRedr = c.reg("ffrd")
    fab = c.sbuf("ffab", [128, TB], F32)
    rFab = c.reg("ffab")
    rxr = c.sbuf("ffxr", [1, TB], F32)
    rRxr = c.reg("ffxr")
    rowr = c.sbuf("ffrr", [1, 2 * TB], F16)
    rRowr = c.reg("ffrr")
    rmaxb = c.sbuf("ffmb", [128, TB], F32)
    rRmaxb = c.reg("ffmb")
    s2 = c.sbuf("ff2", [128, 3 * TB], F32)
    rS2 = c.reg("ff2")
    rx1 = c.sbuf("ffr", [128, 2 * TB], F32)
    rRx1 = [c.reg("ffr0"), c.reg("ffr1")]
    x2 = x1
    rX2 = rX1
    psG2 = [c.psum(f"ffpg{i}", [128, TB]) for i in range(2)]
    psU2 = [c.psum(f"ffpu{i}", [128, TB]) for i in range(2)]
    rPsG2 = [c.reg(f"ffpg{i}") for i in range(2)]
    rPsU2 = [c.reg(f"ffpu{i}") for i in range(2)]
    wgc2 = c.dram["wgc"].rearrange("(ch o p) q -> o p ch q", o=NOC, p=128)
    wuc2 = c.dram["wuc"].rearrange("(ch o p) q -> o p ch q", o=NOC, p=128)
    wdc2 = c.dram["wdc"].rearrange("(ch o p) q -> o p ch q", o=8, p=128)

    a_ = s2[:, 0:TB]
    b_ = s2[:, TB:2 * TB]
    f_ = s2[:, 2 * TB:3 * TB]
    for tb in range(NTB):
        tsl = slice(tb * TB, (tb + 1) * TB)
        for oc in range(NOC):
            bG = oc % 2
            bU = 2 + oc % 2
            psG, rPsG = psG2[oc % 2], rPsG2[oc % 2]
            psU, rPsU = psU2[oc % 2], rPsU2[oc % 2]
            DMA(c, wb[bG][:, :], wgc2[oc, :, :, :], reads=[rWb[bG]], writes=[rWb[bG]])
            DMA(c, wb[bU][:, :], wuc2[oc, :, :, :], reads=[rWb[bU]], writes=[rWb[bU]])
            for ch in range(8):
                MM(c, psG[:, :], wb[bG][:, ch * 128:(ch + 1) * 128],
                   xq[:, ch * S + tb * TB: ch * S + (tb + 1) * TB],
                   ch == 0, ch == 7, reads=[rWb[bG], rXq], writes=[rPsG])
            for ch in range(8):
                MM(c, psU[:, :], wb[bU][:, ch * 128:(ch + 1) * 128],
                   xq[:, ch * S + tb * TB: ch * S + (tb + 1) * TB],
                   ch == 0, ch == 7, reads=[rWb[bU], rXq], writes=[rPsU])
            osl = slice(oc * TB, (oc + 1) * TB)
            p.op("VE", lambda e, osl=osl, tb=tb, psG=psG: e.scalar_tensor_tensor(
                prod[:, osl], psG[:, :], mbar[:, 0:1], sx[:, S + tb * TB: S + (tb + 1) * TB],
                ALU.mult, ALU.mult), reads=[rPsG, rMbar, rSx], writes=[rProdL[oc]])
            gsl, rGsl = gsl2[oc % 2], rGsl2[oc % 2]
            ACTV(c, gsl[:, :], prod[:, osl], ACT.Silu, reads=[rProdL[oc]], writes=[rGsl])
            TT(c, prod[:, osl], gsl[:, :], psU[:, :], ALU.mult, [rGsl, rPsU],
               [rProdL[oc]])
            if oc == 0:
                p.op("VE", lambda e, osl=osl: e.scalar_tensor_tensor(
                    rmax[:, :], prod[:, osl], -1.0, prod[:, osl], ALU.mult, ALU.max),
                    reads=[rProdL[oc]], writes=[rRmax])
            else:
                p.op("VE", lambda e, osl=osl: e.scalar_tensor_tensor(
                    fab[:, :], prod[:, osl], -1.0, prod[:, osl], ALU.mult, ALU.max),
                    reads=[rProdL[oc]], writes=[rFab])
                TT(c, rmax[:, :], rmax[:, :], fab[:, :], ALU.max,
                   [rFab, rRmax], [rRmax])
        cur = 128
        while cur > 1:
            h = cur // 2
            DMA(c, redr[0:h, 0:TB], rmax[h:cur, :], reads=[rRmax], writes=[rRedr],
                eng="GE")
            TT(c, rmax[0:h, :], rmax[0:h, :], redr[0:h, 0:TB], ALU.max,
               [rRedr, rRmax], [rRmax])
            cur = h
        TS(c, rxr[0:1, 0:TB], rmax[0:1, :], 2.0**-4, None, ALU.mult, ALU.bypass,
           [rRmax], [rRxr])
        CP(c, rowr[0:1, 0:TB], rxr[0:1, 0:TB], reads=[rRxr], writes=[rRowr])
        CP(c, redr[0:1, 0:TB], rowr[0:1, 0:TB], reads=[rRowr], writes=[rRedr])
        TT(c, rxr[0:1, 0:TB], rxr[0:1, 0:TB], redr[0:1, 0:TB], ALU.subtract,
           [rRedr, rRxr], [rRxr])
        CP(c, rowr[0:1, TB:2 * TB], rxr[0:1, 0:TB], reads=[rRxr], writes=[rRowr])
        for j in range(TB // 128):
            MM(c, psBB[:, :], c.t["onesrow"][0:1, :], rowr[0:1, j * 128:(j + 1) * 128],
               True, False, reads=[rRowr, rC], writes=[rPsBB])
            MM(c, psBB[:, :], c.t["onesrow"][0:1, :],
               rowr[0:1, TB + j * 128: TB + (j + 1) * 128],
               False, True, reads=[rRowr, rC], writes=[rPsBB])
            CP(c, rmaxb[:, j * 128:(j + 1) * 128], psBB[:, :], reads=[rPsBB],
               writes=[rRmaxb])
        p.op("VE", lambda e, tb=tb: e.scalar_tensor_tensor(
            a_, rmaxb[:, :], mbar[:, 7:8], sx[:, S + tb * TB: S + (tb + 1) * TB],
            ALU.mult, ALU.mult), reads=[rRmaxb, rMbar, rSx], writes=[rS2])
        TS(c, a_, a_, 1e-5, None, ALU.max, ALU.bypass, [rS2], [rS2])
        RECIP(c, b_, a_, [rS2], [rS2])
        TS(c, b_, b_, 127.0, None, ALU.mult, ALU.bypass, [rS2], [rS2])
        RECIP(c, f_, b_, [rS2], [rS2])
        TS(c, f_, f_, mbar[:, 2:3], None, ALU.mult, ALU.bypass, [rS2, rMbar], [rS2])
        p.op("VE", lambda e, tb=tb: e.scalar_tensor_tensor(
            a_, b_, mbar[:, 1:2], sx[:, S + tb * TB: S + (tb + 1) * TB],
            ALU.mult, ALU.mult), reads=[rS2, rMbar, rSx], writes=[rS2])
        if _os.environ.get("KDBG_SXD"):
            DMA(c, c.dram["outT"][300:301, tsl], rmaxb[0:1, :], reads=[rRmaxb],
                writes=[c.reg("out")])
            DMA(c, c.dram["outT"][301:302, tsl], s2[0:1, TB:2 * TB], reads=[rS2],
                writes=[c.reg("out")])
            DMA(c, c.dram["outT"][302:303, tsl], s2[0:1, 2 * TB:3 * TB], reads=[rS2],
                writes=[c.reg("out")])
            DMA(c, c.dram["outT"][303:304, tsl], s2[0:1, 0:TB], reads=[rS2],
                writes=[c.reg("out")])
        for oc in range(NOC):
            osl = slice(oc * TB, (oc + 1) * TB)
            TT(c, prod[:, osl], prod[:, osl], a_, ALU.mult, [rProdL[oc], rS2],
               [rProdL[oc]])
            round_ip(c, prod[:, osl], [rProdL[oc]], [rProdL[oc]])
            TS(c, pq[:, osl], prod[:, osl], 127.0, -128.0, ALU.min, ALU.max,
               [rProdL[oc]], [rPqL[oc]])
        if tb == 0 and getattr(c, "tap", None) == "ffn":
            DMA(c, c.dram["dbgC"][:, 0:NOC * TB], pq[0:128, 0:NOC * TB],
                reads=rPqL, writes=[c.dreg["dbgC"]])
        if tb == 0 and getattr(c, "tap", None) in ("ffn", "x2pre"):
            DMA(c, c.dram["dbgE"][:, 0:3 * TB], s2[0:128, 0:3 * TB],
                reads=[rS2], writes=[c.dreg["dbgE"]])
        for dc in range(8):
            bu = dc % 2
            DMA(c, rx1[:, bu * TB:(bu + 1) * TB],
                c.dram["x1s"][dc * 128:(dc + 1) * 128, tsl],
                reads=[rRx1[bu], c.dreg["x1s"]], writes=[rRx1[bu]])
            DMA(c, dwb[bu][:, :], wdc2[dc, :, :, :], reads=[rDwb[bu]], writes=[rDwb[bu]])
            psG, rPsG = psG2[dc % 2], rPsG2[dc % 2]
            for ic in range(NOC):
                MM(c, psG[:, :], dwb[bu][:, ic * 128:(ic + 1) * 128],
                   pq[:, ic * TB:(ic + 1) * TB], ic == 0, ic == NOC - 1,
                   reads=[rDwb[bu], rPqL[ic]], writes=[rPsG])
            xsl = slice(dc * S + tb * TB, dc * S + (tb + 1) * TB)
            TT(c, x2[:, xsl], psG[:, :], f_, ALU.mult, [rPsG, rS2], [rX2])
            TT(c, x2[:, xsl], x2[:, xsl], rx1[:, bu * TB:(bu + 1) * TB], ALU.add,
               [rRx1[bu], rX2], [rX2])
    c.free(prod, pq, gsl2[0], gsl2[1], rmax, redr, fab, rxr, rowr, rmaxb, s2, rx1, sx,
           wb[0], wb[1], wb[2], wb[3], dwb[0], dwb[1],
           psU2[1], psU2[0], psG2[1], psG2[0], psBB)



def l1_attn(c, x2, rX2, s16, rS16):
    S = c.S
    TB = min(512, S)
    NTB = S // TB
    NSC = S // 128
    WM = 2 * S - 128
    p = c.p
    rC = c.reg("consts")
    x2_16 = s16
    rX16 = rS16

    ps = c.psum("l1p", [128, TB])
    rPS = c.reg("l1p")

    qt = c.sbuf("l1q", [128, 8 * S], F16)
    rQ = c.reg("l1q")
    wq, rwq = load_w16(c, "wq", "wqT", D, D)
    for oc in range(8):
        for tb in range(NTB):
            for ch in range(8):
                MM(c, ps[:, :], wq[:, ch * D + oc * 128: ch * D + (oc + 1) * 128],
                   x2_16[:, ch * S + tb * TB: ch * S + (tb + 1) * TB],
                   ch == 0, ch == 7, reads=[rwq, rX16], writes=[rPS])
            TS(c, qt[:, oc * S + tb * TB: oc * S + (tb + 1) * TB], ps[:, :],
               0.125, None, ALU.mult, ALU.bypass, [rPS], [rQ])
    c.free(wq)

    kk = [c.sbuf(f"l1k{t}", [128, S], F16) for t in range(2)]
    rKK = [c.reg(f"l1k{t}") for t in range(2)]
    wk, rwk = load_w16(c, "wk", "wkT", D, KVH * DH)
    for t in range(2):
        for tb in range(NTB):
            for ch in range(8):
                MM(c, ps[:, :], wk[:, ch * (KVH * DH) + t * 128: ch * (KVH * DH) + (t + 1) * 128],
                   x2_16[:, ch * S + tb * TB: ch * S + (tb + 1) * TB],
                   ch == 0, ch == 7, reads=[rwk, rX16], writes=[rPS])
            CP(c, kk[t][:, tb * TB:(tb + 1) * TB], ps[:, :], reads=[rPS],
               writes=[rKK[t]], eng="SE")
    c.free(wk)

    vb = [c.sbuf(f"l1v{kv}", [128, NSC * 66], F16) for kv in range(KVH)]
    rVb = [c.reg(f"l1v{kv}") for kv in range(KVH)]
    psV = c.psum("l1vv", [128, 64])
    rPsV = c.reg("l1vv")
    wv, rwv = load_w16(c, "wv", "wvT", D, KVH * DH)
    for kv in range(KVH):
        MEMSET(c, vb[kv][:, 64::66], 1.0, writes=[rVb[kv]])
        for tc in range(NSC):
            for ch in range(8):
                MM(c, psV[:, :], x2_16[:, ch * S + tc * 128: ch * S + (tc + 1) * 128],
                   wv[:, ch * (KVH * DH) + kv * DH: ch * (KVH * DH) + (kv + 1) * DH],
                   ch == 0, ch == 7, reads=[rwv, rX16], writes=[rPsV])
            CP(c, vb[kv][:, tc * 66: tc * 66 + 64], psV[:, :], reads=[rPsV],
               writes=[rVb[kv]], eng="SE")
    c.free(wv, psV)

    sqc = c.sbuf("l1sx", [128, S], F16)
    rSqc = c.reg("l1sx")
    psN = c.psum("l1n", [1, TB])
    rPsN = c.reg("l1n")
    hp = c.t["hpones"]
    kn = c.sbuf("l1kn", [1, 8], F32)
    rKn = c.reg("l1kn")
    for kv in range(KVH):
        t, rb = divmod(kv, 2)
        rb *= 64
        ACTV(c, sqc[rb:rb + 64, :], kk[t][rb:rb + 64, :], ACT.Square,
             reads=[rKK[t]], writes=[rSqc])
        for tb in range(NTB):
            MM(c, psN[0:1, :], hp[rb:rb + 64, rb // 64: rb // 64 + 1],
               sqc[rb:rb + 64, tb * TB:(tb + 1) * TB],
               True, True, reads=[rSqc, rC], writes=[rPsN], tp=(rb, 0))
            slot = kv if tb == 0 else 4 + kv
            p.op("VE", lambda e, slot=slot: e.tensor_reduce(
                kn[0:1, slot: slot + 1], psN[0:1, :], AX.X, ALU.max),
                reads=[rPsN], writes=[rKn])
            if tb > 0:
                TT(c, kn[0:1, kv], kn[0:1, kv], kn[0:1, 4 + kv], ALU.max, [rKn], [rKn])
    ACTV(c, kn[0:1, 0:4], kn[0:1, 0:4], ACT.Sqrt, reads=[rKn], writes=[rKn],
         scale=2.0**-4)
    TS(c, kn[0:1, 4:8], kn[0:1, 0:4], -16.0, None, ALU.mult, ALU.bypass, [rKn], [rKn])
    qn = c.sbuf("l1qn", [1, S], F32)
    rQn = c.reg("l1qn")
    ncr = c.sbuf("l1cr", [1, S], F16)
    rNcr = c.reg("l1cr")
    ebias = c.sbuf("l1eb", [128, 1], F32)
    rEb = c.reg("l1eb")
    MEMSET(c, ebias[:, :], 10.0, writes=[rEb])

    slopes = c.consts["slopes"]
    escr = c.sbuf("l1es", [128, WM], F32)
    rEsc = c.reg("l1es")
    DMA(c, escr[:, :], c.dram["relc"], reads=[rEsc], writes=[rEsc])
    em = c.sbuf("l1e", [128, WM], F16)
    rEm = c.reg("l1e")
    GW = min(512, S)
    NTBG = GW // TB if TB < 512 else 1
    NTG = S // GW
    psS2 = [c.psum(f"l1sc{i}", [128, GW]) for i in range(3)]
    rSc2 = [c.reg(f"l1sc{i}") for i in range(3)]
    psPV2 = [c.psum(f"l1pv{i}", [65, GW]) for i in range(2)]
    rPV2 = [c.reg(f"l1pv{i}") for i in range(2)]
    psB = c.psum("l1bb", [64, GW])
    rB = c.reg("l1bb")
    pexp2 = [c.sbuf(f"l1pe{i}", [128, GW], F16) for i in range(3)]
    rPexp2 = [c.reg(f"l1pe{i}") for i in range(3)]
    dn = c.sbuf("l1dn", [1, 3 * TB], F32)
    rDn = c.reg("l1dn")
    dn16 = c.sbuf("l1d6", [1, TB], F16)
    rDn16 = c.reg("l1d6")
    bcs = c.sbuf("l1bc", [64, TB], F32)
    rBcs = c.reg("l1bcs")
    aodd = c.sbuf("l1ao", [64, TB], F16)
    rAodd = c.reg("l1ao")
    atp = c.sbuf("l1at", [64, TB], F32)
    rAtp = c.reg("l1at")
    onesrow = c.t["onesrow"]
    tails = [None]

    def flush_tail():
        if tails[0] is None:
            return
        ch, half, tb, psPV, rPV = tails[0]
        tails[0] = None
        CP(c, dn[0:1, 0:GW], psPV[64:65, :], reads=[rPV], writes=[rDn])
        ACTV(c, dn[0:1, GW:2 * GW], dn[0:1, 0:GW], ACT.Ln, reads=[rDn],
             writes=[rDn], scale=2.0**-10)
        ACTV(c, dn[0:1, 2 * GW:3 * GW], dn[0:1, GW:2 * GW], ACT.Exp,
             reads=[rDn], writes=[rDn], scale=-0.5)
        TS(c, dn16[0:1, 0:GW], dn[0:1, 2 * GW:3 * GW], 2.0**-5, None,
           ALU.mult, ALU.bypass, [rDn], [rDn16])
        MM(c, psB[0:64, :], onesrow[0:1, 0:64], dn16[0:1, 0:GW], True, True,
           reads=[rDn16, rC], writes=[rB])
        CP(c, bcs[:, 0:GW], psB[0:64, :], reads=[rB], writes=[rBcs], eng="SE")
        TT(c, atp[0:64, 0:GW], psPV[0:64, :], bcs[:, 0:GW], ALU.mult,
           [rPV, rBcs], [rAtp])
        if half == 0:
            TT(c, s16[0:64, ch * S + tb * GW: ch * S + (tb + 1) * GW],
               atp[0:64, 0:GW], bcs[:, 0:GW], ALU.mult, [rAtp, rBcs], [rS16])
        else:
            TT(c, aodd[0:64, 0:GW], atp[0:64, 0:GW], bcs[:, 0:GW], ALU.mult,
               [rAtp, rBcs], [rAodd])
            DMA(c, s16[64:128, ch * S + tb * GW: ch * S + (tb + 1) * GW],
                aodd[0:64, 0:GW], reads=[rAodd], writes=[rS16], eng="GE")

    for h in range(H):
        kv = h % KVH
        ch, half = divmod(h, 2)
        base = half * 64
        ACTV(c, sqc[base:base + 64, :], qt[base:base + 64, ch * S:(ch + 1) * S],
             ACT.Square, reads=[rQ], writes=[rSqc])
        for tb in range(NTB):
            MM(c, psN[0:1, :], hp[base:base + 64, half: half + 1],
               sqc[base:base + 64, tb * TB:(tb + 1) * TB],
               True, True, reads=[rSqc, rC], writes=[rPsN], tp=(base, 0))
            ACTV(c, qn[0:1, tb * TB:(tb + 1) * TB], psN[0:1, :], ACT.Sqrt,
                 reads=[rPsN], writes=[rQn], scale=2.0**-4)
        TS(c, ncr[0:1, :], qn[0:1, 0:S], kn[0:1, 4 + kv], None, ALU.mult,
           ALU.bypass, [rQn, rKn], [rNcr])
        ACTV(c, em[:, :], escr[:, :], ACT.Exp, reads=[rEsc], writes=[rEm],
             scale=float(slopes[h]) * 0.125)
        for tg in range(NTG):
            tb = tg
            pv = (h * NTG + tg) % 2
            psPV, rPV = psPV2[pv], rPV2[pv]
            for sc in range(NSC):
                pi = sc % 3
                psS, rSc = psS2[pi], rSc2[pi]
                pexp, rPexp = pexp2[pi], rPexp2[pi]
                MM(c, psS[:, :],
                   onesrow[0:1, :], ncr[0:1, tb * GW:(tb + 1) * GW],
                   True, False, reads=[rNcr, rC], writes=[rSc])
                MM(c, psS[:, :],
                   kk[kv // 2][base: base + 64, sc * 128:(sc + 1) * 128],
                   qt[base: base + 64, ch * S + tb * GW: ch * S + (tb + 1) * GW],
                   False, True, reads=[rKK[kv // 2], rQ], writes=[rSc], tp=(base, 0))
                ACTV(c, pexp[:, :], psS[:, :], ACT.Exp, reads=[rSc, rEb], writes=[rPexp],
                     bias=ebias[:, 0:1])
                off = tg * GW - sc * 128 + (S - 128)
                TT(c, pexp[:, :], pexp[:, :], em[:, off: off + GW], ALU.mult,
                   [rPexp, rEm], [rPexp])
                if sc == 1:
                    flush_tail()
                if sc >= 1:
                    pp = (sc - 1) % 3
                    MM(c, psPV[0:65, :], vb[kv][:, (sc - 1) * 66: (sc - 1) * 66 + 65],
                       pexp2[pp][:, :], sc - 1 == 0, False,
                       reads=[rVb[kv], rPexp2[pp]], writes=[rPV])
            pp = (NSC - 1) % 3
            MM(c, psPV[0:65, :], vb[kv][:, (NSC - 1) * 66: (NSC - 1) * 66 + 65],
               pexp2[pp][:, :], NSC == 1, True,
               reads=[rVb[kv], rPexp2[pp]], writes=[rPV])
            tails[0] = (ch, half, tb, psPV, rPV)
    flush_tail()
    c.free(pexp2[0], pexp2[1], pexp2[2], dn, dn16, bcs, aodd, atp, em, escr, ebias, ncr,
           qn, kn, sqc, qt, kk[0], kk[1], vb[0], vb[1], vb[2], vb[3],
           psB, psPV2[1], psPV2[0], psS2[2], psS2[1], psS2[0], psN)

    wp, rwp = load_w16(c, "wp", "wpT", D, D)
    bpj = c.sbuf("l1bp", [128, 8], F32)
    rBpj = c.reg("l1bp")
    DMA(c, bpj[:, :], c.dram["bprojT"], reads=[rBpj], writes=[rBpj])
    for dc in range(8):
        for tb in range(NTB):
            for oc in range(8):
                MM(c, ps[:, :], wp[:, oc * D + dc * 128: oc * D + (dc + 1) * 128],
                   s16[:, oc * S + tb * TB: oc * S + (tb + 1) * TB],
                   oc == 0, oc == 7, reads=[rwp, rS16], writes=[rPS])
            p.op("VE", lambda e, dc=dc, tb=tb: e.scalar_tensor_tensor(
                x2[:, dc * S + tb * TB: dc * S + (tb + 1) * TB], ps[:, :],
                bpj[:, dc], x2[:, dc * S + tb * TB: dc * S + (tb + 1) * TB],
                ALU.add, ALU.add), reads=[rPS, rBpj, rX2], writes=[rX2])
    c.free(wp, bpj, ps)



def precache_w12(c):
    rC = c.reg("consts")
    st2 = [c.sbuf(f"f1s{i}", [128, FF], F32) for i in range(2)]
    rSt2 = [c.reg(f"f1s{i}") for i in range(2)]
    t162 = [c.sbuf(f"f1t{i}", [128, FF], F16) for i in range(2)]
    rT162 = [c.reg(f"f1t{i}") for i in range(2)]
    for key, skey, rows, cols in (("w1T", "w1c", D, FF), ("w2T", "w2c", FF, D)):
        for ch in range(rows // 128):
            st, rSt = st2[ch % 2], rSt2[ch % 2]
            t16, rT16 = t162[ch % 2], rT162[ch % 2]
            DMA(c, st[:, 0:cols], c.dram[key][ch * 128:(ch + 1) * 128, :],
                reads=[rSt], writes=[rSt])
            CP(c, t16[:, 0:cols], st[:, 0:cols], reads=[rSt], writes=[rT16], eng="SE")
            DMA(c, c.dram[skey][ch * (cols // 128) * 128:(ch + 1) * (cols // 128) * 128, :]
                .rearrange("(o p) q -> p o q", p=128),
                t16[:, 0:cols], reads=[rT16], writes=[c.dreg[skey]])
    c.free(t162[1], t162[0], st2[1], st2[0])


def l1_ffn(c, x3, rX3, s16, rS16):
    S = c.S
    TB = min(512, S)
    NTB = S // TB
    NOC = FF // 128
    NQ = 8
    QOC = NOC // NQ
    p = c.p
    rC = c.reg("consts")
    cst = c.t["cst"]
    P0 = c.PWL0
    pw = c.consts["pwl"]
    x3_16 = s16
    rX16 = rS16

    kxb = c.sbuf("f1kx", [128, 31], F32)
    rKxb = c.reg("f1kx")
    for k in range(31):
        MEMSET(c, kxb[:, k:k + 1], -float(pw["kx"][k]), writes=[rKxb])
        w1c2 = c.dram["w1c"].rearrange("(ch o p) q -> o p ch q", o=NOC, p=128)
    w2c2 = c.dram["w2c"].rearrange("(ch o p) q -> o p ch q", o=8, p=128)

    b1 = c.sbuf("f1b1", [128, 32], F32)
    rB1 = c.reg("f1b1")
    DMA(c, b1[:, :], c.dram["b1T"], reads=[rB1], writes=[rB1])
    b2 = c.sbuf("f1b2", [128, 8], F32)
    rB2 = c.reg("f1b2")
    DMA(c, b2[:, :], c.dram["b2T"], reads=[rB2], writes=[rB2])

    for ch in range(8):
        DMA(c, c.dram["x3s"][ch * 128:(ch + 1) * 128, :], x3[:, ch * S:(ch + 1) * S],
            reads=[rX3], writes=[c.dreg["x3s"]])

    QW = QOC * TB
    vq = c.sbuf("f1v", [128, QW], F32)
    rV = c.reg("f1v")
    acc = c.sbuf("f1a", [128, QW], F32)
    rA = c.reg("f1a")
    tk16 = [c.sbuf(f"f1k6{i}", [128, QW], F16) for i in range(2)]
    rTk16 = [c.reg(f"f1k6{i}") for i in range(2)]
    tkr = c.sbuf("f1ke", [128, QW], F32)
    rK = c.reg("f1ke")
    idt = c.sbuf("f1id", [128, 128], F16)
    rIdt = c.reg("f1id")
    DMA(c, idt[:, :], c.dram["identc"], reads=[rIdt], writes=[rIdt])
    wdg = c.sbuf("f1wd", [128, 31 * 128], F16)
    rWdg = c.reg("f1wd")
    for k in range(31):
        TS(c, wdg[:, k * 128:(k + 1) * 128], idt[:, :], cst[:, P0 + 2 + k: P0 + 3 + k],
           None, ALU.mult, ALU.bypass, [rIdt, rC], [rWdg])
    psA = c.psum("f1pa", [128, QW])
    rPsA = c.reg("f1pa")
    hq = c.sbuf("f1h", [128, NOC * TB], F16)
    rHq = c.reg("f1h")
    wb = [c.sbuf(f"f1w{i}", [128, 8 * 128], F16) for i in range(2)]
    rWb = [c.reg(f"f1w{i}") for i in range(2)]
    w2b = [c.sbuf(f"f1u{i}", [128, NOC * 128], F16) for i in range(2)]
    rW2b = [c.reg(f"f1u{i}") for i in range(2)]
    rx3 = c.sbuf("f1r", [128, 2 * TB], F32)
    rRx3 = [c.reg("f1r0"), c.reg("f1r1")]
    x4 = x3
    rX4 = rX3
    psH2 = [c.psum(f"f1p{i}", [128, TB]) for i in range(2)]
    rPsH2 = [c.reg(f"f1p{i}") for i in range(2)]

    for tb in range(NTB):
        tsl = slice(tb * TB, (tb + 1) * TB)
        for qh in range(NQ):
            for j in range(QOC):
                oc = qh * QOC + j
                b = oc % 2
                psH, rPsH = psH2[oc % 2], rPsH2[oc % 2]
                DMA(c, wb[b][:, :], w1c2[oc, :, :, :], reads=[rWb[b]], writes=[rWb[b]])
                for ch in range(8):
                    MM(c, psH[:, :], wb[b][:, ch * 128:(ch + 1) * 128],
                       x3_16[:, ch * S + tb * TB: ch * S + (tb + 1) * TB],
                       ch == 0, ch == 7, reads=[rWb[b], rX16], writes=[rPsH])
                TS(c, vq[:, j * TB:(j + 1) * TB], psH[:, :], b1[:, oc], 256.0,
                   ALU.add, ALU.mult, [rPsH, rB1], [rV])
            round_ip(c, vq[:, :], [rV], [rV])
            TS(c, acc[:, :], vq[:, :], cst[:, P0 + 1: P0 + 2], cst[:, P0: P0 + 1],
               ALU.mult, ALU.add, [rV, rC], [rA])
            KT = 21
            for k in range(31):
                tk, rTk = tk16[k % 2], rTk16[k % 2]
                ACTV(c, tk[:, :], vq[:, :], ACT.Relu, reads=[rV, rC], writes=[rTk],
                     bias=kxb[:, k:k + 1])
                if k < KT:
                    for blk in range(QW // 512):
                        MM(c, psA[:, blk * 512:(blk + 1) * 512],
                           wdg[:, k * 128:(k + 1) * 128],
                           tk[:, blk * 512:(blk + 1) * 512], k == 0, k == KT - 1,
                           reads=[rWdg, rTk], writes=[rPsA])
                else:
                    p.op("VE", lambda e, k=k, tk=tk: e.scalar_tensor_tensor(
                        acc[:, :], tk[:, :], cst[:, P0 + 2 + k: P0 + 3 + k], acc[:, :],
                        ALU.mult, ALU.add), reads=[rTk, rC, rA], writes=[rA])
            TT(c, acc[:, :], acc[:, :], psA[:, :], ALU.add, [rPsA, rA], [rA])
            TS(c, tkr[:, :], vq[:, :], 1024.0, -1024.0, ALU.max, ALU.add, [rV], [rK])
            TS(c, tkr[:, :], tkr[:, :], cst[:, P0 + 33: P0 + 34], None, ALU.mult,
               ALU.bypass, [rK, rC], [rK])
            TT(c, acc[:, :], acc[:, :], tkr[:, :], ALU.add, [rK, rA], [rA])
            TS(c, tkr[:, :], vq[:, :], 1024.0, cst[:, P0 + 34: P0 + 35], ALU.is_gt,
               ALU.mult, [rV, rC], [rK])
            TT(c, acc[:, :], acc[:, :], tkr[:, :], ALU.add, [rK, rA], [rA])
            TS(c, tkr[:, :], vq[:, :], -1024.0, 1024.0, ALU.min, ALU.add, [rV], [rK])
            TS(c, tkr[:, :], tkr[:, :], cst[:, P0 + 35: P0 + 36], None, ALU.mult,
               ALU.bypass, [rK, rC], [rK])
            TT(c, acc[:, :], acc[:, :], tkr[:, :], ALU.add, [rK, rA], [rA])
            TS(c, tkr[:, :], vq[:, :], -1024.0, cst[:, P0 + 37: P0 + 38], ALU.is_lt,
               ALU.mult, [rV, rC], [rK])
            TT(c, acc[:, :], acc[:, :], tkr[:, :], ALU.add, [rK, rA], [rA])
            TS(c, acc[:, :], acc[:, :], 256.0, None, ALU.mult, ALU.bypass, [rA], [rA])
            round_ip(c, acc[:, :], [rA], [rA])
            TS(c, hq[:, qh * QW:(qh + 1) * QW], acc[:, :], 2.0**-8, None, ALU.mult,
               ALU.bypass, [rA], [rHq])
        for dc in range(8):
            bu = dc % 2
            DMA(c, rx3[:, bu * TB:(bu + 1) * TB],
                c.dram["x3s"][dc * 128:(dc + 1) * 128, tsl],
                reads=[rRx3[bu], c.dreg["x3s"]], writes=[rRx3[bu]])
            DMA(c, w2b[bu][:, :], w2c2[dc, :, :, :], reads=[rW2b[bu]], writes=[rW2b[bu]])
            psH, rPsH = psH2[dc % 2], rPsH2[dc % 2]
            for ic in range(NOC):
                MM(c, psH[:, :], w2b[bu][:, ic * 128:(ic + 1) * 128],
                   hq[:, ic * TB:(ic + 1) * TB], ic == 0, ic == NOC - 1,
                   reads=[rW2b[bu], rHq], writes=[rPsH])
            xsl = slice(dc * S + tb * TB, dc * S + (tb + 1) * TB)
            TS(c, x4[:, xsl], psH[:, :], b2[:, dc], 256.0,
               ALU.add, ALU.mult, [rPsH, rB2], [rX4])
            round_ip(c, x4[:, xsl], [rX4], [rX4])
            p.op("VE", lambda e, xsl=xsl, bu=bu: e.scalar_tensor_tensor(
                x4[:, xsl], x4[:, xsl], 2.0**-8, rx3[:, bu * TB:(bu + 1) * TB],
                ALU.mult, ALU.add), reads=[rRx3[bu], rX4], writes=[rX4])
    c.free(vq, acc, tkr, tk16[0], tk16[1], idt, wdg, hq,
           wb[0], wb[1], w2b[0], w2b[1], rx3, b1, b2, kxb,
           psA, psH2[1], psH2[0])



def _flex(c, S, n32, n16):
    if S < 2048:
        n32 = max(n32, 24576)
        n16 = max(n16, 49152)
    c.open_flex(n32=int(n32), n16=int(n16))


def build_program(S=2048, tap=None):
    consts = _host_consts(S)
    nc = bass.Bass("TRN2", target_bir_lowering=False)
    c = Ctx(nc, S, consts)

    def din(name, shape, dtype=F32):
        c.dram[name] = nc.dram_tensor(name, list(shape), dtype, kind="ExternalInput").ap()
        c.dreg[name] = c.reg("d_" + name)

    def dscr(name, shape, dtype):
        c.dram[name] = nc.dram_tensor(name, list(shape), dtype, kind="Internal").ap()
        c.dreg[name] = c.reg("d_" + name)

    din("xT", (D, S))
    out_ap = nc.dram_tensor("outT", [D, S], F32, kind="ExternalOutput").ap()
    c.dram["outT"] = out_ap
    din("wdqT", (D, DLAT)); din("wuqT", (DLAT, H * DH)); din("wqrT", (DLAT, H * DHR))
    din("wdkvT", (D, DLAT)); din("wukT", (DLAT, H * DH)); din("wuvT", (DLAT, H * DH))
    din("wkrT", (D, DHR)); din("woT", (H * DH, D))
    din("wgT", (D, FF)); din("wuT", (D, FF)); din("wdT", (FF, D))
    din("wqT", (D, D)); din("wkT", (D, KVH * DH)); din("wvT", (D, KVH * DH))
    din("wpT", (D, D)); din("bprojT", (128, 8))
    din("w1T", (D, FF)); din("b1T", (128, 32)); din("w2T", (FF, D)); din("b2T", (128, 8))
    din("cosb", (128, S)); din("sinb", (128, S))
    din("m01c", (128, 128), F16)
    din("relc", (128, 2 * S - 128))
    din("identc", (128, 128), F16)
    din("sel2", (2, 256))
    dscr("wgc", (8 * 32 * 128, 128), F16)
    dscr("wuc", (8 * 32 * 128, 128), F16)
    dscr("wdc", (32 * 8 * 128, 128), F16)
    dscr("w1c", (8 * 32 * 128, 128), F16)
    dscr("w2c", (32 * 8 * 128, 128), F16)
    dscr("x1s", (D, S), F32)
    dscr("dbgA", (128, 8 * S), F16)
    dscr("dbgB", (128, 2 * S), F32)
    dscr("dbgC", (128, 64 * 256), F16)
    dscr("dbgD", (128, 64), F32)
    dscr("dbgE", (128, 2048), F32)
    dscr("dbgX", (D, S), F32)
    dscr("x3s", (D, S), F32)

    setup_consts(c)
    _kph = _os.environ.get("KPH")
    _phset = set(_kph.split(",")) if _kph else None
    _ph = lambda nm: _phset is None or nm in _phset
    s16 = c.perm("stream16", [128, 8 * S], F16)
    rS16 = c.reg("S16")

    if _ph("A"):
        _flex(c, S, 4 * S + 5888, 29 * S + 1024)
        l0_attn(c, s16, rS16)
        c.close_flex()

    s32 = c.perm("stream32", [128, 8 * S], F32)
    rS32 = c.reg("S32")

    if _ph("B"):
        _flex(c, S, 2 * min(512, S) + 4 * FF + 1280, 4 * S + 2 * FF + 2048)
        l0_wo(c, s16, rS16, s32, rS32)
        prequant(c)
        precache_w12(c)
        c.close_flex()

    if _ph("L1"):
        _flex(c, S, 4 * min(512, S) + 512, 16 * S + 4 * min(512, S) + 1024)
        post_ln(c, s32, rS32, None, None, "a")
        c.close_flex()

    c.tap = tap
    if _ph("C"):
        _flex(c, S, 2 * S + 2 * FF + 3072, 64 * min(256, S) + FF + 2048)
        l0_ffn(c, s32, rS32, s16, rS16)
        c.close_flex()

    if tap == "x2pre":
        for ch in range(8):
            DMA(c, c.dram["dbgX"][ch * 128:(ch + 1) * 128, :], s32[:, ch * S:(ch + 1) * S],
                reads=[rS32], writes=[c.dreg["dbgX"]])
    if _ph("L2"):
        _flex(c, S, 4 * min(512, S) + 512, 16 * S + 4 * min(512, S) + 1024)
        post_ln(c, s32, rS32, s16, rS16, "b")
        c.close_flex()
    if tap == "x2":
        for ch in range(8):
            DMA(c, c.dram["dbgX"][ch * 128:(ch + 1) * 128, :], s32[:, ch * S:(ch + 1) * S],
                reads=[rS32], writes=[c.dreg["dbgX"]])

    if _ph("D"):
        _flex(c, S, 2 * S + 4608, 17 * S + 1024)
        l1_attn(c, s32, rS32, s16, rS16)
        c.close_flex()

    if _ph("L3"):
        _flex(c, S, 4 * min(512, S) + 512, 16 * S + 4 * min(512, S) + 1024)
        post_ln(c, s32, rS32, s16, rS16, "c")
        c.close_flex()

    if _ph("E"):
        _flex(c, S, 8 * min(512, S) + FF + 1536, 38 * min(512, S) + FF + 12288)
        l1_ffn(c, s32, rS32, s16, rS16)
        c.close_flex()

    if _ph("L4"):
        _flex(c, S, 4 * min(512, S) + 512, 16 * S + 4 * min(512, S) + 1024)
        post_ln(c, s32, rS32, None, None, "d")
        c.close_flex()

    rout = c.reg("out")
    if _ph("OUT") or _phset is None:
        for ch in range(8):
            DMA(c, out_ap[ch * 128:(ch + 1) * 128, :], s32[:, ch * S:(ch + 1) * S],
                reads=[rS32], writes=[rout])
    c.p.op("SQ", lambda e: e.nop(), reads=[rout], writes=[])

    c.p.check_deadlock()
    sems = {}
    cms = []
    for e in Prog.SEMS:
        cm = nc.semaphore(f"sem_{e}")
        sems[e] = cm.__enter__()
        cms.append(cm)
    bcm = nc.Block()
    block = bcm.__enter__()
    c.p.emit(block, sems)
    bcm.__exit__(None, None, None)
    for cm in cms:
        cm.__exit__(None, None, None)
    return nc


def _prep_inputs(S, consts, **inp):
    T = lambda a: np.ascontiguousarray(np.asarray(a, dtype=np.float32).T)
    wqrT = T(inp["L0_attn_W_QR"])
    perm = [h * 32 + 2 * f + half for half in range(2) for h in range(H) for f in range(16)]
    wqrT = np.ascontiguousarray(wqrT[:, perm])
    wkrT = T(inp["L0_attn_W_KR"])
    kperm = [2 * f for f in range(16)] + [2 * f + 1 for f in range(16)]
    wkrT = np.ascontiguousarray(wkrT[:, kperm])
    wkv = np.asarray(inp["L1_attn_Wkv"], dtype=np.float32)
    wkT = T(wkv[0:256, :])
    base = {
        "wdqT": T(inp["L0_attn_W_DQ"]), "wuqT": T(inp["L0_attn_W_UQ"]),
        "wqrT": wqrT, "wdkvT": T(inp["L0_attn_W_DKV"]),
        "wukT": T(inp["L0_attn_W_UK"]), "wuvT": T(inp["L0_attn_W_UV"]),
        "wkrT": wkrT, "woT": T(inp["L0_attn_W_O"]),
        "wgT": T(inp["L0_ffn_w_gate"]), "wuT": T(inp["L0_ffn_w_up"]),
        "wdT": T(inp["L0_ffn_w_down"]),
        "wqT": T(inp["L1_attn_Wq"]), "wkT": np.ascontiguousarray(wkT),
        "wvT": T(wkv[256:512, :]), "wpT": T(inp["L1_attn_Wproj"]),
        "bprojT": np.ascontiguousarray(
            np.asarray(inp["L1_attn_bproj"], dtype=np.float32).reshape(8, 128).T),
        "w1T": np.ascontiguousarray(np.asarray(inp["L1_ffn_w1"], dtype=np.float32)),
        "b1T": np.ascontiguousarray(
            np.asarray(inp["L1_ffn_b1"], dtype=np.float32).reshape(32, 128).T),
        "w2T": np.ascontiguousarray(np.asarray(inp["L1_ffn_w2"], dtype=np.float32)),
        "b2T": np.ascontiguousarray(
            np.asarray(inp["L1_ffn_b2"], dtype=np.float32).reshape(8, 128).T),
        "cosb": consts["cosb"], "sinb": consts["sinb"], "m01c": consts["m01"],
        "relc": consts["relc"], "identc": consts["ident"],
        "sel2": np.concatenate([np.concatenate([np.ones((1, 128), np.float32), np.zeros((1, 128), np.float32)], axis=1),
                                np.concatenate([np.zeros((1, 128), np.float32), np.ones((1, 128), np.float32)], axis=1)], axis=0),
    }
    x = np.asarray(inp["L0_attn_x"], dtype=np.float32)
    return [dict(base, xT=np.ascontiguousarray(x[b].T)) for b in range(x.shape[0])]


def kernel(**inputs):
    S = int(inputs["L0_attn_x"].shape[1])
    consts = _host_consts(S)
    in_maps = _prep_inputs(S, consts, **inputs)
    nc = build_program(S)
    res = run_bass_kernel_spmd(nc, in_maps, core_ids=list(range(len(in_maps))))
    outs = [np.ascontiguousarray(np.asarray(r["outT"]).T) for r in res.results]
    return np.stack(outs, axis=0).astype(np.float32)
